# version 18
# baseline (speedup 1.0000x reference)
import numpy as np
import ml_dtypes
import concourse.bass as bass
import concourse.mybir as mybir
from concourse.bass_utils import run_bass_kernel_spmd

F32 = mybir.dt.float32
BF16 = mybir.dt.bfloat16
I32 = mybir.dt.int32
AF = mybir.ActivationFunctionType
ALU = mybir.AluOpType
AXX = mybir.AxisListType.X

B, L, D = 4, 8192, 1024
HALF = L // 2
DC = 8
G0 = 8
TG = 1024
S8 = 8
CH = 128
WCOL = 1538
GW = 4
NST = 8
TS = 512
HB = 24
CMAGIC = 12582912.0
ENGS = ("tensor", "vector", "scalar", "gpsimd", "sync")

DEBUG = False
L0_SUB = 9
SC_SUB = 9
L1_SUB = 9
L0_STAGE = 99


class Res:
    __slots__ = ("w", "r", "pend")
    def __init__(self):
        self.w = None
        self.r = []
        self.pend = []


class Bld:
    def __init__(self, nc, stack):
        self.nc = nc
        self.q = {e: [] for e in ENGS}
        self.scnt = {}
        self.waited = {e: {} for e in ENGS}
        self.esem = {}
        for e in ENGS:
            h = stack.enter_context(nc.semaphore("se_" + e))
            self.esem[e] = h
            self.scnt[id(h)] = 0
            self._hmap = getattr(self, "_hmap", {})
            self._hmap[id(h)] = h
        self._stack = stack
        self._chan = {}

    def _wait(self, eng, tok):
        if tok is None:
            return
        sem, val, peng = tok
        if peng == eng:
            pass
        if self.waited[eng].get(id(sem), 0) >= val:
            return
        self.waited[eng][id(sem)] = val
        self.q[eng].append(lambda h, s=sem, v=val: h.wait_ge(s, v))

    def _hazard(self, eng, reads, writes):
        toks = []
        for r in reads:
            toks.append(r.w)
        for w in writes:
            toks.append(w.w)
            toks.extend(w.r)
        for t in toks:
            self._wait(eng, t)

    def op(self, eng, fn, reads=(), writes=()):
        self._hazard(eng, reads, writes)
        sem = self.esem[eng]
        self.scnt[id(sem)] += 1
        cnt = self.scnt[id(sem)]
        def run(h, fn=fn, sem=sem):
            ins = fn(h)
            ins.then_inc(sem, 1)
        self.q[eng].append(run)
        tok = (sem, cnt, eng)
        for r in reads:
            r.r.append(tok)
        for w in writes:
            w.w = tok
            w.r = []
        return tok

    def dma(self, eng, ch, out_ap, in_ap, reads=(), writes=(), **kw):
        if ch not in self._chan:
            h = self._stack.enter_context(self.nc.semaphore("q_" + ch))
            self._chan[ch] = h
            self.scnt[id(h)] = 0
            self._hmap[id(h)] = h
        qsem = self._chan[ch]
        prev = self.scnt[id(qsem)]
        if prev:
            self._wait(eng, (qsem, prev, None))
        self._hazard(eng, reads, writes)
        self.scnt[id(qsem)] = prev + 16
        cnt = prev + 16
        def run(h, o=out_ap, i=in_ap, qs=qsem, kw=kw):
            h.dma_start(o, i, **kw).then_inc(qs, 16)
        self.q[eng].append(run)
        tok = (qsem, cnt, None)
        for r in reads:
            r.r.append(tok)
        for w in writes:
            w.w = tok
            w.r = []
        return tok

    def barrier(self):
        sems = [(self._hmap[k], v) for k, v in self.scnt.items()]
        for e in ENGS:
            for sem, val in sems:
                if val > 0:
                    self._wait(e, (sem, val, None))

    def cp(self, eng, out, in_, reads, writes):
        if eng == "scalar":
            return self.op(eng, lambda h: h.copy(out, in_), reads, writes)
        return self.op(eng, lambda h: h.tensor_copy(out, in_), reads, writes)

    def tt(self, eng, out, a, bb, alu, reads, writes):
        return self.op(eng, lambda h: h.tensor_tensor(out, a, bb, alu), reads, writes)

    def ts(self, eng, out, in0, s1, s2, op0, op1, reads, writes, accum=None):
        if eng == "scalar":
            assert op0 == ALU.mult and s2 is None and accum is None
            return self.op(eng, lambda h: h.activation(out, in0, AF.Identity,
                                                       scale=s1), reads, writes)
        if s2 is None and accum is None:
            return self.op(eng, lambda h: h.tensor_single_scalar(out, in0, s1, op0),
                           reads, writes)
        return self.op(eng, lambda h: h.tensor_scalar(out, in0, s1, s2, op0, op1,
                                                      accum_out=accum), reads, writes)

    def stt(self, eng, out, in0, sc, in1, op0, op1, reads, writes):
        return self.op(eng, lambda h: h.scalar_tensor_tensor(out, in0, sc, in1, op0, op1),
                       reads, writes)

    def act(self, out, in_, func, reads, writes, bias=0.0, scale=1.0, accum=None):
        return self.op("scalar",
                       lambda h: h.activation(out, in_, func, bias=bias, scale=scale,
                                              accum_out=accum), reads, writes)

    def red(self, out, in_, alu, reads, writes, absval=False):
        return self.op("vector",
                       lambda h: h.tensor_reduce(out, in_, AXX, alu,
                                                 apply_absolute_value=absval),
                       reads, writes)

    def mm(self, bank, out, lhsT, rhs, start, stop, reads, first, last):
        if first:
            self._hazard("tensor", (), (bank,))
            bank.pend = []
        self._hazard("tensor", reads, ())
        bank.pend.extend(reads)
        if not last:
            def run0(h, o=out, lt=lhsT, r=rhs, st=start, sp=stop):
                h.matmul(o, lt, r, start=st, stop=sp)
            self.q["tensor"].append(run0)
            return None
        sem = self.esem["tensor"]
        self.scnt[id(sem)] += 1
        cnt = self.scnt[id(sem)]
        def run(h, o=out, lt=lhsT, r=rhs, st=start, sp=stop, sem=sem):
            h.matmul(o, lt, r, start=st, stop=sp).then_inc(sem, 1)
        self.q["tensor"].append(run)
        tok = (sem, cnt, "tensor")
        for rr in bank.pend:
            rr.r.append(tok)
        bank.pend = []
        bank.w = tok
        bank.r = []
        return tok


def build_nc(debug=False, n_g0=G0, do_w=True, do_l1=True):
    global L0_STAGE, L0_SUB, SC_SUB, L1_SUB
    nc = bass.Bass("TRN2", target_bir_lowering=False)
    import contextlib
    stack = contextlib.ExitStack()

    d_in = {}
    def din(name, shape, dt):
        d_in[name] = nc.dram_tensor(name, shape, dt, kind="ExternalInput")
        return d_in[name]
    x_full = din("x_full", [L, D], F32)
    x_own = din("x_own", [HALF, D], F32)
    wproj = din("wproj", [D, WCOL], F32)
    wo_in = din("wo", [D, D], F32)
    convw_in = din("convw", [128, 32], F32)
    normw_in = din("normw", [1, 512], F32)
    packed_in = din("packed", [256, D], I32)
    wscale_in = din("wscale", [1, 1], F32)
    bias_in = din("bias", [1, D], F32)
    ar_in = din("Ar", [128, 1], F32)
    ai_in = din("Ai", [128, 1], F32)
    bmat_in = din("Bmat", [128, D], F32)
    cmat_in = din("Cmat", [D, 128], F32)
    wg_in = din("wg", [3072, D], I32)
    wu_in = din("wu", [3072, D], I32)
    wd_in = din("wd", [D, 3072], I32)
    idn_in = din("idn", [128, 128], BF16)
    masks_in = din("masks", [128, 3 * 128], BF16)
    ones512_in = din("ones512", [128, 512], BF16)
    onescol_in = din("onescol", [1, 128], BF16)
    pcsc_in = din("pcsc", [128, 4], F32)
    kconst_in = din("kconst", [128, 6], F32)

    out_d = nc.dram_tensor("out", [HALF, D], F32, kind="ExternalOutput")
    dbg = {}
    if debug:
        for nm, shp, dt in (("d_hT", [128, DC * TG], BF16),
                            ("d_qkg", [128, S8 * 512], BF16),
                            ("d_gated", [L, 512], BF16),
                            ("d_x1", [128, 32 * D], BF16),
                            ("d_g0", [128, D], BF16),
                            ("d_xint", [128, D], BF16),
                            ("d_ffnsb", [128, D], F32),
                            ("d_ffnraw", [128, D], F32),
                            ("d_biasb", [128, D], F32),
                            ("d_qcol", [128, 8], F32),
                            ("d_s0", [128, D], F32),
                            ("d_hre", [128, 4096], BF16),
                            ("d_y", [128, 4 * D], BF16),
                            ("d_sb", [128, 512], BF16)):
            dbg[nm] = nc.dram_tensor(nm, shp, dt, kind="ExternalOutput")

    gat_in = nc.dram_tensor("gat_in", [L, 512], BF16)
    gat_out = nc.dram_tensor("gat_out", [2 * L, 512], BF16)
    e_in = nc.dram_tensor("e_in", [128, 2], F32)
    e_out = nc.dram_tensor("e_out", [256, 2], F32)
    wgt_d = nc.dram_tensor("wgt_d", [HB * DC * 128, 128], BF16)
    wut_d = nc.dram_tensor("wut_d", [HB * DC * 128, 128], BF16)
    wdt_d = nc.dram_tensor("wdt_d", [HB * 128, D], BF16)
    r_gin, r_gout = Res(), Res()
    r_ein, r_eout = Res(), Res()
    r_wgt, r_wut, r_wdt = Res(), Res(), Res()
    r_out = Res()

    bld = Bld(nc, stack)
    st_pro = contextlib.ExitStack()
    st_lw = contextlib.ExitStack()
    st_l0 = contextlib.ExitStack()
    st_w = contextlib.ExitStack()
    st_l1 = contextlib.ExitStack()

    def sb(name, shape, dt, stk=None):
        h = (stk or stack).enter_context(nc.sbuf_tensor("s_" + name, shape, dt))
        return h, Res()

    def pb(name, cols=512):
        h = stack.enter_context(nc.psum_tensor(name, [128, cols], F32))
        return h, Res()

    idn, r_idn = sb("idn", [128, 128], BF16)
    msk, r_msk = sb("msk", [128, 3 * 128], BF16)
    ones512, r_ones = sb("ones512", [128, 512], BF16)
    onescol, r_onescol = sb("onescol", [1, 128], BF16)
    pcsc, r_pcsc = sb("pcsc", [128, 4], F32)
    kconst, r_kc = sb("kconst", [128, 6], F32)
    x1_d = nc.dram_tensor("x1_d", [HALF, D], BF16)
    r_x1 = Res()
    x1_dr = x1_d.ap().rearrange("(t p) d -> t p d", p=128)
    stat_r, r_statr = sb("stat_r", [128, 64], F32)
    stat_m, r_statm = sb("stat_m", [128, 64], F32)
    statw_r, r_statwr = sb("statw_r", [128, 32], F32)
    statw_m, r_statwm = sb("statw_m", [128, 32], F32)
    stat1_r, r_stat1r = sb("stat1_r", [128, 32], F32)
    stat1_m, r_stat1m = sb("stat1_m", [128, 32], F32)
    pm_re, r_pmre = sb("pm_re", [128, 512], F32)
    pm_im, r_pmim = sb("pm_im", [128, 512], F32)
    pp_re, r_ppre = sb("pp_re", [128, 512], F32)
    pp_im, r_ppim = sb("pp_im", [128, 512], F32)
    are_c, r_are = sb("are_c", [128, 1], F32)
    aim_c, r_aim = sb("aim_c", [128, 1], F32)
    ct_t, r_ct = sb("ct_t", [128, D], BF16)
    bdt_re, r_bdtre = sb("bdt_re", [128, DC * 128], BF16)
    bdt_im, r_bdtim = sb("bdt_im", [128, DC * 128], BF16)
    trash, r_trash = sb("trash", [128, 1024], BF16)
    sums_g, r_sumsg = sb("sums_g", [128, S8], F32)
    sqs_g, r_sqsg = sb("sqs_g", [128, S8], F32)
    sum1_g, r_sum1g = sb("sum1_g", [128, S8], F32)
    sq1_g, r_sq1g = sb("sq1_g", [128, S8], F32)
    wst = [sb(f"wst{i}", [128, 3072], BF16) for i in range(2)]
    wtt = [sb(f"wtt{i}", [128, 3072], BF16) for i in range(2)]
    scr, r_scr = sb("scr", [128, 16], F32)
    scr2, r_scr2 = sb("scr2", [128, 16], F32)

    pbank = []
    for i in range(8):
        pbank.append(pb(f"pb{i}"))
    PB = [p[0] for p in pbank]
    RB = [p[1] for p in pbank]

    MSL = msk[:, 0:128]
    MSU = msk[:, 128:256]
    MTI = msk[:, 256:384]

    rot_state = [0]
    def rotbank():
        i = rot_state[0]
        rot_state[0] = (i + 1) % 3
        return PB[i], RB[i]

    def evac(eng, out_ap, bank_res, psum_ap, writes, scale=None):
        if scale is None:
            return bld.cp(eng, out_ap, psum_ap, [bank_res], writes)
        return bld.act(out_ap, psum_ap, AF.Copy, [bank_res], writes, scale=scale)

    def transpose128(src_ap, src_res, dst_ap, dst_res, rhs_ap=None, rhs_res=None,
                     eng="vector"):
        bk, bkr = rotbank()
        r = [src_res] + ([rhs_res] if rhs_res is not None else [r_idn])
        bld.mm(bkr, bk[:, 0:128], src_ap, rhs_ap if rhs_ap is not None else idn[:, :],
               True, True, r, first=True, last=True)
        evac(eng, dst_ap, bkr, bk[:, 0:128], [dst_res])

    bld.dma("sync", "cst", idn[:, :], idn_in.ap(), (), (r_idn,))
    bld.dma("sync", "cst2", msk[:, :], masks_in.ap(), (), (r_msk,))
    bld.dma("sync", "cst3", ones512[:, :], ones512_in.ap(), (), (r_ones,))
    bld.dma("sync", "cst4", onescol[:, :], onescol_in.ap(), (), (r_onescol,))
    bld.dma("sync", "cst5", pcsc[:, :], pcsc_in.ap(), (), (r_pcsc,))
    bld.dma("sync", "cst6", kconst[:, :], kconst_in.ap(), (), (r_kc,))

    arc, r_arc = sb("arc", [128, 1], F32, st_pro)
    aic, r_aic = sb("aic", [128, 1], F32, st_pro)
    bld.dma("sync", "ar", arc[:, :], ar_in.ap(), (), (r_arc,))
    bld.dma("sync", "ai", aic[:, :], ai_in.ap(), (), (r_aic,))
    bmat, r_bmat = sb("bmat", [128, D], F32, st_pro)
    bld.dma("sync", "bm", bmat[:, :], bmat_in.ap(), (), (r_bmat,))
    cmat_b, r_cmatb = sb("cmat_b", [128, DC * 128], BF16, st_pro)
    bld.dma("gpsimd", "cm", cmat_b[:, :].rearrange("p (c w) -> p c w", w=128),
            cmat_in.ap().rearrange("(c p) w -> p c w", p=128), (), (r_cmatb,))

    pstage, r_pst = sb("pstage", [128, 2 * D], I32, st_pro)
    bld.dma("sync", "pk", pstage[:, :].rearrange("p (c w) -> p c w", w=D),
            packed_in.ap().rearrange("(c p) w -> p c w", p=128), (), (r_pst,))
    tern, r_tern = sb("tern", [128, DC * D], BF16, st_pro)
    ei, r_ei = sb("ei", [128, 2 * D], I32, st_pro)
    for sh in range(4):
        if sh == 0:
            bld.ts("vector", ei[:, :], pstage[:, :], 3, None,
                   ALU.bitwise_and, ALU.bypass, [r_pst], [r_ei])
        else:
            bld.ts("vector", ei[:, :], pstage[:, :], 2 * sh, 3,
                   ALU.logical_shift_right, ALU.bitwise_and, [r_pst], [r_ei])
        bld.ts("vector", tern[:, 2 * sh * D:(2 * sh + 2) * D], ei[:, :], -1.0, None,
               ALU.add, ALU.bypass, [r_ei], [r_tern])
    ternT_d = nc.dram_tensor("ternT_d", [D, D], BF16)
    r_ternTd = Res()
    tstg, r_tstg = sb("tstg", [128, D], BF16, st_pro)
    ternTd_r = ternT_d.ap().rearrange("(ic p) o -> ic p o", p=128)
    for ic in range(DC):
        for oc in range(DC):
            transpose128(tern[:, oc * D + ic * 128: oc * D + (ic + 1) * 128], r_tern,
                         tstg[:, oc * 128:(oc + 1) * 128], r_tstg,
                         eng=("vector" if (oc + ic) % 2 == 0 else "scalar"))
        bld.dma("sync", "ttd", ternTd_r[ic], tstg[:, :], [r_tstg], [r_ternTd])

    dre, r_dre = sb("dre", [128, 1], F32, st_pro)
    dim, r_dim = sb("dim", [128, 1], F32, st_pro)
    nre, r_nre = sb("nre", [128, 1], F32, st_pro)
    nim, r_nim = sb("nim", [128, 1], F32, st_pro)
    ire, r_ire = sb("ire", [128, 1], F32, st_pro)
    iim, r_iim = sb("iim", [128, 1], F32, st_pro)
    tmp1, r_tmp1 = sb("tmp1", [128, 4], F32, st_pro)
    bld.ts("vector", dre[:, :], arc[:, :], -0.005, 1.0, ALU.mult, ALU.add,
           [r_arc], [r_dre])
    bld.ts("vector", dim[:, :], aic[:, :], -0.005, None, ALU.mult, ALU.bypass,
           [r_aic], [r_dim])
    bld.ts("vector", nre[:, :], arc[:, :], 0.005, 1.0, ALU.mult, ALU.add,
           [r_arc], [r_nre])
    bld.ts("vector", nim[:, :], aic[:, :], 0.005, None, ALU.mult, ALU.bypass,
           [r_aic], [r_nim])
    bld.tt("vector", tmp1[:, 0:1], dre[:, :], dre[:, :], ALU.mult, [r_dre], [r_tmp1])
    bld.stt("vector", tmp1[:, 1:2], dim[:, :], 1.0, dim[:, :], ALU.mult, ALU.mult,
            [r_dim], [r_tmp1])
    bld.tt("vector", tmp1[:, 0:1], tmp1[:, 0:1], tmp1[:, 1:2], ALU.add,
           [r_tmp1], [r_tmp1])
    bld.op("vector", lambda h: h.reciprocal(tmp1[:, 2:3], tmp1[:, 0:1]),
           [r_tmp1], [r_tmp1])
    bld.tt("vector", ire[:, :], dre[:, :], tmp1[:, 2:3], ALU.mult,
           [r_dre, r_tmp1], [r_ire])
    bld.stt("vector", iim[:, :], dim[:, :], -1.0, tmp1[:, 2:3], ALU.mult, ALU.mult,
            [r_dim, r_tmp1], [r_iim])
    bld.tt("vector", tmp1[:, 0:1], nre[:, :], ire[:, :], ALU.mult, [r_nre, r_ire], [r_tmp1])
    bld.tt("vector", tmp1[:, 1:2], nim[:, :], iim[:, :], ALU.mult, [r_nim, r_iim], [r_tmp1])
    bld.tt("vector", are_c[:, :], tmp1[:, 0:1], tmp1[:, 1:2], ALU.subtract,
           [r_tmp1], [r_are])
    bld.tt("vector", tmp1[:, 0:1], nre[:, :], iim[:, :], ALU.mult, [r_nre, r_iim], [r_tmp1])
    bld.tt("vector", tmp1[:, 1:2], nim[:, :], ire[:, :], ALU.mult, [r_nim, r_ire], [r_tmp1])
    bld.tt("vector", aim_c[:, :], tmp1[:, 0:1], tmp1[:, 1:2], ALU.add,
           [r_tmp1], [r_aim])
    iar, r_iar = sb("iar", [128, 1], F32, st_pro)
    iai, r_iai = sb("iai", [128, 1], F32, st_pro)
    bld.tt("vector", tmp1[:, 0:1], are_c[:, :], are_c[:, :], ALU.mult, [r_are], [r_tmp1])
    bld.stt("vector", tmp1[:, 1:2], aim_c[:, :], 1.0, aim_c[:, :], ALU.mult, ALU.mult,
            [r_aim], [r_tmp1])
    bld.tt("vector", tmp1[:, 0:1], tmp1[:, 0:1], tmp1[:, 1:2], ALU.add, [r_tmp1], [r_tmp1])
    bld.op("vector", lambda h: h.reciprocal(tmp1[:, 2:3], tmp1[:, 0:1]), [r_tmp1], [r_tmp1])
    bld.tt("vector", iar[:, :], are_c[:, :], tmp1[:, 2:3], ALU.mult, [r_are, r_tmp1], [r_iar])
    bld.stt("vector", iai[:, :], aim_c[:, :], -1.0, tmp1[:, 2:3], ALU.mult, ALU.mult,
            [r_aim, r_tmp1], [r_iai])

    def build_powers(pre, pim, r_pre, r_pim, base_re_ap, base_im_ap, r_bre, r_bim):
        cwre, r_cwre = sb(f"cw_{pre.name}r", [128, 1], F32, st_pro)
        cwim, r_cwim = sb(f"cw_{pre.name}i", [128, 1], F32, st_pro)
        bld.cp("vector", cwre[:, :], base_re_ap, [r_bre], [r_cwre])
        bld.cp("vector", cwim[:, :], base_im_ap, [r_bim], [r_cwim])
        bld.op("vector", lambda h: h.memset(pre[:, 0:1], 1.0), (), (r_pre,))
        bld.op("vector", lambda h: h.memset(pim[:, 0:1], 0.0), (), (r_pim,))
        w = 1
        tre, r_tre = sb(f"tp_{pre.name}r", [128, 256], F32, st_pro)
        tim, r_tim = sb(f"tp_{pre.name}i", [128, 256], F32, st_pro)
        while w < 512:
            wl = min(w, 512 - w)
            bld.ts("vector", tre[:, 0:wl], pim[:, 0:wl], cwim[:, :], None,
                   ALU.mult, ALU.bypass, [r_pim, r_cwim], [r_tre])
            bld.stt("vector", pre[:, w:w + wl], pre[:, 0:wl], cwre[:, :], tre[:, 0:wl],
                    ALU.mult, ALU.subtract, [r_pre, r_cwre, r_tre], [r_pre])
            bld.ts("vector", tim[:, 0:wl], pim[:, 0:wl], cwre[:, :], None,
                   ALU.mult, ALU.bypass, [r_pim, r_cwre], [r_tim])
            bld.stt("vector", pim[:, w:w + wl], pre[:, 0:wl], cwim[:, :], tim[:, 0:wl],
                    ALU.mult, ALU.add, [r_pre, r_cwim, r_tim], [r_pim])
            bld.tt("vector", tre[:, 0:1], cwre[:, :], cwre[:, :], ALU.mult,
                   [r_cwre], [r_tre])
            bld.tt("vector", tim[:, 0:1], cwim[:, :], cwim[:, :], ALU.mult,
                   [r_cwim], [r_tim])
            bld.tt("vector", tre[:, 1:2], cwre[:, :], cwim[:, :], ALU.mult,
                   [r_cwre, r_cwim], [r_tre])
            bld.tt("vector", cwre[:, :], tre[:, 0:1], tim[:, 0:1], ALU.subtract,
                   [r_tre, r_tim], [r_cwre])
            bld.ts("vector", cwim[:, :], tre[:, 1:2], 2.0, None, ALU.mult, ALU.bypass,
                   [r_tre], [r_cwim])
            w *= 2

    build_powers(pm_re, pm_im, r_pmre, r_pmim, iar[:, :], iai[:, :], r_iar, r_iai)
    build_powers(pp_re, pp_im, r_ppre, r_ppim, are_c[:, :], aim_c[:, :], r_are, r_aim)

    sdre, r_sdre = sb("sdre", [128, 1], F32, st_pro)
    sdim, r_sdim = sb("sdim", [128, 1], F32, st_pro)
    bld.ts("vector", sdre[:, :], ire[:, :], 0.1, None, ALU.mult, ALU.bypass,
           [r_ire], [r_sdre])
    bld.ts("vector", sdim[:, :], iim[:, :], 0.1, None, ALU.mult, ALU.bypass,
           [r_iim], [r_sdim])
    bre_t, r_bret = sb("bre_t", [128, D], BF16, st_pro)
    bim_t, r_bimt = sb("bim_t", [128, D], BF16, st_pro)
    bld.ts("vector", bre_t[:, :], bmat[:, :], sdre[:, :], None, ALU.mult, ALU.bypass,
           [r_bmat, r_sdre], [r_bret])
    bld.ts("vector", bim_t[:, :], bmat[:, :], sdim[:, :], None, ALU.mult, ALU.bypass,
           [r_bmat, r_sdim], [r_bimt])
    for c in range(DC):
        transpose128(bre_t[:, c * 128:(c + 1) * 128], r_bret,
                     bdt_re[:, c * 128:(c + 1) * 128], r_bdtre, eng="vector")
        transpose128(bim_t[:, c * 128:(c + 1) * 128], r_bimt,
                     bdt_im[:, c * 128:(c + 1) * 128], r_bdtim, eng="scalar")
    for c in range(DC):
        transpose128(cmat_b[:, c * 128:(c + 1) * 128], r_cmatb,
                     ct_t[:, c * 128:(c + 1) * 128], r_ct,
                     eng=("vector" if c % 2 == 0 else "scalar"))

    wprep_jobs = []
    for (src, dst, rdst) in ((wg_in, wgt_d, r_wgt), (wu_in, wut_d, r_wut)):
        src_r = src.ap().rearrange("(m p) d -> m p d", p=128)
        dst_r = dst.ap().rearrange("(m k p) h -> m p k h", k=DC, p=128)
        for m in range(HB):
            def job(m=m, src_r=src_r, dst_r=dst_r, rdst=rdst):
                w0, rw0 = wst[m % 2]
                w1, rw1 = wtt[m % 2]
                bld.dma("gpsimd", "wl1%d" % (m % 2), w0[:, 0:D], src_r[m],
                        [rw0], [rw0])
                for kc in range(DC):
                    transpose128(w0[:, kc * 128:(kc + 1) * 128], rw0,
                                 w1[:, kc * 128:(kc + 1) * 128], rw1,
                                 eng=("vector" if kc % 2 == 0 else "scalar"))
                bld.dma("sync", "ws1%d" % (m % 2), dst_r[m],
                        w1[:, 0:DC * 128].rearrange("p (k h) -> p k h", h=128),
                        [rw1], [rdst])
            wprep_jobs.append(job)
    wdt_r = wdt_d.ap().rearrange("(m p) d -> p m d", p=128)
    for rt in range(DC):
        def job(rt=rt):
            w0, rw0 = wst[rt % 2]
            w1, rw1 = wtt[rt % 2]
            bld.dma("gpsimd", "wl1%d" % (rt % 2), w0[:, 0:3072],
                    wd_in.ap().rearrange("(r p) h -> r p h", p=128)[rt],
                    [rw0], [rw0])
            for m in range(HB):
                transpose128(w0[:, m * 128:(m + 1) * 128], rw0,
                             w1[:, m * 128:(m + 1) * 128], rw1,
                             eng=("vector" if m % 2 == 0 else "scalar"))
            bld.dma("sync", "ws1%d" % (rt % 2), wdt_r[:, :, rt * 128:(rt + 1) * 128],
                    w1[:, 0:HB * 128].rearrange("p (m c) -> p m c", c=128),
                    [rw1], [r_wdt])
        wprep_jobs.append(job)

    st_pro.close()
    bld.barrier()

    x_g, r_xg = sb("x_g", [128, S8 * D], BF16, st_lw)
    wpj_b, r_wpjb = sb("wpj_b", [128, DC * WCOL], BF16, st_l0)
    bld.dma("gpsimd", "wpj", wpj_b[:, :].rearrange("p (c w) -> p c w", w=WCOL),
            wproj.ap().rearrange("(c p) w -> p c w", p=128), (), (r_wpjb,))
    convw, r_convw = sb("convw", [128, 32], F32, st_l0)
    bld.dma("sync", "cvw", convw[:, :], convw_in.ap(), (), (r_convw,))
    dconv, r_dconv = sb("dconv", [128, 32 * 128], BF16, st_l0)
    for cc in range(DC):
        for tap in range(4):
            bld.ts("vector", dconv[:, (cc * 4 + tap) * 128:(cc * 4 + tap + 1) * 128],
                   idn[:, :], convw[:, cc * 4 + tap:cc * 4 + tap + 1], None,
                   ALU.mult, ALU.bypass, [r_idn, r_convw], [r_dconv])
    normw1, r_normw1 = sb("normw1", [1, 512], BF16, st_l0)
    bld.dma("gpsimd", "nw", normw1[:, :], normw_in.ap(), (), (r_normw1,))
    normwb, r_normwb = sb("normwb", [128, 512], BF16, st_l0)
    bk, bkr = rotbank()
    bld.mm(bkr, bk[:, 0:512], onescol[:, :], normw1[:, :], True, True,
           [r_onescol, r_normw1], first=True, last=True)
    evac("vector", normwb[:, :], bkr, bk[:, 0:512], [r_normwb])
    xnT, r_xnT = sb("xnT", [128, DC * (TG + 3)], BF16, st_l0)
    hT, r_hT = sb("hT", [128, DC * TG], BF16, st_l0)
    qkg, r_qkg = sb("qkg", [128, S8 * 512], BF16, st_l0)
    vg, r_vg = sb("vg", [128, S8 * 512], BF16, st_l0)
    sgnw, r_sgnw = sb("sgnw", [128, S8 * 512], BF16, st_l0)
    beta_g, r_betag = sb("beta_g", [128, S8 * 2], F32, st_l0)
    ssq_g, r_ssqg = sb("ssq_g", [128, S8 * 4], F32, st_l0)
    scl_g, r_sclg = sb("scl_g", [128, S8 * 4], F32, st_l0)
    gated_g, r_gatedg = sb("gated_g", [128, S8 * 512], BF16, st_l0)
    xn_bf, r_xnbf = sb("xn_bf", [128, D], BF16, st_l0)
    NI = 8
    scant = []
    for i in range(NI):
        d = {}
        for nm, w in (("dgq", 128), ("dgk", 128), ("dgkb", 128), ("qTn", 128),
                      ("kTn", 128), ("kbTn", 128), ("kn", 128), ("Y", 384),
                      ("TT", 128), ("Pa", 128), ("PTa", 128), ("Pb", 128),
                      ("PTb", 128), ("nW2T", 128), ("W1", 256), ("U", 256),
                      ("Mp", 128)):
            d[nm] = sb(f"sc{nm}_{i}", [128, w], BF16, st_l0)
        scant.append(d)
    S_f32, r_Sf = sb("S_f32", [128, 512], F32, st_l0)
    S_bf, r_Sbf = sb("S_bf", [128, 512], BF16, st_l0)
    betasc, r_betasc = sb("betasc", [128, 8], F32, st_l0)
    orst, r_orst = sb("orst", [128, 8], F32, st_l0)

    bld.op("vector", lambda h: h.memset(S_f32[:, :], 0.0), (), (r_Sf,))
    bld.op("vector", lambda h: h.memset(S_bf[:, :], 0.0), (), (r_Sbf,))
    bld.op("vector", lambda h: h.memset(
        xnT[:, :].rearrange("p (c w) -> p c w", w=TG + 3)[:, :, 0:3], 0.0),
        (), (r_xnT,))

    x_fr = x_full.ap().rearrange("(g s p) d -> g p s d", p=128, s=S8)

    def ln_stats(x_res, x_ap_sub, scol, sqcol, rs=None, rq=None):
        bld.act(trash[:, 0:1024], x_ap_sub, AF.Identity, [x_res],
                [r_trash, rs or r_sumsg], accum=scol)
        bld.act(trash[:, 0:1024], x_ap_sub, AF.Square, [x_res],
                [r_trash, rq or r_sqsg], accum=sqcol)

    def ln_rstd(scol_ap, sqcol_ap, n_inv, rst_ap, mrs_ap, r_rst, r_mrs, ncols,
                rs=None, rq=None):
        m = scr[:, 0:ncols]
        v = scr[:, 8:8 + ncols] if ncols <= 8 else None
        bld.ts("vector", m, scol_ap, n_inv, None, ALU.mult, ALU.bypass,
               [rs or r_sumsg], [r_scr])
        bld.ts("vector", scr2[:, 0:ncols], sqcol_ap, n_inv, None, ALU.mult, ALU.bypass,
               [rq or r_sqsg], [r_scr2])
        bld.stt("vector", v, m, -1.0, m, ALU.mult, ALU.mult, [r_scr], [r_scr])
        bld.tt("vector", v, scr2[:, 0:ncols], v, ALU.add, [r_scr2, r_scr], [r_scr])
        bld.act(scr2[:, 0:ncols], v, AF.Sqrt, [r_scr, r_kc], [r_scr2],
                bias=kconst[:, 0:1])
        bld.op("vector", lambda h, o=rst_ap, i=scr2[:, 0:ncols]: h.reciprocal(o, i),
               [r_scr2], [r_rst])
        bld.ts("vector", scr2[:, 8:8 + ncols], v, 1e-5, None, ALU.add, ALU.bypass,
               [r_scr], [r_scr2])
        bld.tt("vector", v, rst_ap, rst_ap, ALU.mult, [r_rst], [r_scr])
        bld.tt("vector", v, v, scr2[:, 8:8 + ncols], ALU.mult, [r_scr, r_scr2], [r_scr])
        bld.ts("vector", v, v, -0.5, 1.5, ALU.mult, ALU.add, [r_scr], [r_scr])
        bld.tt("vector", rst_ap, rst_ap, v, ALU.mult, [r_rst, r_scr], [r_rst])
        bld.stt("vector", mrs_ap, m, -1.0, rst_ap, ALU.mult, ALU.mult,
                [r_scr, r_rst], [r_mrs])

    def emitA1(g):
        xv = x_g[:, :].rearrange("p (s d) -> p s d", d=D)
        bld.dma("gpsimd", "xg", xv[:, 0:4], x_fr[g][:, 0:4], [r_xg], [r_xg])
        bld.dma("gpsimd", "xgb", xv[:, 4:8], x_fr[g][:, 4:8], [r_xg], [r_xg])
        for s in range(S8):
            ln_stats(r_xg, x_g[:, s * D:(s + 1) * D], sums_g[:, s:s + 1],
                     sqs_g[:, s:s + 1])
        ln_rstd(sums_g[:, 0:S8], sqs_g[:, 0:S8], 1.0 / D,
                stat_r[:, g * 8:(g + 1) * 8], stat_m[:, g * 8:(g + 1) * 8],
                r_statr, r_statm, 8)

    def emitA2(g):

        if g > 0:
            xr3 = xnT[:, :].rearrange("p (c w) -> p c w", w=TG + 3)
            bld.cp("vector", xr3[:, :, 0:3], xr3[:, :, TG:TG + 3], [r_xnT], [r_xnT])
        for s in range(S8):
            bld.act(xn_bf[:, 0:D], x_g[:, s * D:(s + 1) * D], AF.Identity,
                    [r_xg, r_statr, r_statm], [r_xnbf],
                    scale=stat_r[:, g * 8 + s:g * 8 + s + 1],
                    bias=stat_m[:, g * 8 + s:g * 8 + s + 1])
            for c in range(DC):
                transpose128(xn_bf[:, c * 128:(c + 1) * 128], r_xnbf,
                             xnT[:, c * (TG + 3) + 3 + s * 128:
                                 c * (TG + 3) + 3 + (s + 1) * 128], r_xnT,
                             eng=("vector" if (s + c) % 2 else "scalar"))
        for c in range(DC):
            src = xnT[:, c * (TG + 3):(c + 1) * (TG + 3)]
            for nh in range(2):
                bk, bkr = rotbank()
                for tap in range(4):
                    bld.mm(bkr, bk[:, 0:512],
                           dconv[:, (c * 4 + tap) * 128:(c * 4 + tap + 1) * 128],
                           src[:, nh * 512 + tap: nh * 512 + tap + 512],
                           tap == 0, tap == 3, [r_dconv, r_xnT],
                           first=(tap == 0), last=(tap == 3))
                bld.act(hT[:, c * TG + nh * 512: c * TG + (nh + 1) * 512],
                        bk[:, 0:512], AF.Silu, [bkr], [r_hT])


    def emitB(g):
        for s in range(S8):
            if s % 2 == 0:
                bq, bk_, bv, bg = 5, 6, 7, 3
            else:
                bq, bk_, bv, bg = 0, 1, 2, 4
            for kc in range(DC):
                lt = hT[:, kc * TG + s * 128: kc * TG + (s + 1) * 128]
                w0 = kc * WCOL
                first, last = kc == 0, kc == DC - 1
                bld.mm(RB[bq], PB[bq][:, 0:256], lt, wpj_b[:, w0:w0 + 256],
                       first, False, [r_hT, r_wpjb], first=first, last=False)
                bld.mm(RB[bq], PB[bq][:, 256:258], lt, wpj_b[:, w0 + 1536:w0 + 1538],
                       False, last, [r_hT, r_wpjb], first=False, last=last)
                bld.mm(RB[bk_], PB[bk_][:, 0:256], lt, wpj_b[:, w0 + 256:w0 + 512],
                       first, last, [r_hT, r_wpjb], first=first, last=last)
                bld.mm(RB[bv], PB[bv][:, 0:512], lt, wpj_b[:, w0 + 512:w0 + 1024],
                       first, last, [r_hT, r_wpjb], first=first, last=last)
                bld.mm(RB[bg], PB[bg][:, 0:512], lt, wpj_b[:, w0 + 1024:w0 + 1536],
                       first, last, [r_hT, r_wpjb], first=first, last=last)
            bld.cp("vector", qkg[:, s * 512:s * 512 + 256], PB[bq][:, 0:256],
                   [RB[bq]], [r_qkg])
            bld.cp("vector", scr[:, 10:12], PB[bq][:, 256:258], [RB[bq]], [r_scr])
            bld.act(scr[:, 8:10], scr[:, 10:12], AF.Tanh, [r_scr, r_kc],
                    [r_scr], scale=kconst[:, 1:2])
            bld.ts("vector", beta_g[:, s * 2:s * 2 + 2], scr[:, 8:10], 0.5, 0.5,
                   ALU.mult, ALU.add, [r_scr], [r_betag])
            bld.cp("vector", qkg[:, s * 512 + 256:s * 512 + 512], PB[bk_][:, 0:256],
                   [RB[bk_]], [r_qkg])
            bld.cp("scalar", vg[:, s * 512:(s + 1) * 512], PB[bv][:, 0:512],
                   [RB[bv]], [r_vg])
            bld.act(sgnw[:, s * 512:(s + 1) * 512], PB[bg][:, 0:512], AF.Silu,
                    [RB[bg]], [r_sgnw])
            bld.tt("vector", sgnw[:, s * 512:(s + 1) * 512],
                   sgnw[:, s * 512:(s + 1) * 512], normwb[:, :], ALU.mult,
                   [r_sgnw, r_normwb], [r_sgnw])
            bld.tt("vector", trash[:, 0:512], qkg[:, s * 512:(s + 1) * 512],
                   qkg[:, s * 512:(s + 1) * 512], ALU.mult, [r_qkg], [r_trash])
            bld.red(ssq_g[:, s * 4:(s + 1) * 4],
                    trash[:, 0:512].rearrange("p (h t) -> p h t", h=4),
                    ALU.add, [r_trash], [r_ssqg])
        bld.act(scl_g[:, :], ssq_g[:, :], AF.Sqrt, [r_ssqg], [r_sclg])
        bld.ts("vector", scl_g[:, :], scl_g[:, :], 1e-12, None, ALU.max, ALU.bypass,
               [r_sclg], [r_sclg])
        bld.op("vector", lambda h: h.reciprocal(scl_g[:, :], scl_g[:, :]),
               [r_sclg], [r_sclg])


    def emitC(g):
        for half in range(2):
            insts = []
            for cb in range(4):
                s4 = half * 4 + cb
                for hh in range(2):
                    insts.append((cb * 2 + hh, s4, hh))
            for i, s4, hh in insts:
                T = scant[i]
                kt = qkg[:, s4 * 512 + 256 + hh * 128: s4 * 512 + 256 + (hh + 1) * 128]
                vt = vg[:, s4 * 512 + hh * 256: s4 * 512 + (hh + 1) * 256]
                bcol = beta_g[:, s4 * 2 + hh: s4 * 2 + hh + 1]
                sqc = scl_g[:, s4 * 4 + hh: s4 * 4 + hh + 1]
                skc = scl_g[:, s4 * 4 + 2 + hh: s4 * 4 + 3 + hh]
                bld.ts("vector", T["dgq"][0][:, :], idn[:, :], sqc, None,
                       ALU.mult, ALU.bypass, [r_idn, r_sclg], [T["dgq"][1]])
                bld.ts("vector", T["dgk"][0][:, :], idn[:, :], skc, None,
                       ALU.mult, ALU.bypass, [r_idn, r_sclg], [T["dgk"][1]])
                bld.tt("vector", betasc[:, i:i + 1], bcol, skc, ALU.mult,
                       [r_betag, r_sclg], [r_betasc])
                bld.ts("vector", T["dgkb"][0][:, :], idn[:, :], betasc[:, i:i + 1],
                       None, ALU.mult, ALU.bypass, [r_idn, r_betasc], [T["dgkb"][1]])
                bld.ts("scalar", T["kn"][0][:, :], kt, skc, None, ALU.mult,
                       ALU.bypass, [r_qkg, r_sclg], [T["kn"][1]])
                bld.ts("scalar", T["Y"][0][:, 256:384], kt, betasc[:, i:i + 1],
                       None, ALU.mult, ALU.bypass, [r_qkg, r_betasc], [T["Y"][1]])
                bld.ts("scalar", T["Y"][0][:, 0:256], vt, bcol, None,
                       ALU.mult, ALU.bypass, [r_vg, r_betag], [T["Y"][1]])
            for i, s4, hh in insts:
                T = scant[i]
                qt = qkg[:, s4 * 512 + hh * 128: s4 * 512 + (hh + 1) * 128]
                kt = qkg[:, s4 * 512 + 256 + hh * 128: s4 * 512 + 256 + (hh + 1) * 128]
                transpose128(qt, r_qkg, T["qTn"][0][:, :], T["qTn"][1],
                             T["dgq"][0][:, :], T["dgq"][1],
                             "vector" if i % 2 == 0 else "scalar")
                transpose128(kt, r_qkg, T["kTn"][0][:, :], T["kTn"][1],
                             T["dgk"][0][:, :], T["dgk"][1],
                             "scalar" if i % 2 == 0 else "vector")
                transpose128(kt, r_qkg, T["kbTn"][0][:, :], T["kbTn"][1],
                             T["dgkb"][0][:, :], T["dgkb"][1],
                             "vector" if i % 2 == 0 else "scalar")
            for i, s4, hh in insts:
                T = scant[i]
                bk, bkr = rotbank()
                bld.mm(bkr, bk[:, 0:128], T["kbTn"][0][:, :], T["kTn"][0][:, :],
                       True, True, [T["kbTn"][1], T["kTn"][1]], first=True, last=True)
                bld.tt("vector", T["Pa"][0][:, :], bk[:, 0:128], MSL, ALU.mult,
                       [bkr, r_msk], [T["Pa"][1]])
                bk, bkr = rotbank()
                bld.mm(bkr, bk[:, 0:128], T["kTn"][0][:, :], T["kbTn"][0][:, :],
                       True, True, [T["kTn"][1], T["kbTn"][1]], first=True, last=True)
                bld.tt("vector", T["PTa"][0][:, :], bk[:, 0:128], MSU, ALU.mult,
                       [bkr, r_msk], [T["PTa"][1]])
                bld.stt("vector", T["TT"][0][:, :], T["PTa"][0][:, :], -1.0,
                        idn[:, :], ALU.mult, ALU.add, [T["PTa"][1], r_idn],
                        [T["TT"][1]])
            curs = {i: ("Pa", "PTa") for i, _, _ in insts}
            nxts = {i: ("Pb", "PTb") for i, _, _ in insts}
            for kk in range(1, 7):
                for i, s4, hh in insts:
                    T = scant[i]
                    pc, ptc = (T[curs[i][0]], T[curs[i][1]])
                    pn, ptn = (T[nxts[i][0]], T[nxts[i][1]])
                    bk, bkr = rotbank()
                    bld.mm(bkr, bk[:, 0:128], ptc[0][:, :], pc[0][:, :], True, True,
                           [ptc[1], pc[1]], first=True, last=True)
                    evac("vector" if i % 2 else "scalar", pn[0][:, :], bkr,
                         bk[:, 0:128], [pn[1]])
                    if kk < 6:
                        bk, bkr = rotbank()
                        bld.mm(bkr, bk[:, 0:128], pc[0][:, :], ptc[0][:, :], True,
                               True, [pc[1], ptc[1]], first=True, last=True)
                        evac("scalar" if i % 2 else "vector", ptn[0][:, :], bkr,
                             bk[:, 0:128], [ptn[1]])
                for i, s4, hh in insts:
                    T = scant[i]
                    pn = T[nxts[i][0]]
                    bk, bkr = rotbank()
                    bld.mm(bkr, bk[:, 0:128], pn[0][:, :], T["TT"][0][:, :], True,
                           True, [pn[1], T["TT"][1]], first=True, last=True)
                    bld.tt("vector", T["TT"][0][:, :], T["TT"][0][:, :],
                           bk[:, 0:128], ALU.add, [T["TT"][1], bkr], [T["TT"][1]])
                    curs[i], nxts[i] = nxts[i], curs[i]
            for i, s4, hh in insts:
                T = scant[i]
                bk, bkr = rotbank()
                bld.mm(bkr, bk[:, 0:128], T["Y"][0][:, 256:384], T["TT"][0][:, :],
                       True, True, [T["Y"][1], T["TT"][1]], first=True, last=True)
                evac("scalar", T["nW2T"][0][:, :], bkr, bk[:, 0:128],
                     [T["nW2T"][1]], scale=-1.0)
                bk, bkr = rotbank()
                bld.mm(bkr, bk[:, 0:256], T["TT"][0][:, :], T["Y"][0][:, 0:256],
                       True, True, [T["TT"][1], T["Y"][1]], first=True, last=True)
                evac("vector", T["W1"][0][:, :], bkr, bk[:, 0:256], [T["W1"][1]])
            for cb in range(4):
                s4 = half * 4 + cb
                UB = [(PB[5], RB[5]), (PB[6], RB[6])]
                OB = [(PB[3], RB[3]), (PB[7], RB[7])]
                for hh in range(2):
                    i = cb * 2 + hh
                    T = scant[i]
                    Ssl = S_bf[:, hh * 256:(hh + 1) * 256]
                    ub, ubr = UB[hh]
                    bld.mm(ubr, ub[:, 0:256], T["nW2T"][0][:, :], Ssl, True, True,
                           [T["nW2T"][1], r_Sbf], first=True, last=True)
                    bld.tt("vector", T["U"][0][:, :], ub[:, 0:256],
                           T["W1"][0][:, :], ALU.add, [ubr, T["W1"][1]],
                           [T["U"][1]])
                    bk, bkr = rotbank()
                    bld.mm(bkr, bk[:, 0:128], T["kTn"][0][:, :], T["qTn"][0][:, :],
                           True, True, [T["kTn"][1], T["qTn"][1]], first=True,
                           last=True)
                    bld.tt("vector", T["Mp"][0][:, :], bk[:, 0:128], MTI, ALU.mult,
                           [bkr, r_msk], [T["Mp"][1]])
                for hh in range(2):
                    i = cb * 2 + hh
                    T = scant[i]
                    Ssl = S_bf[:, hh * 256:(hh + 1) * 256]
                    ob, obr = OB[hh]
                    bld.mm(obr, ob[:, 0:256], T["qTn"][0][:, :], Ssl, True, False,
                           [T["qTn"][1], r_Sbf], first=True, last=False)
                    bld.mm(obr, ob[:, 0:256], T["Mp"][0][:, :], T["U"][0][:, :],
                           False, True, [T["Mp"][1], T["U"][1]], first=False,
                           last=True)
                for hh in range(2):
                    i = cb * 2 + hh
                    T = scant[i]
                    bld.mm(RB[4], PB[4][:, hh * 256:(hh + 1) * 256],
                           T["kn"][0][:, :], T["U"][0][:, :], hh == 0, hh == 1,
                           [T["kn"][1], T["U"][1]], first=(hh == 0),
                           last=(hh == 1))
                for hh in range(2):
                    i = cb * 2 + hh
                    ob, obr = OB[hh]
                    oc0 = 4 * hh
                    bld.act(trash[:, 256 * hh:256 * hh + 256], ob[:, 0:256],
                            AF.Square, [obr], [r_trash, r_orst],
                            accum=orst[:, oc0:oc0 + 1])
                    bld.act(orst[:, oc0 + 1:oc0 + 2], orst[:, oc0:oc0 + 1],
                            AF.Sqrt, [r_orst, r_kc], [r_orst],
                            scale=kconst[:, 2:3], bias=kconst[:, 0:1])
                    bld.op("vector", lambda h, a=orst[:, oc0 + 2:oc0 + 3],
                           b=orst[:, oc0 + 1:oc0 + 2]: h.reciprocal(a, b),
                           [r_orst], [r_orst])
                    bld.stt("vector",
                            gated_g[:, s4 * 512 + hh * 256: s4 * 512 + (hh + 1) * 256],
                            ob[:, 0:256], orst[:, oc0 + 2:oc0 + 3],
                            sgnw[:, s4 * 512 + hh * 256: s4 * 512 + (hh + 1) * 256],
                            ALU.mult, ALU.mult, [obr, r_orst, r_sgnw], [r_gatedg])
                bld.tt("vector", S_f32[:, :], S_f32[:, :], PB[4][:, 0:512], ALU.add,
                       [r_Sf, RB[4]], [r_Sf])
                bld.cp("vector", S_bf[:, :], S_f32[:, :], [r_Sf], [r_Sbf])
        bld.dma("sync", "gat",
                gat_in.ap().rearrange("(g s p) c -> g p s c", p=128, s=S8)[g],
                gated_g[:, :].rearrange("p (s c) -> p s c", c=512),
                [r_gatedg], [r_gin])
        if debug:
            bld.dma("gpsimd", "dbg",
                    dbg["d_gated"].ap().rearrange("(g s p) c -> g p s c", p=128,
                                                  s=S8)[g],
                    gated_g[:, :].rearrange("p (s c) -> p s c", c=512),
                    [r_gatedg], [Res()])
        if debug and g == 0:
            bld.dma("gpsimd", "dbg", dbg["d_hT"].ap(), hT[:, :], [r_hT], [Res()])
            bld.dma("gpsimd", "dbg", dbg["d_qkg"].ap(), qkg[:, :], [r_qkg], [Res()])


    wjob_i = [0]
    def emit_wjobs(k):
        while wjob_i[0] < len(wprep_jobs) and k > 0:
            wprep_jobs[wjob_i[0]]()
            wjob_i[0] += 1
            k -= 1
    if n_g0 > 0:
        emitA1(0)
        emitA2(0)
    for g in range(n_g0):
        emitB(g)
        if g + 1 < n_g0:
            emitA1(g + 1)
        emitC(g)
        if g + 1 < n_g0:
            emitA2(g + 1)
        emit_wjobs(8)
    emit_wjobs(99)
    bld.barrier()
    st_l0.close()

    if do_w:
        bld.op("gpsimd", lambda h: h.collective_compute(
            "AllGather", ALU.bypass,
            replica_groups=[[0, 1], [2, 3], [4, 5], [6, 7]],
            ins=[gat_in.ap()], outs=[gat_out.ap()]), [r_gin], [r_gout])

    x_g2, r_xg2 = sb("x_g2", [128, S8 * D], BF16, st_w)
    xo_b = [(x_g, r_xg), (x_g2, r_xg2)]
    ternT, r_ternT = sb("ternT", [128, DC * D], BF16, st_w)
    bld.dma("sync", "ttl", ternT[:, :].rearrange("p (c w) -> p c w", w=D),
            ternT_d.ap().rearrange("(c p) w -> p c w", p=128),
            [r_ternTd], (r_ternT,))
    wo_b, r_wob = sb("wo_b", [128, DC * D], BF16, st_w)
    bld.dma("gpsimd", "wol", wo_b[:, :].rearrange("p (c w) -> p c w", w=D),
            wo_in.ap().rearrange("(c p) w -> p c w", p=128), (), (r_wob,))
    bias1, r_bias1 = sb("bias1", [1, D], BF16, st_w)
    bld.dma("gpsimd", "bi", bias1[:, :], bias_in.ap(), (), (r_bias1,))
    wsc1, r_wsc1 = sb("wsc1", [1, 1], BF16, st_w)
    bld.dma("gpsimd", "wsc", wsc1[:, :], wscale_in.ap(), (), (r_wsc1,))
    biasb, r_biasb = sb("biasb", [128, D], F32, st_w)
    for half in range(2):
        bk, bkr = rotbank()
        bld.mm(bkr, bk[:, 0:512], onescol[:, :], bias1[:, half * 512:(half + 1) * 512],
               True, True, [r_onescol, r_bias1], first=True, last=True)
        evac("vector", biasb[:, half * 512:(half + 1) * 512], bkr, bk[:, 0:512],
             [r_biasb])
    ws_col, r_ws = sb("ws_col", [128, 1], F32, st_w)
    bk, bkr = rotbank()
    bld.mm(bkr, bk[:, 0:1], onescol[:, :], wsc1[:, :], True, True,
           [r_onescol, r_wsc1], first=True, last=True)
    evac("vector", ws_col[:, :], bkr, bk[:, 0:1], [r_ws])
    xn_f = [sb(f"xn_f{i}", [128, D], F32, st_w) for i in range(2)]
    tq1 = [sb(f"tq1_{i}", [128, D], F32, st_w) for i in range(2)]
    tq2_s = sb("tq2_s", [128, D], F32, st_w)
    tq2 = [tq2_s, tq2_s]
    xint = [sb(f"xint{i}", [128, D], BF16, st_w) for i in range(2)]
    xq32_s = sb("xq32_s", [128, D], I32, st_w)
    xq32 = [xq32_s, xq32_s]
    xiT_all, r_xiTa = sb("xiT_all", [128, S8 * 1024], BF16, st_w)
    g_all, r_gall = sb("g_all", [128, S8 * D], BF16, st_w)
    gsel0, r_gsel0 = sb("gsel0", [128, S8 * 512], BF16, st_w)
    gsel1, r_gsel1 = sb("gsel1", [128, S8 * 512], BF16, st_w)
    glo, r_glo = sb("glo", [128, S8 * 512], BF16, st_w)
    gT = [sb(f"gT{i}", [128, DC * 128], BF16, st_w) for i in range(2)]
    qcol, r_qcol = sb("qcol", [128, 64], F32, st_w)
    x1t = [sb(f"x1t{i}", [128, D], BF16, st_w) for i in range(2)]

    xo_r = x_own.ap().rearrange("(g s p) d -> g p s d", p=128, s=S8)
    go_r = gat_out.ap().rearrange("(r h g s p) c -> r h g p s c",
                                  p=128, s=S8, g=GW, h=2)
    out_r = out_d.ap().rearrange("(g s p) d -> g s p d", p=128, s=S8)

    MLO = pcsc[:, 0:1]
    MHI = pcsc[:, 1:2]
    ZMK = pcsc[:, 2:3]

    def emitWA(wg, s):
        p = s % 2
        xo_g, r_xog = xo_b[wg % 2]
        bld.act(xn_f[p][0][:, :], xo_g[:, s * D:(s + 1) * D], AF.Identity,
                [r_xog, r_statwr, r_statwm], [xn_f[p][1]],
                scale=statw_r[:, wg * 8 + s:wg * 8 + s + 1],
                bias=statw_m[:, wg * 8 + s:wg * 8 + s + 1])
        q8 = s * 8
        bld.red(qcol[:, q8 + 0:q8 + 1], xn_f[p][0][:, :], ALU.max,
                [xn_f[p][1]], [r_qcol], absval=True)
        bld.ts("vector", qcol[:, q8 + 1:q8 + 2], qcol[:, q8 + 0:q8 + 1], 1e-5, None,
               ALU.max, ALU.bypass, [r_qcol], [r_qcol])
        bld.op("vector", lambda h, a=qcol[:, q8 + 2:q8 + 3],
               b=qcol[:, q8 + 1:q8 + 2]: h.reciprocal(a, b), [r_qcol], [r_qcol])
        bld.ts("vector", qcol[:, q8 + 3:q8 + 4], qcol[:, q8 + 2:q8 + 3], 127.0, None,
               ALU.mult, ALU.bypass, [r_qcol], [r_qcol])
        bld.stt("vector", qcol[:, q8 + 5:q8 + 6], qcol[:, q8 + 1:q8 + 2], 1.0 / 127,
                ws_col[:, :], ALU.mult, ALU.mult, [r_qcol, r_ws], [r_qcol])
        bld.ts("vector", tq1[p][0][:, :], xn_f[p][0][:, :], qcol[:, q8 + 3:q8 + 4],
               None, ALU.mult, ALU.bypass, [xn_f[p][1], r_qcol], [tq1[p][1]])
        bld.cp("vector", xq32[p][0][:, :], tq1[p][0][:, :], [tq1[p][1]], [xq32[p][1]])
        bld.ts("vector", xint[p][0][:, :], xq32[p][0][:, :], 127.0, -128.0, ALU.min,
               ALU.max, [xq32[p][1]], [xint[p][1]])
        for c in range(DC):
            transpose128(xint[p][0][:, c * 128:(c + 1) * 128], xint[p][1],
                         xiT_all[:, s * 1024 + c * 128: s * 1024 + (c + 1) * 128],
                         r_xiTa, eng=("vector" if c % 2 else "scalar"))

    def emitWB(wg, s):
        p = s % 2
        f0, f1 = (5, 6) if s % 2 == 0 else (7, 3)
        q8 = s * 8
        for c in range(DC):
            first, last = c == 0, c == DC - 1
            xs = xiT_all[:, s * 1024 + c * 128: s * 1024 + (c + 1) * 128]
            bld.mm(RB[f0], PB[f0][:, 0:512], xs, ternT[:, c * D:c * D + 512],
                   first, last, [r_xiTa, r_ternT], first=first, last=last)
            bld.mm(RB[f1], PB[f1][:, 0:512], xs, ternT[:, c * D + 512:(c + 1) * D],
                   first, last, [r_xiTa, r_ternT], first=first, last=last)
        bld.stt("vector", tq1[p][0][:, 0:512], PB[f0][:, 0:512],
                qcol[:, q8 + 5:q8 + 6], biasb[:, 0:512], ALU.mult, ALU.add,
                [RB[f0], r_qcol, r_biasb], [tq1[p][1]])
        bld.stt("vector", tq1[p][0][:, 512:1024], PB[f1][:, 0:512],
                qcol[:, q8 + 5:q8 + 6], biasb[:, 512:1024], ALU.mult, ALU.add,
                [RB[f1], r_qcol, r_biasb], [tq1[p][1]])
        bld.act(g_all[:, s * D:(s + 1) * D], tq1[p][0][:, :], AF.Silu,
                [tq1[p][1]], [r_gall])

    def emitWC(wg, s):
        p = s % 2
        for c in range(DC):
            src = gsel0 if c < 4 else gsel1
            sres = r_gsel0 if c < 4 else r_gsel1
            cc = c % 4
            transpose128(src[:, s * 512 + cc * 128: s * 512 + (cc + 1) * 128],
                         sres, gT[p][0][:, c * 128:(c + 1) * 128], gT[p][1],
                         eng=("vector" if c % 2 else "scalar"))
        w0b, w1b = (7, 3) if s % 2 == 0 else (5, 6)
        for c in range(DC):
            first, last = c == 0, c == DC - 1
            bld.mm(RB[w0b], PB[w0b][:, 0:512], gT[p][0][:, c * 128:(c + 1) * 128],
                   wo_b[:, c * D:c * D + 512], first, last,
                   [gT[p][1], r_wob], first=first, last=last)
            bld.mm(RB[w1b], PB[w1b][:, 0:512], gT[p][0][:, c * 128:(c + 1) * 128],
                   wo_b[:, c * D + 512:(c + 1) * D], first, last,
                   [gT[p][1], r_wob], first=first, last=last)
        ti = wg * 8 + s
        xo_g, r_xog = xo_b[wg % 2]
        bld.tt("vector", tq2[p][0][:, 0:512], PB[w0b][:, 0:512],
               g_all[:, s * D:s * D + 512], ALU.mult, [RB[w0b], r_gall],
               [tq2[p][1]])
        bld.tt("vector", x1t[p][0][:, 0:512], tq2[p][0][:, 0:512],
               xo_g[:, s * D:s * D + 512], ALU.add, [tq2[p][1], r_xog], [x1t[p][1]])
        bld.tt("vector", tq2[p][0][:, 512:1024], PB[w1b][:, 0:512],
               g_all[:, s * D + 512:(s + 1) * D], ALU.mult, [RB[w1b], r_gall],
               [tq2[p][1]])
        bld.tt("vector", x1t[p][0][:, 512:1024], tq2[p][0][:, 512:1024],
               xo_g[:, s * D + 512:(s + 1) * D], ALU.add, [tq2[p][1], r_xog],
               [x1t[p][1]])
        ln_stats(x1t[p][1], x1t[p][0][:, :], sum1_g[:, s:s + 1],
                 sq1_g[:, s:s + 1], r_sum1g, r_sq1g)
        bld.dma("sync", "x1s%d" % p, x1_dr[ti], x1t[p][0][:, :], [x1t[p][1]], [r_x1])

    for wg in range(GW if do_w else 0):
        xo_g, r_xog = xo_b[wg % 2]
        bld.dma("gpsimd", "xg%d" % (wg % 2),
                xo_g[:, :].rearrange("p (s d) -> p s d", d=D),
                xo_r[wg], [r_xog], [r_xog])
        for s in range(S8):
            ln_stats(r_xog, xo_g[:, s * D:(s + 1) * D], sums_g[:, s:s + 1],
                     sqs_g[:, s:s + 1])
        ln_rstd(sums_g[:, 0:S8], sqs_g[:, 0:S8], 1.0 / D,
                statw_r[:, wg * 8:(wg + 1) * 8], statw_m[:, wg * 8:(wg + 1) * 8],
                r_statwr, r_statwm, 8)
        for s in range(S8):
            emitWA(wg, s)
            emitWB(wg, s)
        for sl, gres, gsl in ((0, r_gsel0, gsel0), (1, r_gsel1, gsel1)):
            bld.dma("sync", "glo", glo[:, :].rearrange("p (s c) -> p s c", c=512),
                    go_r[sl, 0, wg], [r_glo, r_gout], [r_glo])
            bld.ts("vector", gsl[:, :], glo[:, :], MLO, None, ALU.mult, ALU.bypass,
                   [r_glo, r_pcsc], [gres])
            bld.dma("sync", "glo", glo[:, :].rearrange("p (s c) -> p s c", c=512),
                    go_r[sl, 1, wg], [r_glo, r_gout], [r_glo])
            bld.stt("vector", gsl[:, :], glo[:, :], MHI, gsl[:, :], ALU.mult, ALU.add,
                    [r_glo, r_pcsc, gres], [gres])
        for s in range(S8):
            emitWC(wg, s)
        ln_rstd(sum1_g[:, 0:S8], sq1_g[:, 0:S8], 1.0 / D,
                stat1_r[:, wg * 8:(wg + 1) * 8], stat1_m[:, wg * 8:(wg + 1) * 8],
                r_stat1r, r_stat1m, 8, rs=r_sum1g, rq=r_sq1g)
    if debug and do_w:
        bld.dma("gpsimd", "dbg", dbg["d_x1"].ap().rearrange("p (t d) -> p t d", d=D),
                x1_d.ap().rearrange("(t p) d -> p t d", p=128), [r_x1], [Res()])

    bld.barrier()

    st_w.close()
    st_lw.close()

    xn1, r_xn1 = sb("xn1", [128, D], BF16, st_l1)
    x1_st2 = [sb(f"x1_st{i}", [128, 4 * D], BF16, st_l1) for i in range(2)]
    xn1T, r_xn1T = sb("xn1T", [128, DC * TS], BF16, st_l1)
    hre_all, r_hre = sb("hre_all", [128, HALF], BF16, st_l1)
    cre, r_cre = sb("cre", [128, TS], F32, st_l1)
    cim, r_cim = sb("cim", [128, TS], F32, st_l1)
    zre, r_zre = sb("zre", [128, TS], F32, st_l1)
    zim, r_zim = sb("zim", [128, TS], F32, st_l1)
    tl1, r_tl1 = sb("tl1", [128, TS], F32, st_l1)
    tl2, r_tl2 = sb("tl2", [128, TS], F32, st_l1)
    eloc, r_eloc = sb("eloc", [128, 8], F32, st_l1)
    hidT, r_hidT = sb("hidT", [128, HB * TS], BF16, st_l1)
    wdt_s, r_wdts = sb("wdt_s", [128, HB * D], BF16, st_l1)
    wgsl, r_wgsl = sb("wgsl", [128, 2 * DC * 128], BF16, st_l1)
    wusl, r_wusl = sb("wusl", [128, 2 * DC * 128], BF16, st_l1)
    y_bf, r_ybf = sb("y_bf", [128, 4 * D], BF16, st_l1)
    g2_bf, r_g2bf = sb("g2_bf", [128, D], BF16, st_l1)
    out_f, r_outf = sb("out_f", [128, D], F32, st_l1)

    def xn1_load(st):
        x1_st, r_x1st = x1_st2[st % 2]
        bld.dma("sync", "x1l%d" % (st % 2),
                x1_st[:, :].rearrange("p (t d) -> p t d", d=D),
                x1_dr[st * 4:(st + 1) * 4].rearrange("t p d -> p t d"),
                [r_x1, r_x1st], [r_x1st])

    def xn1_transpose(st):
        x1_st, r_x1st = x1_st2[st % 2]
        for sub in range(4):
            ti = st * 4 + sub
            bld.act(xn1[:, :], x1_st[:, sub * D:(sub + 1) * D], AF.Identity,
                    [r_x1st, r_stat1r, r_stat1m], [r_xn1],
                    scale=stat1_r[:, ti:ti + 1], bias=stat1_m[:, ti:ti + 1])
            for c in range(DC):
                transpose128(xn1[:, c * 128:(c + 1) * 128], r_xn1,
                             xn1T[:, c * TS + sub * 128: c * TS + (sub + 1) * 128],
                             r_xn1T, eng=("vector" if (sub + c) % 2 else "scalar"))

    bld.op("vector", lambda h: h.memset(eloc[:, 2:4], 0.0), (), (r_eloc,))
    n_l1a = (NST if do_l1 else 0) if L1_SUB >= 2 else 0
    if n_l1a:
        xn1_load(0)
    for st in range(n_l1a):
        xn1_transpose(st)
        if st + 1 < n_l1a:
            xn1_load(st + 1)
        b0_, b1_ = (5, 6) if st % 2 == 0 else (7, 3)
        for c in range(DC):
            first, last = c == 0, c == DC - 1
            bld.mm(RB[b0_], PB[b0_][:, 0:512], bdt_re[:, c * 128:(c + 1) * 128],
                   xn1T[:, c * TS:(c + 1) * TS], first, last,
                   [r_bdtre, r_xn1T], first=first, last=last)
            bld.mm(RB[b1_], PB[b1_][:, 0:512], bdt_im[:, c * 128:(c + 1) * 128],
                   xn1T[:, c * TS:(c + 1) * TS], first, last,
                   [r_bdtim, r_xn1T], first=first, last=last)
        bld.tt("vector", tl1[:, :], PB[b0_][:, 0:512], pm_re[:, :], ALU.mult,
               [RB[b0_], r_pmre], [r_tl1])
        bld.tt("vector", tl2[:, :], PB[b1_][:, 0:512], pm_im[:, :], ALU.mult,
               [RB[b1_], r_pmim], [r_tl2])
        bld.tt("vector", cre[:, :], tl1[:, :], tl2[:, :], ALU.subtract,
               [r_tl1, r_tl2], [r_cre])
        bld.tt("vector", tl1[:, :], PB[b1_][:, 0:512], pm_re[:, :], ALU.mult,
               [RB[b1_], r_pmre], [r_tl1])
        bld.tt("vector", tl2[:, :], PB[b0_][:, 0:512], pm_im[:, :], ALU.mult,
               [RB[b0_], r_pmim], [r_tl2])
        bld.tt("vector", cim[:, :], tl1[:, :], tl2[:, :], ALU.add,
               [r_tl1, r_tl2], [r_cim])
        bld.op("vector", lambda h: h.tensor_tensor_scan(
            zre[:, :], ones512[:, :], cre[:, :], eloc[:, 2:3], ALU.mult, ALU.add),
            [r_ones, r_cre, r_eloc], [r_zre])
        bld.op("vector", lambda h: h.tensor_tensor_scan(
            zim[:, :], ones512[:, :], cim[:, :], eloc[:, 3:4], ALU.mult, ALU.add),
            [r_ones, r_cim, r_eloc], [r_zim])
        bld.tt("vector", tl1[:, :], zre[:, :], pp_re[:, :], ALU.mult,
               [r_zre, r_ppre], [r_tl1])
        bld.tt("vector", tl2[:, :], zim[:, :], pp_im[:, :], ALU.mult,
               [r_zim, r_ppim], [r_tl2])
        bld.tt("vector", hre_all[:, st * TS:(st + 1) * TS], tl1[:, :], tl2[:, :],
               ALU.subtract, [r_tl1, r_tl2], [r_hre])
        bld.tt("vector", scr[:, 0:1], pp_re[:, 511:512], zre[:, 511:512], ALU.mult,
               [r_ppre, r_zre], [r_scr])
        bld.tt("vector", scr[:, 1:2], pp_im[:, 511:512], zim[:, 511:512], ALU.mult,
               [r_ppim, r_zim], [r_scr])
        bld.tt("vector", eloc[:, 0:1], scr[:, 0:1], scr[:, 1:2], ALU.subtract,
               [r_scr], [r_eloc])
        bld.tt("vector", scr[:, 0:1], pp_re[:, 511:512], zim[:, 511:512], ALU.mult,
               [r_ppre, r_zim], [r_scr])
        bld.tt("vector", scr[:, 1:2], pp_im[:, 511:512], zre[:, 511:512], ALU.mult,
               [r_ppim, r_zre], [r_scr])
        bld.tt("vector", eloc[:, 1:2], scr[:, 0:1], scr[:, 1:2], ALU.add,
               [r_scr], [r_eloc])
        bld.tt("vector", scr[:, 0:1], are_c[:, :], eloc[:, 0:1], ALU.mult,
               [r_are, r_eloc], [r_scr])
        bld.tt("vector", scr[:, 1:2], aim_c[:, :], eloc[:, 1:2], ALU.mult,
               [r_aim, r_eloc], [r_scr])
        bld.tt("vector", scr[:, 2:3], are_c[:, :], eloc[:, 1:2], ALU.mult,
               [r_are, r_eloc], [r_scr])
        bld.tt("vector", scr[:, 3:4], aim_c[:, :], eloc[:, 0:1], ALU.mult,
               [r_aim, r_eloc], [r_scr])
        bld.tt("vector", eloc[:, 2:3], scr[:, 0:1], scr[:, 1:2], ALU.subtract,
               [r_scr], [r_eloc])
        bld.tt("vector", eloc[:, 3:4], scr[:, 2:3], scr[:, 3:4], ALU.add,
               [r_scr], [r_eloc])

    if not do_l1 or L1_SUB < 3:
        bld.barrier()
        _finalize(nc, bld, stack, st_l1)
        return nc
    if debug:
        bld.dma("gpsimd", "dbg", dbg["d_hre"].ap(), hre_all[:, :], [r_hre], [Res()])

    wgt_r = wgt_d.ap().rearrange("(m k p) h -> m p k h", k=DC, p=128)
    wut_r = wut_d.ap().rearrange("(m k p) h -> m p k h", k=DC, p=128)
    wdt_rr = wdt_d.ap().rearrange("(m p) d -> p m d", p=128)
    n_l1b = NST if L1_SUB >= 4 else 0
    if n_l1b:
        bld.dma("sync", "wdt", wdt_s[:, :].rearrange("p (m d) -> p m d", d=D),
                wdt_rr, [r_wdt], [r_wdts])
        xn1_load(0)
    for st in range(n_l1b):
        xn1_transpose(st)
        for m in range(HB):
            mb = m % 2
            bld.dma("sync", "wg%d" % mb,
                    wgsl[:, mb * DC * 128:(mb + 1) * DC * 128].rearrange(
                        "p (k h) -> p k h", h=128), wgt_r[m], [r_wgt], [r_wgsl])
            bld.dma("sync", "wu%d" % mb,
                    wusl[:, mb * DC * 128:(mb + 1) * DC * 128].rearrange(
                        "p (k h) -> p k h", h=128), wut_r[m], [r_wut], [r_wusl])
            g0_, g1_ = (5, 6) if m % 2 == 0 else (7, 3)
            for c in range(DC):
                first, last = c == 0, c == DC - 1
                bld.mm(RB[g0_], PB[g0_][:, 0:512],
                       wgsl[:, mb * DC * 128 + c * 128: mb * DC * 128 + (c + 1) * 128],
                       xn1T[:, c * TS:(c + 1) * TS], first, last,
                       [r_wgsl, r_xn1T], first=first, last=last)
                bld.mm(RB[g1_], PB[g1_][:, 0:512],
                       wusl[:, mb * DC * 128 + c * 128: mb * DC * 128 + (c + 1) * 128],
                       xn1T[:, c * TS:(c + 1) * TS], first, last,
                       [r_wusl, r_xn1T], first=first, last=last)
            bld.act(trash[:, 512 * (m % 2):512 * (m % 2) + 512], PB[g0_][:, 0:512],
                    AF.Silu, [RB[g0_], r_kc], [r_trash], scale=kconst[:, 3:4])
            bld.tt("vector", hidT[:, m * TS:(m + 1) * TS], PB[g1_][:, 0:512],
                   trash[:, 512 * (m % 2):512 * (m % 2) + 512], ALU.mult,
                   [RB[g1_], r_trash], [r_hidT])
        if st == 0:
            bld.dma("sync", "ein", e_in.ap(), eloc[:, 0:2], [r_eloc], [r_ein])
            bld.op("gpsimd", lambda h: h.collective_compute(
                "AllGather", ALU.bypass,
                replica_groups=[[0, 1], [2, 3], [4, 5], [6, 7]],
                ins=[e_in.ap()], outs=[e_out.ap()]), [r_ein], [r_eout])
            bld.dma("sync", "eout", scr2[:, 0:2], e_out.ap()[0:128, :], [r_eout], [r_scr2])
            bld.ts("vector", eloc[:, 4:5], scr2[:, 0:1], ZMK, None, ALU.mult, ALU.bypass,
                   [r_scr2, r_pcsc], [r_eloc])
            bld.ts("vector", eloc[:, 5:6], scr2[:, 1:2], ZMK, None, ALU.mult, ALU.bypass,
                   [r_scr2, r_pcsc], [r_eloc])
            bld.tt("vector", scr[:, 0:1], pp_re[:, 511:512], are_c[:, :], ALU.mult,
                   [r_ppre, r_are], [r_scr])
            bld.tt("vector", scr[:, 1:2], pp_im[:, 511:512], aim_c[:, :], ALU.mult,
                   [r_ppim, r_aim], [r_scr])
            bld.tt("vector", scr[:, 4:5], scr[:, 0:1], scr[:, 1:2], ALU.subtract,
                   [r_scr], [r_scr])
            bld.tt("vector", scr[:, 0:1], pp_re[:, 511:512], aim_c[:, :], ALU.mult,
                   [r_ppre, r_aim], [r_scr])
            bld.tt("vector", scr[:, 1:2], pp_im[:, 511:512], are_c[:, :], ALU.mult,
                   [r_ppim, r_are], [r_scr])
            bld.tt("vector", scr[:, 5:6], scr[:, 0:1], scr[:, 1:2], ALU.add,
                   [r_scr], [r_scr])
            bld.cp("vector", scr[:, 6:7], are_c[:, :], [r_are], [r_scr])
            bld.cp("vector", scr[:, 7:8], aim_c[:, :], [r_aim], [r_scr])
            for stc in range(NST):
                bld.tt("vector", scr2[:, 0:1], scr[:, 6:7], eloc[:, 4:5], ALU.mult,
                       [r_scr, r_eloc], [r_scr2])
                bld.tt("vector", scr2[:, 1:2], scr[:, 7:8], eloc[:, 5:6], ALU.mult,
                       [r_scr, r_eloc], [r_scr2])
                bld.tt("vector", scr2[:, 4:5], scr2[:, 0:1], scr2[:, 1:2], ALU.subtract,
                       [r_scr2], [r_scr2])
                bld.tt("vector", scr2[:, 0:1], scr[:, 6:7], eloc[:, 5:6], ALU.mult,
                       [r_scr, r_eloc], [r_scr2])
                bld.tt("vector", scr2[:, 1:2], scr[:, 7:8], eloc[:, 4:5], ALU.mult,
                       [r_scr, r_eloc], [r_scr2])
                bld.tt("vector", scr2[:, 5:6], scr2[:, 0:1], scr2[:, 1:2], ALU.add,
                       [r_scr2], [r_scr2])
                bld.ts("vector", scr2[:, 6:7], scr2[:, 5:6], -1.0, None, ALU.mult, ALU.bypass,
                       [r_scr2], [r_scr2])
                hsl = hre_all[:, stc * TS:(stc + 1) * TS]
                bld.stt("vector", hsl, pp_re[:, :], scr2[:, 4:5], hsl, ALU.mult, ALU.add,
                        [r_ppre, r_scr2, r_hre], [r_hre])
                bld.stt("vector", hsl, pp_im[:, :], scr2[:, 6:7], hsl, ALU.mult, ALU.add,
                        [r_ppim, r_scr2, r_hre], [r_hre])
                bld.tt("vector", scr2[:, 0:1], scr[:, 6:7], scr[:, 4:5], ALU.mult,
                       [r_scr], [r_scr2])
                bld.tt("vector", scr2[:, 1:2], scr[:, 7:8], scr[:, 5:6], ALU.mult,
                       [r_scr], [r_scr2])
                bld.tt("vector", scr2[:, 2:3], scr[:, 6:7], scr[:, 5:6], ALU.mult,
                       [r_scr], [r_scr2])
                bld.tt("vector", scr2[:, 3:4], scr[:, 7:8], scr[:, 4:5], ALU.mult,
                       [r_scr], [r_scr2])
                bld.tt("vector", scr[:, 6:7], scr2[:, 0:1], scr2[:, 1:2], ALU.subtract,
                       [r_scr2], [r_scr])
                bld.tt("vector", scr[:, 7:8], scr2[:, 2:3], scr2[:, 3:4], ALU.add,
                       [r_scr2], [r_scr])

        if st + 1 < n_l1b:
            xn1_load(st + 1)
        for sub in range(4):
            ti = st * 4 + sub
            d0_, d1_ = (5, 6) if sub % 2 == 0 else (7, 3)
            y0_, y1_ = (2, 4) if sub % 2 == 0 else (0, 1)
            bld.mm(RB[y0_], PB[y0_][:, 0:512],
                   hre_all[:, ti * 128:(ti + 1) * 128], ct_t[:, 0:512],
                   True, True, [r_hre, r_ct], first=True, last=True)
            bld.mm(RB[y1_], PB[y1_][:, 0:512],
                   hre_all[:, ti * 128:(ti + 1) * 128], ct_t[:, 512:1024],
                   True, True, [r_hre, r_ct], first=True, last=True)
            bld.cp("scalar", y_bf[:, sub * D:sub * D + 512], PB[y0_][:, 0:512],
                   [RB[y0_]], [r_ybf])
            bld.cp("scalar", y_bf[:, sub * D + 512:(sub + 1) * D], PB[y1_][:, 0:512],
                   [RB[y1_]], [r_ybf])
            for m in range(HB):
                first, last = m == 0, m == HB - 1
                lt = hidT[:, m * TS + sub * 128: m * TS + (sub + 1) * 128]
                bld.mm(RB[d0_], PB[d0_][:, 0:512], lt, wdt_s[:, m * D:m * D + 512],
                       first, last, [r_hidT, r_wdts], first=first, last=last)
                bld.mm(RB[d1_], PB[d1_][:, 0:512], lt, wdt_s[:, m * D + 512:(m + 1) * D],
                       first, last, [r_hidT, r_wdts], first=first, last=last)
            bld.act(g2_bf[:, 0:512], PB[d0_][:, 0:512], AF.Silu, [RB[d0_], r_kc],
                    [r_g2bf], scale=kconst[:, 4:5])
            bld.act(g2_bf[:, 512:1024], PB[d1_][:, 0:512], AF.Silu, [RB[d1_], r_kc],
                    [r_g2bf], scale=kconst[:, 4:5])
            bld.tt("vector", out_f[:, 0:512], y_bf[:, sub * D:sub * D + 512],
                   g2_bf[:, 0:512], ALU.mult, [r_ybf, r_g2bf], [r_outf])
            bld.tt("vector", out_f[:, 512:1024],
                   y_bf[:, sub * D + 512:(sub + 1) * D], g2_bf[:, 512:1024],
                   ALU.mult, [r_ybf, r_g2bf], [r_outf])
            x1_stc, r_x1stc = x1_st2[st % 2]
            bld.tt("vector", out_f[:, 0:512], out_f[:, 0:512],
                   x1_stc[:, sub * D:sub * D + 512],
                   ALU.add, [r_outf, r_x1stc], [r_outf])
            bld.tt("vector", out_f[:, 512:1024], out_f[:, 512:1024],
                   x1_stc[:, sub * D + 512:(sub + 1) * D],
                   ALU.add, [r_outf, r_x1stc], [r_outf])
            bld.dma("gpsimd", "out", out_r[st // 2, (st % 2) * 4 + sub], out_f[:, :],
                    [r_outf], [r_out])
        if debug and st == 0:
            bld.dma("gpsimd", "dbg", dbg["d_y"].ap(), y_bf[:, :], [r_ybf], [Res()])

    bld.barrier()
    _finalize(nc, bld, stack, st_l1)
    return nc


def _finalize(nc, bld, stack, st_l1):
    with nc.Block() as blk:
        @blk.tensor
        def _(h):
            for f in bld.q["tensor"]:
                f(h)

        @blk.vector
        def _(h):
            for f in bld.q["vector"]:
                f(h)

        @blk.scalar
        def _(h):
            for f in bld.q["scalar"]:
                f(h)

        @blk.gpsimd
        def _(h):
            for f in bld.q["gpsimd"]:
                f(h)

        @blk.sync
        def _(h):
            for f in bld.q["sync"]:
                f(h)

    st_l1.close()
    stack.close()


def host_inputs(inputs):
    x = np.ascontiguousarray(inputs["L0_attn_hidden_states"], dtype=np.float32)
    Wq = np.asarray(inputs["L0_attn_Wq"], np.float32)
    Wk = np.asarray(inputs["L0_attn_Wk"], np.float32)
    Wv = np.asarray(inputs["L0_attn_Wv"], np.float32)
    Wb = np.asarray(inputs["L0_attn_Wb"], np.float32)
    Wg = np.asarray(inputs["L0_attn_Wg"], np.float32)
    conv_w = np.asarray(inputs["L0_attn_conv_w"], np.float32)
    convw_r = np.ascontiguousarray(
        conv_w.reshape(DC, 128, 4).transpose(1, 0, 2).reshape(128, 32))
    masks = np.zeros((128, 3 * 128), np.float32)
    ii = np.arange(128)
    masks[:, 0:128] = (ii[None, :] < ii[:, None])
    masks[:, 128:256] = (ii[None, :] > ii[:, None])
    masks[:, 256:384] = (ii[None, :] >= ii[:, None])
    bf = ml_dtypes.bfloat16
    common = dict(
        convw=convw_r,
        wo=np.ascontiguousarray(np.asarray(inputs["L0_attn_Wo"], np.float32)),
        normw=np.tile(np.asarray(inputs["L0_attn_norm_w"], np.float32), 2)[None, :],
        packed=np.ascontiguousarray(
            np.asarray(inputs["L0_ffn_packed_weight"], np.int32).reshape(256, D)),
        wscale=np.asarray(inputs["L0_ffn_weight_scale"], np.float32).reshape(1, 1),
        bias=np.asarray(inputs["L0_ffn_bias"], np.float32).reshape(1, D),
        Ar=np.asarray(inputs["L1_attn_A_real"], np.float32).reshape(128, 1),
        Ai=np.asarray(inputs["L1_attn_A_imag"], np.float32).reshape(128, 1),
        Bmat=np.ascontiguousarray(np.asarray(inputs["L1_attn_B"], np.float32)),
        Cmat=np.ascontiguousarray(np.asarray(inputs["L1_attn_C"], np.float32)),
        wg=np.ascontiguousarray(np.asarray(inputs["L1_ffn_w_gate"], np.int32)),
        wu=np.ascontiguousarray(np.asarray(inputs["L1_ffn_w_up"], np.int32)),
        wd=np.ascontiguousarray(np.asarray(inputs["L1_ffn_w_down"], np.int32)),
        idn=np.eye(128).astype(bf),
        masks=masks.astype(bf),
        ones512=np.ones((128, 512)).astype(bf),
        onescol=np.ones((1, 128)).astype(bf),
        kconst=np.tile(np.array([1e-5, 0.5, 1.0 / 256, 1.0 / 127,
                                 1.0 / 127 / 127, 0.0], np.float32), (128, 1)),
    )
    in_maps = []
    for c in range(8):
        b, z = c >> 1, c & 1
        h0 = 2 * z
        wproj = np.concatenate([
            Wq[:, h0 * 128:(h0 + 2) * 128], Wk[:, h0 * 128:(h0 + 2) * 128],
            Wv[:, h0 * 256:(h0 + 2) * 256], Wg[:, h0 * 256:(h0 + 2) * 256],
            Wb[:, h0:h0 + 2]], axis=1)
        pc = np.zeros((128, 4), np.float32)
        pc[:, 0] = 1.0 if z == 0 else 0.0
        pc[:, 1] = 1.0 if z == 1 else 0.0
        pc[:, 2] = 1.0 if z == 1 else 0.0
        m = dict(common)
        m.update(
            x_full=x[b],
            x_own=np.ascontiguousarray(x[b, z * HALF:(z + 1) * HALF]),
            wproj=np.ascontiguousarray(wproj),
            pcsc=pc,
        )
        in_maps.append(m)
    return in_maps


def kernel(**inputs):
    in_maps = host_inputs(inputs)
    nc = build_nc(debug=DEBUG)
    res = run_bass_kernel_spmd(nc, in_maps, core_ids=list(range(8)))
    out = np.empty((B, L, D), np.float32)
    for c in range(8):
        b, z = c >> 1, c & 1
        out[b, z * HALF:(z + 1) * HALF] = res.results[c]["out"]
    return out


# version 19
# speedup vs baseline: 1.1646x; 1.1646x over previous
import numpy as np
import ml_dtypes
import concourse.bass as bass
import concourse.mybir as mybir
from concourse.bass_utils import run_bass_kernel_spmd

F32 = mybir.dt.float32
BF16 = mybir.dt.bfloat16
I32 = mybir.dt.int32
AF = mybir.ActivationFunctionType
ALU = mybir.AluOpType
AXX = mybir.AxisListType.X

B, L, D = 4, 8192, 1024
HALF = L // 2
DC = 8
G0 = 8
TG = 1024
S8 = 8
CH = 128
WCOL = 1538
GW = 4
NST = 8
TS = 512
HB = 24
CMAGIC = 12582912.0
ENGS = ("tensor", "vector", "scalar", "gpsimd", "sync")

DEBUG = False
L0_SUB = 9
SC_SUB = 9
L1_SUB = 9
L0_STAGE = 99


class Res:
    __slots__ = ("w", "r", "pend")
    def __init__(self):
        self.w = None
        self.r = []
        self.pend = []


class Bld:
    def __init__(self, nc, stack):
        self.nc = nc
        self.q = {e: [] for e in ENGS}
        self.scnt = {}
        self.waited = {e: {} for e in ENGS}
        self.esem = {}
        for e in ENGS:
            h = stack.enter_context(nc.semaphore("se_" + e))
            self.esem[e] = h
            self.scnt[id(h)] = 0
            self._hmap = getattr(self, "_hmap", {})
            self._hmap[id(h)] = h
        self._stack = stack
        self._chan = {}

    def _wait(self, eng, tok):
        if tok is None:
            return
        sem, val, peng = tok
        if peng == eng:
            pass
        if self.waited[eng].get(id(sem), 0) >= val:
            return
        self.waited[eng][id(sem)] = val
        self.q[eng].append(lambda h, s=sem, v=val: h.wait_ge(s, v))

    def _hazard(self, eng, reads, writes):
        toks = []
        for r in reads:
            toks.append(r.w)
        for w in writes:
            toks.append(w.w)
            toks.extend(w.r)
        for t in toks:
            self._wait(eng, t)

    def op(self, eng, fn, reads=(), writes=()):
        self._hazard(eng, reads, writes)
        sem = self.esem[eng]
        self.scnt[id(sem)] += 1
        cnt = self.scnt[id(sem)]
        def run(h, fn=fn, sem=sem):
            ins = fn(h)
            ins.then_inc(sem, 1)
        self.q[eng].append(run)
        tok = (sem, cnt, eng)
        for r in reads:
            r.r.append(tok)
        for w in writes:
            w.w = tok
            w.r = []
        return tok

    def dma(self, eng, ch, out_ap, in_ap, reads=(), writes=(), **kw):
        if ch not in self._chan:
            h = self._stack.enter_context(self.nc.semaphore("q_" + ch))
            self._chan[ch] = h
            self.scnt[id(h)] = 0
            self._hmap[id(h)] = h
        qsem = self._chan[ch]
        prev = self.scnt[id(qsem)]
        if prev:
            self._wait(eng, (qsem, prev, None))
        self._hazard(eng, reads, writes)
        self.scnt[id(qsem)] = prev + 16
        cnt = prev + 16
        def run(h, o=out_ap, i=in_ap, qs=qsem, kw=kw):
            h.dma_start(o, i, **kw).then_inc(qs, 16)
        self.q[eng].append(run)
        tok = (qsem, cnt, None)
        for r in reads:
            r.r.append(tok)
        for w in writes:
            w.w = tok
            w.r = []
        return tok

    def barrier(self):
        sems = [(self._hmap[k], v) for k, v in self.scnt.items()]
        for e in ENGS:
            for sem, val in sems:
                if val > 0:
                    self._wait(e, (sem, val, None))

    def cp(self, eng, out, in_, reads, writes):
        if eng == "scalar":
            return self.op(eng, lambda h: h.copy(out, in_), reads, writes)
        return self.op(eng, lambda h: h.tensor_copy(out, in_), reads, writes)

    def tt(self, eng, out, a, bb, alu, reads, writes):
        return self.op(eng, lambda h: h.tensor_tensor(out, a, bb, alu), reads, writes)

    def ts(self, eng, out, in0, s1, s2, op0, op1, reads, writes, accum=None):
        if eng == "scalar":
            assert op0 == ALU.mult and s2 is None and accum is None
            return self.op(eng, lambda h: h.activation(out, in0, AF.Identity,
                                                       scale=s1), reads, writes)
        if s2 is None and accum is None:
            return self.op(eng, lambda h: h.tensor_single_scalar(out, in0, s1, op0),
                           reads, writes)
        return self.op(eng, lambda h: h.tensor_scalar(out, in0, s1, s2, op0, op1,
                                                      accum_out=accum), reads, writes)

    def stt(self, eng, out, in0, sc, in1, op0, op1, reads, writes):
        return self.op(eng, lambda h: h.scalar_tensor_tensor(out, in0, sc, in1, op0, op1),
                       reads, writes)

    def act(self, out, in_, func, reads, writes, bias=0.0, scale=1.0, accum=None):
        return self.op("scalar",
                       lambda h: h.activation(out, in_, func, bias=bias, scale=scale,
                                              accum_out=accum), reads, writes)

    def red(self, out, in_, alu, reads, writes, absval=False):
        return self.op("vector",
                       lambda h: h.tensor_reduce(out, in_, AXX, alu,
                                                 apply_absolute_value=absval),
                       reads, writes)

    def mm(self, bank, out, lhsT, rhs, start, stop, reads, first, last):
        if first:
            self._hazard("tensor", (), (bank,))
            bank.pend = []
        self._hazard("tensor", reads, ())
        bank.pend.extend(reads)
        if not last:
            def run0(h, o=out, lt=lhsT, r=rhs, st=start, sp=stop):
                h.matmul(o, lt, r, start=st, stop=sp)
            self.q["tensor"].append(run0)
            return None
        sem = self.esem["tensor"]
        self.scnt[id(sem)] += 1
        cnt = self.scnt[id(sem)]
        def run(h, o=out, lt=lhsT, r=rhs, st=start, sp=stop, sem=sem):
            h.matmul(o, lt, r, start=st, stop=sp).then_inc(sem, 1)
        self.q["tensor"].append(run)
        tok = (sem, cnt, "tensor")
        for rr in bank.pend:
            rr.r.append(tok)
        bank.pend = []
        bank.w = tok
        bank.r = []
        return tok


def build_nc(debug=False, n_g0=G0, do_w=True, do_l1=True):
    global L0_STAGE, L0_SUB, SC_SUB, L1_SUB
    nc = bass.Bass("TRN2", target_bir_lowering=False)
    import contextlib
    stack = contextlib.ExitStack()

    d_in = {}
    def din(name, shape, dt):
        d_in[name] = nc.dram_tensor(name, shape, dt, kind="ExternalInput")
        return d_in[name]
    x_full = din("x_full", [L, D], F32)
    x_own = din("x_own", [HALF, D], F32)
    wproj = din("wproj", [D, WCOL], F32)
    wo_in = din("wo", [D, D], F32)
    convw_in = din("convw", [128, 32], F32)
    normw_in = din("normw", [1, 512], F32)
    packed_in = din("packed", [256, D], I32)
    wscale_in = din("wscale", [1, 1], F32)
    bias_in = din("bias", [1, D], F32)
    ar_in = din("Ar", [128, 1], F32)
    ai_in = din("Ai", [128, 1], F32)
    bmat_in = din("Bmat", [128, D], F32)
    cmat_in = din("Cmat", [D, 128], F32)
    wg_in = din("wg", [3072, D], I32)
    wu_in = din("wu", [3072, D], I32)
    wd_in = din("wd", [D, 3072], I32)
    idn_in = din("idn", [128, 128], BF16)
    masks_in = din("masks", [128, 3 * 128], BF16)
    ones512_in = din("ones512", [128, 512], BF16)
    onescol_in = din("onescol", [1, 128], BF16)
    pcsc_in = din("pcsc", [128, 4], F32)
    kconst_in = din("kconst", [128, 6], F32)

    out_d = nc.dram_tensor("out", [HALF, D], F32, kind="ExternalOutput")
    dbg = {}
    if debug:
        for nm, shp, dt in (("d_hT", [128, DC * TG], BF16),
                            ("d_qkg", [128, S8 * 512], BF16),
                            ("d_gated", [L, 512], BF16),
                            ("d_x1", [128, 32 * D], BF16),
                            ("d_g0", [128, D], BF16),
                            ("d_xint", [128, D], BF16),
                            ("d_ffnsb", [128, D], F32),
                            ("d_ffnraw", [128, D], F32),
                            ("d_biasb", [128, D], F32),
                            ("d_qcol", [128, 8], F32),
                            ("d_s0", [128, D], F32),
                            ("d_hre", [128, 4096], BF16),
                            ("d_y", [128, 4 * D], BF16),
                            ("d_sb", [128, 512], BF16)):
            dbg[nm] = nc.dram_tensor(nm, shp, dt, kind="ExternalOutput")

    gat_in = nc.dram_tensor("gat_in", [L, 512], BF16)
    gat_out = nc.dram_tensor("gat_out", [2 * L, 512], BF16)
    e_in = nc.dram_tensor("e_in", [128, 2], F32)
    e_out = nc.dram_tensor("e_out", [256, 2], F32)
    wgt_d = nc.dram_tensor("wgt_d", [HB * DC * 128, 128], BF16)
    wut_d = nc.dram_tensor("wut_d", [HB * DC * 128, 128], BF16)
    wdt_d = nc.dram_tensor("wdt_d", [HB * 128, D], BF16)
    r_gin, r_gout = Res(), Res()
    r_ein, r_eout = Res(), Res()
    r_wgt, r_wut, r_wdt = Res(), Res(), Res()
    r_out = Res()

    bld = Bld(nc, stack)
    st_pro = contextlib.ExitStack()
    st_lw = contextlib.ExitStack()
    st_l0 = contextlib.ExitStack()
    st_w = contextlib.ExitStack()
    st_l1 = contextlib.ExitStack()

    def sb(name, shape, dt, stk=None):
        h = (stk or stack).enter_context(nc.sbuf_tensor("s_" + name, shape, dt))
        return h, Res()

    def pb(name, cols=512):
        h = stack.enter_context(nc.psum_tensor(name, [128, cols], F32))
        return h, Res()

    idn, r_idn = sb("idn", [128, 128], BF16)
    msk, r_msk = sb("msk", [128, 3 * 128], BF16)
    ones512, r_ones = sb("ones512", [128, 512], BF16)
    onescol, r_onescol = sb("onescol", [1, 128], BF16)
    pcsc, r_pcsc = sb("pcsc", [128, 4], F32)
    kconst, r_kc = sb("kconst", [128, 6], F32)
    x1_d = nc.dram_tensor("x1_d", [HALF, D], BF16)
    r_x1 = Res()
    x1_dr = x1_d.ap().rearrange("(t p) d -> t p d", p=128)
    stat_r, r_statr = sb("stat_r", [128, 64], F32)
    stat_m, r_statm = sb("stat_m", [128, 64], F32)
    statw_r, r_statwr = sb("statw_r", [128, 32], F32)
    statw_m, r_statwm = sb("statw_m", [128, 32], F32)
    stat1_r, r_stat1r = sb("stat1_r", [128, 32], F32)
    stat1_m, r_stat1m = sb("stat1_m", [128, 32], F32)
    pm_re, r_pmre = sb("pm_re", [128, 512], F32)
    pm_im, r_pmim = sb("pm_im", [128, 512], F32)
    pp_re, r_ppre = sb("pp_re", [128, 512], F32)
    pp_im, r_ppim = sb("pp_im", [128, 512], F32)
    are_c, r_are = sb("are_c", [128, 1], F32)
    aim_c, r_aim = sb("aim_c", [128, 1], F32)
    ct_t, r_ct = sb("ct_t", [128, D], BF16)
    bdt_re, r_bdtre = sb("bdt_re", [128, DC * 128], BF16)
    bdt_im, r_bdtim = sb("bdt_im", [128, DC * 128], BF16)
    trash, r_trash = sb("trash", [128, 1024], BF16)
    sums_g, r_sumsg = sb("sums_g", [128, S8], F32)
    sqs_g, r_sqsg = sb("sqs_g", [128, S8], F32)
    sum1_g, r_sum1g = sb("sum1_g", [128, S8], F32)
    sq1_g, r_sq1g = sb("sq1_g", [128, S8], F32)
    wst = [sb(f"wst{i}", [128, 3072], BF16) for i in range(2)]
    wtt = [sb(f"wtt{i}", [128, 3072], BF16) for i in range(2)]
    scr, r_scr = sb("scr", [128, 16], F32)
    scr2, r_scr2 = sb("scr2", [128, 16], F32)

    pbank = []
    for i in range(8):
        pbank.append(pb(f"pb{i}"))
    PB = [p[0] for p in pbank]
    RB = [p[1] for p in pbank]

    MSL = msk[:, 0:128]
    MSU = msk[:, 128:256]
    MTI = msk[:, 256:384]

    rot_state = [0]
    def rotbank():
        i = rot_state[0]
        rot_state[0] = (i + 1) % 3
        return PB[i], RB[i]

    def evac(eng, out_ap, bank_res, psum_ap, writes, scale=None):
        if scale is None:
            return bld.cp(eng, out_ap, psum_ap, [bank_res], writes)
        return bld.act(out_ap, psum_ap, AF.Copy, [bank_res], writes, scale=scale)

    def transpose128(src_ap, src_res, dst_ap, dst_res, rhs_ap=None, rhs_res=None,
                     eng="vector"):
        bk, bkr = rotbank()
        r = [src_res] + ([rhs_res] if rhs_res is not None else [r_idn])
        bld.mm(bkr, bk[:, 0:128], src_ap, rhs_ap if rhs_ap is not None else idn[:, :],
               True, True, r, first=True, last=True)
        evac(eng, dst_ap, bkr, bk[:, 0:128], [dst_res])

    bld.dma("sync", "cst", idn[:, :], idn_in.ap(), (), (r_idn,))
    bld.dma("sync", "cst2", msk[:, :], masks_in.ap(), (), (r_msk,))
    bld.dma("sync", "cst3", ones512[:, :], ones512_in.ap(), (), (r_ones,))
    bld.dma("sync", "cst4", onescol[:, :], onescol_in.ap(), (), (r_onescol,))
    bld.dma("sync", "cst5", pcsc[:, :], pcsc_in.ap(), (), (r_pcsc,))
    bld.dma("sync", "cst6", kconst[:, :], kconst_in.ap(), (), (r_kc,))

    arc, r_arc = sb("arc", [128, 1], F32, st_pro)
    aic, r_aic = sb("aic", [128, 1], F32, st_pro)
    bld.dma("sync", "ar", arc[:, :], ar_in.ap(), (), (r_arc,))
    bld.dma("sync", "ai", aic[:, :], ai_in.ap(), (), (r_aic,))
    bmat, r_bmat = sb("bmat", [128, D], F32, st_pro)
    bld.dma("sync", "bm", bmat[:, :], bmat_in.ap(), (), (r_bmat,))
    cmat_b, r_cmatb = sb("cmat_b", [128, DC * 128], BF16, st_pro)
    bld.dma("gpsimd", "cm", cmat_b[:, :].rearrange("p (c w) -> p c w", w=128),
            cmat_in.ap().rearrange("(c p) w -> p c w", p=128), (), (r_cmatb,))

    pstage, r_pst = sb("pstage", [128, 2 * D], I32, st_pro)
    bld.dma("sync", "pk", pstage[:, :].rearrange("p (c w) -> p c w", w=D),
            packed_in.ap().rearrange("(c p) w -> p c w", p=128), (), (r_pst,))
    tern, r_tern = sb("tern", [128, DC * D], BF16, st_pro)
    ei, r_ei = sb("ei", [128, 2 * D], I32, st_pro)
    for sh in range(4):
        if sh == 0:
            bld.ts("vector", ei[:, :], pstage[:, :], 3, None,
                   ALU.bitwise_and, ALU.bypass, [r_pst], [r_ei])
        else:
            bld.ts("vector", ei[:, :], pstage[:, :], 2 * sh, 3,
                   ALU.logical_shift_right, ALU.bitwise_and, [r_pst], [r_ei])
        bld.ts("vector", tern[:, 2 * sh * D:(2 * sh + 2) * D], ei[:, :], -1.0, None,
               ALU.add, ALU.bypass, [r_ei], [r_tern])
    ternT_d = nc.dram_tensor("ternT_d", [D, D], BF16)
    r_ternTd = Res()
    tstg, r_tstg = sb("tstg", [128, D], BF16, st_pro)
    ternTd_r = ternT_d.ap().rearrange("(ic p) o -> ic p o", p=128)
    for ic in range(DC):
        for oc in range(DC):
            transpose128(tern[:, oc * D + ic * 128: oc * D + (ic + 1) * 128], r_tern,
                         tstg[:, oc * 128:(oc + 1) * 128], r_tstg,
                         eng=("vector" if (oc + ic) % 2 == 0 else "scalar"))
        bld.dma("sync", "ttd", ternTd_r[ic], tstg[:, :], [r_tstg], [r_ternTd])

    dre, r_dre = sb("dre", [128, 1], F32, st_pro)
    dim, r_dim = sb("dim", [128, 1], F32, st_pro)
    nre, r_nre = sb("nre", [128, 1], F32, st_pro)
    nim, r_nim = sb("nim", [128, 1], F32, st_pro)
    ire, r_ire = sb("ire", [128, 1], F32, st_pro)
    iim, r_iim = sb("iim", [128, 1], F32, st_pro)
    tmp1, r_tmp1 = sb("tmp1", [128, 4], F32, st_pro)
    bld.ts("vector", dre[:, :], arc[:, :], -0.005, 1.0, ALU.mult, ALU.add,
           [r_arc], [r_dre])
    bld.ts("vector", dim[:, :], aic[:, :], -0.005, None, ALU.mult, ALU.bypass,
           [r_aic], [r_dim])
    bld.ts("vector", nre[:, :], arc[:, :], 0.005, 1.0, ALU.mult, ALU.add,
           [r_arc], [r_nre])
    bld.ts("vector", nim[:, :], aic[:, :], 0.005, None, ALU.mult, ALU.bypass,
           [r_aic], [r_nim])
    bld.tt("vector", tmp1[:, 0:1], dre[:, :], dre[:, :], ALU.mult, [r_dre], [r_tmp1])
    bld.stt("vector", tmp1[:, 1:2], dim[:, :], 1.0, dim[:, :], ALU.mult, ALU.mult,
            [r_dim], [r_tmp1])
    bld.tt("vector", tmp1[:, 0:1], tmp1[:, 0:1], tmp1[:, 1:2], ALU.add,
           [r_tmp1], [r_tmp1])
    bld.op("vector", lambda h: h.reciprocal(tmp1[:, 2:3], tmp1[:, 0:1]),
           [r_tmp1], [r_tmp1])
    bld.tt("vector", ire[:, :], dre[:, :], tmp1[:, 2:3], ALU.mult,
           [r_dre, r_tmp1], [r_ire])
    bld.stt("vector", iim[:, :], dim[:, :], -1.0, tmp1[:, 2:3], ALU.mult, ALU.mult,
            [r_dim, r_tmp1], [r_iim])
    bld.tt("vector", tmp1[:, 0:1], nre[:, :], ire[:, :], ALU.mult, [r_nre, r_ire], [r_tmp1])
    bld.tt("vector", tmp1[:, 1:2], nim[:, :], iim[:, :], ALU.mult, [r_nim, r_iim], [r_tmp1])
    bld.tt("vector", are_c[:, :], tmp1[:, 0:1], tmp1[:, 1:2], ALU.subtract,
           [r_tmp1], [r_are])
    bld.tt("vector", tmp1[:, 0:1], nre[:, :], iim[:, :], ALU.mult, [r_nre, r_iim], [r_tmp1])
    bld.tt("vector", tmp1[:, 1:2], nim[:, :], ire[:, :], ALU.mult, [r_nim, r_ire], [r_tmp1])
    bld.tt("vector", aim_c[:, :], tmp1[:, 0:1], tmp1[:, 1:2], ALU.add,
           [r_tmp1], [r_aim])
    iar, r_iar = sb("iar", [128, 1], F32, st_pro)
    iai, r_iai = sb("iai", [128, 1], F32, st_pro)
    bld.tt("vector", tmp1[:, 0:1], are_c[:, :], are_c[:, :], ALU.mult, [r_are], [r_tmp1])
    bld.stt("vector", tmp1[:, 1:2], aim_c[:, :], 1.0, aim_c[:, :], ALU.mult, ALU.mult,
            [r_aim], [r_tmp1])
    bld.tt("vector", tmp1[:, 0:1], tmp1[:, 0:1], tmp1[:, 1:2], ALU.add, [r_tmp1], [r_tmp1])
    bld.op("vector", lambda h: h.reciprocal(tmp1[:, 2:3], tmp1[:, 0:1]), [r_tmp1], [r_tmp1])
    bld.tt("vector", iar[:, :], are_c[:, :], tmp1[:, 2:3], ALU.mult, [r_are, r_tmp1], [r_iar])
    bld.stt("vector", iai[:, :], aim_c[:, :], -1.0, tmp1[:, 2:3], ALU.mult, ALU.mult,
            [r_aim, r_tmp1], [r_iai])

    def build_powers(pre, pim, r_pre, r_pim, base_re_ap, base_im_ap, r_bre, r_bim):
        cwre, r_cwre = sb(f"cw_{pre.name}r", [128, 1], F32, st_pro)
        cwim, r_cwim = sb(f"cw_{pre.name}i", [128, 1], F32, st_pro)
        bld.cp("vector", cwre[:, :], base_re_ap, [r_bre], [r_cwre])
        bld.cp("vector", cwim[:, :], base_im_ap, [r_bim], [r_cwim])
        bld.op("vector", lambda h: h.memset(pre[:, 0:1], 1.0), (), (r_pre,))
        bld.op("vector", lambda h: h.memset(pim[:, 0:1], 0.0), (), (r_pim,))
        w = 1
        tre, r_tre = sb(f"tp_{pre.name}r", [128, 256], F32, st_pro)
        tim, r_tim = sb(f"tp_{pre.name}i", [128, 256], F32, st_pro)
        while w < 512:
            wl = min(w, 512 - w)
            bld.ts("vector", tre[:, 0:wl], pim[:, 0:wl], cwim[:, :], None,
                   ALU.mult, ALU.bypass, [r_pim, r_cwim], [r_tre])
            bld.stt("vector", pre[:, w:w + wl], pre[:, 0:wl], cwre[:, :], tre[:, 0:wl],
                    ALU.mult, ALU.subtract, [r_pre, r_cwre, r_tre], [r_pre])
            bld.ts("vector", tim[:, 0:wl], pim[:, 0:wl], cwre[:, :], None,
                   ALU.mult, ALU.bypass, [r_pim, r_cwre], [r_tim])
            bld.stt("vector", pim[:, w:w + wl], pre[:, 0:wl], cwim[:, :], tim[:, 0:wl],
                    ALU.mult, ALU.add, [r_pre, r_cwim, r_tim], [r_pim])
            bld.tt("vector", tre[:, 0:1], cwre[:, :], cwre[:, :], ALU.mult,
                   [r_cwre], [r_tre])
            bld.tt("vector", tim[:, 0:1], cwim[:, :], cwim[:, :], ALU.mult,
                   [r_cwim], [r_tim])
            bld.tt("vector", tre[:, 1:2], cwre[:, :], cwim[:, :], ALU.mult,
                   [r_cwre, r_cwim], [r_tre])
            bld.tt("vector", cwre[:, :], tre[:, 0:1], tim[:, 0:1], ALU.subtract,
                   [r_tre, r_tim], [r_cwre])
            bld.ts("vector", cwim[:, :], tre[:, 1:2], 2.0, None, ALU.mult, ALU.bypass,
                   [r_tre], [r_cwim])
            w *= 2

    build_powers(pm_re, pm_im, r_pmre, r_pmim, iar[:, :], iai[:, :], r_iar, r_iai)
    build_powers(pp_re, pp_im, r_ppre, r_ppim, are_c[:, :], aim_c[:, :], r_are, r_aim)

    sdre, r_sdre = sb("sdre", [128, 1], F32, st_pro)
    sdim, r_sdim = sb("sdim", [128, 1], F32, st_pro)
    bld.ts("vector", sdre[:, :], ire[:, :], 0.1, None, ALU.mult, ALU.bypass,
           [r_ire], [r_sdre])
    bld.ts("vector", sdim[:, :], iim[:, :], 0.1, None, ALU.mult, ALU.bypass,
           [r_iim], [r_sdim])
    bre_t, r_bret = sb("bre_t", [128, D], BF16, st_pro)
    bim_t, r_bimt = sb("bim_t", [128, D], BF16, st_pro)
    bld.ts("vector", bre_t[:, :], bmat[:, :], sdre[:, :], None, ALU.mult, ALU.bypass,
           [r_bmat, r_sdre], [r_bret])
    bld.ts("vector", bim_t[:, :], bmat[:, :], sdim[:, :], None, ALU.mult, ALU.bypass,
           [r_bmat, r_sdim], [r_bimt])
    for c in range(DC):
        transpose128(bre_t[:, c * 128:(c + 1) * 128], r_bret,
                     bdt_re[:, c * 128:(c + 1) * 128], r_bdtre, eng="vector")
        transpose128(bim_t[:, c * 128:(c + 1) * 128], r_bimt,
                     bdt_im[:, c * 128:(c + 1) * 128], r_bdtim, eng="scalar")
    for c in range(DC):
        transpose128(cmat_b[:, c * 128:(c + 1) * 128], r_cmatb,
                     ct_t[:, c * 128:(c + 1) * 128], r_ct,
                     eng=("vector" if c % 2 == 0 else "scalar"))

    wprep_jobs = []
    for (src, dst, rdst) in ((wg_in, wgt_d, r_wgt), (wu_in, wut_d, r_wut)):
        src_r = src.ap().rearrange("(m p) d -> m p d", p=128)
        dst_r = dst.ap().rearrange("(m k p) h -> m p k h", k=DC, p=128)
        for m in range(HB):
            def job(m=m, src_r=src_r, dst_r=dst_r, rdst=rdst):
                w0, rw0 = wst[m % 2]
                w1, rw1 = wtt[m % 2]
                bld.dma("gpsimd", "wl1%d" % (m % 2), w0[:, 0:D], src_r[m],
                        [rw0], [rw0])
                for kc in range(DC):
                    transpose128(w0[:, kc * 128:(kc + 1) * 128], rw0,
                                 w1[:, kc * 128:(kc + 1) * 128], rw1,
                                 eng=("vector" if kc % 2 == 0 else "scalar"))
                bld.dma("sync", "ws1%d" % (m % 2), dst_r[m],
                        w1[:, 0:DC * 128].rearrange("p (k h) -> p k h", h=128),
                        [rw1], [rdst])
            wprep_jobs.append(job)
    wdt_r = wdt_d.ap().rearrange("(m p) d -> p m d", p=128)
    for rt in range(DC):
        def job(rt=rt):
            w0, rw0 = wst[rt % 2]
            w1, rw1 = wtt[rt % 2]
            bld.dma("gpsimd", "wl1%d" % (rt % 2), w0[:, 0:3072],
                    wd_in.ap().rearrange("(r p) h -> r p h", p=128)[rt],
                    [rw0], [rw0])
            for m in range(HB):
                transpose128(w0[:, m * 128:(m + 1) * 128], rw0,
                             w1[:, m * 128:(m + 1) * 128], rw1,
                             eng=("vector" if m % 2 == 0 else "scalar"))
            bld.dma("sync", "ws1%d" % (rt % 2), wdt_r[:, :, rt * 128:(rt + 1) * 128],
                    w1[:, 0:HB * 128].rearrange("p (m c) -> p m c", c=128),
                    [rw1], [r_wdt])
        wprep_jobs.append(job)

    st_pro.close()
    bld.barrier()

    x_g, r_xg = sb("x_g", [128, S8 * D], BF16, st_lw)
    wpj_b, r_wpjb = sb("wpj_b", [128, DC * WCOL], BF16, st_l0)
    bld.dma("gpsimd", "wpj", wpj_b[:, :].rearrange("p (c w) -> p c w", w=WCOL),
            wproj.ap().rearrange("(c p) w -> p c w", p=128), (), (r_wpjb,))
    convw, r_convw = sb("convw", [128, 32], F32, st_l0)
    bld.dma("sync", "cvw", convw[:, :], convw_in.ap(), (), (r_convw,))
    dconv, r_dconv = sb("dconv", [128, 32 * 128], BF16, st_l0)
    for cc in range(DC):
        for tap in range(4):
            bld.ts("vector", dconv[:, (cc * 4 + tap) * 128:(cc * 4 + tap + 1) * 128],
                   idn[:, :], convw[:, cc * 4 + tap:cc * 4 + tap + 1], None,
                   ALU.mult, ALU.bypass, [r_idn, r_convw], [r_dconv])
    normw1, r_normw1 = sb("normw1", [1, 512], BF16, st_l0)
    bld.dma("gpsimd", "nw", normw1[:, :], normw_in.ap(), (), (r_normw1,))
    normwb, r_normwb = sb("normwb", [128, 512], BF16, st_l0)
    bk, bkr = rotbank()
    bld.mm(bkr, bk[:, 0:512], onescol[:, :], normw1[:, :], True, True,
           [r_onescol, r_normw1], first=True, last=True)
    evac("vector", normwb[:, :], bkr, bk[:, 0:512], [r_normwb])
    xnT, r_xnT = sb("xnT", [128, DC * (TG + 3)], BF16, st_l0)
    hT, r_hT = sb("hT", [128, DC * TG], BF16, st_l0)
    qkg, r_qkg = sb("qkg", [128, S8 * 512], BF16, st_l0)
    vg, r_vg = sb("vg", [128, S8 * 512], BF16, st_l0)
    sgnw, r_sgnw = sb("sgnw", [128, S8 * 512], BF16, st_l0)
    beta_g, r_betag = sb("beta_g", [128, S8 * 2], F32, st_l0)
    ssq_g, r_ssqg = sb("ssq_g", [128, S8 * 4], F32, st_l0)
    scl_g, r_sclg = sb("scl_g", [128, S8 * 4], F32, st_l0)
    gated_g, r_gatedg = sb("gated_g", [128, S8 * 512], BF16, st_l0)
    xn_bf, r_xnbf = sb("xn_bf", [128, D], BF16, st_l0)
    NI = 8
    scant = []
    for i in range(NI):
        d = {}
        for nm, w in (("dgq", 128), ("dgk", 128), ("dgkb", 128), ("qTn", 128),
                      ("kTn", 128), ("kbTn", 128), ("kn", 128), ("Y", 384),
                      ("TT", 128), ("Pa", 128), ("PTa", 128), ("Pb", 128),
                      ("PTb", 128), ("nW2T", 128), ("W1", 256), ("U", 256),
                      ("Mp", 128)):
            d[nm] = sb(f"sc{nm}_{i}", [128, w], BF16, st_l0)
        scant.append(d)
    S_f32, r_Sf = sb("S_f32", [128, 512], F32, st_l0)
    S_bf, r_Sbf = sb("S_bf", [128, 512], BF16, st_l0)
    betasc, r_betasc = sb("betasc", [128, 8], F32, st_l0)
    orst, r_orst = sb("orst", [128, 8], F32, st_l0)

    bld.op("vector", lambda h: h.memset(S_f32[:, :], 0.0), (), (r_Sf,))
    bld.op("vector", lambda h: h.memset(S_bf[:, :], 0.0), (), (r_Sbf,))
    bld.op("vector", lambda h: h.memset(
        xnT[:, :].rearrange("p (c w) -> p c w", w=TG + 3)[:, :, 0:3], 0.0),
        (), (r_xnT,))

    x_fr = x_full.ap().rearrange("(g s p) d -> g p s d", p=128, s=S8)

    def ln_stats(x_res, x_ap_sub, scol, sqcol, rs=None, rq=None):
        bld.act(trash[:, 0:1024], x_ap_sub, AF.Identity, [x_res],
                [r_trash, rs or r_sumsg], accum=scol)
        bld.act(trash[:, 0:1024], x_ap_sub, AF.Square, [x_res],
                [r_trash, rq or r_sqsg], accum=sqcol)

    def ln_rstd(scol_ap, sqcol_ap, n_inv, rst_ap, mrs_ap, r_rst, r_mrs, ncols,
                rs=None, rq=None):
        m = scr[:, 0:ncols]
        v = scr[:, 8:8 + ncols] if ncols <= 8 else None
        bld.ts("vector", m, scol_ap, n_inv, None, ALU.mult, ALU.bypass,
               [rs or r_sumsg], [r_scr])
        bld.ts("vector", scr2[:, 0:ncols], sqcol_ap, n_inv, None, ALU.mult, ALU.bypass,
               [rq or r_sqsg], [r_scr2])
        bld.stt("vector", v, m, -1.0, m, ALU.mult, ALU.mult, [r_scr], [r_scr])
        bld.tt("vector", v, scr2[:, 0:ncols], v, ALU.add, [r_scr2, r_scr], [r_scr])
        bld.act(scr2[:, 0:ncols], v, AF.Sqrt, [r_scr, r_kc], [r_scr2],
                bias=kconst[:, 0:1])
        bld.op("vector", lambda h, o=rst_ap, i=scr2[:, 0:ncols]: h.reciprocal(o, i),
               [r_scr2], [r_rst])
        bld.ts("vector", scr2[:, 8:8 + ncols], v, 1e-5, None, ALU.add, ALU.bypass,
               [r_scr], [r_scr2])
        bld.tt("vector", v, rst_ap, rst_ap, ALU.mult, [r_rst], [r_scr])
        bld.tt("vector", v, v, scr2[:, 8:8 + ncols], ALU.mult, [r_scr, r_scr2], [r_scr])
        bld.ts("vector", v, v, -0.5, 1.5, ALU.mult, ALU.add, [r_scr], [r_scr])
        bld.tt("vector", rst_ap, rst_ap, v, ALU.mult, [r_rst, r_scr], [r_rst])
        bld.stt("vector", mrs_ap, m, -1.0, rst_ap, ALU.mult, ALU.mult,
                [r_scr, r_rst], [r_mrs])

    def emitA1(g):
        xv = x_g[:, :].rearrange("p (s d) -> p s d", d=D)
        bld.dma("gpsimd", "xg", xv[:, 0:4], x_fr[g][:, 0:4], [r_xg], [r_xg])
        bld.dma("gpsimd", "xgb", xv[:, 4:8], x_fr[g][:, 4:8], [r_xg], [r_xg])
        for s in range(S8):
            ln_stats(r_xg, x_g[:, s * D:(s + 1) * D], sums_g[:, s:s + 1],
                     sqs_g[:, s:s + 1])
        ln_rstd(sums_g[:, 0:S8], sqs_g[:, 0:S8], 1.0 / D,
                stat_r[:, g * 8:(g + 1) * 8], stat_m[:, g * 8:(g + 1) * 8],
                r_statr, r_statm, 8)

    def emitA2(g):

        if g > 0:
            xr3 = xnT[:, :].rearrange("p (c w) -> p c w", w=TG + 3)
            bld.cp("vector", xr3[:, :, 0:3], xr3[:, :, TG:TG + 3], [r_xnT], [r_xnT])
        for s in range(S8):
            bld.act(xn_bf[:, 0:D], x_g[:, s * D:(s + 1) * D], AF.Identity,
                    [r_xg, r_statr, r_statm], [r_xnbf],
                    scale=stat_r[:, g * 8 + s:g * 8 + s + 1],
                    bias=stat_m[:, g * 8 + s:g * 8 + s + 1])
            for c in range(DC):
                transpose128(xn_bf[:, c * 128:(c + 1) * 128], r_xnbf,
                             xnT[:, c * (TG + 3) + 3 + s * 128:
                                 c * (TG + 3) + 3 + (s + 1) * 128], r_xnT,
                             eng=("vector" if (s + c) % 2 else "scalar"))
        for c in range(DC):
            src = xnT[:, c * (TG + 3):(c + 1) * (TG + 3)]
            for nh in range(2):
                bk, bkr = rotbank()
                for tap in range(4):
                    bld.mm(bkr, bk[:, 0:512],
                           dconv[:, (c * 4 + tap) * 128:(c * 4 + tap + 1) * 128],
                           src[:, nh * 512 + tap: nh * 512 + tap + 512],
                           tap == 0, tap == 3, [r_dconv, r_xnT],
                           first=(tap == 0), last=(tap == 3))
                bld.act(hT[:, c * TG + nh * 512: c * TG + (nh + 1) * 512],
                        bk[:, 0:512], AF.Silu, [bkr], [r_hT])


    def emitB(g):
        for s in range(S8):
            if s % 2 == 0:
                bq, bk_, bv, bg = 5, 6, 7, 3
            else:
                bq, bk_, bv, bg = 0, 1, 2, 4
            for kc in range(DC):
                lt = hT[:, kc * TG + s * 128: kc * TG + (s + 1) * 128]
                w0 = kc * WCOL
                first, last = kc == 0, kc == DC - 1
                bld.mm(RB[bq], PB[bq][:, 0:256], lt, wpj_b[:, w0:w0 + 256],
                       first, False, [r_hT, r_wpjb], first=first, last=False)
                bld.mm(RB[bq], PB[bq][:, 256:258], lt, wpj_b[:, w0 + 1536:w0 + 1538],
                       False, last, [r_hT, r_wpjb], first=False, last=last)
                bld.mm(RB[bk_], PB[bk_][:, 0:256], lt, wpj_b[:, w0 + 256:w0 + 512],
                       first, last, [r_hT, r_wpjb], first=first, last=last)
                bld.mm(RB[bv], PB[bv][:, 0:512], lt, wpj_b[:, w0 + 512:w0 + 1024],
                       first, last, [r_hT, r_wpjb], first=first, last=last)
                bld.mm(RB[bg], PB[bg][:, 0:512], lt, wpj_b[:, w0 + 1024:w0 + 1536],
                       first, last, [r_hT, r_wpjb], first=first, last=last)
            bld.cp("vector", qkg[:, s * 512:s * 512 + 256], PB[bq][:, 0:256],
                   [RB[bq]], [r_qkg])
            bld.cp("vector", scr[:, 10:12], PB[bq][:, 256:258], [RB[bq]], [r_scr])
            bld.act(scr[:, 8:10], scr[:, 10:12], AF.Tanh, [r_scr, r_kc],
                    [r_scr], scale=kconst[:, 1:2])
            bld.ts("vector", beta_g[:, s * 2:s * 2 + 2], scr[:, 8:10], 0.5, 0.5,
                   ALU.mult, ALU.add, [r_scr], [r_betag])
            bld.cp("vector", qkg[:, s * 512 + 256:s * 512 + 512], PB[bk_][:, 0:256],
                   [RB[bk_]], [r_qkg])
            bld.cp("scalar", vg[:, s * 512:(s + 1) * 512], PB[bv][:, 0:512],
                   [RB[bv]], [r_vg])
            bld.act(sgnw[:, s * 512:(s + 1) * 512], PB[bg][:, 0:512], AF.Silu,
                    [RB[bg]], [r_sgnw])
            bld.tt("vector", sgnw[:, s * 512:(s + 1) * 512],
                   sgnw[:, s * 512:(s + 1) * 512], normwb[:, :], ALU.mult,
                   [r_sgnw, r_normwb], [r_sgnw])
            bld.tt("vector", trash[:, 0:512], qkg[:, s * 512:(s + 1) * 512],
                   qkg[:, s * 512:(s + 1) * 512], ALU.mult, [r_qkg], [r_trash])
            bld.red(ssq_g[:, s * 4:(s + 1) * 4],
                    trash[:, 0:512].rearrange("p (h t) -> p h t", h=4),
                    ALU.add, [r_trash], [r_ssqg])
        bld.act(scl_g[:, :], ssq_g[:, :], AF.Sqrt, [r_ssqg], [r_sclg])
        bld.ts("vector", scl_g[:, :], scl_g[:, :], 1e-12, None, ALU.max, ALU.bypass,
               [r_sclg], [r_sclg])
        bld.op("vector", lambda h: h.reciprocal(scl_g[:, :], scl_g[:, :]),
               [r_sclg], [r_sclg])


    def emitC(g):
        for half in range(2):
            insts = []
            for cb in range(4):
                s4 = half * 4 + cb
                for hh in range(2):
                    insts.append((cb * 2 + hh, s4, hh))
            for i, s4, hh in insts:
                T = scant[i]
                kt = qkg[:, s4 * 512 + 256 + hh * 128: s4 * 512 + 256 + (hh + 1) * 128]
                vt = vg[:, s4 * 512 + hh * 256: s4 * 512 + (hh + 1) * 256]
                bcol = beta_g[:, s4 * 2 + hh: s4 * 2 + hh + 1]
                sqc = scl_g[:, s4 * 4 + hh: s4 * 4 + hh + 1]
                skc = scl_g[:, s4 * 4 + 2 + hh: s4 * 4 + 3 + hh]
                bld.ts("vector", T["dgq"][0][:, :], idn[:, :], sqc, None,
                       ALU.mult, ALU.bypass, [r_idn, r_sclg], [T["dgq"][1]])
                bld.ts("vector", T["dgk"][0][:, :], idn[:, :], skc, None,
                       ALU.mult, ALU.bypass, [r_idn, r_sclg], [T["dgk"][1]])
                bld.tt("vector", betasc[:, i:i + 1], bcol, skc, ALU.mult,
                       [r_betag, r_sclg], [r_betasc])
                bld.ts("vector", T["dgkb"][0][:, :], idn[:, :], betasc[:, i:i + 1],
                       None, ALU.mult, ALU.bypass, [r_idn, r_betasc], [T["dgkb"][1]])
                bld.ts("scalar", T["kn"][0][:, :], kt, skc, None, ALU.mult,
                       ALU.bypass, [r_qkg, r_sclg], [T["kn"][1]])
                bld.ts("scalar", T["Y"][0][:, 256:384], kt, betasc[:, i:i + 1],
                       None, ALU.mult, ALU.bypass, [r_qkg, r_betasc], [T["Y"][1]])
                bld.ts("scalar", T["Y"][0][:, 0:256], vt, bcol, None,
                       ALU.mult, ALU.bypass, [r_vg, r_betag], [T["Y"][1]])
            for i, s4, hh in insts:
                T = scant[i]
                qt = qkg[:, s4 * 512 + hh * 128: s4 * 512 + (hh + 1) * 128]
                kt = qkg[:, s4 * 512 + 256 + hh * 128: s4 * 512 + 256 + (hh + 1) * 128]
                transpose128(qt, r_qkg, T["qTn"][0][:, :], T["qTn"][1],
                             T["dgq"][0][:, :], T["dgq"][1],
                             "vector" if i % 2 == 0 else "scalar")
                transpose128(kt, r_qkg, T["kTn"][0][:, :], T["kTn"][1],
                             T["dgk"][0][:, :], T["dgk"][1],
                             "scalar" if i % 2 == 0 else "vector")
                transpose128(kt, r_qkg, T["kbTn"][0][:, :], T["kbTn"][1],
                             T["dgkb"][0][:, :], T["dgkb"][1],
                             "vector" if i % 2 == 0 else "scalar")
            for i, s4, hh in insts:
                T = scant[i]
                bk, bkr = rotbank()
                bld.mm(bkr, bk[:, 0:128], T["kbTn"][0][:, :], T["kTn"][0][:, :],
                       True, True, [T["kbTn"][1], T["kTn"][1]], first=True, last=True)
                bld.tt("vector", T["Pa"][0][:, :], bk[:, 0:128], MSL, ALU.mult,
                       [bkr, r_msk], [T["Pa"][1]])
                bk, bkr = rotbank()
                bld.mm(bkr, bk[:, 0:128], T["kTn"][0][:, :], T["kbTn"][0][:, :],
                       True, True, [T["kTn"][1], T["kbTn"][1]], first=True, last=True)
                bld.tt("vector", T["PTa"][0][:, :], bk[:, 0:128], MSU, ALU.mult,
                       [bkr, r_msk], [T["PTa"][1]])
                bld.stt("vector", T["TT"][0][:, :], T["PTa"][0][:, :], -1.0,
                        idn[:, :], ALU.mult, ALU.add, [T["PTa"][1], r_idn],
                        [T["TT"][1]])
            curs = {i: ("Pa", "PTa") for i, _, _ in insts}
            nxts = {i: ("Pb", "PTb") for i, _, _ in insts}
            for kk in range(1, 7):
                for i, s4, hh in insts:
                    T = scant[i]
                    pc, ptc = (T[curs[i][0]], T[curs[i][1]])
                    pn, ptn = (T[nxts[i][0]], T[nxts[i][1]])
                    bk, bkr = rotbank()
                    bld.mm(bkr, bk[:, 0:128], ptc[0][:, :], pc[0][:, :], True, True,
                           [ptc[1], pc[1]], first=True, last=True)
                    evac("vector" if i % 2 else "scalar", pn[0][:, :], bkr,
                         bk[:, 0:128], [pn[1]])
                    if kk < 6:
                        bk, bkr = rotbank()
                        bld.mm(bkr, bk[:, 0:128], pc[0][:, :], ptc[0][:, :], True,
                               True, [pc[1], ptc[1]], first=True, last=True)
                        evac("scalar" if i % 2 else "vector", ptn[0][:, :], bkr,
                             bk[:, 0:128], [ptn[1]])
                for i, s4, hh in insts:
                    T = scant[i]
                    pn = T[nxts[i][0]]
                    bk, bkr = rotbank()
                    bld.mm(bkr, bk[:, 0:128], pn[0][:, :], T["TT"][0][:, :], True,
                           True, [pn[1], T["TT"][1]], first=True, last=True)
                    bld.tt("vector", T["TT"][0][:, :], T["TT"][0][:, :],
                           bk[:, 0:128], ALU.add, [T["TT"][1], bkr], [T["TT"][1]])
                    curs[i], nxts[i] = nxts[i], curs[i]
            for i, s4, hh in insts:
                T = scant[i]
                bk, bkr = rotbank()
                bld.mm(bkr, bk[:, 0:128], T["Y"][0][:, 256:384], T["TT"][0][:, :],
                       True, True, [T["Y"][1], T["TT"][1]], first=True, last=True)
                evac("scalar", T["nW2T"][0][:, :], bkr, bk[:, 0:128],
                     [T["nW2T"][1]], scale=-1.0)
                bk, bkr = rotbank()
                bld.mm(bkr, bk[:, 0:256], T["TT"][0][:, :], T["Y"][0][:, 0:256],
                       True, True, [T["TT"][1], T["Y"][1]], first=True, last=True)
                evac("vector", T["W1"][0][:, :], bkr, bk[:, 0:256], [T["W1"][1]])
            for cb in range(4):
                s4 = half * 4 + cb
                UB = [(PB[5], RB[5]), (PB[6], RB[6])]
                OB = [(PB[3], RB[3]), (PB[7], RB[7])]
                for hh in range(2):
                    i = cb * 2 + hh
                    T = scant[i]
                    Ssl = S_bf[:, hh * 256:(hh + 1) * 256]
                    ub, ubr = UB[hh]
                    bld.mm(ubr, ub[:, 0:256], T["nW2T"][0][:, :], Ssl, True, True,
                           [T["nW2T"][1], r_Sbf], first=True, last=True)
                    bld.tt("vector", T["U"][0][:, :], ub[:, 0:256],
                           T["W1"][0][:, :], ALU.add, [ubr, T["W1"][1]],
                           [T["U"][1]])
                    bk, bkr = rotbank()
                    bld.mm(bkr, bk[:, 0:128], T["kTn"][0][:, :], T["qTn"][0][:, :],
                           True, True, [T["kTn"][1], T["qTn"][1]], first=True,
                           last=True)
                    bld.tt("vector", T["Mp"][0][:, :], bk[:, 0:128], MTI, ALU.mult,
                           [bkr, r_msk], [T["Mp"][1]])
                for hh in range(2):
                    i = cb * 2 + hh
                    T = scant[i]
                    Ssl = S_bf[:, hh * 256:(hh + 1) * 256]
                    ob, obr = OB[hh]
                    bld.mm(obr, ob[:, 0:256], T["qTn"][0][:, :], Ssl, True, False,
                           [T["qTn"][1], r_Sbf], first=True, last=False)
                    bld.mm(obr, ob[:, 0:256], T["Mp"][0][:, :], T["U"][0][:, :],
                           False, True, [T["Mp"][1], T["U"][1]], first=False,
                           last=True)
                for hh in range(2):
                    i = cb * 2 + hh
                    T = scant[i]
                    bld.mm(RB[4], PB[4][:, hh * 256:(hh + 1) * 256],
                           T["kn"][0][:, :], T["U"][0][:, :], hh == 0, hh == 1,
                           [T["kn"][1], T["U"][1]], first=(hh == 0),
                           last=(hh == 1))
                for hh in range(2):
                    i = cb * 2 + hh
                    ob, obr = OB[hh]
                    oc0 = 4 * hh
                    bld.act(trash[:, 256 * hh:256 * hh + 256], ob[:, 0:256],
                            AF.Square, [obr], [r_trash, r_orst],
                            accum=orst[:, oc0:oc0 + 1])
                    bld.act(orst[:, oc0 + 1:oc0 + 2], orst[:, oc0:oc0 + 1],
                            AF.Sqrt, [r_orst, r_kc], [r_orst],
                            scale=kconst[:, 2:3], bias=kconst[:, 0:1])
                    bld.op("vector", lambda h, a=orst[:, oc0 + 2:oc0 + 3],
                           b=orst[:, oc0 + 1:oc0 + 2]: h.reciprocal(a, b),
                           [r_orst], [r_orst])
                    bld.stt("vector",
                            gated_g[:, s4 * 512 + hh * 256: s4 * 512 + (hh + 1) * 256],
                            ob[:, 0:256], orst[:, oc0 + 2:oc0 + 3],
                            sgnw[:, s4 * 512 + hh * 256: s4 * 512 + (hh + 1) * 256],
                            ALU.mult, ALU.mult, [obr, r_orst, r_sgnw], [r_gatedg])
                bld.tt("vector", S_f32[:, :], S_f32[:, :], PB[4][:, 0:512], ALU.add,
                       [r_Sf, RB[4]], [r_Sf])
                bld.cp("vector", S_bf[:, :], S_f32[:, :], [r_Sf], [r_Sbf])
        bld.dma("sync", "gat",
                gat_in.ap().rearrange("(g s p) c -> g p s c", p=128, s=S8)[g],
                gated_g[:, :].rearrange("p (s c) -> p s c", c=512),
                [r_gatedg], [r_gin])
        if debug:
            bld.dma("gpsimd", "dbg",
                    dbg["d_gated"].ap().rearrange("(g s p) c -> g p s c", p=128,
                                                  s=S8)[g],
                    gated_g[:, :].rearrange("p (s c) -> p s c", c=512),
                    [r_gatedg], [Res()])
        if debug and g == 0:
            bld.dma("gpsimd", "dbg", dbg["d_hT"].ap(), hT[:, :], [r_hT], [Res()])
            bld.dma("gpsimd", "dbg", dbg["d_qkg"].ap(), qkg[:, :], [r_qkg], [Res()])


    wjob_i = [0]
    def emit_wjobs(k):
        while wjob_i[0] < len(wprep_jobs) and k > 0:
            wprep_jobs[wjob_i[0]]()
            wjob_i[0] += 1
            k -= 1
    if n_g0 > 0:
        emitA1(0)
        emitA2(0)
    for g in range(n_g0):
        emitB(g)
        if g + 1 < n_g0:
            emitA1(g + 1)
        emitC(g)
        emit_wjobs(8)
        if g + 1 < n_g0:
            emitA2(g + 1)
    emit_wjobs(99)
    bld.barrier()
    st_l0.close()

    if do_w:
        bld.op("gpsimd", lambda h: h.collective_compute(
            "AllGather", ALU.bypass,
            replica_groups=[[0, 1], [2, 3], [4, 5], [6, 7]],
            ins=[gat_in.ap()], outs=[gat_out.ap()]), [r_gin], [r_gout])

    x_g2, r_xg2 = sb("x_g2", [128, S8 * D], BF16, st_w)
    xo_b = [(x_g, r_xg), (x_g2, r_xg2)]
    ternT, r_ternT = sb("ternT", [128, DC * D], BF16, st_w)
    bld.dma("sync", "ttl", ternT[:, :].rearrange("p (c w) -> p c w", w=D),
            ternT_d.ap().rearrange("(c p) w -> p c w", p=128),
            [r_ternTd], (r_ternT,))
    wo_b, r_wob = sb("wo_b", [128, DC * D], BF16, st_w)
    bld.dma("gpsimd", "wol", wo_b[:, :].rearrange("p (c w) -> p c w", w=D),
            wo_in.ap().rearrange("(c p) w -> p c w", p=128), (), (r_wob,))
    bias1, r_bias1 = sb("bias1", [1, D], BF16, st_w)
    bld.dma("gpsimd", "bi", bias1[:, :], bias_in.ap(), (), (r_bias1,))
    wsc1, r_wsc1 = sb("wsc1", [1, 1], BF16, st_w)
    bld.dma("gpsimd", "wsc", wsc1[:, :], wscale_in.ap(), (), (r_wsc1,))
    biasb, r_biasb = sb("biasb", [128, D], F32, st_w)
    for half in range(2):
        bk, bkr = rotbank()
        bld.mm(bkr, bk[:, 0:512], onescol[:, :], bias1[:, half * 512:(half + 1) * 512],
               True, True, [r_onescol, r_bias1], first=True, last=True)
        evac("vector", biasb[:, half * 512:(half + 1) * 512], bkr, bk[:, 0:512],
             [r_biasb])
    ws_col, r_ws = sb("ws_col", [128, 1], F32, st_w)
    bk, bkr = rotbank()
    bld.mm(bkr, bk[:, 0:1], onescol[:, :], wsc1[:, :], True, True,
           [r_onescol, r_wsc1], first=True, last=True)
    evac("vector", ws_col[:, :], bkr, bk[:, 0:1], [r_ws])
    xn_f = [sb(f"xn_f{i}", [128, D], F32, st_w) for i in range(2)]
    tq1 = [sb(f"tq1_{i}", [128, D], F32, st_w) for i in range(2)]
    tq2_s = sb("tq2_s", [128, D], F32, st_w)
    tq2 = [tq2_s, tq2_s]
    xint = [sb(f"xint{i}", [128, D], BF16, st_w) for i in range(2)]
    xq32_s = sb("xq32_s", [128, D], I32, st_w)
    xq32 = [xq32_s, xq32_s]
    xiT_all, r_xiTa = sb("xiT_all", [128, S8 * 1024], BF16, st_w)
    g_all, r_gall = sb("g_all", [128, S8 * D], BF16, st_w)
    gsel0, r_gsel0 = sb("gsel0", [128, S8 * 512], BF16, st_w)
    gsel1, r_gsel1 = sb("gsel1", [128, S8 * 512], BF16, st_w)
    glo, r_glo = sb("glo", [128, S8 * 512], BF16, st_w)
    gT = [sb(f"gT{i}", [128, DC * 128], BF16, st_w) for i in range(2)]
    qcol, r_qcol = sb("qcol", [128, 64], F32, st_w)
    x1t = [sb(f"x1t{i}", [128, D], BF16, st_w) for i in range(2)]

    xo_r = x_own.ap().rearrange("(g s p) d -> g p s d", p=128, s=S8)
    go_r = gat_out.ap().rearrange("(r h g s p) c -> r h g p s c",
                                  p=128, s=S8, g=GW, h=2)
    out_r = out_d.ap().rearrange("(g s p) d -> g s p d", p=128, s=S8)

    MLO = pcsc[:, 0:1]
    MHI = pcsc[:, 1:2]
    ZMK = pcsc[:, 2:3]

    def emitWA(wg, s):
        p = s % 2
        xo_g, r_xog = xo_b[wg % 2]
        bld.act(xn_f[p][0][:, :], xo_g[:, s * D:(s + 1) * D], AF.Identity,
                [r_xog, r_statwr, r_statwm], [xn_f[p][1]],
                scale=statw_r[:, wg * 8 + s:wg * 8 + s + 1],
                bias=statw_m[:, wg * 8 + s:wg * 8 + s + 1])
        q8 = s * 8
        bld.red(qcol[:, q8 + 0:q8 + 1], xn_f[p][0][:, :], ALU.max,
                [xn_f[p][1]], [r_qcol], absval=True)
        bld.ts("vector", qcol[:, q8 + 1:q8 + 2], qcol[:, q8 + 0:q8 + 1], 1e-5, None,
               ALU.max, ALU.bypass, [r_qcol], [r_qcol])
        bld.op("vector", lambda h, a=qcol[:, q8 + 2:q8 + 3],
               b=qcol[:, q8 + 1:q8 + 2]: h.reciprocal(a, b), [r_qcol], [r_qcol])
        bld.ts("vector", qcol[:, q8 + 3:q8 + 4], qcol[:, q8 + 2:q8 + 3], 127.0, None,
               ALU.mult, ALU.bypass, [r_qcol], [r_qcol])
        bld.stt("vector", qcol[:, q8 + 5:q8 + 6], qcol[:, q8 + 1:q8 + 2], 1.0 / 127,
                ws_col[:, :], ALU.mult, ALU.mult, [r_qcol, r_ws], [r_qcol])
        bld.ts("vector", tq1[p][0][:, :], xn_f[p][0][:, :], qcol[:, q8 + 3:q8 + 4],
               None, ALU.mult, ALU.bypass, [xn_f[p][1], r_qcol], [tq1[p][1]])
        bld.cp("vector", xq32[p][0][:, :], tq1[p][0][:, :], [tq1[p][1]], [xq32[p][1]])
        bld.ts("vector", xint[p][0][:, :], xq32[p][0][:, :], 127.0, -128.0, ALU.min,
               ALU.max, [xq32[p][1]], [xint[p][1]])
        for c in range(DC):
            transpose128(xint[p][0][:, c * 128:(c + 1) * 128], xint[p][1],
                         xiT_all[:, s * 1024 + c * 128: s * 1024 + (c + 1) * 128],
                         r_xiTa, eng=("vector" if c % 2 else "scalar"))

    def emitWB(wg, s):
        p = s % 2
        f0, f1 = (5, 6) if s % 2 == 0 else (7, 3)
        q8 = s * 8
        for c in range(DC):
            first, last = c == 0, c == DC - 1
            xs = xiT_all[:, s * 1024 + c * 128: s * 1024 + (c + 1) * 128]
            bld.mm(RB[f0], PB[f0][:, 0:512], xs, ternT[:, c * D:c * D + 512],
                   first, last, [r_xiTa, r_ternT], first=first, last=last)
            bld.mm(RB[f1], PB[f1][:, 0:512], xs, ternT[:, c * D + 512:(c + 1) * D],
                   first, last, [r_xiTa, r_ternT], first=first, last=last)
        bld.stt("vector", tq1[p][0][:, 0:512], PB[f0][:, 0:512],
                qcol[:, q8 + 5:q8 + 6], biasb[:, 0:512], ALU.mult, ALU.add,
                [RB[f0], r_qcol, r_biasb], [tq1[p][1]])
        bld.stt("vector", tq1[p][0][:, 512:1024], PB[f1][:, 0:512],
                qcol[:, q8 + 5:q8 + 6], biasb[:, 512:1024], ALU.mult, ALU.add,
                [RB[f1], r_qcol, r_biasb], [tq1[p][1]])
        bld.act(g_all[:, s * D:(s + 1) * D], tq1[p][0][:, :], AF.Silu,
                [tq1[p][1]], [r_gall])

    def emitWC(wg, s):
        p = s % 2
        for c in range(DC):
            src = gsel0 if c < 4 else gsel1
            sres = r_gsel0 if c < 4 else r_gsel1
            cc = c % 4
            transpose128(src[:, s * 512 + cc * 128: s * 512 + (cc + 1) * 128],
                         sres, gT[p][0][:, c * 128:(c + 1) * 128], gT[p][1],
                         eng=("vector" if c % 2 else "scalar"))
        w0b, w1b = (7, 3) if s % 2 == 0 else (5, 6)
        for c in range(DC):
            first, last = c == 0, c == DC - 1
            bld.mm(RB[w0b], PB[w0b][:, 0:512], gT[p][0][:, c * 128:(c + 1) * 128],
                   wo_b[:, c * D:c * D + 512], first, last,
                   [gT[p][1], r_wob], first=first, last=last)
            bld.mm(RB[w1b], PB[w1b][:, 0:512], gT[p][0][:, c * 128:(c + 1) * 128],
                   wo_b[:, c * D + 512:(c + 1) * D], first, last,
                   [gT[p][1], r_wob], first=first, last=last)
        ti = wg * 8 + s
        xo_g, r_xog = xo_b[wg % 2]
        bld.tt("vector", tq2[p][0][:, 0:512], PB[w0b][:, 0:512],
               g_all[:, s * D:s * D + 512], ALU.mult, [RB[w0b], r_gall],
               [tq2[p][1]])
        bld.tt("vector", x1t[p][0][:, 0:512], tq2[p][0][:, 0:512],
               xo_g[:, s * D:s * D + 512], ALU.add, [tq2[p][1], r_xog], [x1t[p][1]])
        bld.tt("vector", tq2[p][0][:, 512:1024], PB[w1b][:, 0:512],
               g_all[:, s * D + 512:(s + 1) * D], ALU.mult, [RB[w1b], r_gall],
               [tq2[p][1]])
        bld.tt("vector", x1t[p][0][:, 512:1024], tq2[p][0][:, 512:1024],
               xo_g[:, s * D + 512:(s + 1) * D], ALU.add, [tq2[p][1], r_xog],
               [x1t[p][1]])
        ln_stats(x1t[p][1], x1t[p][0][:, :], sum1_g[:, s:s + 1],
                 sq1_g[:, s:s + 1], r_sum1g, r_sq1g)
        bld.dma("sync", "x1s%d" % p, x1_dr[ti], x1t[p][0][:, :], [x1t[p][1]], [r_x1])

    for wg in range(GW if do_w else 0):
        xo_g, r_xog = xo_b[wg % 2]
        bld.dma("gpsimd", "xg%d" % (wg % 2),
                xo_g[:, :].rearrange("p (s d) -> p s d", d=D),
                xo_r[wg], [r_xog], [r_xog])
        for s in range(S8):
            ln_stats(r_xog, xo_g[:, s * D:(s + 1) * D], sums_g[:, s:s + 1],
                     sqs_g[:, s:s + 1])
        ln_rstd(sums_g[:, 0:S8], sqs_g[:, 0:S8], 1.0 / D,
                statw_r[:, wg * 8:(wg + 1) * 8], statw_m[:, wg * 8:(wg + 1) * 8],
                r_statwr, r_statwm, 8)
        for s in range(S8):
            emitWA(wg, s)
            emitWB(wg, s)
        for sl, gres, gsl in ((0, r_gsel0, gsel0), (1, r_gsel1, gsel1)):
            bld.dma("sync", "glo", glo[:, :].rearrange("p (s c) -> p s c", c=512),
                    go_r[sl, 0, wg], [r_glo, r_gout], [r_glo])
            bld.ts("vector", gsl[:, :], glo[:, :], MLO, None, ALU.mult, ALU.bypass,
                   [r_glo, r_pcsc], [gres])
            bld.dma("sync", "glo", glo[:, :].rearrange("p (s c) -> p s c", c=512),
                    go_r[sl, 1, wg], [r_glo, r_gout], [r_glo])
            bld.stt("vector", gsl[:, :], glo[:, :], MHI, gsl[:, :], ALU.mult, ALU.add,
                    [r_glo, r_pcsc, gres], [gres])
        for s in range(S8):
            emitWC(wg, s)
        ln_rstd(sum1_g[:, 0:S8], sq1_g[:, 0:S8], 1.0 / D,
                stat1_r[:, wg * 8:(wg + 1) * 8], stat1_m[:, wg * 8:(wg + 1) * 8],
                r_stat1r, r_stat1m, 8, rs=r_sum1g, rq=r_sq1g)
    if debug and do_w:
        bld.dma("gpsimd", "dbg", dbg["d_x1"].ap().rearrange("p (t d) -> p t d", d=D),
                x1_d.ap().rearrange("(t p) d -> p t d", p=128), [r_x1], [Res()])

    bld.barrier()

    st_w.close()
    st_lw.close()

    xn1, r_xn1 = sb("xn1", [128, D], BF16, st_l1)
    x1_st2 = [sb(f"x1_st{i}", [128, 4 * D], BF16, st_l1) for i in range(2)]
    xn1T, r_xn1T = sb("xn1T", [128, DC * TS], BF16, st_l1)
    hre_all, r_hre = sb("hre_all", [128, HALF], BF16, st_l1)
    cre, r_cre = sb("cre", [128, TS], F32, st_l1)
    cim, r_cim = sb("cim", [128, TS], F32, st_l1)
    zre, r_zre = sb("zre", [128, TS], F32, st_l1)
    zim, r_zim = sb("zim", [128, TS], F32, st_l1)
    tl1, r_tl1 = sb("tl1", [128, TS], F32, st_l1)
    tl2, r_tl2 = sb("tl2", [128, TS], F32, st_l1)
    eloc, r_eloc = sb("eloc", [128, 8], F32, st_l1)
    hidT, r_hidT = sb("hidT", [128, HB * TS], BF16, st_l1)
    wdt_s, r_wdts = sb("wdt_s", [128, HB * D], BF16, st_l1)
    wgsl, r_wgsl = sb("wgsl", [128, 2 * DC * 128], BF16, st_l1)
    wusl, r_wusl = sb("wusl", [128, 2 * DC * 128], BF16, st_l1)
    y_bf, r_ybf = sb("y_bf", [128, 4 * D], BF16, st_l1)
    g2_bf, r_g2bf = sb("g2_bf", [128, D], BF16, st_l1)
    out_f, r_outf = sb("out_f", [128, D], F32, st_l1)

    def xn1_load(st):
        x1_st, r_x1st = x1_st2[st % 2]
        bld.dma("sync", "x1l%d" % (st % 2),
                x1_st[:, :].rearrange("p (t d) -> p t d", d=D),
                x1_dr[st * 4:(st + 1) * 4].rearrange("t p d -> p t d"),
                [r_x1, r_x1st], [r_x1st])

    def xn1_transpose(st):
        x1_st, r_x1st = x1_st2[st % 2]
        for sub in range(4):
            ti = st * 4 + sub
            bld.act(xn1[:, :], x1_st[:, sub * D:(sub + 1) * D], AF.Identity,
                    [r_x1st, r_stat1r, r_stat1m], [r_xn1],
                    scale=stat1_r[:, ti:ti + 1], bias=stat1_m[:, ti:ti + 1])
            for c in range(DC):
                transpose128(xn1[:, c * 128:(c + 1) * 128], r_xn1,
                             xn1T[:, c * TS + sub * 128: c * TS + (sub + 1) * 128],
                             r_xn1T, eng=("vector" if (sub + c) % 2 else "scalar"))

    bld.op("vector", lambda h: h.memset(eloc[:, 2:4], 0.0), (), (r_eloc,))
    n_l1a = (NST if do_l1 else 0) if L1_SUB >= 2 else 0
    if n_l1a:
        xn1_load(0)
    for st in range(n_l1a):
        xn1_transpose(st)
        if st + 1 < n_l1a:
            xn1_load(st + 1)
        b0_, b1_ = (5, 6) if st % 2 == 0 else (7, 3)
        for c in range(DC):
            first, last = c == 0, c == DC - 1
            bld.mm(RB[b0_], PB[b0_][:, 0:512], bdt_re[:, c * 128:(c + 1) * 128],
                   xn1T[:, c * TS:(c + 1) * TS], first, last,
                   [r_bdtre, r_xn1T], first=first, last=last)
            bld.mm(RB[b1_], PB[b1_][:, 0:512], bdt_im[:, c * 128:(c + 1) * 128],
                   xn1T[:, c * TS:(c + 1) * TS], first, last,
                   [r_bdtim, r_xn1T], first=first, last=last)
        bld.tt("vector", tl1[:, :], PB[b0_][:, 0:512], pm_re[:, :], ALU.mult,
               [RB[b0_], r_pmre], [r_tl1])
        bld.tt("vector", tl2[:, :], PB[b1_][:, 0:512], pm_im[:, :], ALU.mult,
               [RB[b1_], r_pmim], [r_tl2])
        bld.tt("vector", cre[:, :], tl1[:, :], tl2[:, :], ALU.subtract,
               [r_tl1, r_tl2], [r_cre])
        bld.tt("vector", tl1[:, :], PB[b1_][:, 0:512], pm_re[:, :], ALU.mult,
               [RB[b1_], r_pmre], [r_tl1])
        bld.tt("vector", tl2[:, :], PB[b0_][:, 0:512], pm_im[:, :], ALU.mult,
               [RB[b0_], r_pmim], [r_tl2])
        bld.tt("vector", cim[:, :], tl1[:, :], tl2[:, :], ALU.add,
               [r_tl1, r_tl2], [r_cim])
        bld.op("vector", lambda h: h.tensor_tensor_scan(
            zre[:, :], ones512[:, :], cre[:, :], eloc[:, 2:3], ALU.mult, ALU.add),
            [r_ones, r_cre, r_eloc], [r_zre])
        bld.op("vector", lambda h: h.tensor_tensor_scan(
            zim[:, :], ones512[:, :], cim[:, :], eloc[:, 3:4], ALU.mult, ALU.add),
            [r_ones, r_cim, r_eloc], [r_zim])
        bld.tt("vector", tl1[:, :], zre[:, :], pp_re[:, :], ALU.mult,
               [r_zre, r_ppre], [r_tl1])
        bld.tt("vector", tl2[:, :], zim[:, :], pp_im[:, :], ALU.mult,
               [r_zim, r_ppim], [r_tl2])
        bld.tt("vector", hre_all[:, st * TS:(st + 1) * TS], tl1[:, :], tl2[:, :],
               ALU.subtract, [r_tl1, r_tl2], [r_hre])
        bld.tt("vector", scr[:, 0:1], pp_re[:, 511:512], zre[:, 511:512], ALU.mult,
               [r_ppre, r_zre], [r_scr])
        bld.tt("vector", scr[:, 1:2], pp_im[:, 511:512], zim[:, 511:512], ALU.mult,
               [r_ppim, r_zim], [r_scr])
        bld.tt("vector", eloc[:, 0:1], scr[:, 0:1], scr[:, 1:2], ALU.subtract,
               [r_scr], [r_eloc])
        bld.tt("vector", scr[:, 0:1], pp_re[:, 511:512], zim[:, 511:512], ALU.mult,
               [r_ppre, r_zim], [r_scr])
        bld.tt("vector", scr[:, 1:2], pp_im[:, 511:512], zre[:, 511:512], ALU.mult,
               [r_ppim, r_zre], [r_scr])
        bld.tt("vector", eloc[:, 1:2], scr[:, 0:1], scr[:, 1:2], ALU.add,
               [r_scr], [r_eloc])
        bld.tt("vector", scr[:, 0:1], are_c[:, :], eloc[:, 0:1], ALU.mult,
               [r_are, r_eloc], [r_scr])
        bld.tt("vector", scr[:, 1:2], aim_c[:, :], eloc[:, 1:2], ALU.mult,
               [r_aim, r_eloc], [r_scr])
        bld.tt("vector", scr[:, 2:3], are_c[:, :], eloc[:, 1:2], ALU.mult,
               [r_are, r_eloc], [r_scr])
        bld.tt("vector", scr[:, 3:4], aim_c[:, :], eloc[:, 0:1], ALU.mult,
               [r_aim, r_eloc], [r_scr])
        bld.tt("vector", eloc[:, 2:3], scr[:, 0:1], scr[:, 1:2], ALU.subtract,
               [r_scr], [r_eloc])
        bld.tt("vector", eloc[:, 3:4], scr[:, 2:3], scr[:, 3:4], ALU.add,
               [r_scr], [r_eloc])

    if not do_l1 or L1_SUB < 3:
        bld.barrier()
        _finalize(nc, bld, stack, st_l1)
        return nc
    if debug:
        bld.dma("gpsimd", "dbg", dbg["d_hre"].ap(), hre_all[:, :], [r_hre], [Res()])

    wgt_r = wgt_d.ap().rearrange("(m k p) h -> m p k h", k=DC, p=128)
    wut_r = wut_d.ap().rearrange("(m k p) h -> m p k h", k=DC, p=128)
    wdt_rr = wdt_d.ap().rearrange("(m p) d -> p m d", p=128)
    n_l1b = NST if L1_SUB >= 4 else 0
    if n_l1b:
        bld.dma("sync", "wdt", wdt_s[:, :].rearrange("p (m d) -> p m d", d=D),
                wdt_rr, [r_wdt], [r_wdts])
        xn1_load(0)
    for st in range(n_l1b):
        xn1_transpose(st)
        for m in range(HB):
            mb = m % 2
            bld.dma("sync", "wg%d" % mb,
                    wgsl[:, mb * DC * 128:(mb + 1) * DC * 128].rearrange(
                        "p (k h) -> p k h", h=128), wgt_r[m], [r_wgt], [r_wgsl])
            bld.dma("sync", "wu%d" % mb,
                    wusl[:, mb * DC * 128:(mb + 1) * DC * 128].rearrange(
                        "p (k h) -> p k h", h=128), wut_r[m], [r_wut], [r_wusl])
            g0_, g1_ = (5, 6) if m % 2 == 0 else (7, 3)
            for c in range(DC):
                first, last = c == 0, c == DC - 1
                bld.mm(RB[g0_], PB[g0_][:, 0:512],
                       wgsl[:, mb * DC * 128 + c * 128: mb * DC * 128 + (c + 1) * 128],
                       xn1T[:, c * TS:(c + 1) * TS], first, last,
                       [r_wgsl, r_xn1T], first=first, last=last)
                bld.mm(RB[g1_], PB[g1_][:, 0:512],
                       wusl[:, mb * DC * 128 + c * 128: mb * DC * 128 + (c + 1) * 128],
                       xn1T[:, c * TS:(c + 1) * TS], first, last,
                       [r_wusl, r_xn1T], first=first, last=last)
            bld.act(trash[:, 512 * (m % 2):512 * (m % 2) + 512], PB[g0_][:, 0:512],
                    AF.Silu, [RB[g0_], r_kc], [r_trash], scale=kconst[:, 3:4])
            bld.tt("vector", hidT[:, m * TS:(m + 1) * TS], PB[g1_][:, 0:512],
                   trash[:, 512 * (m % 2):512 * (m % 2) + 512], ALU.mult,
                   [RB[g1_], r_trash], [r_hidT])
        if st == 0:
            bld.dma("sync", "ein", e_in.ap(), eloc[:, 0:2], [r_eloc], [r_ein])
            bld.op("gpsimd", lambda h: h.collective_compute(
                "AllGather", ALU.bypass,
                replica_groups=[[0, 1], [2, 3], [4, 5], [6, 7]],
                ins=[e_in.ap()], outs=[e_out.ap()]), [r_ein], [r_eout])
            bld.dma("sync", "eout", scr2[:, 0:2], e_out.ap()[0:128, :], [r_eout], [r_scr2])
            bld.ts("vector", eloc[:, 4:5], scr2[:, 0:1], ZMK, None, ALU.mult, ALU.bypass,
                   [r_scr2, r_pcsc], [r_eloc])
            bld.ts("vector", eloc[:, 5:6], scr2[:, 1:2], ZMK, None, ALU.mult, ALU.bypass,
                   [r_scr2, r_pcsc], [r_eloc])
            bld.tt("vector", scr[:, 0:1], pp_re[:, 511:512], are_c[:, :], ALU.mult,
                   [r_ppre, r_are], [r_scr])
            bld.tt("vector", scr[:, 1:2], pp_im[:, 511:512], aim_c[:, :], ALU.mult,
                   [r_ppim, r_aim], [r_scr])
            bld.tt("vector", scr[:, 4:5], scr[:, 0:1], scr[:, 1:2], ALU.subtract,
                   [r_scr], [r_scr])
            bld.tt("vector", scr[:, 0:1], pp_re[:, 511:512], aim_c[:, :], ALU.mult,
                   [r_ppre, r_aim], [r_scr])
            bld.tt("vector", scr[:, 1:2], pp_im[:, 511:512], are_c[:, :], ALU.mult,
                   [r_ppim, r_are], [r_scr])
            bld.tt("vector", scr[:, 5:6], scr[:, 0:1], scr[:, 1:2], ALU.add,
                   [r_scr], [r_scr])
            bld.cp("vector", scr[:, 6:7], are_c[:, :], [r_are], [r_scr])
            bld.cp("vector", scr[:, 7:8], aim_c[:, :], [r_aim], [r_scr])
            for stc in range(NST):
                bld.tt("vector", scr2[:, 0:1], scr[:, 6:7], eloc[:, 4:5], ALU.mult,
                       [r_scr, r_eloc], [r_scr2])
                bld.tt("vector", scr2[:, 1:2], scr[:, 7:8], eloc[:, 5:6], ALU.mult,
                       [r_scr, r_eloc], [r_scr2])
                bld.tt("vector", scr2[:, 4:5], scr2[:, 0:1], scr2[:, 1:2], ALU.subtract,
                       [r_scr2], [r_scr2])
                bld.tt("vector", scr2[:, 0:1], scr[:, 6:7], eloc[:, 5:6], ALU.mult,
                       [r_scr, r_eloc], [r_scr2])
                bld.tt("vector", scr2[:, 1:2], scr[:, 7:8], eloc[:, 4:5], ALU.mult,
                       [r_scr, r_eloc], [r_scr2])
                bld.tt("vector", scr2[:, 5:6], scr2[:, 0:1], scr2[:, 1:2], ALU.add,
                       [r_scr2], [r_scr2])
                bld.ts("vector", scr2[:, 6:7], scr2[:, 5:6], -1.0, None, ALU.mult, ALU.bypass,
                       [r_scr2], [r_scr2])
                hsl = hre_all[:, stc * TS:(stc + 1) * TS]
                bld.stt("vector", hsl, pp_re[:, :], scr2[:, 4:5], hsl, ALU.mult, ALU.add,
                        [r_ppre, r_scr2, r_hre], [r_hre])
                bld.stt("vector", hsl, pp_im[:, :], scr2[:, 6:7], hsl, ALU.mult, ALU.add,
                        [r_ppim, r_scr2, r_hre], [r_hre])
                bld.tt("vector", scr2[:, 0:1], scr[:, 6:7], scr[:, 4:5], ALU.mult,
                       [r_scr], [r_scr2])
                bld.tt("vector", scr2[:, 1:2], scr[:, 7:8], scr[:, 5:6], ALU.mult,
                       [r_scr], [r_scr2])
                bld.tt("vector", scr2[:, 2:3], scr[:, 6:7], scr[:, 5:6], ALU.mult,
                       [r_scr], [r_scr2])
                bld.tt("vector", scr2[:, 3:4], scr[:, 7:8], scr[:, 4:5], ALU.mult,
                       [r_scr], [r_scr2])
                bld.tt("vector", scr[:, 6:7], scr2[:, 0:1], scr2[:, 1:2], ALU.subtract,
                       [r_scr2], [r_scr])
                bld.tt("vector", scr[:, 7:8], scr2[:, 2:3], scr2[:, 3:4], ALU.add,
                       [r_scr2], [r_scr])

        if st + 1 < n_l1b:
            xn1_load(st + 1)
        for sub in range(4):
            ti = st * 4 + sub
            d0_, d1_ = (5, 6) if sub % 2 == 0 else (7, 3)
            y0_, y1_ = (2, 4) if sub % 2 == 0 else (0, 1)
            bld.mm(RB[y0_], PB[y0_][:, 0:512],
                   hre_all[:, ti * 128:(ti + 1) * 128], ct_t[:, 0:512],
                   True, True, [r_hre, r_ct], first=True, last=True)
            bld.mm(RB[y1_], PB[y1_][:, 0:512],
                   hre_all[:, ti * 128:(ti + 1) * 128], ct_t[:, 512:1024],
                   True, True, [r_hre, r_ct], first=True, last=True)
            bld.cp("scalar", y_bf[:, sub * D:sub * D + 512], PB[y0_][:, 0:512],
                   [RB[y0_]], [r_ybf])
            bld.cp("scalar", y_bf[:, sub * D + 512:(sub + 1) * D], PB[y1_][:, 0:512],
                   [RB[y1_]], [r_ybf])
            for m in range(HB):
                first, last = m == 0, m == HB - 1
                lt = hidT[:, m * TS + sub * 128: m * TS + (sub + 1) * 128]
                bld.mm(RB[d0_], PB[d0_][:, 0:512], lt, wdt_s[:, m * D:m * D + 512],
                       first, last, [r_hidT, r_wdts], first=first, last=last)
                bld.mm(RB[d1_], PB[d1_][:, 0:512], lt, wdt_s[:, m * D + 512:(m + 1) * D],
                       first, last, [r_hidT, r_wdts], first=first, last=last)
            bld.act(g2_bf[:, 0:512], PB[d0_][:, 0:512], AF.Silu, [RB[d0_], r_kc],
                    [r_g2bf], scale=kconst[:, 4:5])
            bld.act(g2_bf[:, 512:1024], PB[d1_][:, 0:512], AF.Silu, [RB[d1_], r_kc],
                    [r_g2bf], scale=kconst[:, 4:5])
            bld.tt("vector", out_f[:, 0:512], y_bf[:, sub * D:sub * D + 512],
                   g2_bf[:, 0:512], ALU.mult, [r_ybf, r_g2bf], [r_outf])
            bld.tt("vector", out_f[:, 512:1024],
                   y_bf[:, sub * D + 512:(sub + 1) * D], g2_bf[:, 512:1024],
                   ALU.mult, [r_ybf, r_g2bf], [r_outf])
            x1_stc, r_x1stc = x1_st2[st % 2]
            bld.tt("vector", out_f[:, 0:512], out_f[:, 0:512],
                   x1_stc[:, sub * D:sub * D + 512],
                   ALU.add, [r_outf, r_x1stc], [r_outf])
            bld.tt("vector", out_f[:, 512:1024], out_f[:, 512:1024],
                   x1_stc[:, sub * D + 512:(sub + 1) * D],
                   ALU.add, [r_outf, r_x1stc], [r_outf])
            bld.dma("gpsimd", "out", out_r[st // 2, (st % 2) * 4 + sub], out_f[:, :],
                    [r_outf], [r_out])
        if debug and st == 0:
            bld.dma("gpsimd", "dbg", dbg["d_y"].ap(), y_bf[:, :], [r_ybf], [Res()])

    bld.barrier()
    _finalize(nc, bld, stack, st_l1)
    return nc


def _finalize(nc, bld, stack, st_l1):
    with nc.Block() as blk:
        @blk.tensor
        def _(h):
            for f in bld.q["tensor"]:
                f(h)

        @blk.vector
        def _(h):
            for f in bld.q["vector"]:
                f(h)

        @blk.scalar
        def _(h):
            for f in bld.q["scalar"]:
                f(h)

        @blk.gpsimd
        def _(h):
            for f in bld.q["gpsimd"]:
                f(h)

        @blk.sync
        def _(h):
            for f in bld.q["sync"]:
                f(h)

    st_l1.close()
    stack.close()


def host_inputs(inputs):
    x = np.ascontiguousarray(inputs["L0_attn_hidden_states"], dtype=np.float32)
    Wq = np.asarray(inputs["L0_attn_Wq"], np.float32)
    Wk = np.asarray(inputs["L0_attn_Wk"], np.float32)
    Wv = np.asarray(inputs["L0_attn_Wv"], np.float32)
    Wb = np.asarray(inputs["L0_attn_Wb"], np.float32)
    Wg = np.asarray(inputs["L0_attn_Wg"], np.float32)
    conv_w = np.asarray(inputs["L0_attn_conv_w"], np.float32)
    convw_r = np.ascontiguousarray(
        conv_w.reshape(DC, 128, 4).transpose(1, 0, 2).reshape(128, 32))
    masks = np.zeros((128, 3 * 128), np.float32)
    ii = np.arange(128)
    masks[:, 0:128] = (ii[None, :] < ii[:, None])
    masks[:, 128:256] = (ii[None, :] > ii[:, None])
    masks[:, 256:384] = (ii[None, :] >= ii[:, None])
    bf = ml_dtypes.bfloat16
    common = dict(
        convw=convw_r,
        wo=np.ascontiguousarray(np.asarray(inputs["L0_attn_Wo"], np.float32)),
        normw=np.tile(np.asarray(inputs["L0_attn_norm_w"], np.float32), 2)[None, :],
        packed=np.ascontiguousarray(
            np.asarray(inputs["L0_ffn_packed_weight"], np.int32).reshape(256, D)),
        wscale=np.asarray(inputs["L0_ffn_weight_scale"], np.float32).reshape(1, 1),
        bias=np.asarray(inputs["L0_ffn_bias"], np.float32).reshape(1, D),
        Ar=np.asarray(inputs["L1_attn_A_real"], np.float32).reshape(128, 1),
        Ai=np.asarray(inputs["L1_attn_A_imag"], np.float32).reshape(128, 1),
        Bmat=np.ascontiguousarray(np.asarray(inputs["L1_attn_B"], np.float32)),
        Cmat=np.ascontiguousarray(np.asarray(inputs["L1_attn_C"], np.float32)),
        wg=np.ascontiguousarray(np.asarray(inputs["L1_ffn_w_gate"], np.int32)),
        wu=np.ascontiguousarray(np.asarray(inputs["L1_ffn_w_up"], np.int32)),
        wd=np.ascontiguousarray(np.asarray(inputs["L1_ffn_w_down"], np.int32)),
        idn=np.eye(128).astype(bf),
        masks=masks.astype(bf),
        ones512=np.ones((128, 512)).astype(bf),
        onescol=np.ones((1, 128)).astype(bf),
        kconst=np.tile(np.array([1e-5, 0.5, 1.0 / 256, 1.0 / 127,
                                 1.0 / 127 / 127, 0.0], np.float32), (128, 1)),
    )
    in_maps = []
    for c in range(8):
        b, z = c >> 1, c & 1
        h0 = 2 * z
        wproj = np.concatenate([
            Wq[:, h0 * 128:(h0 + 2) * 128], Wk[:, h0 * 128:(h0 + 2) * 128],
            Wv[:, h0 * 256:(h0 + 2) * 256], Wg[:, h0 * 256:(h0 + 2) * 256],
            Wb[:, h0:h0 + 2]], axis=1)
        pc = np.zeros((128, 4), np.float32)
        pc[:, 0] = 1.0 if z == 0 else 0.0
        pc[:, 1] = 1.0 if z == 1 else 0.0
        pc[:, 2] = 1.0 if z == 1 else 0.0
        m = dict(common)
        m.update(
            x_full=x[b],
            x_own=np.ascontiguousarray(x[b, z * HALF:(z + 1) * HALF]),
            wproj=np.ascontiguousarray(wproj),
            pcsc=pc,
        )
        in_maps.append(m)
    return in_maps


def kernel(**inputs):
    in_maps = host_inputs(inputs)
    nc = build_nc(debug=DEBUG)
    res = run_bass_kernel_spmd(nc, in_maps, core_ids=list(range(8)))
    out = np.empty((B, L, D), np.float32)
    for c in range(8):
        b, z = c >> 1, c & 1
        out[b, z * HALF:(z + 1) * HALF] = res.results[c]["out"]
    return out


# version 20
# speedup vs baseline: 1.1869x; 1.0191x over previous
import numpy as np
import ml_dtypes
import concourse.bass as bass
import concourse.mybir as mybir
from concourse.bass_utils import run_bass_kernel_spmd

F32 = mybir.dt.float32
BF16 = mybir.dt.bfloat16
I32 = mybir.dt.int32
AF = mybir.ActivationFunctionType
ALU = mybir.AluOpType
AXX = mybir.AxisListType.X

B, L, D = 4, 8192, 1024
HALF = L // 2
DC = 8
G0 = 8
TG = 1024
S8 = 8
CH = 128
WCOL = 1538
GW = 4
NST = 8
TS = 512
HB = 24
CMAGIC = 12582912.0
ENGS = ("tensor", "vector", "scalar", "gpsimd", "sync")

DEBUG = False
L0_SUB = 9
SC_SUB = 9
L1_SUB = 9
L0_STAGE = 99


class Res:
    __slots__ = ("w", "r", "pend")
    def __init__(self):
        self.w = None
        self.r = []
        self.pend = []


class Bld:
    def __init__(self, nc, stack):
        self.nc = nc
        self.q = {e: [] for e in ENGS}
        self.scnt = {}
        self.waited = {e: {} for e in ENGS}
        self.esem = {}
        for e in ENGS:
            h = stack.enter_context(nc.semaphore("se_" + e))
            self.esem[e] = h
            self.scnt[id(h)] = 0
            self._hmap = getattr(self, "_hmap", {})
            self._hmap[id(h)] = h
        self._stack = stack
        self._chan = {}

    def _wait(self, eng, tok):
        if tok is None:
            return
        sem, val, peng = tok
        if peng == eng:
            pass
        if self.waited[eng].get(id(sem), 0) >= val:
            return
        self.waited[eng][id(sem)] = val
        self.q[eng].append(lambda h, s=sem, v=val: h.wait_ge(s, v))

    def _hazard(self, eng, reads, writes):
        toks = []
        for r in reads:
            toks.append(r.w)
        for w in writes:
            toks.append(w.w)
            toks.extend(w.r)
        for t in toks:
            self._wait(eng, t)

    def op(self, eng, fn, reads=(), writes=()):
        self._hazard(eng, reads, writes)
        sem = self.esem[eng]
        self.scnt[id(sem)] += 1
        cnt = self.scnt[id(sem)]
        def run(h, fn=fn, sem=sem):
            ins = fn(h)
            ins.then_inc(sem, 1)
        self.q[eng].append(run)
        tok = (sem, cnt, eng)
        for r in reads:
            r.r.append(tok)
        for w in writes:
            w.w = tok
            w.r = []
        return tok

    def dma(self, eng, ch, out_ap, in_ap, reads=(), writes=(), **kw):
        if ch not in self._chan:
            h = self._stack.enter_context(self.nc.semaphore("q_" + ch))
            self._chan[ch] = h
            self.scnt[id(h)] = 0
            self._hmap[id(h)] = h
        qsem = self._chan[ch]
        prev = self.scnt[id(qsem)]
        if prev:
            self._wait(eng, (qsem, prev, None))
        self._hazard(eng, reads, writes)
        self.scnt[id(qsem)] = prev + 16
        cnt = prev + 16
        def run(h, o=out_ap, i=in_ap, qs=qsem, kw=kw):
            h.dma_start(o, i, **kw).then_inc(qs, 16)
        self.q[eng].append(run)
        tok = (qsem, cnt, None)
        for r in reads:
            r.r.append(tok)
        for w in writes:
            w.w = tok
            w.r = []
        return tok

    def barrier(self):
        sems = [(self._hmap[k], v) for k, v in self.scnt.items()]
        for e in ENGS:
            for sem, val in sems:
                if val > 0:
                    self._wait(e, (sem, val, None))

    def cp(self, eng, out, in_, reads, writes):
        if eng == "scalar":
            return self.op(eng, lambda h: h.copy(out, in_), reads, writes)
        return self.op(eng, lambda h: h.tensor_copy(out, in_), reads, writes)

    def tt(self, eng, out, a, bb, alu, reads, writes):
        return self.op(eng, lambda h: h.tensor_tensor(out, a, bb, alu), reads, writes)

    def ts(self, eng, out, in0, s1, s2, op0, op1, reads, writes, accum=None):
        if eng == "scalar":
            assert op0 == ALU.mult and s2 is None and accum is None
            return self.op(eng, lambda h: h.activation(out, in0, AF.Identity,
                                                       scale=s1), reads, writes)
        if s2 is None and accum is None:
            return self.op(eng, lambda h: h.tensor_single_scalar(out, in0, s1, op0),
                           reads, writes)
        return self.op(eng, lambda h: h.tensor_scalar(out, in0, s1, s2, op0, op1,
                                                      accum_out=accum), reads, writes)

    def stt(self, eng, out, in0, sc, in1, op0, op1, reads, writes):
        return self.op(eng, lambda h: h.scalar_tensor_tensor(out, in0, sc, in1, op0, op1),
                       reads, writes)

    def act(self, out, in_, func, reads, writes, bias=0.0, scale=1.0, accum=None):
        return self.op("scalar",
                       lambda h: h.activation(out, in_, func, bias=bias, scale=scale,
                                              accum_out=accum), reads, writes)

    def red(self, out, in_, alu, reads, writes, absval=False):
        return self.op("vector",
                       lambda h: h.tensor_reduce(out, in_, AXX, alu,
                                                 apply_absolute_value=absval),
                       reads, writes)

    def mm(self, bank, out, lhsT, rhs, start, stop, reads, first, last):
        if first:
            self._hazard("tensor", (), (bank,))
            bank.pend = []
        self._hazard("tensor", reads, ())
        bank.pend.extend(reads)
        if not last:
            def run0(h, o=out, lt=lhsT, r=rhs, st=start, sp=stop):
                h.matmul(o, lt, r, start=st, stop=sp)
            self.q["tensor"].append(run0)
            return None
        sem = self.esem["tensor"]
        self.scnt[id(sem)] += 1
        cnt = self.scnt[id(sem)]
        def run(h, o=out, lt=lhsT, r=rhs, st=start, sp=stop, sem=sem):
            h.matmul(o, lt, r, start=st, stop=sp).then_inc(sem, 1)
        self.q["tensor"].append(run)
        tok = (sem, cnt, "tensor")
        for rr in bank.pend:
            rr.r.append(tok)
        bank.pend = []
        bank.w = tok
        bank.r = []
        return tok


def build_nc(debug=False, n_g0=G0, do_w=True, do_l1=True):
    global L0_STAGE, L0_SUB, SC_SUB, L1_SUB
    nc = bass.Bass("TRN2", target_bir_lowering=False)
    import contextlib
    stack = contextlib.ExitStack()

    d_in = {}
    def din(name, shape, dt):
        d_in[name] = nc.dram_tensor(name, shape, dt, kind="ExternalInput")
        return d_in[name]
    x_full = din("x_full", [L, D], F32)
    x_own = din("x_own", [HALF, D], F32)
    wproj = din("wproj", [D, WCOL], F32)
    wo_in = din("wo", [D, D], F32)
    convw_in = din("convw", [128, 32], F32)
    normw_in = din("normw", [1, 512], F32)
    packed_in = din("packed", [256, D], I32)
    wscale_in = din("wscale", [1, 1], F32)
    bias_in = din("bias", [1, D], F32)
    ar_in = din("Ar", [128, 1], F32)
    ai_in = din("Ai", [128, 1], F32)
    bmat_in = din("Bmat", [128, D], F32)
    cmat_in = din("Cmat", [D, 128], F32)
    wg_in = din("wg", [3072, D], I32)
    wu_in = din("wu", [3072, D], I32)
    wd_in = din("wd", [D, 3072], I32)
    idn_in = din("idn", [128, 128], BF16)
    masks_in = din("masks", [128, 3 * 128], BF16)
    ones512_in = din("ones512", [128, 512], BF16)
    onescol_in = din("onescol", [1, 128], BF16)
    pcsc_in = din("pcsc", [128, 4], F32)
    kconst_in = din("kconst", [128, 6], F32)

    out_d = nc.dram_tensor("out", [HALF, D], F32, kind="ExternalOutput")
    dbg = {}
    if debug:
        for nm, shp, dt in (("d_hT", [128, DC * TG], BF16),
                            ("d_qkg", [128, S8 * 512], BF16),
                            ("d_gated", [L, 512], BF16),
                            ("d_x1", [128, 32 * D], BF16),
                            ("d_g0", [128, D], BF16),
                            ("d_xint", [128, D], BF16),
                            ("d_ffnsb", [128, D], F32),
                            ("d_ffnraw", [128, D], F32),
                            ("d_biasb", [128, D], F32),
                            ("d_qcol", [128, 8], F32),
                            ("d_s0", [128, D], F32),
                            ("d_hre", [128, 4096], BF16),
                            ("d_y", [128, 4 * D], BF16),
                            ("d_sb", [128, 512], BF16)):
            dbg[nm] = nc.dram_tensor(nm, shp, dt, kind="ExternalOutput")

    gat_in = nc.dram_tensor("gat_in", [L, 512], BF16)
    gat_out = nc.dram_tensor("gat_out", [2 * L, 512], BF16)
    e_in = nc.dram_tensor("e_in", [128, 2], F32)
    e_out = nc.dram_tensor("e_out", [256, 2], F32)
    wgt_d = nc.dram_tensor("wgt_d", [HB * DC * 128, 128], BF16)
    wut_d = nc.dram_tensor("wut_d", [HB * DC * 128, 128], BF16)
    wdt_d = nc.dram_tensor("wdt_d", [HB * 128, D], BF16)
    r_gin, r_gout = Res(), Res()
    r_ein, r_eout = Res(), Res()
    r_wgt, r_wut, r_wdt = Res(), Res(), Res()
    r_out = Res()

    bld = Bld(nc, stack)
    st_pro = contextlib.ExitStack()
    st_lw = contextlib.ExitStack()
    st_l0 = contextlib.ExitStack()
    st_w = contextlib.ExitStack()
    st_l1 = contextlib.ExitStack()

    def sb(name, shape, dt, stk=None):
        h = (stk or stack).enter_context(nc.sbuf_tensor("s_" + name, shape, dt))
        return h, Res()

    def pb(name, cols=512):
        h = stack.enter_context(nc.psum_tensor(name, [128, cols], F32))
        return h, Res()

    idn, r_idn = sb("idn", [128, 128], BF16)
    msk, r_msk = sb("msk", [128, 3 * 128], BF16)
    ones512, r_ones = sb("ones512", [128, 512], BF16)
    onescol, r_onescol = sb("onescol", [1, 128], BF16)
    pcsc, r_pcsc = sb("pcsc", [128, 4], F32)
    kconst, r_kc = sb("kconst", [128, 6], F32)
    x1_d = nc.dram_tensor("x1_d", [HALF, D], BF16)
    r_x1 = Res()
    x1_dr = x1_d.ap().rearrange("(t p) d -> t p d", p=128)
    stat_r, r_statr = sb("stat_r", [128, 64], F32)
    stat_m, r_statm = sb("stat_m", [128, 64], F32)
    statw_r, r_statwr = sb("statw_r", [128, 32], F32)
    statw_m, r_statwm = sb("statw_m", [128, 32], F32)
    stat1_r, r_stat1r = sb("stat1_r", [128, 32], F32)
    stat1_m, r_stat1m = sb("stat1_m", [128, 32], F32)
    pm_re, r_pmre = sb("pm_re", [128, 512], F32)
    pm_im, r_pmim = sb("pm_im", [128, 512], F32)
    pp_re, r_ppre = sb("pp_re", [128, 512], F32)
    pp_im, r_ppim = sb("pp_im", [128, 512], F32)
    are_c, r_are = sb("are_c", [128, 1], F32)
    aim_c, r_aim = sb("aim_c", [128, 1], F32)
    ct_t, r_ct = sb("ct_t", [128, D], BF16)
    bdt_re, r_bdtre = sb("bdt_re", [128, DC * 128], BF16)
    bdt_im, r_bdtim = sb("bdt_im", [128, DC * 128], BF16)
    trash, r_trash = sb("trash", [128, 1024], BF16)
    sums_g, r_sumsg = sb("sums_g", [128, S8], F32)
    sqs_g, r_sqsg = sb("sqs_g", [128, S8], F32)
    sum1_g, r_sum1g = sb("sum1_g", [128, S8], F32)
    sq1_g, r_sq1g = sb("sq1_g", [128, S8], F32)
    wst = [sb(f"wst{i}", [128, 3072], BF16) for i in range(2)]
    wtt = [sb(f"wtt{i}", [128, 3072], BF16) for i in range(2)]
    scr, r_scr = sb("scr", [128, 16], F32)
    scr2, r_scr2 = sb("scr2", [128, 16], F32)

    pbank = []
    for i in range(8):
        pbank.append(pb(f"pb{i}"))
    PB = [p[0] for p in pbank]
    RB = [p[1] for p in pbank]

    MSL = msk[:, 0:128]
    MSU = msk[:, 128:256]
    MTI = msk[:, 256:384]

    rot_state = [0]
    def rotbank():
        i = rot_state[0]
        rot_state[0] = (i + 1) % 3
        return PB[i], RB[i]

    def evac(eng, out_ap, bank_res, psum_ap, writes, scale=None):
        if scale is None:
            return bld.cp(eng, out_ap, psum_ap, [bank_res], writes)
        return bld.act(out_ap, psum_ap, AF.Copy, [bank_res], writes, scale=scale)

    def transpose128(src_ap, src_res, dst_ap, dst_res, rhs_ap=None, rhs_res=None,
                     eng="vector"):
        bk, bkr = rotbank()
        r = [src_res] + ([rhs_res] if rhs_res is not None else [r_idn])
        bld.mm(bkr, bk[:, 0:128], src_ap, rhs_ap if rhs_ap is not None else idn[:, :],
               True, True, r, first=True, last=True)
        evac(eng, dst_ap, bkr, bk[:, 0:128], [dst_res])

    bld.dma("sync", "cst", idn[:, :], idn_in.ap(), (), (r_idn,))
    bld.dma("sync", "cst2", msk[:, :], masks_in.ap(), (), (r_msk,))
    bld.dma("sync", "cst3", ones512[:, :], ones512_in.ap(), (), (r_ones,))
    bld.dma("sync", "cst4", onescol[:, :], onescol_in.ap(), (), (r_onescol,))
    bld.dma("sync", "cst5", pcsc[:, :], pcsc_in.ap(), (), (r_pcsc,))
    bld.dma("sync", "cst6", kconst[:, :], kconst_in.ap(), (), (r_kc,))

    arc, r_arc = sb("arc", [128, 1], F32, st_pro)
    aic, r_aic = sb("aic", [128, 1], F32, st_pro)
    bld.dma("sync", "ar", arc[:, :], ar_in.ap(), (), (r_arc,))
    bld.dma("sync", "ai", aic[:, :], ai_in.ap(), (), (r_aic,))
    bmat, r_bmat = sb("bmat", [128, D], F32, st_pro)
    bld.dma("sync", "bm", bmat[:, :], bmat_in.ap(), (), (r_bmat,))
    cmat_b, r_cmatb = sb("cmat_b", [128, DC * 128], BF16, st_pro)
    bld.dma("gpsimd", "cm", cmat_b[:, :].rearrange("p (c w) -> p c w", w=128),
            cmat_in.ap().rearrange("(c p) w -> p c w", p=128), (), (r_cmatb,))

    pstage, r_pst = sb("pstage", [128, 2 * D], I32, st_pro)
    bld.dma("sync", "pk", pstage[:, :].rearrange("p (c w) -> p c w", w=D),
            packed_in.ap().rearrange("(c p) w -> p c w", p=128), (), (r_pst,))
    tern, r_tern = sb("tern", [128, DC * D], BF16, st_pro)
    ei, r_ei = sb("ei", [128, 2 * D], I32, st_pro)
    for sh in range(4):
        if sh == 0:
            bld.ts("vector", ei[:, :], pstage[:, :], 3, None,
                   ALU.bitwise_and, ALU.bypass, [r_pst], [r_ei])
        else:
            bld.ts("vector", ei[:, :], pstage[:, :], 2 * sh, 3,
                   ALU.logical_shift_right, ALU.bitwise_and, [r_pst], [r_ei])
        bld.ts("vector", tern[:, 2 * sh * D:(2 * sh + 2) * D], ei[:, :], -1.0, None,
               ALU.add, ALU.bypass, [r_ei], [r_tern])
    ternT_d = nc.dram_tensor("ternT_d", [D, D], BF16)
    r_ternTd = Res()
    tstg, r_tstg = sb("tstg", [128, D], BF16, st_pro)
    ternTd_r = ternT_d.ap().rearrange("(ic p) o -> ic p o", p=128)
    for ic in range(DC):
        for oc in range(DC):
            transpose128(tern[:, oc * D + ic * 128: oc * D + (ic + 1) * 128], r_tern,
                         tstg[:, oc * 128:(oc + 1) * 128], r_tstg,
                         eng=("vector" if (oc + ic) % 2 == 0 else "scalar"))
        bld.dma("sync", "ttd", ternTd_r[ic], tstg[:, :], [r_tstg], [r_ternTd])

    dre, r_dre = sb("dre", [128, 1], F32, st_pro)
    dim, r_dim = sb("dim", [128, 1], F32, st_pro)
    nre, r_nre = sb("nre", [128, 1], F32, st_pro)
    nim, r_nim = sb("nim", [128, 1], F32, st_pro)
    ire, r_ire = sb("ire", [128, 1], F32, st_pro)
    iim, r_iim = sb("iim", [128, 1], F32, st_pro)
    tmp1, r_tmp1 = sb("tmp1", [128, 4], F32, st_pro)
    bld.ts("vector", dre[:, :], arc[:, :], -0.005, 1.0, ALU.mult, ALU.add,
           [r_arc], [r_dre])
    bld.ts("vector", dim[:, :], aic[:, :], -0.005, None, ALU.mult, ALU.bypass,
           [r_aic], [r_dim])
    bld.ts("vector", nre[:, :], arc[:, :], 0.005, 1.0, ALU.mult, ALU.add,
           [r_arc], [r_nre])
    bld.ts("vector", nim[:, :], aic[:, :], 0.005, None, ALU.mult, ALU.bypass,
           [r_aic], [r_nim])
    bld.tt("vector", tmp1[:, 0:1], dre[:, :], dre[:, :], ALU.mult, [r_dre], [r_tmp1])
    bld.stt("vector", tmp1[:, 1:2], dim[:, :], 1.0, dim[:, :], ALU.mult, ALU.mult,
            [r_dim], [r_tmp1])
    bld.tt("vector", tmp1[:, 0:1], tmp1[:, 0:1], tmp1[:, 1:2], ALU.add,
           [r_tmp1], [r_tmp1])
    bld.op("vector", lambda h: h.reciprocal(tmp1[:, 2:3], tmp1[:, 0:1]),
           [r_tmp1], [r_tmp1])
    bld.tt("vector", ire[:, :], dre[:, :], tmp1[:, 2:3], ALU.mult,
           [r_dre, r_tmp1], [r_ire])
    bld.stt("vector", iim[:, :], dim[:, :], -1.0, tmp1[:, 2:3], ALU.mult, ALU.mult,
            [r_dim, r_tmp1], [r_iim])
    bld.tt("vector", tmp1[:, 0:1], nre[:, :], ire[:, :], ALU.mult, [r_nre, r_ire], [r_tmp1])
    bld.tt("vector", tmp1[:, 1:2], nim[:, :], iim[:, :], ALU.mult, [r_nim, r_iim], [r_tmp1])
    bld.tt("vector", are_c[:, :], tmp1[:, 0:1], tmp1[:, 1:2], ALU.subtract,
           [r_tmp1], [r_are])
    bld.tt("vector", tmp1[:, 0:1], nre[:, :], iim[:, :], ALU.mult, [r_nre, r_iim], [r_tmp1])
    bld.tt("vector", tmp1[:, 1:2], nim[:, :], ire[:, :], ALU.mult, [r_nim, r_ire], [r_tmp1])
    bld.tt("vector", aim_c[:, :], tmp1[:, 0:1], tmp1[:, 1:2], ALU.add,
           [r_tmp1], [r_aim])
    iar, r_iar = sb("iar", [128, 1], F32, st_pro)
    iai, r_iai = sb("iai", [128, 1], F32, st_pro)
    bld.tt("vector", tmp1[:, 0:1], are_c[:, :], are_c[:, :], ALU.mult, [r_are], [r_tmp1])
    bld.stt("vector", tmp1[:, 1:2], aim_c[:, :], 1.0, aim_c[:, :], ALU.mult, ALU.mult,
            [r_aim], [r_tmp1])
    bld.tt("vector", tmp1[:, 0:1], tmp1[:, 0:1], tmp1[:, 1:2], ALU.add, [r_tmp1], [r_tmp1])
    bld.op("vector", lambda h: h.reciprocal(tmp1[:, 2:3], tmp1[:, 0:1]), [r_tmp1], [r_tmp1])
    bld.tt("vector", iar[:, :], are_c[:, :], tmp1[:, 2:3], ALU.mult, [r_are, r_tmp1], [r_iar])
    bld.stt("vector", iai[:, :], aim_c[:, :], -1.0, tmp1[:, 2:3], ALU.mult, ALU.mult,
            [r_aim, r_tmp1], [r_iai])

    def build_powers(pre, pim, r_pre, r_pim, base_re_ap, base_im_ap, r_bre, r_bim):
        cwre, r_cwre = sb(f"cw_{pre.name}r", [128, 1], F32, st_pro)
        cwim, r_cwim = sb(f"cw_{pre.name}i", [128, 1], F32, st_pro)
        bld.cp("vector", cwre[:, :], base_re_ap, [r_bre], [r_cwre])
        bld.cp("vector", cwim[:, :], base_im_ap, [r_bim], [r_cwim])
        bld.op("vector", lambda h: h.memset(pre[:, 0:1], 1.0), (), (r_pre,))
        bld.op("vector", lambda h: h.memset(pim[:, 0:1], 0.0), (), (r_pim,))
        w = 1
        tre, r_tre = sb(f"tp_{pre.name}r", [128, 256], F32, st_pro)
        tim, r_tim = sb(f"tp_{pre.name}i", [128, 256], F32, st_pro)
        while w < 512:
            wl = min(w, 512 - w)
            bld.ts("vector", tre[:, 0:wl], pim[:, 0:wl], cwim[:, :], None,
                   ALU.mult, ALU.bypass, [r_pim, r_cwim], [r_tre])
            bld.stt("vector", pre[:, w:w + wl], pre[:, 0:wl], cwre[:, :], tre[:, 0:wl],
                    ALU.mult, ALU.subtract, [r_pre, r_cwre, r_tre], [r_pre])
            bld.ts("vector", tim[:, 0:wl], pim[:, 0:wl], cwre[:, :], None,
                   ALU.mult, ALU.bypass, [r_pim, r_cwre], [r_tim])
            bld.stt("vector", pim[:, w:w + wl], pre[:, 0:wl], cwim[:, :], tim[:, 0:wl],
                    ALU.mult, ALU.add, [r_pre, r_cwim, r_tim], [r_pim])
            bld.tt("vector", tre[:, 0:1], cwre[:, :], cwre[:, :], ALU.mult,
                   [r_cwre], [r_tre])
            bld.tt("vector", tim[:, 0:1], cwim[:, :], cwim[:, :], ALU.mult,
                   [r_cwim], [r_tim])
            bld.tt("vector", tre[:, 1:2], cwre[:, :], cwim[:, :], ALU.mult,
                   [r_cwre, r_cwim], [r_tre])
            bld.tt("vector", cwre[:, :], tre[:, 0:1], tim[:, 0:1], ALU.subtract,
                   [r_tre, r_tim], [r_cwre])
            bld.ts("vector", cwim[:, :], tre[:, 1:2], 2.0, None, ALU.mult, ALU.bypass,
                   [r_tre], [r_cwim])
            w *= 2

    build_powers(pm_re, pm_im, r_pmre, r_pmim, iar[:, :], iai[:, :], r_iar, r_iai)
    build_powers(pp_re, pp_im, r_ppre, r_ppim, are_c[:, :], aim_c[:, :], r_are, r_aim)

    sdre, r_sdre = sb("sdre", [128, 1], F32, st_pro)
    sdim, r_sdim = sb("sdim", [128, 1], F32, st_pro)
    bld.ts("vector", sdre[:, :], ire[:, :], 0.1, None, ALU.mult, ALU.bypass,
           [r_ire], [r_sdre])
    bld.ts("vector", sdim[:, :], iim[:, :], 0.1, None, ALU.mult, ALU.bypass,
           [r_iim], [r_sdim])
    bre_t, r_bret = sb("bre_t", [128, D], BF16, st_pro)
    bim_t, r_bimt = sb("bim_t", [128, D], BF16, st_pro)
    bld.ts("vector", bre_t[:, :], bmat[:, :], sdre[:, :], None, ALU.mult, ALU.bypass,
           [r_bmat, r_sdre], [r_bret])
    bld.ts("vector", bim_t[:, :], bmat[:, :], sdim[:, :], None, ALU.mult, ALU.bypass,
           [r_bmat, r_sdim], [r_bimt])
    for c in range(DC):
        transpose128(bre_t[:, c * 128:(c + 1) * 128], r_bret,
                     bdt_re[:, c * 128:(c + 1) * 128], r_bdtre, eng="vector")
        transpose128(bim_t[:, c * 128:(c + 1) * 128], r_bimt,
                     bdt_im[:, c * 128:(c + 1) * 128], r_bdtim, eng="scalar")
    for c in range(DC):
        transpose128(cmat_b[:, c * 128:(c + 1) * 128], r_cmatb,
                     ct_t[:, c * 128:(c + 1) * 128], r_ct,
                     eng=("vector" if c % 2 == 0 else "scalar"))

    wprep_jobs = []
    for (src, dst, rdst) in ((wg_in, wgt_d, r_wgt), (wu_in, wut_d, r_wut)):
        src_r = src.ap().rearrange("(m p) d -> m p d", p=128)
        dst_r = dst.ap().rearrange("(m k p) h -> m p k h", k=DC, p=128)
        for m in range(HB):
            def job(m=m, src_r=src_r, dst_r=dst_r, rdst=rdst):
                w0, rw0 = wst[m % 2]
                w1, rw1 = wtt[m % 2]
                bld.dma("gpsimd", "wl1%d" % (m % 2), w0[:, 0:D], src_r[m],
                        [rw0], [rw0])
                for kc in range(DC):
                    transpose128(w0[:, kc * 128:(kc + 1) * 128], rw0,
                                 w1[:, kc * 128:(kc + 1) * 128], rw1,
                                 eng=("vector" if kc % 2 == 0 else "scalar"))
                bld.dma("sync", "ws1%d" % (m % 2), dst_r[m],
                        w1[:, 0:DC * 128].rearrange("p (k h) -> p k h", h=128),
                        [rw1], [rdst])
            wprep_jobs.append(job)
    wdt_r = wdt_d.ap().rearrange("(m p) d -> p m d", p=128)
    for rt in range(DC):
        def job(rt=rt):
            w0, rw0 = wst[rt % 2]
            w1, rw1 = wtt[rt % 2]
            bld.dma("gpsimd", "wl1%d" % (rt % 2), w0[:, 0:3072],
                    wd_in.ap().rearrange("(r p) h -> r p h", p=128)[rt],
                    [rw0], [rw0])
            for m in range(HB):
                transpose128(w0[:, m * 128:(m + 1) * 128], rw0,
                             w1[:, m * 128:(m + 1) * 128], rw1,
                             eng=("vector" if m % 2 == 0 else "scalar"))
            bld.dma("sync", "ws1%d" % (rt % 2), wdt_r[:, :, rt * 128:(rt + 1) * 128],
                    w1[:, 0:HB * 128].rearrange("p (m c) -> p m c", c=128),
                    [rw1], [r_wdt])
        wprep_jobs.append(job)

    st_pro.close()
    bld.barrier()

    x_g, r_xg = sb("x_g", [128, S8 * D], BF16, st_lw)
    wpj_b, r_wpjb = sb("wpj_b", [128, DC * WCOL], BF16, st_l0)
    bld.dma("gpsimd", "wpj", wpj_b[:, :].rearrange("p (c w) -> p c w", w=WCOL),
            wproj.ap().rearrange("(c p) w -> p c w", p=128), (), (r_wpjb,))
    convw, r_convw = sb("convw", [128, 32], F32, st_l0)
    bld.dma("sync", "cvw", convw[:, :], convw_in.ap(), (), (r_convw,))
    dconv, r_dconv = sb("dconv", [128, 32 * 128], BF16, st_l0)
    for cc in range(DC):
        for tap in range(4):
            bld.ts("vector", dconv[:, (cc * 4 + tap) * 128:(cc * 4 + tap + 1) * 128],
                   idn[:, :], convw[:, cc * 4 + tap:cc * 4 + tap + 1], None,
                   ALU.mult, ALU.bypass, [r_idn, r_convw], [r_dconv])
    normw1, r_normw1 = sb("normw1", [1, 512], BF16, st_l0)
    bld.dma("gpsimd", "nw", normw1[:, :], normw_in.ap(), (), (r_normw1,))
    normwb, r_normwb = sb("normwb", [128, 512], BF16, st_l0)
    bk, bkr = rotbank()
    bld.mm(bkr, bk[:, 0:512], onescol[:, :], normw1[:, :], True, True,
           [r_onescol, r_normw1], first=True, last=True)
    evac("vector", normwb[:, :], bkr, bk[:, 0:512], [r_normwb])
    xnT, r_xnT = sb("xnT", [128, DC * (TG + 3)], BF16, st_l0)
    hT, r_hT = sb("hT", [128, DC * TG], BF16, st_l0)
    qkg, r_qkg = sb("qkg", [128, S8 * 512], BF16, st_l0)
    vg, r_vg = sb("vg", [128, S8 * 512], BF16, st_l0)
    sgnw, r_sgnw = sb("sgnw", [128, S8 * 512], BF16, st_l0)
    beta_g, r_betag = sb("beta_g", [128, S8 * 2], F32, st_l0)
    ssq_g, r_ssqg = sb("ssq_g", [128, S8 * 4], F32, st_l0)
    scl_g, r_sclg = sb("scl_g", [128, S8 * 4], F32, st_l0)
    gated_g, r_gatedg = sb("gated_g", [128, S8 * 512], BF16, st_l0)
    xn_bf, r_xnbf = sb("xn_bf", [128, D], BF16, st_l0)
    NI = 8
    scant = []
    for i in range(NI):
        d = {}
        for nm, w in (("dgq", 128), ("dgk", 128), ("dgkb", 128), ("qTn", 128),
                      ("kTn", 128), ("kbTn", 128), ("kn", 128), ("Y", 384),
                      ("TT", 128), ("Pa", 128), ("PTa", 128), ("Pb", 128),
                      ("PTb", 128), ("nW2T", 128), ("W1", 256), ("U", 256),
                      ("Mp", 128)):
            d[nm] = sb(f"sc{nm}_{i}", [128, w], BF16, st_l0)
        scant.append(d)
    S_f32, r_Sf = sb("S_f32", [128, 512], F32, st_l0)
    S_bf, r_Sbf = sb("S_bf", [128, 512], BF16, st_l0)
    betasc, r_betasc = sb("betasc", [128, 8], F32, st_l0)
    orst, r_orst = sb("orst", [128, 8], F32, st_l0)

    bld.op("vector", lambda h: h.memset(S_f32[:, :], 0.0), (), (r_Sf,))
    bld.op("vector", lambda h: h.memset(S_bf[:, :], 0.0), (), (r_Sbf,))
    bld.op("vector", lambda h: h.memset(
        xnT[:, :].rearrange("p (c w) -> p c w", w=TG + 3)[:, :, 0:3], 0.0),
        (), (r_xnT,))

    x_fr = x_full.ap().rearrange("(g s p) d -> g p s d", p=128, s=S8)

    def ln_stats(x_res, x_ap_sub, scol, sqcol, rs=None, rq=None):
        bld.act(trash[:, 0:1024], x_ap_sub, AF.Identity, [x_res],
                [r_trash, rs or r_sumsg], accum=scol)
        bld.act(trash[:, 0:1024], x_ap_sub, AF.Square, [x_res],
                [r_trash, rq or r_sqsg], accum=sqcol)

    def ln_rstd(scol_ap, sqcol_ap, n_inv, rst_ap, mrs_ap, r_rst, r_mrs, ncols,
                rs=None, rq=None):
        m = scr[:, 0:ncols]
        v = scr[:, 8:8 + ncols] if ncols <= 8 else None
        bld.ts("vector", m, scol_ap, n_inv, None, ALU.mult, ALU.bypass,
               [rs or r_sumsg], [r_scr])
        bld.ts("vector", scr2[:, 0:ncols], sqcol_ap, n_inv, None, ALU.mult, ALU.bypass,
               [rq or r_sqsg], [r_scr2])
        bld.stt("vector", v, m, -1.0, m, ALU.mult, ALU.mult, [r_scr], [r_scr])
        bld.tt("vector", v, scr2[:, 0:ncols], v, ALU.add, [r_scr2, r_scr], [r_scr])
        bld.act(scr2[:, 0:ncols], v, AF.Sqrt, [r_scr, r_kc], [r_scr2],
                bias=kconst[:, 0:1])
        bld.op("vector", lambda h, o=rst_ap, i=scr2[:, 0:ncols]: h.reciprocal(o, i),
               [r_scr2], [r_rst])
        bld.ts("vector", scr2[:, 8:8 + ncols], v, 1e-5, None, ALU.add, ALU.bypass,
               [r_scr], [r_scr2])
        bld.tt("vector", v, rst_ap, rst_ap, ALU.mult, [r_rst], [r_scr])
        bld.tt("vector", v, v, scr2[:, 8:8 + ncols], ALU.mult, [r_scr, r_scr2], [r_scr])
        bld.ts("vector", v, v, -0.5, 1.5, ALU.mult, ALU.add, [r_scr], [r_scr])
        bld.tt("vector", rst_ap, rst_ap, v, ALU.mult, [r_rst, r_scr], [r_rst])
        bld.stt("vector", mrs_ap, m, -1.0, rst_ap, ALU.mult, ALU.mult,
                [r_scr, r_rst], [r_mrs])

    def emitA1(g):
        xv = x_g[:, :].rearrange("p (s d) -> p s d", d=D)
        bld.dma("gpsimd", "xg", xv[:, 0:4], x_fr[g][:, 0:4], [r_xg], [r_xg])
        bld.dma("gpsimd", "xgb", xv[:, 4:8], x_fr[g][:, 4:8], [r_xg], [r_xg])
        for s in range(S8):
            ln_stats(r_xg, x_g[:, s * D:(s + 1) * D], sums_g[:, s:s + 1],
                     sqs_g[:, s:s + 1])
        ln_rstd(sums_g[:, 0:S8], sqs_g[:, 0:S8], 1.0 / D,
                stat_r[:, g * 8:(g + 1) * 8], stat_m[:, g * 8:(g + 1) * 8],
                r_statr, r_statm, 8)

    def emitA2(g):

        if g > 0:
            xr3 = xnT[:, :].rearrange("p (c w) -> p c w", w=TG + 3)
            bld.cp("vector", xr3[:, :, 0:3], xr3[:, :, TG:TG + 3], [r_xnT], [r_xnT])
        for s in range(S8):
            bld.act(xn_bf[:, 0:D], x_g[:, s * D:(s + 1) * D], AF.Identity,
                    [r_xg, r_statr, r_statm], [r_xnbf],
                    scale=stat_r[:, g * 8 + s:g * 8 + s + 1],
                    bias=stat_m[:, g * 8 + s:g * 8 + s + 1])
            for c in range(DC):
                transpose128(xn_bf[:, c * 128:(c + 1) * 128], r_xnbf,
                             xnT[:, c * (TG + 3) + 3 + s * 128:
                                 c * (TG + 3) + 3 + (s + 1) * 128], r_xnT,
                             eng=("vector" if (s + c) % 2 else "scalar"))
        for c in range(DC):
            src = xnT[:, c * (TG + 3):(c + 1) * (TG + 3)]
            for nh in range(2):
                bk, bkr = rotbank()
                for tap in range(4):
                    bld.mm(bkr, bk[:, 0:512],
                           dconv[:, (c * 4 + tap) * 128:(c * 4 + tap + 1) * 128],
                           src[:, nh * 512 + tap: nh * 512 + tap + 512],
                           tap == 0, tap == 3, [r_dconv, r_xnT],
                           first=(tap == 0), last=(tap == 3))
                bld.act(hT[:, c * TG + nh * 512: c * TG + (nh + 1) * 512],
                        bk[:, 0:512], AF.Silu, [bkr], [r_hT])


    def emitB(g):
        for s in range(S8):
            if s % 2 == 0:
                bq, bk_, bv, bg = 5, 6, 7, 3
            else:
                bq, bk_, bv, bg = 0, 1, 2, 4
            for kc in range(DC):
                lt = hT[:, kc * TG + s * 128: kc * TG + (s + 1) * 128]
                w0 = kc * WCOL
                first, last = kc == 0, kc == DC - 1
                bld.mm(RB[bq], PB[bq][:, 0:256], lt, wpj_b[:, w0:w0 + 256],
                       first, False, [r_hT, r_wpjb], first=first, last=False)
                bld.mm(RB[bq], PB[bq][:, 256:258], lt, wpj_b[:, w0 + 1536:w0 + 1538],
                       False, last, [r_hT, r_wpjb], first=False, last=last)
                bld.mm(RB[bk_], PB[bk_][:, 0:256], lt, wpj_b[:, w0 + 256:w0 + 512],
                       first, last, [r_hT, r_wpjb], first=first, last=last)
                bld.mm(RB[bv], PB[bv][:, 0:512], lt, wpj_b[:, w0 + 512:w0 + 1024],
                       first, last, [r_hT, r_wpjb], first=first, last=last)
                bld.mm(RB[bg], PB[bg][:, 0:512], lt, wpj_b[:, w0 + 1024:w0 + 1536],
                       first, last, [r_hT, r_wpjb], first=first, last=last)
            bld.cp("vector", qkg[:, s * 512:s * 512 + 256], PB[bq][:, 0:256],
                   [RB[bq]], [r_qkg])
            bld.cp("vector", scr[:, 10:12], PB[bq][:, 256:258], [RB[bq]], [r_scr])
            bld.act(scr[:, 8:10], scr[:, 10:12], AF.Tanh, [r_scr, r_kc],
                    [r_scr], scale=kconst[:, 1:2])
            bld.ts("vector", beta_g[:, s * 2:s * 2 + 2], scr[:, 8:10], 0.5, 0.5,
                   ALU.mult, ALU.add, [r_scr], [r_betag])
            bld.cp("vector", qkg[:, s * 512 + 256:s * 512 + 512], PB[bk_][:, 0:256],
                   [RB[bk_]], [r_qkg])
            bld.cp("scalar", vg[:, s * 512:(s + 1) * 512], PB[bv][:, 0:512],
                   [RB[bv]], [r_vg])
            bld.act(sgnw[:, s * 512:(s + 1) * 512], PB[bg][:, 0:512], AF.Silu,
                    [RB[bg]], [r_sgnw])
            bld.tt("vector", sgnw[:, s * 512:(s + 1) * 512],
                   sgnw[:, s * 512:(s + 1) * 512], normwb[:, :], ALU.mult,
                   [r_sgnw, r_normwb], [r_sgnw])
            bld.tt("vector", trash[:, 0:512], qkg[:, s * 512:(s + 1) * 512],
                   qkg[:, s * 512:(s + 1) * 512], ALU.mult, [r_qkg], [r_trash])
            bld.red(ssq_g[:, s * 4:(s + 1) * 4],
                    trash[:, 0:512].rearrange("p (h t) -> p h t", h=4),
                    ALU.add, [r_trash], [r_ssqg])
        bld.act(scl_g[:, :], ssq_g[:, :], AF.Sqrt, [r_ssqg], [r_sclg])
        bld.ts("vector", scl_g[:, :], scl_g[:, :], 1e-12, None, ALU.max, ALU.bypass,
               [r_sclg], [r_sclg])
        bld.op("vector", lambda h: h.reciprocal(scl_g[:, :], scl_g[:, :]),
               [r_sclg], [r_sclg])


    def emitC(g):
        for half in range(2):
            insts = []
            for cb in range(4):
                s4 = half * 4 + cb
                for hh in range(2):
                    insts.append((cb * 2 + hh, s4, hh))
            for i, s4, hh in insts:
                T = scant[i]
                kt = qkg[:, s4 * 512 + 256 + hh * 128: s4 * 512 + 256 + (hh + 1) * 128]
                vt = vg[:, s4 * 512 + hh * 256: s4 * 512 + (hh + 1) * 256]
                bcol = beta_g[:, s4 * 2 + hh: s4 * 2 + hh + 1]
                sqc = scl_g[:, s4 * 4 + hh: s4 * 4 + hh + 1]
                skc = scl_g[:, s4 * 4 + 2 + hh: s4 * 4 + 3 + hh]
                bld.ts("vector", T["dgq"][0][:, :], idn[:, :], sqc, None,
                       ALU.mult, ALU.bypass, [r_idn, r_sclg], [T["dgq"][1]])
                bld.ts("vector", T["dgk"][0][:, :], idn[:, :], skc, None,
                       ALU.mult, ALU.bypass, [r_idn, r_sclg], [T["dgk"][1]])
                bld.tt("vector", betasc[:, i:i + 1], bcol, skc, ALU.mult,
                       [r_betag, r_sclg], [r_betasc])
                bld.ts("vector", T["dgkb"][0][:, :], idn[:, :], betasc[:, i:i + 1],
                       None, ALU.mult, ALU.bypass, [r_idn, r_betasc], [T["dgkb"][1]])
                bld.ts("scalar", T["kn"][0][:, :], kt, skc, None, ALU.mult,
                       ALU.bypass, [r_qkg, r_sclg], [T["kn"][1]])
                bld.ts("scalar", T["Y"][0][:, 256:384], kt, betasc[:, i:i + 1],
                       None, ALU.mult, ALU.bypass, [r_qkg, r_betasc], [T["Y"][1]])
                bld.ts("scalar", T["Y"][0][:, 0:256], vt, bcol, None,
                       ALU.mult, ALU.bypass, [r_vg, r_betag], [T["Y"][1]])
            for i, s4, hh in insts:
                T = scant[i]
                qt = qkg[:, s4 * 512 + hh * 128: s4 * 512 + (hh + 1) * 128]
                kt = qkg[:, s4 * 512 + 256 + hh * 128: s4 * 512 + 256 + (hh + 1) * 128]
                transpose128(qt, r_qkg, T["qTn"][0][:, :], T["qTn"][1],
                             T["dgq"][0][:, :], T["dgq"][1],
                             "vector" if i % 2 == 0 else "scalar")
                transpose128(kt, r_qkg, T["kTn"][0][:, :], T["kTn"][1],
                             T["dgk"][0][:, :], T["dgk"][1],
                             "scalar" if i % 2 == 0 else "vector")
                transpose128(kt, r_qkg, T["kbTn"][0][:, :], T["kbTn"][1],
                             T["dgkb"][0][:, :], T["dgkb"][1],
                             "vector" if i % 2 == 0 else "scalar")
            for i, s4, hh in insts:
                T = scant[i]
                bk, bkr = rotbank()
                bld.mm(bkr, bk[:, 0:128], T["kbTn"][0][:, :], T["kTn"][0][:, :],
                       True, True, [T["kbTn"][1], T["kTn"][1]], first=True, last=True)
                bld.tt("vector", T["Pa"][0][:, :], bk[:, 0:128], MSL, ALU.mult,
                       [bkr, r_msk], [T["Pa"][1]])
                bk, bkr = rotbank()
                bld.mm(bkr, bk[:, 0:128], T["kTn"][0][:, :], T["kbTn"][0][:, :],
                       True, True, [T["kTn"][1], T["kbTn"][1]], first=True, last=True)
                bld.tt("vector", T["PTa"][0][:, :], bk[:, 0:128], MSU, ALU.mult,
                       [bkr, r_msk], [T["PTa"][1]])
                bld.stt("vector", T["TT"][0][:, :], T["PTa"][0][:, :], -1.0,
                        idn[:, :], ALU.mult, ALU.add, [T["PTa"][1], r_idn],
                        [T["TT"][1]])
            curs = {i: ("Pa", "PTa") for i, _, _ in insts}
            nxts = {i: ("Pb", "PTb") for i, _, _ in insts}
            for kk in range(1, 7):
                for i, s4, hh in insts:
                    T = scant[i]
                    pc, ptc = (T[curs[i][0]], T[curs[i][1]])
                    pn, ptn = (T[nxts[i][0]], T[nxts[i][1]])
                    bk, bkr = rotbank()
                    bld.mm(bkr, bk[:, 0:128], ptc[0][:, :], pc[0][:, :], True, True,
                           [ptc[1], pc[1]], first=True, last=True)
                    evac("vector" if i % 2 else "scalar", pn[0][:, :], bkr,
                         bk[:, 0:128], [pn[1]])
                    if kk < 6:
                        bk, bkr = rotbank()
                        bld.mm(bkr, bk[:, 0:128], pc[0][:, :], ptc[0][:, :], True,
                               True, [pc[1], ptc[1]], first=True, last=True)
                        evac("scalar" if i % 2 else "vector", ptn[0][:, :], bkr,
                             bk[:, 0:128], [ptn[1]])
                for i, s4, hh in insts:
                    T = scant[i]
                    pn = T[nxts[i][0]]
                    bk, bkr = rotbank()
                    bld.mm(bkr, bk[:, 0:128], pn[0][:, :], T["TT"][0][:, :], True,
                           True, [pn[1], T["TT"][1]], first=True, last=True)
                    bld.tt("vector", T["TT"][0][:, :], T["TT"][0][:, :],
                           bk[:, 0:128], ALU.add, [T["TT"][1], bkr], [T["TT"][1]])
                    curs[i], nxts[i] = nxts[i], curs[i]
            for i, s4, hh in insts:
                T = scant[i]
                bk, bkr = rotbank()
                bld.mm(bkr, bk[:, 0:128], T["Y"][0][:, 256:384], T["TT"][0][:, :],
                       True, True, [T["Y"][1], T["TT"][1]], first=True, last=True)
                evac("scalar", T["nW2T"][0][:, :], bkr, bk[:, 0:128],
                     [T["nW2T"][1]], scale=-1.0)
                bk, bkr = rotbank()
                bld.mm(bkr, bk[:, 0:256], T["TT"][0][:, :], T["Y"][0][:, 0:256],
                       True, True, [T["TT"][1], T["Y"][1]], first=True, last=True)
                evac("vector", T["W1"][0][:, :], bkr, bk[:, 0:256], [T["W1"][1]])
            for cb in range(4):
                s4 = half * 4 + cb
                UB = [(PB[5], RB[5]), (PB[6], RB[6])]
                OB = [(PB[3], RB[3]), (PB[7], RB[7])]
                for hh in range(2):
                    i = cb * 2 + hh
                    T = scant[i]
                    Ssl = S_bf[:, hh * 256:(hh + 1) * 256]
                    ub, ubr = UB[hh]
                    bld.mm(ubr, ub[:, 0:256], T["nW2T"][0][:, :], Ssl, True, True,
                           [T["nW2T"][1], r_Sbf], first=True, last=True)
                    bld.tt("vector", T["U"][0][:, :], ub[:, 0:256],
                           T["W1"][0][:, :], ALU.add, [ubr, T["W1"][1]],
                           [T["U"][1]])
                    bk, bkr = rotbank()
                    bld.mm(bkr, bk[:, 0:128], T["kTn"][0][:, :], T["qTn"][0][:, :],
                           True, True, [T["kTn"][1], T["qTn"][1]], first=True,
                           last=True)
                    bld.tt("vector", T["Mp"][0][:, :], bk[:, 0:128], MTI, ALU.mult,
                           [bkr, r_msk], [T["Mp"][1]])
                for hh in range(2):
                    i = cb * 2 + hh
                    T = scant[i]
                    Ssl = S_bf[:, hh * 256:(hh + 1) * 256]
                    ob, obr = OB[hh]
                    bld.mm(obr, ob[:, 0:256], T["qTn"][0][:, :], Ssl, True, False,
                           [T["qTn"][1], r_Sbf], first=True, last=False)
                    bld.mm(obr, ob[:, 0:256], T["Mp"][0][:, :], T["U"][0][:, :],
                           False, True, [T["Mp"][1], T["U"][1]], first=False,
                           last=True)
                for hh in range(2):
                    i = cb * 2 + hh
                    T = scant[i]
                    bld.mm(RB[4], PB[4][:, hh * 256:(hh + 1) * 256],
                           T["kn"][0][:, :], T["U"][0][:, :], hh == 0, hh == 1,
                           [T["kn"][1], T["U"][1]], first=(hh == 0),
                           last=(hh == 1))
                for hh in range(2):
                    i = cb * 2 + hh
                    ob, obr = OB[hh]
                    oc0 = 4 * hh
                    bld.act(trash[:, 256 * hh:256 * hh + 256], ob[:, 0:256],
                            AF.Square, [obr], [r_trash, r_orst],
                            accum=orst[:, oc0:oc0 + 1])
                    bld.act(orst[:, oc0 + 1:oc0 + 2], orst[:, oc0:oc0 + 1],
                            AF.Sqrt, [r_orst, r_kc], [r_orst],
                            scale=kconst[:, 2:3], bias=kconst[:, 0:1])
                    bld.op("vector", lambda h, a=orst[:, oc0 + 2:oc0 + 3],
                           b=orst[:, oc0 + 1:oc0 + 2]: h.reciprocal(a, b),
                           [r_orst], [r_orst])
                    bld.stt("vector",
                            gated_g[:, s4 * 512 + hh * 256: s4 * 512 + (hh + 1) * 256],
                            ob[:, 0:256], orst[:, oc0 + 2:oc0 + 3],
                            sgnw[:, s4 * 512 + hh * 256: s4 * 512 + (hh + 1) * 256],
                            ALU.mult, ALU.mult, [obr, r_orst, r_sgnw], [r_gatedg])
                bld.tt("vector", S_f32[:, :], S_f32[:, :], PB[4][:, 0:512], ALU.add,
                       [r_Sf, RB[4]], [r_Sf])
                bld.cp("vector", S_bf[:, :], S_f32[:, :], [r_Sf], [r_Sbf])
        bld.dma("sync", "gat",
                gat_in.ap().rearrange("(g s p) c -> g p s c", p=128, s=S8)[g],
                gated_g[:, :].rearrange("p (s c) -> p s c", c=512),
                [r_gatedg], [r_gin])
        if debug:
            bld.dma("gpsimd", "dbg",
                    dbg["d_gated"].ap().rearrange("(g s p) c -> g p s c", p=128,
                                                  s=S8)[g],
                    gated_g[:, :].rearrange("p (s c) -> p s c", c=512),
                    [r_gatedg], [Res()])
        if debug and g == 0:
            bld.dma("gpsimd", "dbg", dbg["d_hT"].ap(), hT[:, :], [r_hT], [Res()])
            bld.dma("gpsimd", "dbg", dbg["d_qkg"].ap(), qkg[:, :], [r_qkg], [Res()])


    wjob_i = [0]
    def emit_wjobs(k):
        while wjob_i[0] < len(wprep_jobs) and k > 0:
            wprep_jobs[wjob_i[0]]()
            wjob_i[0] += 1
            k -= 1
    if n_g0 > 0:
        emitA1(0)
        emitA2(0)
    for g in range(n_g0):
        emitB(g)
        if g + 1 < n_g0:
            emitA1(g + 1)
        emitC(g)
        emit_wjobs(8)
        if g + 1 < n_g0:
            emitA2(g + 1)
    emit_wjobs(99)
    bld.barrier()
    st_l0.close()

    if do_w:
        bld.op("gpsimd", lambda h: h.collective_compute(
            "AllGather", ALU.bypass,
            replica_groups=[[0, 1], [2, 3], [4, 5], [6, 7]],
            ins=[gat_in.ap()], outs=[gat_out.ap()]), [r_gin], [r_gout])

    x_g2, r_xg2 = sb("x_g2", [128, S8 * D], BF16, st_w)
    xo_b = [(x_g, r_xg), (x_g2, r_xg2)]
    ternT, r_ternT = sb("ternT", [128, DC * D], BF16, st_w)
    bld.dma("sync", "ttl", ternT[:, :].rearrange("p (c w) -> p c w", w=D),
            ternT_d.ap().rearrange("(c p) w -> p c w", p=128),
            [r_ternTd], (r_ternT,))
    wo_b, r_wob = sb("wo_b", [128, DC * D], BF16, st_w)
    bld.dma("gpsimd", "wol", wo_b[:, :].rearrange("p (c w) -> p c w", w=D),
            wo_in.ap().rearrange("(c p) w -> p c w", p=128), (), (r_wob,))
    bias1, r_bias1 = sb("bias1", [1, D], BF16, st_w)
    bld.dma("gpsimd", "bi", bias1[:, :], bias_in.ap(), (), (r_bias1,))
    wsc1, r_wsc1 = sb("wsc1", [1, 1], BF16, st_w)
    bld.dma("gpsimd", "wsc", wsc1[:, :], wscale_in.ap(), (), (r_wsc1,))
    biasb, r_biasb = sb("biasb", [128, D], F32, st_w)
    for half in range(2):
        bk, bkr = rotbank()
        bld.mm(bkr, bk[:, 0:512], onescol[:, :], bias1[:, half * 512:(half + 1) * 512],
               True, True, [r_onescol, r_bias1], first=True, last=True)
        evac("vector", biasb[:, half * 512:(half + 1) * 512], bkr, bk[:, 0:512],
             [r_biasb])
    ws_col, r_ws = sb("ws_col", [128, 1], F32, st_w)
    bk, bkr = rotbank()
    bld.mm(bkr, bk[:, 0:1], onescol[:, :], wsc1[:, :], True, True,
           [r_onescol, r_wsc1], first=True, last=True)
    evac("vector", ws_col[:, :], bkr, bk[:, 0:1], [r_ws])
    xn_f = [sb(f"xn_f{i}", [128, D], F32, st_w) for i in range(2)]
    tq1 = [sb(f"tq1_{i}", [128, D], F32, st_w) for i in range(2)]
    tq2_s = sb("tq2_s", [128, D], F32, st_w)
    tq2 = [tq2_s, tq2_s]
    xint = [sb(f"xint{i}", [128, D], BF16, st_w) for i in range(2)]
    xq32_s = sb("xq32_s", [128, D], I32, st_w)
    xq32 = [xq32_s, xq32_s]
    xiT_all, r_xiTa = sb("xiT_all", [128, S8 * 1024], BF16, st_w)
    g_all, r_gall = sb("g_all", [128, S8 * D], BF16, st_w)
    gsel0, r_gsel0 = sb("gsel0", [128, S8 * 512], BF16, st_w)
    gsel1, r_gsel1 = sb("gsel1", [128, S8 * 512], BF16, st_w)
    glo, r_glo = sb("glo", [128, S8 * 512], BF16, st_w)
    gT = [sb(f"gT{i}", [128, DC * 128], BF16, st_w) for i in range(2)]
    qcol, r_qcol = sb("qcol", [128, 64], F32, st_w)
    x1t = [sb(f"x1t{i}", [128, D], BF16, st_w) for i in range(2)]

    xo_r = x_own.ap().rearrange("(g s p) d -> g p s d", p=128, s=S8)
    go_r = gat_out.ap().rearrange("(r h g s p) c -> r h g p s c",
                                  p=128, s=S8, g=GW, h=2)
    out_r = out_d.ap().rearrange("(g s p) d -> g s p d", p=128, s=S8)

    MLO = pcsc[:, 0:1]
    MHI = pcsc[:, 1:2]
    ZMK = pcsc[:, 2:3]

    def emitWA(wg, s):
        p = s % 2
        xo_g, r_xog = xo_b[wg % 2]
        bld.act(xn_f[p][0][:, :], xo_g[:, s * D:(s + 1) * D], AF.Identity,
                [r_xog, r_statwr, r_statwm], [xn_f[p][1]],
                scale=statw_r[:, wg * 8 + s:wg * 8 + s + 1],
                bias=statw_m[:, wg * 8 + s:wg * 8 + s + 1])
        q8 = s * 8
        bld.red(qcol[:, q8 + 0:q8 + 1], xn_f[p][0][:, :], ALU.max,
                [xn_f[p][1]], [r_qcol], absval=True)
        bld.ts("vector", qcol[:, q8 + 1:q8 + 2], qcol[:, q8 + 0:q8 + 1], 1e-5, None,
               ALU.max, ALU.bypass, [r_qcol], [r_qcol])
        bld.op("vector", lambda h, a=qcol[:, q8 + 2:q8 + 3],
               b=qcol[:, q8 + 1:q8 + 2]: h.reciprocal(a, b), [r_qcol], [r_qcol])
        bld.ts("vector", qcol[:, q8 + 3:q8 + 4], qcol[:, q8 + 2:q8 + 3], 127.0, None,
               ALU.mult, ALU.bypass, [r_qcol], [r_qcol])
        bld.stt("vector", qcol[:, q8 + 5:q8 + 6], qcol[:, q8 + 1:q8 + 2], 1.0 / 127,
                ws_col[:, :], ALU.mult, ALU.mult, [r_qcol, r_ws], [r_qcol])
        bld.act(tq1[p][0][:, :], xn_f[p][0][:, :], AF.Identity,
                [xn_f[p][1], r_qcol], [tq1[p][1]],
                scale=qcol[:, q8 + 3:q8 + 4])
        bld.cp("vector", xq32[p][0][:, :], tq1[p][0][:, :], [tq1[p][1]], [xq32[p][1]])
        bld.ts("vector", xint[p][0][:, :], xq32[p][0][:, :], 127.0, -128.0, ALU.min,
               ALU.max, [xq32[p][1]], [xint[p][1]])
        for c in range(DC):
            transpose128(xint[p][0][:, c * 128:(c + 1) * 128], xint[p][1],
                         xiT_all[:, s * 1024 + c * 128: s * 1024 + (c + 1) * 128],
                         r_xiTa, eng=("vector" if c % 2 else "scalar"))

    def emitWB(wg, s):
        p = s % 2
        f0, f1 = (5, 6) if s % 2 == 0 else (7, 3)
        q8 = s * 8
        for c in range(DC):
            first, last = c == 0, c == DC - 1
            xs = xiT_all[:, s * 1024 + c * 128: s * 1024 + (c + 1) * 128]
            bld.mm(RB[f0], PB[f0][:, 0:512], xs, ternT[:, c * D:c * D + 512],
                   first, last, [r_xiTa, r_ternT], first=first, last=last)
            bld.mm(RB[f1], PB[f1][:, 0:512], xs, ternT[:, c * D + 512:(c + 1) * D],
                   first, last, [r_xiTa, r_ternT], first=first, last=last)
        bld.stt("vector", tq1[p][0][:, 0:512], PB[f0][:, 0:512],
                qcol[:, q8 + 5:q8 + 6], biasb[:, 0:512], ALU.mult, ALU.add,
                [RB[f0], r_qcol, r_biasb], [tq1[p][1]])
        bld.stt("vector", tq1[p][0][:, 512:1024], PB[f1][:, 0:512],
                qcol[:, q8 + 5:q8 + 6], biasb[:, 512:1024], ALU.mult, ALU.add,
                [RB[f1], r_qcol, r_biasb], [tq1[p][1]])
        bld.act(g_all[:, s * D:(s + 1) * D], tq1[p][0][:, :], AF.Silu,
                [tq1[p][1]], [r_gall])

    def emitWC(wg, s):
        p = s % 2
        for c in range(DC):
            src = gsel0 if c < 4 else gsel1
            sres = r_gsel0 if c < 4 else r_gsel1
            cc = c % 4
            transpose128(src[:, s * 512 + cc * 128: s * 512 + (cc + 1) * 128],
                         sres, gT[p][0][:, c * 128:(c + 1) * 128], gT[p][1],
                         eng=("vector" if c % 2 else "scalar"))
        w0b, w1b = (7, 3) if s % 2 == 0 else (5, 6)
        for c in range(DC):
            first, last = c == 0, c == DC - 1
            bld.mm(RB[w0b], PB[w0b][:, 0:512], gT[p][0][:, c * 128:(c + 1) * 128],
                   wo_b[:, c * D:c * D + 512], first, last,
                   [gT[p][1], r_wob], first=first, last=last)
            bld.mm(RB[w1b], PB[w1b][:, 0:512], gT[p][0][:, c * 128:(c + 1) * 128],
                   wo_b[:, c * D + 512:(c + 1) * D], first, last,
                   [gT[p][1], r_wob], first=first, last=last)
        ti = wg * 8 + s
        xo_g, r_xog = xo_b[wg % 2]
        bld.tt("vector", tq2[p][0][:, 0:512], PB[w0b][:, 0:512],
               g_all[:, s * D:s * D + 512], ALU.mult, [RB[w0b], r_gall],
               [tq2[p][1]])
        bld.tt("vector", x1t[p][0][:, 0:512], tq2[p][0][:, 0:512],
               xo_g[:, s * D:s * D + 512], ALU.add, [tq2[p][1], r_xog], [x1t[p][1]])
        bld.tt("vector", tq2[p][0][:, 512:1024], PB[w1b][:, 0:512],
               g_all[:, s * D + 512:(s + 1) * D], ALU.mult, [RB[w1b], r_gall],
               [tq2[p][1]])
        bld.tt("vector", x1t[p][0][:, 512:1024], tq2[p][0][:, 512:1024],
               xo_g[:, s * D + 512:(s + 1) * D], ALU.add, [tq2[p][1], r_xog],
               [x1t[p][1]])
        ln_stats(x1t[p][1], x1t[p][0][:, :], sum1_g[:, s:s + 1],
                 sq1_g[:, s:s + 1], r_sum1g, r_sq1g)
        bld.dma("sync", "x1s%d" % p, x1_dr[ti], x1t[p][0][:, :], [x1t[p][1]], [r_x1])

    for wg in range(GW if do_w else 0):
        xo_g, r_xog = xo_b[wg % 2]
        if wg == 0:
            bld.dma("gpsimd", "xg0",
                    xo_g[:, :].rearrange("p (s d) -> p s d", d=D),
                    xo_r[wg], [r_xog], [r_xog])
        for s in range(S8):
            ln_stats(r_xog, xo_g[:, s * D:(s + 1) * D], sums_g[:, s:s + 1],
                     sqs_g[:, s:s + 1])
        ln_rstd(sums_g[:, 0:S8], sqs_g[:, 0:S8], 1.0 / D,
                statw_r[:, wg * 8:(wg + 1) * 8], statw_m[:, wg * 8:(wg + 1) * 8],
                r_statwr, r_statwm, 8)
        for s in range(S8):
            emitWA(wg, s)
        for s in range(S8):
            emitWB(wg, s)
        if wg + 1 < GW:
            xo_n, r_xon = xo_b[(wg + 1) % 2]
            bld.dma("gpsimd", "xg%d" % ((wg + 1) % 2),
                    xo_n[:, :].rearrange("p (s d) -> p s d", d=D),
                    xo_r[wg + 1], [r_xon], [r_xon])
        for sl, gres, gsl in ((0, r_gsel0, gsel0), (1, r_gsel1, gsel1)):
            bld.dma("sync", "glo", glo[:, :].rearrange("p (s c) -> p s c", c=512),
                    go_r[sl, 0, wg], [r_glo, r_gout], [r_glo])
            bld.ts("vector", gsl[:, :], glo[:, :], MLO, None, ALU.mult, ALU.bypass,
                   [r_glo, r_pcsc], [gres])
            bld.dma("sync", "glo", glo[:, :].rearrange("p (s c) -> p s c", c=512),
                    go_r[sl, 1, wg], [r_glo, r_gout], [r_glo])
            bld.stt("vector", gsl[:, :], glo[:, :], MHI, gsl[:, :], ALU.mult, ALU.add,
                    [r_glo, r_pcsc, gres], [gres])
        for s in range(S8):
            emitWC(wg, s)
        ln_rstd(sum1_g[:, 0:S8], sq1_g[:, 0:S8], 1.0 / D,
                stat1_r[:, wg * 8:(wg + 1) * 8], stat1_m[:, wg * 8:(wg + 1) * 8],
                r_stat1r, r_stat1m, 8, rs=r_sum1g, rq=r_sq1g)
    if debug and do_w:
        bld.dma("gpsimd", "dbg", dbg["d_x1"].ap().rearrange("p (t d) -> p t d", d=D),
                x1_d.ap().rearrange("(t p) d -> p t d", p=128), [r_x1], [Res()])

    bld.barrier()

    st_w.close()
    st_lw.close()

    xn1, r_xn1 = sb("xn1", [128, D], BF16, st_l1)
    x1_st2 = [sb(f"x1_st{i}", [128, 4 * D], BF16, st_l1) for i in range(2)]
    xn1T, r_xn1T = sb("xn1T", [128, DC * TS], BF16, st_l1)
    hre_all, r_hre = sb("hre_all", [128, HALF], BF16, st_l1)
    cre, r_cre = sb("cre", [128, TS], F32, st_l1)
    cim, r_cim = sb("cim", [128, TS], F32, st_l1)
    zre, r_zre = sb("zre", [128, TS], F32, st_l1)
    zim, r_zim = sb("zim", [128, TS], F32, st_l1)
    tl1, r_tl1 = sb("tl1", [128, TS], F32, st_l1)
    tl2, r_tl2 = sb("tl2", [128, TS], F32, st_l1)
    eloc, r_eloc = sb("eloc", [128, 8], F32, st_l1)
    hidT, r_hidT = sb("hidT", [128, HB * TS], BF16, st_l1)
    wdt_s, r_wdts = sb("wdt_s", [128, HB * D], BF16, st_l1)
    wgsl, r_wgsl = sb("wgsl", [128, 2 * DC * 128], BF16, st_l1)
    wusl, r_wusl = sb("wusl", [128, 2 * DC * 128], BF16, st_l1)
    y_bf, r_ybf = sb("y_bf", [128, 4 * D], BF16, st_l1)
    g2_bf, r_g2bf = sb("g2_bf", [128, D], BF16, st_l1)
    out_f, r_outf = sb("out_f", [128, D], F32, st_l1)

    def xn1_load(st):
        x1_st, r_x1st = x1_st2[st % 2]
        bld.dma("sync", "x1l%d" % (st % 2),
                x1_st[:, :].rearrange("p (t d) -> p t d", d=D),
                x1_dr[st * 4:(st + 1) * 4].rearrange("t p d -> p t d"),
                [r_x1, r_x1st], [r_x1st])

    def xn1_transpose(st):
        x1_st, r_x1st = x1_st2[st % 2]
        for sub in range(4):
            ti = st * 4 + sub
            bld.act(xn1[:, :], x1_st[:, sub * D:(sub + 1) * D], AF.Identity,
                    [r_x1st, r_stat1r, r_stat1m], [r_xn1],
                    scale=stat1_r[:, ti:ti + 1], bias=stat1_m[:, ti:ti + 1])
            for c in range(DC):
                transpose128(xn1[:, c * 128:(c + 1) * 128], r_xn1,
                             xn1T[:, c * TS + sub * 128: c * TS + (sub + 1) * 128],
                             r_xn1T, eng=("vector" if (sub + c) % 2 else "scalar"))

    bld.op("vector", lambda h: h.memset(eloc[:, 2:4], 0.0), (), (r_eloc,))
    n_l1a = (NST if do_l1 else 0) if L1_SUB >= 2 else 0
    if n_l1a:
        xn1_load(0)
    for st in range(n_l1a):
        xn1_transpose(st)
        if st + 1 < n_l1a:
            xn1_load(st + 1)
        b0_, b1_ = (5, 6) if st % 2 == 0 else (7, 3)
        for c in range(DC):
            first, last = c == 0, c == DC - 1
            bld.mm(RB[b0_], PB[b0_][:, 0:512], bdt_re[:, c * 128:(c + 1) * 128],
                   xn1T[:, c * TS:(c + 1) * TS], first, last,
                   [r_bdtre, r_xn1T], first=first, last=last)
            bld.mm(RB[b1_], PB[b1_][:, 0:512], bdt_im[:, c * 128:(c + 1) * 128],
                   xn1T[:, c * TS:(c + 1) * TS], first, last,
                   [r_bdtim, r_xn1T], first=first, last=last)
        bld.tt("vector", tl1[:, :], PB[b0_][:, 0:512], pm_re[:, :], ALU.mult,
               [RB[b0_], r_pmre], [r_tl1])
        bld.tt("vector", tl2[:, :], PB[b1_][:, 0:512], pm_im[:, :], ALU.mult,
               [RB[b1_], r_pmim], [r_tl2])
        bld.tt("vector", cre[:, :], tl1[:, :], tl2[:, :], ALU.subtract,
               [r_tl1, r_tl2], [r_cre])
        bld.tt("vector", tl1[:, :], PB[b1_][:, 0:512], pm_re[:, :], ALU.mult,
               [RB[b1_], r_pmre], [r_tl1])
        bld.tt("vector", tl2[:, :], PB[b0_][:, 0:512], pm_im[:, :], ALU.mult,
               [RB[b0_], r_pmim], [r_tl2])
        bld.tt("vector", cim[:, :], tl1[:, :], tl2[:, :], ALU.add,
               [r_tl1, r_tl2], [r_cim])
        bld.op("vector", lambda h: h.tensor_tensor_scan(
            zre[:, :], ones512[:, :], cre[:, :], eloc[:, 2:3], ALU.mult, ALU.add),
            [r_ones, r_cre, r_eloc], [r_zre])
        bld.op("vector", lambda h: h.tensor_tensor_scan(
            zim[:, :], ones512[:, :], cim[:, :], eloc[:, 3:4], ALU.mult, ALU.add),
            [r_ones, r_cim, r_eloc], [r_zim])
        bld.tt("vector", tl1[:, :], zre[:, :], pp_re[:, :], ALU.mult,
               [r_zre, r_ppre], [r_tl1])
        bld.tt("vector", tl2[:, :], zim[:, :], pp_im[:, :], ALU.mult,
               [r_zim, r_ppim], [r_tl2])
        bld.tt("vector", hre_all[:, st * TS:(st + 1) * TS], tl1[:, :], tl2[:, :],
               ALU.subtract, [r_tl1, r_tl2], [r_hre])
        bld.tt("vector", scr[:, 0:1], pp_re[:, 511:512], zre[:, 511:512], ALU.mult,
               [r_ppre, r_zre], [r_scr])
        bld.tt("vector", scr[:, 1:2], pp_im[:, 511:512], zim[:, 511:512], ALU.mult,
               [r_ppim, r_zim], [r_scr])
        bld.tt("vector", eloc[:, 0:1], scr[:, 0:1], scr[:, 1:2], ALU.subtract,
               [r_scr], [r_eloc])
        bld.tt("vector", scr[:, 0:1], pp_re[:, 511:512], zim[:, 511:512], ALU.mult,
               [r_ppre, r_zim], [r_scr])
        bld.tt("vector", scr[:, 1:2], pp_im[:, 511:512], zre[:, 511:512], ALU.mult,
               [r_ppim, r_zre], [r_scr])
        bld.tt("vector", eloc[:, 1:2], scr[:, 0:1], scr[:, 1:2], ALU.add,
               [r_scr], [r_eloc])
        bld.tt("vector", scr[:, 0:1], are_c[:, :], eloc[:, 0:1], ALU.mult,
               [r_are, r_eloc], [r_scr])
        bld.tt("vector", scr[:, 1:2], aim_c[:, :], eloc[:, 1:2], ALU.mult,
               [r_aim, r_eloc], [r_scr])
        bld.tt("vector", scr[:, 2:3], are_c[:, :], eloc[:, 1:2], ALU.mult,
               [r_are, r_eloc], [r_scr])
        bld.tt("vector", scr[:, 3:4], aim_c[:, :], eloc[:, 0:1], ALU.mult,
               [r_aim, r_eloc], [r_scr])
        bld.tt("vector", eloc[:, 2:3], scr[:, 0:1], scr[:, 1:2], ALU.subtract,
               [r_scr], [r_eloc])
        bld.tt("vector", eloc[:, 3:4], scr[:, 2:3], scr[:, 3:4], ALU.add,
               [r_scr], [r_eloc])

    if not do_l1 or L1_SUB < 3:
        bld.barrier()
        _finalize(nc, bld, stack, st_l1)
        return nc
    if debug:
        bld.dma("gpsimd", "dbg", dbg["d_hre"].ap(), hre_all[:, :], [r_hre], [Res()])

    wgt_r = wgt_d.ap().rearrange("(m k p) h -> m p k h", k=DC, p=128)
    wut_r = wut_d.ap().rearrange("(m k p) h -> m p k h", k=DC, p=128)
    wdt_rr = wdt_d.ap().rearrange("(m p) d -> p m d", p=128)
    n_l1b = NST if L1_SUB >= 4 else 0
    if n_l1b:
        bld.dma("sync", "wdt", wdt_s[:, :].rearrange("p (m d) -> p m d", d=D),
                wdt_rr, [r_wdt], [r_wdts])
        xn1_load(0)
    for st in range(n_l1b):
        xn1_transpose(st)
        for m in range(HB):
            mb = m % 2
            bld.dma("sync", "wg%d" % mb,
                    wgsl[:, mb * DC * 128:(mb + 1) * DC * 128].rearrange(
                        "p (k h) -> p k h", h=128), wgt_r[m], [r_wgt], [r_wgsl])
            bld.dma("sync", "wu%d" % mb,
                    wusl[:, mb * DC * 128:(mb + 1) * DC * 128].rearrange(
                        "p (k h) -> p k h", h=128), wut_r[m], [r_wut], [r_wusl])
            g0_, g1_ = (5, 6) if m % 2 == 0 else (7, 3)
            for c in range(DC):
                first, last = c == 0, c == DC - 1
                bld.mm(RB[g0_], PB[g0_][:, 0:512],
                       wgsl[:, mb * DC * 128 + c * 128: mb * DC * 128 + (c + 1) * 128],
                       xn1T[:, c * TS:(c + 1) * TS], first, last,
                       [r_wgsl, r_xn1T], first=first, last=last)
                bld.mm(RB[g1_], PB[g1_][:, 0:512],
                       wusl[:, mb * DC * 128 + c * 128: mb * DC * 128 + (c + 1) * 128],
                       xn1T[:, c * TS:(c + 1) * TS], first, last,
                       [r_wusl, r_xn1T], first=first, last=last)
            bld.act(trash[:, 512 * (m % 2):512 * (m % 2) + 512], PB[g0_][:, 0:512],
                    AF.Silu, [RB[g0_], r_kc], [r_trash], scale=kconst[:, 3:4])
            bld.tt("vector", hidT[:, m * TS:(m + 1) * TS], PB[g1_][:, 0:512],
                   trash[:, 512 * (m % 2):512 * (m % 2) + 512], ALU.mult,
                   [RB[g1_], r_trash], [r_hidT])
        if st == 0:
            bld.dma("sync", "ein", e_in.ap(), eloc[:, 0:2], [r_eloc], [r_ein])
            bld.op("gpsimd", lambda h: h.collective_compute(
                "AllGather", ALU.bypass,
                replica_groups=[[0, 1], [2, 3], [4, 5], [6, 7]],
                ins=[e_in.ap()], outs=[e_out.ap()]), [r_ein], [r_eout])
            bld.dma("sync", "eout", scr2[:, 0:2], e_out.ap()[0:128, :], [r_eout], [r_scr2])
            bld.ts("vector", eloc[:, 4:5], scr2[:, 0:1], ZMK, None, ALU.mult, ALU.bypass,
                   [r_scr2, r_pcsc], [r_eloc])
            bld.ts("vector", eloc[:, 5:6], scr2[:, 1:2], ZMK, None, ALU.mult, ALU.bypass,
                   [r_scr2, r_pcsc], [r_eloc])
            bld.tt("vector", scr[:, 0:1], pp_re[:, 511:512], are_c[:, :], ALU.mult,
                   [r_ppre, r_are], [r_scr])
            bld.tt("vector", scr[:, 1:2], pp_im[:, 511:512], aim_c[:, :], ALU.mult,
                   [r_ppim, r_aim], [r_scr])
            bld.tt("vector", scr[:, 4:5], scr[:, 0:1], scr[:, 1:2], ALU.subtract,
                   [r_scr], [r_scr])
            bld.tt("vector", scr[:, 0:1], pp_re[:, 511:512], aim_c[:, :], ALU.mult,
                   [r_ppre, r_aim], [r_scr])
            bld.tt("vector", scr[:, 1:2], pp_im[:, 511:512], are_c[:, :], ALU.mult,
                   [r_ppim, r_are], [r_scr])
            bld.tt("vector", scr[:, 5:6], scr[:, 0:1], scr[:, 1:2], ALU.add,
                   [r_scr], [r_scr])
            bld.cp("vector", scr[:, 6:7], are_c[:, :], [r_are], [r_scr])
            bld.cp("vector", scr[:, 7:8], aim_c[:, :], [r_aim], [r_scr])
            for stc in range(NST):
                bld.tt("vector", scr2[:, 0:1], scr[:, 6:7], eloc[:, 4:5], ALU.mult,
                       [r_scr, r_eloc], [r_scr2])
                bld.tt("vector", scr2[:, 1:2], scr[:, 7:8], eloc[:, 5:6], ALU.mult,
                       [r_scr, r_eloc], [r_scr2])
                bld.tt("vector", scr2[:, 4:5], scr2[:, 0:1], scr2[:, 1:2], ALU.subtract,
                       [r_scr2], [r_scr2])
                bld.tt("vector", scr2[:, 0:1], scr[:, 6:7], eloc[:, 5:6], ALU.mult,
                       [r_scr, r_eloc], [r_scr2])
                bld.tt("vector", scr2[:, 1:2], scr[:, 7:8], eloc[:, 4:5], ALU.mult,
                       [r_scr, r_eloc], [r_scr2])
                bld.tt("vector", scr2[:, 5:6], scr2[:, 0:1], scr2[:, 1:2], ALU.add,
                       [r_scr2], [r_scr2])
                bld.ts("vector", scr2[:, 6:7], scr2[:, 5:6], -1.0, None, ALU.mult, ALU.bypass,
                       [r_scr2], [r_scr2])
                hsl = hre_all[:, stc * TS:(stc + 1) * TS]
                bld.stt("vector", hsl, pp_re[:, :], scr2[:, 4:5], hsl, ALU.mult, ALU.add,
                        [r_ppre, r_scr2, r_hre], [r_hre])
                bld.stt("vector", hsl, pp_im[:, :], scr2[:, 6:7], hsl, ALU.mult, ALU.add,
                        [r_ppim, r_scr2, r_hre], [r_hre])
                bld.tt("vector", scr2[:, 0:1], scr[:, 6:7], scr[:, 4:5], ALU.mult,
                       [r_scr], [r_scr2])
                bld.tt("vector", scr2[:, 1:2], scr[:, 7:8], scr[:, 5:6], ALU.mult,
                       [r_scr], [r_scr2])
                bld.tt("vector", scr2[:, 2:3], scr[:, 6:7], scr[:, 5:6], ALU.mult,
                       [r_scr], [r_scr2])
                bld.tt("vector", scr2[:, 3:4], scr[:, 7:8], scr[:, 4:5], ALU.mult,
                       [r_scr], [r_scr2])
                bld.tt("vector", scr[:, 6:7], scr2[:, 0:1], scr2[:, 1:2], ALU.subtract,
                       [r_scr2], [r_scr])
                bld.tt("vector", scr[:, 7:8], scr2[:, 2:3], scr2[:, 3:4], ALU.add,
                       [r_scr2], [r_scr])

        if st + 1 < n_l1b:
            xn1_load(st + 1)
        for sub in range(4):
            ti = st * 4 + sub
            d0_, d1_ = (5, 6) if sub % 2 == 0 else (7, 3)
            y0_, y1_ = (2, 4) if sub % 2 == 0 else (0, 1)
            bld.mm(RB[y0_], PB[y0_][:, 0:512],
                   hre_all[:, ti * 128:(ti + 1) * 128], ct_t[:, 0:512],
                   True, True, [r_hre, r_ct], first=True, last=True)
            bld.mm(RB[y1_], PB[y1_][:, 0:512],
                   hre_all[:, ti * 128:(ti + 1) * 128], ct_t[:, 512:1024],
                   True, True, [r_hre, r_ct], first=True, last=True)
            bld.cp("scalar", y_bf[:, sub * D:sub * D + 512], PB[y0_][:, 0:512],
                   [RB[y0_]], [r_ybf])
            bld.cp("scalar", y_bf[:, sub * D + 512:(sub + 1) * D], PB[y1_][:, 0:512],
                   [RB[y1_]], [r_ybf])
            for m in range(HB):
                first, last = m == 0, m == HB - 1
                lt = hidT[:, m * TS + sub * 128: m * TS + (sub + 1) * 128]
                bld.mm(RB[d0_], PB[d0_][:, 0:512], lt, wdt_s[:, m * D:m * D + 512],
                       first, last, [r_hidT, r_wdts], first=first, last=last)
                bld.mm(RB[d1_], PB[d1_][:, 0:512], lt, wdt_s[:, m * D + 512:(m + 1) * D],
                       first, last, [r_hidT, r_wdts], first=first, last=last)
            bld.act(g2_bf[:, 0:512], PB[d0_][:, 0:512], AF.Silu, [RB[d0_], r_kc],
                    [r_g2bf], scale=kconst[:, 4:5])
            bld.act(g2_bf[:, 512:1024], PB[d1_][:, 0:512], AF.Silu, [RB[d1_], r_kc],
                    [r_g2bf], scale=kconst[:, 4:5])
            bld.tt("vector", out_f[:, 0:512], y_bf[:, sub * D:sub * D + 512],
                   g2_bf[:, 0:512], ALU.mult, [r_ybf, r_g2bf], [r_outf])
            bld.tt("vector", out_f[:, 512:1024],
                   y_bf[:, sub * D + 512:(sub + 1) * D], g2_bf[:, 512:1024],
                   ALU.mult, [r_ybf, r_g2bf], [r_outf])
            x1_stc, r_x1stc = x1_st2[st % 2]
            bld.tt("vector", out_f[:, 0:512], out_f[:, 0:512],
                   x1_stc[:, sub * D:sub * D + 512],
                   ALU.add, [r_outf, r_x1stc], [r_outf])
            bld.tt("vector", out_f[:, 512:1024], out_f[:, 512:1024],
                   x1_stc[:, sub * D + 512:(sub + 1) * D],
                   ALU.add, [r_outf, r_x1stc], [r_outf])
            bld.dma("gpsimd", "out", out_r[st // 2, (st % 2) * 4 + sub], out_f[:, :],
                    [r_outf], [r_out])
        if debug and st == 0:
            bld.dma("gpsimd", "dbg", dbg["d_y"].ap(), y_bf[:, :], [r_ybf], [Res()])

    bld.barrier()
    _finalize(nc, bld, stack, st_l1)
    return nc


def _finalize(nc, bld, stack, st_l1):
    with nc.Block() as blk:
        @blk.tensor
        def _(h):
            for f in bld.q["tensor"]:
                f(h)

        @blk.vector
        def _(h):
            for f in bld.q["vector"]:
                f(h)

        @blk.scalar
        def _(h):
            for f in bld.q["scalar"]:
                f(h)

        @blk.gpsimd
        def _(h):
            for f in bld.q["gpsimd"]:
                f(h)

        @blk.sync
        def _(h):
            for f in bld.q["sync"]:
                f(h)

    st_l1.close()
    stack.close()


def host_inputs(inputs):
    x = np.ascontiguousarray(inputs["L0_attn_hidden_states"], dtype=np.float32)
    Wq = np.asarray(inputs["L0_attn_Wq"], np.float32)
    Wk = np.asarray(inputs["L0_attn_Wk"], np.float32)
    Wv = np.asarray(inputs["L0_attn_Wv"], np.float32)
    Wb = np.asarray(inputs["L0_attn_Wb"], np.float32)
    Wg = np.asarray(inputs["L0_attn_Wg"], np.float32)
    conv_w = np.asarray(inputs["L0_attn_conv_w"], np.float32)
    convw_r = np.ascontiguousarray(
        conv_w.reshape(DC, 128, 4).transpose(1, 0, 2).reshape(128, 32))
    masks = np.zeros((128, 3 * 128), np.float32)
    ii = np.arange(128)
    masks[:, 0:128] = (ii[None, :] < ii[:, None])
    masks[:, 128:256] = (ii[None, :] > ii[:, None])
    masks[:, 256:384] = (ii[None, :] >= ii[:, None])
    bf = ml_dtypes.bfloat16
    common = dict(
        convw=convw_r,
        wo=np.ascontiguousarray(np.asarray(inputs["L0_attn_Wo"], np.float32)),
        normw=np.tile(np.asarray(inputs["L0_attn_norm_w"], np.float32), 2)[None, :],
        packed=np.ascontiguousarray(
            np.asarray(inputs["L0_ffn_packed_weight"], np.int32).reshape(256, D)),
        wscale=np.asarray(inputs["L0_ffn_weight_scale"], np.float32).reshape(1, 1),
        bias=np.asarray(inputs["L0_ffn_bias"], np.float32).reshape(1, D),
        Ar=np.asarray(inputs["L1_attn_A_real"], np.float32).reshape(128, 1),
        Ai=np.asarray(inputs["L1_attn_A_imag"], np.float32).reshape(128, 1),
        Bmat=np.ascontiguousarray(np.asarray(inputs["L1_attn_B"], np.float32)),
        Cmat=np.ascontiguousarray(np.asarray(inputs["L1_attn_C"], np.float32)),
        wg=np.ascontiguousarray(np.asarray(inputs["L1_ffn_w_gate"], np.int32)),
        wu=np.ascontiguousarray(np.asarray(inputs["L1_ffn_w_up"], np.int32)),
        wd=np.ascontiguousarray(np.asarray(inputs["L1_ffn_w_down"], np.int32)),
        idn=np.eye(128).astype(bf),
        masks=masks.astype(bf),
        ones512=np.ones((128, 512)).astype(bf),
        onescol=np.ones((1, 128)).astype(bf),
        kconst=np.tile(np.array([1e-5, 0.5, 1.0 / 256, 1.0 / 127,
                                 1.0 / 127 / 127, 0.0], np.float32), (128, 1)),
    )
    in_maps = []
    for c in range(8):
        b, z = c >> 1, c & 1
        h0 = 2 * z
        wproj = np.concatenate([
            Wq[:, h0 * 128:(h0 + 2) * 128], Wk[:, h0 * 128:(h0 + 2) * 128],
            Wv[:, h0 * 256:(h0 + 2) * 256], Wg[:, h0 * 256:(h0 + 2) * 256],
            Wb[:, h0:h0 + 2]], axis=1)
        pc = np.zeros((128, 4), np.float32)
        pc[:, 0] = 1.0 if z == 0 else 0.0
        pc[:, 1] = 1.0 if z == 1 else 0.0
        pc[:, 2] = 1.0 if z == 1 else 0.0
        m = dict(common)
        m.update(
            x_full=x[b],
            x_own=np.ascontiguousarray(x[b, z * HALF:(z + 1) * HALF]),
            wproj=np.ascontiguousarray(wproj),
            pcsc=pc,
        )
        in_maps.append(m)
    return in_maps


def kernel(**inputs):
    in_maps = host_inputs(inputs)
    nc = build_nc(debug=DEBUG)
    res = run_bass_kernel_spmd(nc, in_maps, core_ids=list(range(8)))
    out = np.empty((B, L, D), np.float32)
    for c in range(8):
        b, z = c >> 1, c & 1
        out[b, z * HALF:(z + 1) * HALF] = res.results[c]["out"]
    return out


# version 21
# speedup vs baseline: 1.1878x; 1.0007x over previous
import numpy as np
import ml_dtypes
import concourse.bass as bass
import concourse.mybir as mybir
from concourse.bass_utils import run_bass_kernel_spmd

F32 = mybir.dt.float32
BF16 = mybir.dt.bfloat16
I32 = mybir.dt.int32
AF = mybir.ActivationFunctionType
ALU = mybir.AluOpType
AXX = mybir.AxisListType.X

B, L, D = 4, 8192, 1024
HALF = L // 2
DC = 8
G0 = 8
TG = 1024
S8 = 8
CH = 128
WCOL = 1538
GW = 4
NST = 8
TS = 512
HB = 24
CMAGIC = 12582912.0
ENGS = ("tensor", "vector", "scalar", "gpsimd", "sync")

DEBUG = False
L0_SUB = 9
SC_SUB = 9
L1_SUB = 9
L0_STAGE = 99


class Res:
    __slots__ = ("w", "r", "pend")
    def __init__(self):
        self.w = None
        self.r = []
        self.pend = []


class Bld:
    def __init__(self, nc, stack):
        self.nc = nc
        self.q = {e: [] for e in ENGS}
        self.scnt = {}
        self.waited = {e: {} for e in ENGS}
        self.esem = {}
        for e in ENGS:
            h = stack.enter_context(nc.semaphore("se_" + e))
            self.esem[e] = h
            self.scnt[id(h)] = 0
            self._hmap = getattr(self, "_hmap", {})
            self._hmap[id(h)] = h
        self._stack = stack
        self._chan = {}

    def _wait(self, eng, tok):
        if tok is None:
            return
        sem, val, peng = tok
        if peng == eng:
            pass
        if self.waited[eng].get(id(sem), 0) >= val:
            return
        self.waited[eng][id(sem)] = val
        self.q[eng].append(lambda h, s=sem, v=val: h.wait_ge(s, v))

    def _hazard(self, eng, reads, writes):
        toks = []
        for r in reads:
            toks.append(r.w)
        for w in writes:
            toks.append(w.w)
            toks.extend(w.r)
        for t in toks:
            self._wait(eng, t)

    def op(self, eng, fn, reads=(), writes=()):
        self._hazard(eng, reads, writes)
        sem = self.esem[eng]
        self.scnt[id(sem)] += 1
        cnt = self.scnt[id(sem)]
        def run(h, fn=fn, sem=sem):
            ins = fn(h)
            ins.then_inc(sem, 1)
        self.q[eng].append(run)
        tok = (sem, cnt, eng)
        for r in reads:
            r.r.append(tok)
        for w in writes:
            w.w = tok
            w.r = []
        return tok

    def dma(self, eng, ch, out_ap, in_ap, reads=(), writes=(), **kw):
        if ch not in self._chan:
            h = self._stack.enter_context(self.nc.semaphore("q_" + ch))
            self._chan[ch] = h
            self.scnt[id(h)] = 0
            self._hmap[id(h)] = h
        qsem = self._chan[ch]
        prev = self.scnt[id(qsem)]
        if prev:
            self._wait(eng, (qsem, prev, None))
        self._hazard(eng, reads, writes)
        self.scnt[id(qsem)] = prev + 16
        cnt = prev + 16
        def run(h, o=out_ap, i=in_ap, qs=qsem, kw=kw):
            h.dma_start(o, i, **kw).then_inc(qs, 16)
        self.q[eng].append(run)
        tok = (qsem, cnt, None)
        for r in reads:
            r.r.append(tok)
        for w in writes:
            w.w = tok
            w.r = []
        return tok

    def barrier(self):
        sems = [(self._hmap[k], v) for k, v in self.scnt.items()]
        for e in ENGS:
            for sem, val in sems:
                if val > 0:
                    self._wait(e, (sem, val, None))

    def cp(self, eng, out, in_, reads, writes):
        if eng == "scalar":
            return self.op(eng, lambda h: h.copy(out, in_), reads, writes)
        return self.op(eng, lambda h: h.tensor_copy(out, in_), reads, writes)

    def tt(self, eng, out, a, bb, alu, reads, writes):
        return self.op(eng, lambda h: h.tensor_tensor(out, a, bb, alu), reads, writes)

    def ts(self, eng, out, in0, s1, s2, op0, op1, reads, writes, accum=None):
        if eng == "scalar":
            assert op0 == ALU.mult and s2 is None and accum is None
            return self.op(eng, lambda h: h.activation(out, in0, AF.Identity,
                                                       scale=s1), reads, writes)
        if s2 is None and accum is None:
            return self.op(eng, lambda h: h.tensor_single_scalar(out, in0, s1, op0),
                           reads, writes)
        return self.op(eng, lambda h: h.tensor_scalar(out, in0, s1, s2, op0, op1,
                                                      accum_out=accum), reads, writes)

    def stt(self, eng, out, in0, sc, in1, op0, op1, reads, writes):
        return self.op(eng, lambda h: h.scalar_tensor_tensor(out, in0, sc, in1, op0, op1),
                       reads, writes)

    def act(self, out, in_, func, reads, writes, bias=0.0, scale=1.0, accum=None):
        return self.op("scalar",
                       lambda h: h.activation(out, in_, func, bias=bias, scale=scale,
                                              accum_out=accum), reads, writes)

    def red(self, out, in_, alu, reads, writes, absval=False):
        return self.op("vector",
                       lambda h: h.tensor_reduce(out, in_, AXX, alu,
                                                 apply_absolute_value=absval),
                       reads, writes)

    def mm(self, bank, out, lhsT, rhs, start, stop, reads, first, last):
        if first:
            self._hazard("tensor", (), (bank,))
            bank.pend = []
        self._hazard("tensor", reads, ())
        bank.pend.extend(reads)
        if not last:
            def run0(h, o=out, lt=lhsT, r=rhs, st=start, sp=stop):
                h.matmul(o, lt, r, start=st, stop=sp)
            self.q["tensor"].append(run0)
            return None
        sem = self.esem["tensor"]
        self.scnt[id(sem)] += 1
        cnt = self.scnt[id(sem)]
        def run(h, o=out, lt=lhsT, r=rhs, st=start, sp=stop, sem=sem):
            h.matmul(o, lt, r, start=st, stop=sp).then_inc(sem, 1)
        self.q["tensor"].append(run)
        tok = (sem, cnt, "tensor")
        for rr in bank.pend:
            rr.r.append(tok)
        bank.pend = []
        bank.w = tok
        bank.r = []
        return tok


def build_nc(debug=False, n_g0=G0, do_w=True, do_l1=True):
    global L0_STAGE, L0_SUB, SC_SUB, L1_SUB
    nc = bass.Bass("TRN2", target_bir_lowering=False)
    import contextlib
    stack = contextlib.ExitStack()

    d_in = {}
    def din(name, shape, dt):
        d_in[name] = nc.dram_tensor(name, shape, dt, kind="ExternalInput")
        return d_in[name]
    x_full = din("x_full", [L, D], F32)
    x_own = din("x_own", [HALF, D], F32)
    wproj = din("wproj", [D, WCOL], F32)
    wo_in = din("wo", [D, D], F32)
    convw_in = din("convw", [128, 32], F32)
    normw_in = din("normw", [1, 512], F32)
    packed_in = din("packed", [256, D], I32)
    wscale_in = din("wscale", [1, 1], F32)
    bias_in = din("bias", [1, D], F32)
    ar_in = din("Ar", [128, 1], F32)
    ai_in = din("Ai", [128, 1], F32)
    bmat_in = din("Bmat", [128, D], F32)
    cmat_in = din("Cmat", [D, 128], F32)
    wg_in = din("wg", [3072, D], I32)
    wu_in = din("wu", [3072, D], I32)
    wd_in = din("wd", [D, 3072], I32)
    idn_in = din("idn", [128, 128], BF16)
    masks_in = din("masks", [128, 3 * 128], BF16)
    ones512_in = din("ones512", [128, 512], BF16)
    onescol_in = din("onescol", [1, 128], BF16)
    pcsc_in = din("pcsc", [128, 4], F32)
    kconst_in = din("kconst", [128, 6], F32)

    out_d = nc.dram_tensor("out", [HALF, D], F32, kind="ExternalOutput")
    dbg = {}
    if debug:
        for nm, shp, dt in (("d_hT", [128, DC * TG], BF16),
                            ("d_qkg", [128, S8 * 512], BF16),
                            ("d_gated", [L, 512], BF16),
                            ("d_x1", [128, 32 * D], BF16),
                            ("d_g0", [128, D], BF16),
                            ("d_xint", [128, D], BF16),
                            ("d_ffnsb", [128, D], F32),
                            ("d_ffnraw", [128, D], F32),
                            ("d_biasb", [128, D], F32),
                            ("d_qcol", [128, 8], F32),
                            ("d_s0", [128, D], F32),
                            ("d_hre", [128, 4096], BF16),
                            ("d_y", [128, 4 * D], BF16),
                            ("d_sb", [128, 512], BF16)):
            dbg[nm] = nc.dram_tensor(nm, shp, dt, kind="ExternalOutput")

    gat_in = nc.dram_tensor("gat_in", [L, 512], BF16)
    gat_out = nc.dram_tensor("gat_out", [2 * L, 512], BF16)
    e_in = nc.dram_tensor("e_in", [128, 2], F32)
    e_out = nc.dram_tensor("e_out", [256, 2], F32)
    wgt_d = nc.dram_tensor("wgt_d", [HB * DC * 128, 128], BF16)
    wut_d = nc.dram_tensor("wut_d", [HB * DC * 128, 128], BF16)
    wdt_d = nc.dram_tensor("wdt_d", [HB * 128, D], BF16)
    r_gin, r_gout = Res(), Res()
    r_ein, r_eout = Res(), Res()
    r_wgt, r_wut, r_wdt = Res(), Res(), Res()
    r_out = Res()

    bld = Bld(nc, stack)
    st_pro = contextlib.ExitStack()
    st_lw = contextlib.ExitStack()
    st_l0 = contextlib.ExitStack()
    st_w = contextlib.ExitStack()
    st_l1 = contextlib.ExitStack()

    def sb(name, shape, dt, stk=None):
        h = (stk or stack).enter_context(nc.sbuf_tensor("s_" + name, shape, dt))
        return h, Res()

    def pb(name, cols=512):
        h = stack.enter_context(nc.psum_tensor(name, [128, cols], F32))
        return h, Res()

    idn, r_idn = sb("idn", [128, 128], BF16)
    msk, r_msk = sb("msk", [128, 3 * 128], BF16)
    ones512, r_ones = sb("ones512", [128, 512], BF16)
    onescol, r_onescol = sb("onescol", [1, 128], BF16)
    pcsc, r_pcsc = sb("pcsc", [128, 4], F32)
    kconst, r_kc = sb("kconst", [128, 6], F32)
    x1_d = nc.dram_tensor("x1_d", [HALF, D], BF16)
    r_x1 = Res()
    x1_dr = x1_d.ap().rearrange("(t p) d -> t p d", p=128)
    stat_r, r_statr = sb("stat_r", [128, 64], F32)
    stat_m, r_statm = sb("stat_m", [128, 64], F32)
    statw_r, r_statwr = sb("statw_r", [128, 32], F32)
    statw_m, r_statwm = sb("statw_m", [128, 32], F32)
    stat1_r, r_stat1r = sb("stat1_r", [128, 32], F32)
    stat1_m, r_stat1m = sb("stat1_m", [128, 32], F32)
    pm_re, r_pmre = sb("pm_re", [128, 512], F32)
    pm_im, r_pmim = sb("pm_im", [128, 512], F32)
    pp_re, r_ppre = sb("pp_re", [128, 512], F32)
    pp_im, r_ppim = sb("pp_im", [128, 512], F32)
    are_c, r_are = sb("are_c", [128, 1], F32)
    aim_c, r_aim = sb("aim_c", [128, 1], F32)
    ct_t, r_ct = sb("ct_t", [128, D], BF16)
    bdt_re, r_bdtre = sb("bdt_re", [128, DC * 128], BF16)
    bdt_im, r_bdtim = sb("bdt_im", [128, DC * 128], BF16)
    trash, r_trash = sb("trash", [128, 1024], BF16)
    sums_g, r_sumsg = sb("sums_g", [128, S8], F32)
    sqs_g, r_sqsg = sb("sqs_g", [128, S8], F32)
    sum1_g, r_sum1g = sb("sum1_g", [128, S8], F32)
    sq1_g, r_sq1g = sb("sq1_g", [128, S8], F32)
    wst = [sb(f"wst{i}", [128, 3072], BF16) for i in range(2)]
    wtt = [sb(f"wtt{i}", [128, 3072], BF16) for i in range(2)]
    scr, r_scr = sb("scr", [128, 16], F32)
    scr2, r_scr2 = sb("scr2", [128, 16], F32)

    pbank = []
    for i in range(8):
        pbank.append(pb(f"pb{i}"))
    PB = [p[0] for p in pbank]
    RB = [p[1] for p in pbank]

    MSL = msk[:, 0:128]
    MSU = msk[:, 128:256]
    MTI = msk[:, 256:384]

    rot_state = [0]
    def rotbank():
        i = rot_state[0]
        rot_state[0] = (i + 1) % 3
        return PB[i], RB[i]

    def evac(eng, out_ap, bank_res, psum_ap, writes, scale=None):
        if scale is None:
            return bld.cp(eng, out_ap, psum_ap, [bank_res], writes)
        return bld.act(out_ap, psum_ap, AF.Copy, [bank_res], writes, scale=scale)

    def transpose128(src_ap, src_res, dst_ap, dst_res, rhs_ap=None, rhs_res=None,
                     eng="vector"):
        bk, bkr = rotbank()
        r = [src_res] + ([rhs_res] if rhs_res is not None else [r_idn])
        bld.mm(bkr, bk[:, 0:128], src_ap, rhs_ap if rhs_ap is not None else idn[:, :],
               True, True, r, first=True, last=True)
        evac(eng, dst_ap, bkr, bk[:, 0:128], [dst_res])

    bld.dma("sync", "cst", idn[:, :], idn_in.ap(), (), (r_idn,))
    bld.dma("sync", "cst2", msk[:, :], masks_in.ap(), (), (r_msk,))
    bld.dma("sync", "cst3", ones512[:, :], ones512_in.ap(), (), (r_ones,))
    bld.dma("sync", "cst4", onescol[:, :], onescol_in.ap(), (), (r_onescol,))
    bld.dma("sync", "cst5", pcsc[:, :], pcsc_in.ap(), (), (r_pcsc,))
    bld.dma("sync", "cst6", kconst[:, :], kconst_in.ap(), (), (r_kc,))

    arc, r_arc = sb("arc", [128, 1], F32, st_pro)
    aic, r_aic = sb("aic", [128, 1], F32, st_pro)
    bld.dma("sync", "ar", arc[:, :], ar_in.ap(), (), (r_arc,))
    bld.dma("sync", "ai", aic[:, :], ai_in.ap(), (), (r_aic,))
    bmat, r_bmat = sb("bmat", [128, D], F32, st_pro)
    bld.dma("sync", "bm", bmat[:, :], bmat_in.ap(), (), (r_bmat,))
    cmat_b, r_cmatb = sb("cmat_b", [128, DC * 128], BF16, st_pro)
    bld.dma("gpsimd", "cm", cmat_b[:, :].rearrange("p (c w) -> p c w", w=128),
            cmat_in.ap().rearrange("(c p) w -> p c w", p=128), (), (r_cmatb,))

    pstage, r_pst = sb("pstage", [128, 2 * D], I32, st_pro)
    bld.dma("sync", "pk", pstage[:, :].rearrange("p (c w) -> p c w", w=D),
            packed_in.ap().rearrange("(c p) w -> p c w", p=128), (), (r_pst,))
    tern, r_tern = sb("tern", [128, DC * D], BF16, st_pro)
    ei, r_ei = sb("ei", [128, 2 * D], I32, st_pro)
    for sh in range(4):
        if sh == 0:
            bld.ts("vector", ei[:, :], pstage[:, :], 3, None,
                   ALU.bitwise_and, ALU.bypass, [r_pst], [r_ei])
        else:
            bld.ts("vector", ei[:, :], pstage[:, :], 2 * sh, 3,
                   ALU.logical_shift_right, ALU.bitwise_and, [r_pst], [r_ei])
        bld.ts("vector", tern[:, 2 * sh * D:(2 * sh + 2) * D], ei[:, :], -1.0, None,
               ALU.add, ALU.bypass, [r_ei], [r_tern])
    ternT_d = nc.dram_tensor("ternT_d", [D, D], BF16)
    r_ternTd = Res()
    tstg, r_tstg = sb("tstg", [128, D], BF16, st_pro)
    ternTd_r = ternT_d.ap().rearrange("(ic p) o -> ic p o", p=128)
    for ic in range(DC):
        for oc in range(DC):
            transpose128(tern[:, oc * D + ic * 128: oc * D + (ic + 1) * 128], r_tern,
                         tstg[:, oc * 128:(oc + 1) * 128], r_tstg,
                         eng=("vector" if (oc + ic) % 2 == 0 else "scalar"))
        bld.dma("sync", "ttd", ternTd_r[ic], tstg[:, :], [r_tstg], [r_ternTd])

    dre, r_dre = sb("dre", [128, 1], F32, st_pro)
    dim, r_dim = sb("dim", [128, 1], F32, st_pro)
    nre, r_nre = sb("nre", [128, 1], F32, st_pro)
    nim, r_nim = sb("nim", [128, 1], F32, st_pro)
    ire, r_ire = sb("ire", [128, 1], F32, st_pro)
    iim, r_iim = sb("iim", [128, 1], F32, st_pro)
    tmp1, r_tmp1 = sb("tmp1", [128, 4], F32, st_pro)
    bld.ts("vector", dre[:, :], arc[:, :], -0.005, 1.0, ALU.mult, ALU.add,
           [r_arc], [r_dre])
    bld.ts("vector", dim[:, :], aic[:, :], -0.005, None, ALU.mult, ALU.bypass,
           [r_aic], [r_dim])
    bld.ts("vector", nre[:, :], arc[:, :], 0.005, 1.0, ALU.mult, ALU.add,
           [r_arc], [r_nre])
    bld.ts("vector", nim[:, :], aic[:, :], 0.005, None, ALU.mult, ALU.bypass,
           [r_aic], [r_nim])
    bld.tt("vector", tmp1[:, 0:1], dre[:, :], dre[:, :], ALU.mult, [r_dre], [r_tmp1])
    bld.stt("vector", tmp1[:, 1:2], dim[:, :], 1.0, dim[:, :], ALU.mult, ALU.mult,
            [r_dim], [r_tmp1])
    bld.tt("vector", tmp1[:, 0:1], tmp1[:, 0:1], tmp1[:, 1:2], ALU.add,
           [r_tmp1], [r_tmp1])
    bld.op("vector", lambda h: h.reciprocal(tmp1[:, 2:3], tmp1[:, 0:1]),
           [r_tmp1], [r_tmp1])
    bld.tt("vector", ire[:, :], dre[:, :], tmp1[:, 2:3], ALU.mult,
           [r_dre, r_tmp1], [r_ire])
    bld.stt("vector", iim[:, :], dim[:, :], -1.0, tmp1[:, 2:3], ALU.mult, ALU.mult,
            [r_dim, r_tmp1], [r_iim])
    bld.tt("vector", tmp1[:, 0:1], nre[:, :], ire[:, :], ALU.mult, [r_nre, r_ire], [r_tmp1])
    bld.tt("vector", tmp1[:, 1:2], nim[:, :], iim[:, :], ALU.mult, [r_nim, r_iim], [r_tmp1])
    bld.tt("vector", are_c[:, :], tmp1[:, 0:1], tmp1[:, 1:2], ALU.subtract,
           [r_tmp1], [r_are])
    bld.tt("vector", tmp1[:, 0:1], nre[:, :], iim[:, :], ALU.mult, [r_nre, r_iim], [r_tmp1])
    bld.tt("vector", tmp1[:, 1:2], nim[:, :], ire[:, :], ALU.mult, [r_nim, r_ire], [r_tmp1])
    bld.tt("vector", aim_c[:, :], tmp1[:, 0:1], tmp1[:, 1:2], ALU.add,
           [r_tmp1], [r_aim])
    iar, r_iar = sb("iar", [128, 1], F32, st_pro)
    iai, r_iai = sb("iai", [128, 1], F32, st_pro)
    bld.tt("vector", tmp1[:, 0:1], are_c[:, :], are_c[:, :], ALU.mult, [r_are], [r_tmp1])
    bld.stt("vector", tmp1[:, 1:2], aim_c[:, :], 1.0, aim_c[:, :], ALU.mult, ALU.mult,
            [r_aim], [r_tmp1])
    bld.tt("vector", tmp1[:, 0:1], tmp1[:, 0:1], tmp1[:, 1:2], ALU.add, [r_tmp1], [r_tmp1])
    bld.op("vector", lambda h: h.reciprocal(tmp1[:, 2:3], tmp1[:, 0:1]), [r_tmp1], [r_tmp1])
    bld.tt("vector", iar[:, :], are_c[:, :], tmp1[:, 2:3], ALU.mult, [r_are, r_tmp1], [r_iar])
    bld.stt("vector", iai[:, :], aim_c[:, :], -1.0, tmp1[:, 2:3], ALU.mult, ALU.mult,
            [r_aim, r_tmp1], [r_iai])

    def build_powers(pre, pim, r_pre, r_pim, base_re_ap, base_im_ap, r_bre, r_bim):
        cwre, r_cwre = sb(f"cw_{pre.name}r", [128, 1], F32, st_pro)
        cwim, r_cwim = sb(f"cw_{pre.name}i", [128, 1], F32, st_pro)
        bld.cp("vector", cwre[:, :], base_re_ap, [r_bre], [r_cwre])
        bld.cp("vector", cwim[:, :], base_im_ap, [r_bim], [r_cwim])
        bld.op("vector", lambda h: h.memset(pre[:, 0:1], 1.0), (), (r_pre,))
        bld.op("vector", lambda h: h.memset(pim[:, 0:1], 0.0), (), (r_pim,))
        w = 1
        tre, r_tre = sb(f"tp_{pre.name}r", [128, 256], F32, st_pro)
        tim, r_tim = sb(f"tp_{pre.name}i", [128, 256], F32, st_pro)
        while w < 512:
            wl = min(w, 512 - w)
            bld.ts("vector", tre[:, 0:wl], pim[:, 0:wl], cwim[:, :], None,
                   ALU.mult, ALU.bypass, [r_pim, r_cwim], [r_tre])
            bld.stt("vector", pre[:, w:w + wl], pre[:, 0:wl], cwre[:, :], tre[:, 0:wl],
                    ALU.mult, ALU.subtract, [r_pre, r_cwre, r_tre], [r_pre])
            bld.ts("vector", tim[:, 0:wl], pim[:, 0:wl], cwre[:, :], None,
                   ALU.mult, ALU.bypass, [r_pim, r_cwre], [r_tim])
            bld.stt("vector", pim[:, w:w + wl], pre[:, 0:wl], cwim[:, :], tim[:, 0:wl],
                    ALU.mult, ALU.add, [r_pre, r_cwim, r_tim], [r_pim])
            bld.tt("vector", tre[:, 0:1], cwre[:, :], cwre[:, :], ALU.mult,
                   [r_cwre], [r_tre])
            bld.tt("vector", tim[:, 0:1], cwim[:, :], cwim[:, :], ALU.mult,
                   [r_cwim], [r_tim])
            bld.tt("vector", tre[:, 1:2], cwre[:, :], cwim[:, :], ALU.mult,
                   [r_cwre, r_cwim], [r_tre])
            bld.tt("vector", cwre[:, :], tre[:, 0:1], tim[:, 0:1], ALU.subtract,
                   [r_tre, r_tim], [r_cwre])
            bld.ts("vector", cwim[:, :], tre[:, 1:2], 2.0, None, ALU.mult, ALU.bypass,
                   [r_tre], [r_cwim])
            w *= 2

    build_powers(pm_re, pm_im, r_pmre, r_pmim, iar[:, :], iai[:, :], r_iar, r_iai)
    build_powers(pp_re, pp_im, r_ppre, r_ppim, are_c[:, :], aim_c[:, :], r_are, r_aim)

    sdre, r_sdre = sb("sdre", [128, 1], F32, st_pro)
    sdim, r_sdim = sb("sdim", [128, 1], F32, st_pro)
    bld.ts("vector", sdre[:, :], ire[:, :], 0.1, None, ALU.mult, ALU.bypass,
           [r_ire], [r_sdre])
    bld.ts("vector", sdim[:, :], iim[:, :], 0.1, None, ALU.mult, ALU.bypass,
           [r_iim], [r_sdim])
    bre_t, r_bret = sb("bre_t", [128, D], BF16, st_pro)
    bim_t, r_bimt = sb("bim_t", [128, D], BF16, st_pro)
    bld.ts("vector", bre_t[:, :], bmat[:, :], sdre[:, :], None, ALU.mult, ALU.bypass,
           [r_bmat, r_sdre], [r_bret])
    bld.ts("vector", bim_t[:, :], bmat[:, :], sdim[:, :], None, ALU.mult, ALU.bypass,
           [r_bmat, r_sdim], [r_bimt])
    for c in range(DC):
        transpose128(bre_t[:, c * 128:(c + 1) * 128], r_bret,
                     bdt_re[:, c * 128:(c + 1) * 128], r_bdtre, eng="vector")
        transpose128(bim_t[:, c * 128:(c + 1) * 128], r_bimt,
                     bdt_im[:, c * 128:(c + 1) * 128], r_bdtim, eng="scalar")
    for c in range(DC):
        transpose128(cmat_b[:, c * 128:(c + 1) * 128], r_cmatb,
                     ct_t[:, c * 128:(c + 1) * 128], r_ct,
                     eng=("vector" if c % 2 == 0 else "scalar"))

    wprep_jobs = []
    for (src, dst, rdst) in ((wg_in, wgt_d, r_wgt), (wu_in, wut_d, r_wut)):
        src_r = src.ap().rearrange("(m p) d -> m p d", p=128)
        dst_r = dst.ap().rearrange("(m k p) h -> m p k h", k=DC, p=128)
        for m in range(HB):
            def job(m=m, src_r=src_r, dst_r=dst_r, rdst=rdst):
                w0, rw0 = wst[m % 2]
                w1, rw1 = wtt[m % 2]
                bld.dma("gpsimd", "wl1%d" % (m % 2), w0[:, 0:D], src_r[m],
                        [rw0], [rw0])
                for kc in range(DC):
                    transpose128(w0[:, kc * 128:(kc + 1) * 128], rw0,
                                 w1[:, kc * 128:(kc + 1) * 128], rw1,
                                 eng=("vector" if kc % 2 == 0 else "scalar"))
                bld.dma("sync", "ws1%d" % (m % 2), dst_r[m],
                        w1[:, 0:DC * 128].rearrange("p (k h) -> p k h", h=128),
                        [rw1], [rdst])
            wprep_jobs.append(job)
    wdt_r = wdt_d.ap().rearrange("(m p) d -> p m d", p=128)
    for rt in range(DC):
        def job(rt=rt):
            w0, rw0 = wst[rt % 2]
            w1, rw1 = wtt[rt % 2]
            bld.dma("gpsimd", "wl1%d" % (rt % 2), w0[:, 0:3072],
                    wd_in.ap().rearrange("(r p) h -> r p h", p=128)[rt],
                    [rw0], [rw0])
            for m in range(HB):
                transpose128(w0[:, m * 128:(m + 1) * 128], rw0,
                             w1[:, m * 128:(m + 1) * 128], rw1,
                             eng=("vector" if m % 2 == 0 else "scalar"))
            bld.dma("sync", "ws1%d" % (rt % 2), wdt_r[:, :, rt * 128:(rt + 1) * 128],
                    w1[:, 0:HB * 128].rearrange("p (m c) -> p m c", c=128),
                    [rw1], [r_wdt])
        wprep_jobs.append(job)

    st_pro.close()
    bld.barrier()

    x_g, r_xg = sb("x_g", [128, S8 * D], BF16, st_lw)
    wpj_b, r_wpjb = sb("wpj_b", [128, DC * WCOL], BF16, st_l0)
    bld.dma("gpsimd", "wpj", wpj_b[:, :].rearrange("p (c w) -> p c w", w=WCOL),
            wproj.ap().rearrange("(c p) w -> p c w", p=128), (), (r_wpjb,))
    convw, r_convw = sb("convw", [128, 32], F32, st_l0)
    bld.dma("sync", "cvw", convw[:, :], convw_in.ap(), (), (r_convw,))
    dconv, r_dconv = sb("dconv", [128, 32 * 128], BF16, st_l0)
    for cc in range(DC):
        for tap in range(4):
            bld.ts("vector", dconv[:, (cc * 4 + tap) * 128:(cc * 4 + tap + 1) * 128],
                   idn[:, :], convw[:, cc * 4 + tap:cc * 4 + tap + 1], None,
                   ALU.mult, ALU.bypass, [r_idn, r_convw], [r_dconv])
    normw1, r_normw1 = sb("normw1", [1, 512], BF16, st_l0)
    bld.dma("gpsimd", "nw", normw1[:, :], normw_in.ap(), (), (r_normw1,))
    normwb, r_normwb = sb("normwb", [128, 512], BF16, st_l0)
    bk, bkr = rotbank()
    bld.mm(bkr, bk[:, 0:512], onescol[:, :], normw1[:, :], True, True,
           [r_onescol, r_normw1], first=True, last=True)
    evac("vector", normwb[:, :], bkr, bk[:, 0:512], [r_normwb])
    xnT, r_xnT = sb("xnT", [128, DC * (TG + 3)], BF16, st_l0)
    hT, r_hT = sb("hT", [128, DC * TG], BF16, st_l0)
    qkg, r_qkg = sb("qkg", [128, S8 * 512], BF16, st_l0)
    vg, r_vg = sb("vg", [128, S8 * 512], BF16, st_l0)
    sgnw, r_sgnw = sb("sgnw", [128, S8 * 512], BF16, st_l0)
    beta_g, r_betag = sb("beta_g", [128, S8 * 2], F32, st_l0)
    ssq_g, r_ssqg = sb("ssq_g", [128, S8 * 4], F32, st_l0)
    scl_g, r_sclg = sb("scl_g", [128, S8 * 4], F32, st_l0)
    gated_g, r_gatedg = sb("gated_g", [128, S8 * 512], BF16, st_l0)
    xn_bf, r_xnbf = sb("xn_bf", [128, D], BF16, st_l0)
    NI = 8
    scant = []
    for i in range(NI):
        d = {}
        for nm, w in (("dgq", 128), ("dgk", 128), ("dgkb", 128), ("qTn", 128),
                      ("kTn", 128), ("kbTn", 128), ("kn", 128), ("Y", 384),
                      ("TT", 128), ("Pa", 128), ("PTa", 128), ("Pb", 128),
                      ("PTb", 128), ("nW2T", 128), ("W1", 256), ("U", 256),
                      ("Mp", 128)):
            d[nm] = sb(f"sc{nm}_{i}", [128, w], BF16, st_l0)
        scant.append(d)
    S_f32, r_Sf = sb("S_f32", [128, 512], F32, st_l0)
    S_bf, r_Sbf = sb("S_bf", [128, 512], BF16, st_l0)
    betasc, r_betasc = sb("betasc", [128, 8], F32, st_l0)
    orst, r_orst = sb("orst", [128, 8], F32, st_l0)

    bld.op("vector", lambda h: h.memset(S_f32[:, :], 0.0), (), (r_Sf,))
    bld.op("vector", lambda h: h.memset(S_bf[:, :], 0.0), (), (r_Sbf,))
    bld.op("vector", lambda h: h.memset(
        xnT[:, :].rearrange("p (c w) -> p c w", w=TG + 3)[:, :, 0:3], 0.0),
        (), (r_xnT,))

    x_fr = x_full.ap().rearrange("(g s p) d -> g p s d", p=128, s=S8)

    def ln_stats(x_res, x_ap_sub, scol, sqcol, rs=None, rq=None):
        bld.act(trash[:, 0:1024], x_ap_sub, AF.Identity, [x_res],
                [r_trash, rs or r_sumsg], accum=scol)
        bld.act(trash[:, 0:1024], x_ap_sub, AF.Square, [x_res],
                [r_trash, rq or r_sqsg], accum=sqcol)

    def ln_rstd(scol_ap, sqcol_ap, n_inv, rst_ap, mrs_ap, r_rst, r_mrs, ncols,
                rs=None, rq=None):
        m = scr[:, 0:ncols]
        v = scr[:, 8:8 + ncols] if ncols <= 8 else None
        bld.ts("vector", m, scol_ap, n_inv, None, ALU.mult, ALU.bypass,
               [rs or r_sumsg], [r_scr])
        bld.ts("vector", scr2[:, 0:ncols], sqcol_ap, n_inv, None, ALU.mult, ALU.bypass,
               [rq or r_sqsg], [r_scr2])
        bld.stt("vector", v, m, -1.0, m, ALU.mult, ALU.mult, [r_scr], [r_scr])
        bld.tt("vector", v, scr2[:, 0:ncols], v, ALU.add, [r_scr2, r_scr], [r_scr])
        bld.act(scr2[:, 0:ncols], v, AF.Sqrt, [r_scr, r_kc], [r_scr2],
                bias=kconst[:, 0:1])
        bld.op("vector", lambda h, o=rst_ap, i=scr2[:, 0:ncols]: h.reciprocal(o, i),
               [r_scr2], [r_rst])
        bld.ts("vector", scr2[:, 8:8 + ncols], v, 1e-5, None, ALU.add, ALU.bypass,
               [r_scr], [r_scr2])
        bld.tt("vector", v, rst_ap, rst_ap, ALU.mult, [r_rst], [r_scr])
        bld.tt("vector", v, v, scr2[:, 8:8 + ncols], ALU.mult, [r_scr, r_scr2], [r_scr])
        bld.ts("vector", v, v, -0.5, 1.5, ALU.mult, ALU.add, [r_scr], [r_scr])
        bld.tt("vector", rst_ap, rst_ap, v, ALU.mult, [r_rst, r_scr], [r_rst])
        bld.stt("vector", mrs_ap, m, -1.0, rst_ap, ALU.mult, ALU.mult,
                [r_scr, r_rst], [r_mrs])

    def emitA1(g):
        xv = x_g[:, :].rearrange("p (s d) -> p s d", d=D)
        bld.dma("gpsimd", "xg", xv[:, 0:4], x_fr[g][:, 0:4], [r_xg], [r_xg])
        bld.dma("gpsimd", "xgb", xv[:, 4:8], x_fr[g][:, 4:8], [r_xg], [r_xg])
        for s in range(S8):
            ln_stats(r_xg, x_g[:, s * D:(s + 1) * D], sums_g[:, s:s + 1],
                     sqs_g[:, s:s + 1])
        ln_rstd(sums_g[:, 0:S8], sqs_g[:, 0:S8], 1.0 / D,
                stat_r[:, g * 8:(g + 1) * 8], stat_m[:, g * 8:(g + 1) * 8],
                r_statr, r_statm, 8)

    def emitA2(g):

        if g > 0:
            xr3 = xnT[:, :].rearrange("p (c w) -> p c w", w=TG + 3)
            bld.cp("vector", xr3[:, :, 0:3], xr3[:, :, TG:TG + 3], [r_xnT], [r_xnT])
        for s in range(S8):
            bld.act(xn_bf[:, 0:D], x_g[:, s * D:(s + 1) * D], AF.Identity,
                    [r_xg, r_statr, r_statm], [r_xnbf],
                    scale=stat_r[:, g * 8 + s:g * 8 + s + 1],
                    bias=stat_m[:, g * 8 + s:g * 8 + s + 1])
            for c in range(DC):
                transpose128(xn_bf[:, c * 128:(c + 1) * 128], r_xnbf,
                             xnT[:, c * (TG + 3) + 3 + s * 128:
                                 c * (TG + 3) + 3 + (s + 1) * 128], r_xnT,
                             eng=("vector" if (s + c) % 2 else "scalar"))
        for c in range(DC):
            src = xnT[:, c * (TG + 3):(c + 1) * (TG + 3)]
            for nh in range(2):
                bk, bkr = rotbank()
                for tap in range(4):
                    bld.mm(bkr, bk[:, 0:512],
                           dconv[:, (c * 4 + tap) * 128:(c * 4 + tap + 1) * 128],
                           src[:, nh * 512 + tap: nh * 512 + tap + 512],
                           tap == 0, tap == 3, [r_dconv, r_xnT],
                           first=(tap == 0), last=(tap == 3))
                bld.act(hT[:, c * TG + nh * 512: c * TG + (nh + 1) * 512],
                        bk[:, 0:512], AF.Silu, [bkr], [r_hT])


    def emitB(g):
        for s in range(S8):
            if s % 2 == 0:
                bq, bk_, bv, bg = 5, 6, 7, 3
            else:
                bq, bk_, bv, bg = 0, 1, 2, 4
            for kc in range(DC):
                lt = hT[:, kc * TG + s * 128: kc * TG + (s + 1) * 128]
                w0 = kc * WCOL
                first, last = kc == 0, kc == DC - 1
                bld.mm(RB[bq], PB[bq][:, 0:256], lt, wpj_b[:, w0:w0 + 256],
                       first, False, [r_hT, r_wpjb], first=first, last=False)
                bld.mm(RB[bq], PB[bq][:, 256:258], lt, wpj_b[:, w0 + 1536:w0 + 1538],
                       False, last, [r_hT, r_wpjb], first=False, last=last)
                bld.mm(RB[bk_], PB[bk_][:, 0:256], lt, wpj_b[:, w0 + 256:w0 + 512],
                       first, last, [r_hT, r_wpjb], first=first, last=last)
                bld.mm(RB[bv], PB[bv][:, 0:512], lt, wpj_b[:, w0 + 512:w0 + 1024],
                       first, last, [r_hT, r_wpjb], first=first, last=last)
                bld.mm(RB[bg], PB[bg][:, 0:512], lt, wpj_b[:, w0 + 1024:w0 + 1536],
                       first, last, [r_hT, r_wpjb], first=first, last=last)
            bld.cp("vector", qkg[:, s * 512:s * 512 + 256], PB[bq][:, 0:256],
                   [RB[bq]], [r_qkg])
            bld.cp("vector", scr[:, 10:12], PB[bq][:, 256:258], [RB[bq]], [r_scr])
            bld.act(scr[:, 8:10], scr[:, 10:12], AF.Tanh, [r_scr, r_kc],
                    [r_scr], scale=kconst[:, 1:2])
            bld.ts("vector", beta_g[:, s * 2:s * 2 + 2], scr[:, 8:10], 0.5, 0.5,
                   ALU.mult, ALU.add, [r_scr], [r_betag])
            bld.cp("vector", qkg[:, s * 512 + 256:s * 512 + 512], PB[bk_][:, 0:256],
                   [RB[bk_]], [r_qkg])
            bld.cp("scalar", vg[:, s * 512:(s + 1) * 512], PB[bv][:, 0:512],
                   [RB[bv]], [r_vg])
            bld.act(sgnw[:, s * 512:(s + 1) * 512], PB[bg][:, 0:512], AF.Silu,
                    [RB[bg]], [r_sgnw])
            bld.tt("vector", sgnw[:, s * 512:(s + 1) * 512],
                   sgnw[:, s * 512:(s + 1) * 512], normwb[:, :], ALU.mult,
                   [r_sgnw, r_normwb], [r_sgnw])
            bld.tt("vector", trash[:, 0:512], qkg[:, s * 512:(s + 1) * 512],
                   qkg[:, s * 512:(s + 1) * 512], ALU.mult, [r_qkg], [r_trash])
            bld.red(ssq_g[:, s * 4:(s + 1) * 4],
                    trash[:, 0:512].rearrange("p (h t) -> p h t", h=4),
                    ALU.add, [r_trash], [r_ssqg])
        bld.act(scl_g[:, :], ssq_g[:, :], AF.Sqrt, [r_ssqg], [r_sclg])
        bld.ts("vector", scl_g[:, :], scl_g[:, :], 1e-12, None, ALU.max, ALU.bypass,
               [r_sclg], [r_sclg])
        bld.op("vector", lambda h: h.reciprocal(scl_g[:, :], scl_g[:, :]),
               [r_sclg], [r_sclg])


    def emitC(g):
        for half in range(2):
            insts = []
            for cb in range(4):
                s4 = half * 4 + cb
                for hh in range(2):
                    insts.append((cb * 2 + hh, s4, hh))
            for i, s4, hh in insts:
                T = scant[i]
                kt = qkg[:, s4 * 512 + 256 + hh * 128: s4 * 512 + 256 + (hh + 1) * 128]
                vt = vg[:, s4 * 512 + hh * 256: s4 * 512 + (hh + 1) * 256]
                bcol = beta_g[:, s4 * 2 + hh: s4 * 2 + hh + 1]
                sqc = scl_g[:, s4 * 4 + hh: s4 * 4 + hh + 1]
                skc = scl_g[:, s4 * 4 + 2 + hh: s4 * 4 + 3 + hh]
                bld.ts("vector", T["dgq"][0][:, :], idn[:, :], sqc, None,
                       ALU.mult, ALU.bypass, [r_idn, r_sclg], [T["dgq"][1]])
                bld.ts("vector", T["dgk"][0][:, :], idn[:, :], skc, None,
                       ALU.mult, ALU.bypass, [r_idn, r_sclg], [T["dgk"][1]])
                bld.tt("vector", betasc[:, i:i + 1], bcol, skc, ALU.mult,
                       [r_betag, r_sclg], [r_betasc])
                bld.ts("vector", T["dgkb"][0][:, :], idn[:, :], betasc[:, i:i + 1],
                       None, ALU.mult, ALU.bypass, [r_idn, r_betasc], [T["dgkb"][1]])
                bld.ts("scalar", T["kn"][0][:, :], kt, skc, None, ALU.mult,
                       ALU.bypass, [r_qkg, r_sclg], [T["kn"][1]])
                bld.ts("scalar", T["Y"][0][:, 256:384], kt, betasc[:, i:i + 1],
                       None, ALU.mult, ALU.bypass, [r_qkg, r_betasc], [T["Y"][1]])
                bld.ts("scalar", T["Y"][0][:, 0:256], vt, bcol, None,
                       ALU.mult, ALU.bypass, [r_vg, r_betag], [T["Y"][1]])
            for i, s4, hh in insts:
                T = scant[i]
                qt = qkg[:, s4 * 512 + hh * 128: s4 * 512 + (hh + 1) * 128]
                kt = qkg[:, s4 * 512 + 256 + hh * 128: s4 * 512 + 256 + (hh + 1) * 128]
                transpose128(qt, r_qkg, T["qTn"][0][:, :], T["qTn"][1],
                             T["dgq"][0][:, :], T["dgq"][1],
                             "vector" if i % 2 == 0 else "scalar")
                transpose128(kt, r_qkg, T["kTn"][0][:, :], T["kTn"][1],
                             T["dgk"][0][:, :], T["dgk"][1],
                             "scalar" if i % 2 == 0 else "vector")
                transpose128(kt, r_qkg, T["kbTn"][0][:, :], T["kbTn"][1],
                             T["dgkb"][0][:, :], T["dgkb"][1],
                             "vector" if i % 2 == 0 else "scalar")
            for i, s4, hh in insts:
                T = scant[i]
                bk, bkr = rotbank()
                bld.mm(bkr, bk[:, 0:128], T["kbTn"][0][:, :], T["kTn"][0][:, :],
                       True, True, [T["kbTn"][1], T["kTn"][1]], first=True, last=True)
                bld.tt("vector", T["Pa"][0][:, :], bk[:, 0:128], MSL, ALU.mult,
                       [bkr, r_msk], [T["Pa"][1]])
                bk, bkr = rotbank()
                bld.mm(bkr, bk[:, 0:128], T["kTn"][0][:, :], T["kbTn"][0][:, :],
                       True, True, [T["kTn"][1], T["kbTn"][1]], first=True, last=True)
                bld.tt("vector", T["PTa"][0][:, :], bk[:, 0:128], MSU, ALU.mult,
                       [bkr, r_msk], [T["PTa"][1]])
                bld.stt("vector", T["TT"][0][:, :], T["PTa"][0][:, :], -1.0,
                        idn[:, :], ALU.mult, ALU.add, [T["PTa"][1], r_idn],
                        [T["TT"][1]])
            curs = {i: ("Pa", "PTa") for i, _, _ in insts}
            nxts = {i: ("Pb", "PTb") for i, _, _ in insts}
            for kk in range(1, 7):
                for i, s4, hh in insts:
                    T = scant[i]
                    pc, ptc = (T[curs[i][0]], T[curs[i][1]])
                    pn, ptn = (T[nxts[i][0]], T[nxts[i][1]])
                    bk, bkr = rotbank()
                    bld.mm(bkr, bk[:, 0:128], ptc[0][:, :], pc[0][:, :], True, True,
                           [ptc[1], pc[1]], first=True, last=True)
                    evac("vector" if i % 2 else "scalar", pn[0][:, :], bkr,
                         bk[:, 0:128], [pn[1]])
                    if kk < 6:
                        bk, bkr = rotbank()
                        bld.mm(bkr, bk[:, 0:128], pc[0][:, :], ptc[0][:, :], True,
                               True, [pc[1], ptc[1]], first=True, last=True)
                        evac("scalar" if i % 2 else "vector", ptn[0][:, :], bkr,
                             bk[:, 0:128], [ptn[1]])
                for i, s4, hh in insts:
                    T = scant[i]
                    pn = T[nxts[i][0]]
                    bk, bkr = rotbank()
                    bld.mm(bkr, bk[:, 0:128], pn[0][:, :], T["TT"][0][:, :], True,
                           True, [pn[1], T["TT"][1]], first=True, last=True)
                    bld.tt("vector", T["TT"][0][:, :], T["TT"][0][:, :],
                           bk[:, 0:128], ALU.add, [T["TT"][1], bkr], [T["TT"][1]])
                    curs[i], nxts[i] = nxts[i], curs[i]
            for i, s4, hh in insts:
                T = scant[i]
                bk, bkr = rotbank()
                bld.mm(bkr, bk[:, 0:128], T["Y"][0][:, 256:384], T["TT"][0][:, :],
                       True, True, [T["Y"][1], T["TT"][1]], first=True, last=True)
                evac("scalar", T["nW2T"][0][:, :], bkr, bk[:, 0:128],
                     [T["nW2T"][1]], scale=-1.0)
                bk, bkr = rotbank()
                bld.mm(bkr, bk[:, 0:256], T["TT"][0][:, :], T["Y"][0][:, 0:256],
                       True, True, [T["TT"][1], T["Y"][1]], first=True, last=True)
                evac("vector", T["W1"][0][:, :], bkr, bk[:, 0:256], [T["W1"][1]])
            for cb in range(4):
                s4 = half * 4 + cb
                UB = [(PB[5], RB[5]), (PB[6], RB[6])]
                OB = [(PB[3], RB[3]), (PB[7], RB[7])]
                for hh in range(2):
                    i = cb * 2 + hh
                    T = scant[i]
                    Ssl = S_bf[:, hh * 256:(hh + 1) * 256]
                    ub, ubr = UB[hh]
                    bld.mm(ubr, ub[:, 0:256], T["nW2T"][0][:, :], Ssl, True, True,
                           [T["nW2T"][1], r_Sbf], first=True, last=True)
                    bld.tt("vector", T["U"][0][:, :], ub[:, 0:256],
                           T["W1"][0][:, :], ALU.add, [ubr, T["W1"][1]],
                           [T["U"][1]])
                    bk, bkr = rotbank()
                    bld.mm(bkr, bk[:, 0:128], T["kTn"][0][:, :], T["qTn"][0][:, :],
                           True, True, [T["kTn"][1], T["qTn"][1]], first=True,
                           last=True)
                    bld.tt("vector", T["Mp"][0][:, :], bk[:, 0:128], MTI, ALU.mult,
                           [bkr, r_msk], [T["Mp"][1]])
                for hh in range(2):
                    i = cb * 2 + hh
                    T = scant[i]
                    Ssl = S_bf[:, hh * 256:(hh + 1) * 256]
                    ob, obr = OB[hh]
                    bld.mm(obr, ob[:, 0:256], T["qTn"][0][:, :], Ssl, True, False,
                           [T["qTn"][1], r_Sbf], first=True, last=False)
                    bld.mm(obr, ob[:, 0:256], T["Mp"][0][:, :], T["U"][0][:, :],
                           False, True, [T["Mp"][1], T["U"][1]], first=False,
                           last=True)
                for hh in range(2):
                    i = cb * 2 + hh
                    T = scant[i]
                    bld.mm(RB[4], PB[4][:, hh * 256:(hh + 1) * 256],
                           T["kn"][0][:, :], T["U"][0][:, :], hh == 0, hh == 1,
                           [T["kn"][1], T["U"][1]], first=(hh == 0),
                           last=(hh == 1))
                for hh in range(2):
                    i = cb * 2 + hh
                    ob, obr = OB[hh]
                    oc0 = 4 * hh
                    bld.act(trash[:, 256 * hh:256 * hh + 256], ob[:, 0:256],
                            AF.Square, [obr], [r_trash, r_orst],
                            accum=orst[:, oc0:oc0 + 1])
                    bld.act(orst[:, oc0 + 1:oc0 + 2], orst[:, oc0:oc0 + 1],
                            AF.Sqrt, [r_orst, r_kc], [r_orst],
                            scale=kconst[:, 2:3], bias=kconst[:, 0:1])
                    bld.op("vector", lambda h, a=orst[:, oc0 + 2:oc0 + 3],
                           b=orst[:, oc0 + 1:oc0 + 2]: h.reciprocal(a, b),
                           [r_orst], [r_orst])
                    bld.stt("vector",
                            gated_g[:, s4 * 512 + hh * 256: s4 * 512 + (hh + 1) * 256],
                            ob[:, 0:256], orst[:, oc0 + 2:oc0 + 3],
                            sgnw[:, s4 * 512 + hh * 256: s4 * 512 + (hh + 1) * 256],
                            ALU.mult, ALU.mult, [obr, r_orst, r_sgnw], [r_gatedg])
                bld.tt("vector", S_f32[:, :], S_f32[:, :], PB[4][:, 0:512], ALU.add,
                       [r_Sf, RB[4]], [r_Sf])
                bld.cp("vector", S_bf[:, :], S_f32[:, :], [r_Sf], [r_Sbf])
        bld.dma("sync", "gat",
                gat_in.ap().rearrange("(g s p) c -> g p s c", p=128, s=S8)[g],
                gated_g[:, :].rearrange("p (s c) -> p s c", c=512),
                [r_gatedg], [r_gin])
        if debug:
            bld.dma("gpsimd", "dbg",
                    dbg["d_gated"].ap().rearrange("(g s p) c -> g p s c", p=128,
                                                  s=S8)[g],
                    gated_g[:, :].rearrange("p (s c) -> p s c", c=512),
                    [r_gatedg], [Res()])
        if debug and g == 0:
            bld.dma("gpsimd", "dbg", dbg["d_hT"].ap(), hT[:, :], [r_hT], [Res()])
            bld.dma("gpsimd", "dbg", dbg["d_qkg"].ap(), qkg[:, :], [r_qkg], [Res()])


    wjob_i = [0]
    def emit_wjobs(k):
        while wjob_i[0] < len(wprep_jobs) and k > 0:
            wprep_jobs[wjob_i[0]]()
            wjob_i[0] += 1
            k -= 1
    if n_g0 > 0:
        emitA1(0)
        emitA2(0)
    for g in range(n_g0):
        emitB(g)
        if g + 1 < n_g0:
            emitA1(g + 1)
        emitC(g)
        emit_wjobs(8)
        if g + 1 < n_g0:
            emitA2(g + 1)
    emit_wjobs(99)
    bld.barrier()
    st_l0.close()

    if do_w:
        bld.op("gpsimd", lambda h: h.collective_compute(
            "AllGather", ALU.bypass,
            replica_groups=[[0, 1], [2, 3], [4, 5], [6, 7]],
            ins=[gat_in.ap()], outs=[gat_out.ap()]), [r_gin], [r_gout])

    x_g2, r_xg2 = sb("x_g2", [128, S8 * D], BF16, st_w)
    xo_b = [(x_g, r_xg), (x_g2, r_xg2)]
    ternT, r_ternT = sb("ternT", [128, DC * D], BF16, st_w)
    bld.dma("sync", "ttl", ternT[:, :].rearrange("p (c w) -> p c w", w=D),
            ternT_d.ap().rearrange("(c p) w -> p c w", p=128),
            [r_ternTd], (r_ternT,))
    wo_b, r_wob = sb("wo_b", [128, DC * D], BF16, st_w)
    bld.dma("gpsimd", "wol", wo_b[:, :].rearrange("p (c w) -> p c w", w=D),
            wo_in.ap().rearrange("(c p) w -> p c w", p=128), (), (r_wob,))
    bias1, r_bias1 = sb("bias1", [1, D], BF16, st_w)
    bld.dma("gpsimd", "bi", bias1[:, :], bias_in.ap(), (), (r_bias1,))
    wsc1, r_wsc1 = sb("wsc1", [1, 1], BF16, st_w)
    bld.dma("gpsimd", "wsc", wsc1[:, :], wscale_in.ap(), (), (r_wsc1,))
    biasb, r_biasb = sb("biasb", [128, D], F32, st_w)
    for half in range(2):
        bk, bkr = rotbank()
        bld.mm(bkr, bk[:, 0:512], onescol[:, :], bias1[:, half * 512:(half + 1) * 512],
               True, True, [r_onescol, r_bias1], first=True, last=True)
        evac("vector", biasb[:, half * 512:(half + 1) * 512], bkr, bk[:, 0:512],
             [r_biasb])
    ws_col, r_ws = sb("ws_col", [128, 1], F32, st_w)
    bk, bkr = rotbank()
    bld.mm(bkr, bk[:, 0:1], onescol[:, :], wsc1[:, :], True, True,
           [r_onescol, r_wsc1], first=True, last=True)
    evac("vector", ws_col[:, :], bkr, bk[:, 0:1], [r_ws])
    xn_f = [sb(f"xn_f{i}", [128, D], F32, st_w) for i in range(2)]
    tq1 = [sb(f"tq1_{i}", [128, D], F32, st_w) for i in range(2)]
    tq2_s = sb("tq2_s", [128, D], F32, st_w)
    tq2 = [tq2_s, tq2_s]
    xint = [sb(f"xint{i}", [128, D], BF16, st_w) for i in range(2)]
    xq32_s = sb("xq32_s", [128, D], I32, st_w)
    xq32 = [xq32_s, xq32_s]
    xiT_all, r_xiTa = sb("xiT_all", [128, S8 * 1024], BF16, st_w)
    g_all, r_gall = sb("g_all", [128, S8 * D], BF16, st_w)
    gsel0, r_gsel0 = sb("gsel0", [128, S8 * 512], BF16, st_w)
    gsel1, r_gsel1 = sb("gsel1", [128, S8 * 512], BF16, st_w)
    glo, r_glo = sb("glo", [128, S8 * 512], BF16, st_w)
    gT = [sb(f"gT{i}", [128, DC * 128], BF16, st_w) for i in range(2)]
    qcol, r_qcol = sb("qcol", [128, 64], F32, st_w)
    x1t = [sb(f"x1t{i}", [128, D], BF16, st_w) for i in range(2)]

    xo_r = x_own.ap().rearrange("(g s p) d -> g p s d", p=128, s=S8)
    go_r = gat_out.ap().rearrange("(r h g s p) c -> r h g p s c",
                                  p=128, s=S8, g=GW, h=2)
    out_r = out_d.ap().rearrange("(g s p) d -> g s p d", p=128, s=S8)

    MLO = pcsc[:, 0:1]
    MHI = pcsc[:, 1:2]
    ZMK = pcsc[:, 2:3]

    def emitWA1(wg, s):
        p = s % 2
        xo_g, r_xog = xo_b[wg % 2]
        bld.act(xn_f[p][0][:, :], xo_g[:, s * D:(s + 1) * D], AF.Identity,
                [r_xog, r_statwr, r_statwm], [xn_f[p][1]],
                scale=statw_r[:, wg * 8 + s:wg * 8 + s + 1],
                bias=statw_m[:, wg * 8 + s:wg * 8 + s + 1])
        bld.red(qcol[:, s:s + 1], xn_f[p][0][:, :], ALU.max,
                [xn_f[p][1]], [r_qcol], absval=True)

    def emitWAsc():
        bld.ts("vector", qcol[:, 8:16], qcol[:, 0:8], 1e-5, None,
               ALU.max, ALU.bypass, [r_qcol], [r_qcol])
        bld.op("vector", lambda h, a=qcol[:, 16:24], b=qcol[:, 8:16]:
               h.reciprocal(a, b), [r_qcol], [r_qcol])
        bld.ts("vector", qcol[:, 16:24], qcol[:, 16:24], 127.0, None,
               ALU.mult, ALU.bypass, [r_qcol], [r_qcol])
        bld.ts("vector", qcol[:, 24:32], qcol[:, 8:16], ws_col[:, :], None,
               ALU.mult, ALU.bypass, [r_qcol, r_ws], [r_qcol])
        bld.ts("vector", qcol[:, 24:32], qcol[:, 24:32], 1.0 / 127, None,
               ALU.mult, ALU.bypass, [r_qcol], [r_qcol])

    def emitWA2(wg, s):
        p = s % 2
        xo_g, r_xog = xo_b[wg % 2]
        i = wg * 8 + s
        bld.tt("vector", qcol[:, 32 + s:33 + s], statw_r[:, i:i + 1],
               qcol[:, 16 + s:17 + s], ALU.mult, [r_statwr, r_qcol], [r_qcol])
        bld.tt("vector", qcol[:, 40 + s:41 + s], statw_m[:, i:i + 1],
               qcol[:, 16 + s:17 + s], ALU.mult, [r_statwm, r_qcol], [r_qcol])
        bld.act(tq1[p][0][:, :], xo_g[:, s * D:(s + 1) * D], AF.Identity,
                [r_xog, r_qcol], [tq1[p][1]],
                scale=qcol[:, 32 + s:33 + s], bias=qcol[:, 40 + s:41 + s])
        bld.cp("vector", xq32[p][0][:, :], tq1[p][0][:, :], [tq1[p][1]], [xq32[p][1]])
        bld.ts("vector", xint[p][0][:, :], xq32[p][0][:, :], 127.0, -128.0, ALU.min,
               ALU.max, [xq32[p][1]], [xint[p][1]])
        for c in range(DC):
            transpose128(xint[p][0][:, c * 128:(c + 1) * 128], xint[p][1],
                         xiT_all[:, s * 1024 + c * 128: s * 1024 + (c + 1) * 128],
                         r_xiTa, eng=("vector" if c % 2 else "scalar"))

    def emitWB(wg, s):
        p = s % 2
        f0, f1 = (5, 6) if s % 2 == 0 else (7, 3)
        for c in range(DC):
            first, last = c == 0, c == DC - 1
            xs = xiT_all[:, s * 1024 + c * 128: s * 1024 + (c + 1) * 128]
            bld.mm(RB[f0], PB[f0][:, 0:512], xs, ternT[:, c * D:c * D + 512],
                   first, last, [r_xiTa, r_ternT], first=first, last=last)
            bld.mm(RB[f1], PB[f1][:, 0:512], xs, ternT[:, c * D + 512:(c + 1) * D],
                   first, last, [r_xiTa, r_ternT], first=first, last=last)
        bld.stt("vector", tq1[p][0][:, 0:512], PB[f0][:, 0:512],
                qcol[:, 24 + s:25 + s], biasb[:, 0:512], ALU.mult, ALU.add,
                [RB[f0], r_qcol, r_biasb], [tq1[p][1]])
        bld.stt("vector", tq1[p][0][:, 512:1024], PB[f1][:, 0:512],
                qcol[:, 24 + s:25 + s], biasb[:, 512:1024], ALU.mult, ALU.add,
                [RB[f1], r_qcol, r_biasb], [tq1[p][1]])
        bld.act(g_all[:, s * D:(s + 1) * D], tq1[p][0][:, :], AF.Silu,
                [tq1[p][1]], [r_gall])

    def emitWC(wg, s):
        p = s % 2
        for c in range(DC):
            src = gsel0 if c < 4 else gsel1
            sres = r_gsel0 if c < 4 else r_gsel1
            cc = c % 4
            transpose128(src[:, s * 512 + cc * 128: s * 512 + (cc + 1) * 128],
                         sres, gT[p][0][:, c * 128:(c + 1) * 128], gT[p][1],
                         eng=("vector" if c % 2 else "scalar"))
        w0b, w1b = (7, 3) if s % 2 == 0 else (5, 6)
        for c in range(DC):
            first, last = c == 0, c == DC - 1
            bld.mm(RB[w0b], PB[w0b][:, 0:512], gT[p][0][:, c * 128:(c + 1) * 128],
                   wo_b[:, c * D:c * D + 512], first, last,
                   [gT[p][1], r_wob], first=first, last=last)
            bld.mm(RB[w1b], PB[w1b][:, 0:512], gT[p][0][:, c * 128:(c + 1) * 128],
                   wo_b[:, c * D + 512:(c + 1) * D], first, last,
                   [gT[p][1], r_wob], first=first, last=last)
        ti = wg * 8 + s
        xo_g, r_xog = xo_b[wg % 2]
        bld.tt("vector", tq2[p][0][:, 0:512], PB[w0b][:, 0:512],
               g_all[:, s * D:s * D + 512], ALU.mult, [RB[w0b], r_gall],
               [tq2[p][1]])
        bld.tt("vector", x1t[p][0][:, 0:512], tq2[p][0][:, 0:512],
               xo_g[:, s * D:s * D + 512], ALU.add, [tq2[p][1], r_xog], [x1t[p][1]])
        bld.tt("vector", tq2[p][0][:, 512:1024], PB[w1b][:, 0:512],
               g_all[:, s * D + 512:(s + 1) * D], ALU.mult, [RB[w1b], r_gall],
               [tq2[p][1]])
        bld.tt("vector", x1t[p][0][:, 512:1024], tq2[p][0][:, 512:1024],
               xo_g[:, s * D + 512:(s + 1) * D], ALU.add, [tq2[p][1], r_xog],
               [x1t[p][1]])
        ln_stats(x1t[p][1], x1t[p][0][:, :], sum1_g[:, s:s + 1],
                 sq1_g[:, s:s + 1], r_sum1g, r_sq1g)
        bld.dma("sync", "x1s%d" % p, x1_dr[ti], x1t[p][0][:, :], [x1t[p][1]], [r_x1])

    def w_load_stats(wg):
        xo_g, r_xog = xo_b[wg % 2]
        xv = xo_g[:, :].rearrange("p (s d) -> p s d", d=D)
        bld.dma("gpsimd", "xg0", xv[:, 0:4], xo_r[wg][:, 0:4], [r_xog], [r_xog])
        bld.dma("gpsimd", "xg1", xv[:, 4:8], xo_r[wg][:, 4:8], [r_xog], [r_xog])
        for s in range(S8):
            ln_stats(r_xog, xo_g[:, s * D:(s + 1) * D], sums_g[:, s:s + 1],
                     sqs_g[:, s:s + 1])
        ln_rstd(sums_g[:, 0:S8], sqs_g[:, 0:S8], 1.0 / D,
                statw_r[:, wg * 8:(wg + 1) * 8], statw_m[:, wg * 8:(wg + 1) * 8],
                r_statwr, r_statwm, 8)

    if do_w:
        w_load_stats(0)
    for wg in range(GW if do_w else 0):
        xo_g, r_xog = xo_b[wg % 2]
        for s in range(S8):
            emitWA1(wg, s)
        emitWAsc()
        for s in range(S8):
            emitWA2(wg, s)
        for s in range(S8):
            emitWB(wg, s)
        if wg + 1 < GW:
            w_load_stats(wg + 1)
        for sl, gres, gsl in ((0, r_gsel0, gsel0), (1, r_gsel1, gsel1)):
            bld.dma("sync", "glo", glo[:, :].rearrange("p (s c) -> p s c", c=512),
                    go_r[sl, 0, wg], [r_glo, r_gout], [r_glo])
            bld.ts("vector", gsl[:, :], glo[:, :], MLO, None, ALU.mult, ALU.bypass,
                   [r_glo, r_pcsc], [gres])
            bld.dma("sync", "glo", glo[:, :].rearrange("p (s c) -> p s c", c=512),
                    go_r[sl, 1, wg], [r_glo, r_gout], [r_glo])
            bld.stt("vector", gsl[:, :], glo[:, :], MHI, gsl[:, :], ALU.mult, ALU.add,
                    [r_glo, r_pcsc, gres], [gres])
        for s in range(S8):
            emitWC(wg, s)
        ln_rstd(sum1_g[:, 0:S8], sq1_g[:, 0:S8], 1.0 / D,
                stat1_r[:, wg * 8:(wg + 1) * 8], stat1_m[:, wg * 8:(wg + 1) * 8],
                r_stat1r, r_stat1m, 8, rs=r_sum1g, rq=r_sq1g)
    if debug and do_w:
        bld.dma("gpsimd", "dbg", dbg["d_x1"].ap().rearrange("p (t d) -> p t d", d=D),
                x1_d.ap().rearrange("(t p) d -> p t d", p=128), [r_x1], [Res()])

    bld.barrier()

    st_w.close()
    st_lw.close()

    xn1, r_xn1 = sb("xn1", [128, D], BF16, st_l1)
    x1_st2 = [sb(f"x1_st{i}", [128, 4 * D], BF16, st_l1) for i in range(2)]
    xn1T, r_xn1T = sb("xn1T", [128, DC * TS], BF16, st_l1)
    hre_all, r_hre = sb("hre_all", [128, HALF], BF16, st_l1)
    cre, r_cre = sb("cre", [128, TS], F32, st_l1)
    cim, r_cim = sb("cim", [128, TS], F32, st_l1)
    zre, r_zre = sb("zre", [128, TS], F32, st_l1)
    zim, r_zim = sb("zim", [128, TS], F32, st_l1)
    tl1, r_tl1 = sb("tl1", [128, TS], F32, st_l1)
    tl2, r_tl2 = sb("tl2", [128, TS], F32, st_l1)
    eloc, r_eloc = sb("eloc", [128, 8], F32, st_l1)
    hidT, r_hidT = sb("hidT", [128, HB * TS], BF16, st_l1)
    wdt_s, r_wdts = sb("wdt_s", [128, HB * D], BF16, st_l1)
    wgsl, r_wgsl = sb("wgsl", [128, 2 * DC * 128], BF16, st_l1)
    wusl, r_wusl = sb("wusl", [128, 2 * DC * 128], BF16, st_l1)
    y_bf, r_ybf = sb("y_bf", [128, 4 * D], BF16, st_l1)
    g2_bf, r_g2bf = sb("g2_bf", [128, D], BF16, st_l1)
    out_f, r_outf = sb("out_f", [128, D], F32, st_l1)

    def xn1_load(st):
        x1_st, r_x1st = x1_st2[st % 2]
        bld.dma("sync", "x1l%d" % (st % 2),
                x1_st[:, :].rearrange("p (t d) -> p t d", d=D),
                x1_dr[st * 4:(st + 1) * 4].rearrange("t p d -> p t d"),
                [r_x1, r_x1st], [r_x1st])

    def xn1_transpose(st):
        x1_st, r_x1st = x1_st2[st % 2]
        for sub in range(4):
            ti = st * 4 + sub
            bld.act(xn1[:, :], x1_st[:, sub * D:(sub + 1) * D], AF.Identity,
                    [r_x1st, r_stat1r, r_stat1m], [r_xn1],
                    scale=stat1_r[:, ti:ti + 1], bias=stat1_m[:, ti:ti + 1])
            for c in range(DC):
                transpose128(xn1[:, c * 128:(c + 1) * 128], r_xn1,
                             xn1T[:, c * TS + sub * 128: c * TS + (sub + 1) * 128],
                             r_xn1T, eng=("vector" if (sub + c) % 2 else "scalar"))

    bld.op("vector", lambda h: h.memset(eloc[:, 2:4], 0.0), (), (r_eloc,))
    n_l1a = (NST if do_l1 else 0) if L1_SUB >= 2 else 0
    if n_l1a:
        xn1_load(0)
    for st in range(n_l1a):
        xn1_transpose(st)
        if st + 1 < n_l1a:
            xn1_load(st + 1)
        b0_, b1_ = (5, 6) if st % 2 == 0 else (7, 3)
        for c in range(DC):
            first, last = c == 0, c == DC - 1
            bld.mm(RB[b0_], PB[b0_][:, 0:512], bdt_re[:, c * 128:(c + 1) * 128],
                   xn1T[:, c * TS:(c + 1) * TS], first, last,
                   [r_bdtre, r_xn1T], first=first, last=last)
            bld.mm(RB[b1_], PB[b1_][:, 0:512], bdt_im[:, c * 128:(c + 1) * 128],
                   xn1T[:, c * TS:(c + 1) * TS], first, last,
                   [r_bdtim, r_xn1T], first=first, last=last)
        bld.tt("vector", tl1[:, :], PB[b0_][:, 0:512], pm_re[:, :], ALU.mult,
               [RB[b0_], r_pmre], [r_tl1])
        bld.tt("vector", tl2[:, :], PB[b1_][:, 0:512], pm_im[:, :], ALU.mult,
               [RB[b1_], r_pmim], [r_tl2])
        bld.tt("vector", cre[:, :], tl1[:, :], tl2[:, :], ALU.subtract,
               [r_tl1, r_tl2], [r_cre])
        bld.tt("vector", tl1[:, :], PB[b1_][:, 0:512], pm_re[:, :], ALU.mult,
               [RB[b1_], r_pmre], [r_tl1])
        bld.tt("vector", tl2[:, :], PB[b0_][:, 0:512], pm_im[:, :], ALU.mult,
               [RB[b0_], r_pmim], [r_tl2])
        bld.tt("vector", cim[:, :], tl1[:, :], tl2[:, :], ALU.add,
               [r_tl1, r_tl2], [r_cim])
        bld.op("vector", lambda h: h.tensor_tensor_scan(
            zre[:, :], ones512[:, :], cre[:, :], eloc[:, 2:3], ALU.mult, ALU.add),
            [r_ones, r_cre, r_eloc], [r_zre])
        bld.op("vector", lambda h: h.tensor_tensor_scan(
            zim[:, :], ones512[:, :], cim[:, :], eloc[:, 3:4], ALU.mult, ALU.add),
            [r_ones, r_cim, r_eloc], [r_zim])
        bld.tt("vector", tl1[:, :], zre[:, :], pp_re[:, :], ALU.mult,
               [r_zre, r_ppre], [r_tl1])
        bld.tt("vector", tl2[:, :], zim[:, :], pp_im[:, :], ALU.mult,
               [r_zim, r_ppim], [r_tl2])
        bld.tt("vector", hre_all[:, st * TS:(st + 1) * TS], tl1[:, :], tl2[:, :],
               ALU.subtract, [r_tl1, r_tl2], [r_hre])
        bld.tt("vector", scr[:, 0:1], pp_re[:, 511:512], zre[:, 511:512], ALU.mult,
               [r_ppre, r_zre], [r_scr])
        bld.tt("vector", scr[:, 1:2], pp_im[:, 511:512], zim[:, 511:512], ALU.mult,
               [r_ppim, r_zim], [r_scr])
        bld.tt("vector", eloc[:, 0:1], scr[:, 0:1], scr[:, 1:2], ALU.subtract,
               [r_scr], [r_eloc])
        bld.tt("vector", scr[:, 0:1], pp_re[:, 511:512], zim[:, 511:512], ALU.mult,
               [r_ppre, r_zim], [r_scr])
        bld.tt("vector", scr[:, 1:2], pp_im[:, 511:512], zre[:, 511:512], ALU.mult,
               [r_ppim, r_zre], [r_scr])
        bld.tt("vector", eloc[:, 1:2], scr[:, 0:1], scr[:, 1:2], ALU.add,
               [r_scr], [r_eloc])
        bld.tt("vector", scr[:, 0:1], are_c[:, :], eloc[:, 0:1], ALU.mult,
               [r_are, r_eloc], [r_scr])
        bld.tt("vector", scr[:, 1:2], aim_c[:, :], eloc[:, 1:2], ALU.mult,
               [r_aim, r_eloc], [r_scr])
        bld.tt("vector", scr[:, 2:3], are_c[:, :], eloc[:, 1:2], ALU.mult,
               [r_are, r_eloc], [r_scr])
        bld.tt("vector", scr[:, 3:4], aim_c[:, :], eloc[:, 0:1], ALU.mult,
               [r_aim, r_eloc], [r_scr])
        bld.tt("vector", eloc[:, 2:3], scr[:, 0:1], scr[:, 1:2], ALU.subtract,
               [r_scr], [r_eloc])
        bld.tt("vector", eloc[:, 3:4], scr[:, 2:3], scr[:, 3:4], ALU.add,
               [r_scr], [r_eloc])

    if not do_l1 or L1_SUB < 3:
        bld.barrier()
        _finalize(nc, bld, stack, st_l1)
        return nc
    if debug:
        bld.dma("gpsimd", "dbg", dbg["d_hre"].ap(), hre_all[:, :], [r_hre], [Res()])

    wgt_r = wgt_d.ap().rearrange("(m k p) h -> m p k h", k=DC, p=128)
    wut_r = wut_d.ap().rearrange("(m k p) h -> m p k h", k=DC, p=128)
    wdt_rr = wdt_d.ap().rearrange("(m p) d -> p m d", p=128)
    n_l1b = NST if L1_SUB >= 4 else 0
    if n_l1b:
        bld.dma("sync", "wdt", wdt_s[:, :].rearrange("p (m d) -> p m d", d=D),
                wdt_rr, [r_wdt], [r_wdts])
        xn1_load(0)
    for st in range(n_l1b):
        xn1_transpose(st)
        for m in range(HB):
            mb = m % 2
            bld.dma("sync", "wg%d" % mb,
                    wgsl[:, mb * DC * 128:(mb + 1) * DC * 128].rearrange(
                        "p (k h) -> p k h", h=128), wgt_r[m], [r_wgt], [r_wgsl])
            bld.dma("sync", "wu%d" % mb,
                    wusl[:, mb * DC * 128:(mb + 1) * DC * 128].rearrange(
                        "p (k h) -> p k h", h=128), wut_r[m], [r_wut], [r_wusl])
            g0_, g1_ = (5, 6) if m % 2 == 0 else (7, 3)
            for c in range(DC):
                first, last = c == 0, c == DC - 1
                bld.mm(RB[g0_], PB[g0_][:, 0:512],
                       wgsl[:, mb * DC * 128 + c * 128: mb * DC * 128 + (c + 1) * 128],
                       xn1T[:, c * TS:(c + 1) * TS], first, last,
                       [r_wgsl, r_xn1T], first=first, last=last)
                bld.mm(RB[g1_], PB[g1_][:, 0:512],
                       wusl[:, mb * DC * 128 + c * 128: mb * DC * 128 + (c + 1) * 128],
                       xn1T[:, c * TS:(c + 1) * TS], first, last,
                       [r_wusl, r_xn1T], first=first, last=last)
            bld.act(trash[:, 512 * (m % 2):512 * (m % 2) + 512], PB[g0_][:, 0:512],
                    AF.Silu, [RB[g0_], r_kc], [r_trash], scale=kconst[:, 3:4])
            bld.tt("vector", hidT[:, m * TS:(m + 1) * TS], PB[g1_][:, 0:512],
                   trash[:, 512 * (m % 2):512 * (m % 2) + 512], ALU.mult,
                   [RB[g1_], r_trash], [r_hidT])
        if st == 0:
            bld.dma("sync", "ein", e_in.ap(), eloc[:, 0:2], [r_eloc], [r_ein])
            bld.op("gpsimd", lambda h: h.collective_compute(
                "AllGather", ALU.bypass,
                replica_groups=[[0, 1], [2, 3], [4, 5], [6, 7]],
                ins=[e_in.ap()], outs=[e_out.ap()]), [r_ein], [r_eout])
            bld.dma("sync", "eout", scr2[:, 0:2], e_out.ap()[0:128, :], [r_eout], [r_scr2])
            bld.ts("vector", eloc[:, 4:5], scr2[:, 0:1], ZMK, None, ALU.mult, ALU.bypass,
                   [r_scr2, r_pcsc], [r_eloc])
            bld.ts("vector", eloc[:, 5:6], scr2[:, 1:2], ZMK, None, ALU.mult, ALU.bypass,
                   [r_scr2, r_pcsc], [r_eloc])
            bld.tt("vector", scr[:, 0:1], pp_re[:, 511:512], are_c[:, :], ALU.mult,
                   [r_ppre, r_are], [r_scr])
            bld.tt("vector", scr[:, 1:2], pp_im[:, 511:512], aim_c[:, :], ALU.mult,
                   [r_ppim, r_aim], [r_scr])
            bld.tt("vector", scr[:, 4:5], scr[:, 0:1], scr[:, 1:2], ALU.subtract,
                   [r_scr], [r_scr])
            bld.tt("vector", scr[:, 0:1], pp_re[:, 511:512], aim_c[:, :], ALU.mult,
                   [r_ppre, r_aim], [r_scr])
            bld.tt("vector", scr[:, 1:2], pp_im[:, 511:512], are_c[:, :], ALU.mult,
                   [r_ppim, r_are], [r_scr])
            bld.tt("vector", scr[:, 5:6], scr[:, 0:1], scr[:, 1:2], ALU.add,
                   [r_scr], [r_scr])
            bld.cp("vector", scr[:, 6:7], are_c[:, :], [r_are], [r_scr])
            bld.cp("vector", scr[:, 7:8], aim_c[:, :], [r_aim], [r_scr])
            for stc in range(NST):
                bld.tt("vector", scr2[:, 0:1], scr[:, 6:7], eloc[:, 4:5], ALU.mult,
                       [r_scr, r_eloc], [r_scr2])
                bld.tt("vector", scr2[:, 1:2], scr[:, 7:8], eloc[:, 5:6], ALU.mult,
                       [r_scr, r_eloc], [r_scr2])
                bld.tt("vector", scr2[:, 4:5], scr2[:, 0:1], scr2[:, 1:2], ALU.subtract,
                       [r_scr2], [r_scr2])
                bld.tt("vector", scr2[:, 0:1], scr[:, 6:7], eloc[:, 5:6], ALU.mult,
                       [r_scr, r_eloc], [r_scr2])
                bld.tt("vector", scr2[:, 1:2], scr[:, 7:8], eloc[:, 4:5], ALU.mult,
                       [r_scr, r_eloc], [r_scr2])
                bld.tt("vector", scr2[:, 5:6], scr2[:, 0:1], scr2[:, 1:2], ALU.add,
                       [r_scr2], [r_scr2])
                bld.ts("vector", scr2[:, 6:7], scr2[:, 5:6], -1.0, None, ALU.mult, ALU.bypass,
                       [r_scr2], [r_scr2])
                hsl = hre_all[:, stc * TS:(stc + 1) * TS]
                bld.stt("vector", hsl, pp_re[:, :], scr2[:, 4:5], hsl, ALU.mult, ALU.add,
                        [r_ppre, r_scr2, r_hre], [r_hre])
                bld.stt("vector", hsl, pp_im[:, :], scr2[:, 6:7], hsl, ALU.mult, ALU.add,
                        [r_ppim, r_scr2, r_hre], [r_hre])
                bld.tt("vector", scr2[:, 0:1], scr[:, 6:7], scr[:, 4:5], ALU.mult,
                       [r_scr], [r_scr2])
                bld.tt("vector", scr2[:, 1:2], scr[:, 7:8], scr[:, 5:6], ALU.mult,
                       [r_scr], [r_scr2])
                bld.tt("vector", scr2[:, 2:3], scr[:, 6:7], scr[:, 5:6], ALU.mult,
                       [r_scr], [r_scr2])
                bld.tt("vector", scr2[:, 3:4], scr[:, 7:8], scr[:, 4:5], ALU.mult,
                       [r_scr], [r_scr2])
                bld.tt("vector", scr[:, 6:7], scr2[:, 0:1], scr2[:, 1:2], ALU.subtract,
                       [r_scr2], [r_scr])
                bld.tt("vector", scr[:, 7:8], scr2[:, 2:3], scr2[:, 3:4], ALU.add,
                       [r_scr2], [r_scr])

        if st + 1 < n_l1b:
            xn1_load(st + 1)
        for sub in range(4):
            ti = st * 4 + sub
            d0_, d1_ = (5, 6) if sub % 2 == 0 else (7, 3)
            y0_, y1_ = (2, 4) if sub % 2 == 0 else (0, 1)
            bld.mm(RB[y0_], PB[y0_][:, 0:512],
                   hre_all[:, ti * 128:(ti + 1) * 128], ct_t[:, 0:512],
                   True, True, [r_hre, r_ct], first=True, last=True)
            bld.mm(RB[y1_], PB[y1_][:, 0:512],
                   hre_all[:, ti * 128:(ti + 1) * 128], ct_t[:, 512:1024],
                   True, True, [r_hre, r_ct], first=True, last=True)
            bld.cp("scalar", y_bf[:, sub * D:sub * D + 512], PB[y0_][:, 0:512],
                   [RB[y0_]], [r_ybf])
            bld.cp("scalar", y_bf[:, sub * D + 512:(sub + 1) * D], PB[y1_][:, 0:512],
                   [RB[y1_]], [r_ybf])
            for m in range(HB):
                first, last = m == 0, m == HB - 1
                lt = hidT[:, m * TS + sub * 128: m * TS + (sub + 1) * 128]
                bld.mm(RB[d0_], PB[d0_][:, 0:512], lt, wdt_s[:, m * D:m * D + 512],
                       first, last, [r_hidT, r_wdts], first=first, last=last)
                bld.mm(RB[d1_], PB[d1_][:, 0:512], lt, wdt_s[:, m * D + 512:(m + 1) * D],
                       first, last, [r_hidT, r_wdts], first=first, last=last)
            bld.act(g2_bf[:, 0:512], PB[d0_][:, 0:512], AF.Silu, [RB[d0_], r_kc],
                    [r_g2bf], scale=kconst[:, 4:5])
            bld.act(g2_bf[:, 512:1024], PB[d1_][:, 0:512], AF.Silu, [RB[d1_], r_kc],
                    [r_g2bf], scale=kconst[:, 4:5])
            bld.tt("vector", out_f[:, 0:512], y_bf[:, sub * D:sub * D + 512],
                   g2_bf[:, 0:512], ALU.mult, [r_ybf, r_g2bf], [r_outf])
            bld.tt("vector", out_f[:, 512:1024],
                   y_bf[:, sub * D + 512:(sub + 1) * D], g2_bf[:, 512:1024],
                   ALU.mult, [r_ybf, r_g2bf], [r_outf])
            x1_stc, r_x1stc = x1_st2[st % 2]
            bld.tt("vector", out_f[:, 0:512], out_f[:, 0:512],
                   x1_stc[:, sub * D:sub * D + 512],
                   ALU.add, [r_outf, r_x1stc], [r_outf])
            bld.tt("vector", out_f[:, 512:1024], out_f[:, 512:1024],
                   x1_stc[:, sub * D + 512:(sub + 1) * D],
                   ALU.add, [r_outf, r_x1stc], [r_outf])
            bld.dma("gpsimd", "out", out_r[st // 2, (st % 2) * 4 + sub], out_f[:, :],
                    [r_outf], [r_out])
        if debug and st == 0:
            bld.dma("gpsimd", "dbg", dbg["d_y"].ap(), y_bf[:, :], [r_ybf], [Res()])

    bld.barrier()
    _finalize(nc, bld, stack, st_l1)
    return nc


def _finalize(nc, bld, stack, st_l1):
    with nc.Block() as blk:
        @blk.tensor
        def _(h):
            for f in bld.q["tensor"]:
                f(h)

        @blk.vector
        def _(h):
            for f in bld.q["vector"]:
                f(h)

        @blk.scalar
        def _(h):
            for f in bld.q["scalar"]:
                f(h)

        @blk.gpsimd
        def _(h):
            for f in bld.q["gpsimd"]:
                f(h)

        @blk.sync
        def _(h):
            for f in bld.q["sync"]:
                f(h)

    st_l1.close()
    stack.close()


def host_inputs(inputs):
    x = np.ascontiguousarray(inputs["L0_attn_hidden_states"], dtype=np.float32)
    Wq = np.asarray(inputs["L0_attn_Wq"], np.float32)
    Wk = np.asarray(inputs["L0_attn_Wk"], np.float32)
    Wv = np.asarray(inputs["L0_attn_Wv"], np.float32)
    Wb = np.asarray(inputs["L0_attn_Wb"], np.float32)
    Wg = np.asarray(inputs["L0_attn_Wg"], np.float32)
    conv_w = np.asarray(inputs["L0_attn_conv_w"], np.float32)
    convw_r = np.ascontiguousarray(
        conv_w.reshape(DC, 128, 4).transpose(1, 0, 2).reshape(128, 32))
    masks = np.zeros((128, 3 * 128), np.float32)
    ii = np.arange(128)
    masks[:, 0:128] = (ii[None, :] < ii[:, None])
    masks[:, 128:256] = (ii[None, :] > ii[:, None])
    masks[:, 256:384] = (ii[None, :] >= ii[:, None])
    bf = ml_dtypes.bfloat16
    common = dict(
        convw=convw_r,
        wo=np.ascontiguousarray(np.asarray(inputs["L0_attn_Wo"], np.float32)),
        normw=np.tile(np.asarray(inputs["L0_attn_norm_w"], np.float32), 2)[None, :],
        packed=np.ascontiguousarray(
            np.asarray(inputs["L0_ffn_packed_weight"], np.int32).reshape(256, D)),
        wscale=np.asarray(inputs["L0_ffn_weight_scale"], np.float32).reshape(1, 1),
        bias=np.asarray(inputs["L0_ffn_bias"], np.float32).reshape(1, D),
        Ar=np.asarray(inputs["L1_attn_A_real"], np.float32).reshape(128, 1),
        Ai=np.asarray(inputs["L1_attn_A_imag"], np.float32).reshape(128, 1),
        Bmat=np.ascontiguousarray(np.asarray(inputs["L1_attn_B"], np.float32)),
        Cmat=np.ascontiguousarray(np.asarray(inputs["L1_attn_C"], np.float32)),
        wg=np.ascontiguousarray(np.asarray(inputs["L1_ffn_w_gate"], np.int32)),
        wu=np.ascontiguousarray(np.asarray(inputs["L1_ffn_w_up"], np.int32)),
        wd=np.ascontiguousarray(np.asarray(inputs["L1_ffn_w_down"], np.int32)),
        idn=np.eye(128).astype(bf),
        masks=masks.astype(bf),
        ones512=np.ones((128, 512)).astype(bf),
        onescol=np.ones((1, 128)).astype(bf),
        kconst=np.tile(np.array([1e-5, 0.5, 1.0 / 256, 1.0 / 127,
                                 1.0 / 127 / 127, 0.0], np.float32), (128, 1)),
    )
    in_maps = []
    for c in range(8):
        b, z = c >> 1, c & 1
        h0 = 2 * z
        wproj = np.concatenate([
            Wq[:, h0 * 128:(h0 + 2) * 128], Wk[:, h0 * 128:(h0 + 2) * 128],
            Wv[:, h0 * 256:(h0 + 2) * 256], Wg[:, h0 * 256:(h0 + 2) * 256],
            Wb[:, h0:h0 + 2]], axis=1)
        pc = np.zeros((128, 4), np.float32)
        pc[:, 0] = 1.0 if z == 0 else 0.0
        pc[:, 1] = 1.0 if z == 1 else 0.0
        pc[:, 2] = 1.0 if z == 1 else 0.0
        m = dict(common)
        m.update(
            x_full=x[b],
            x_own=np.ascontiguousarray(x[b, z * HALF:(z + 1) * HALF]),
            wproj=np.ascontiguousarray(wproj),
            pcsc=pc,
        )
        in_maps.append(m)
    return in_maps


def kernel(**inputs):
    in_maps = host_inputs(inputs)
    nc = build_nc(debug=DEBUG)
    res = run_bass_kernel_spmd(nc, in_maps, core_ids=list(range(8)))
    out = np.empty((B, L, D), np.float32)
    for c in range(8):
        b, z = c >> 1, c & 1
        out[b, z * HALF:(z + 1) * HALF] = res.results[c]["out"]
    return out


# version 22
# speedup vs baseline: 1.2044x; 1.0140x over previous
import numpy as np
import ml_dtypes
import concourse.bass as bass
import concourse.mybir as mybir
from concourse.bass_utils import run_bass_kernel_spmd

F32 = mybir.dt.float32
BF16 = mybir.dt.bfloat16
I32 = mybir.dt.int32
AF = mybir.ActivationFunctionType
ALU = mybir.AluOpType
AXX = mybir.AxisListType.X

B, L, D = 4, 8192, 1024
HALF = L // 2
DC = 8
G0 = 8
TG = 1024
S8 = 8
CH = 128
WCOL = 1538
GW = 4
NST = 8
TS = 512
HB = 24
CMAGIC = 12582912.0
ENGS = ("tensor", "vector", "scalar", "gpsimd", "sync")

DEBUG = False
L0_SUB = 9
SC_SUB = 9
L1_SUB = 9
L0_STAGE = 99


class Res:
    __slots__ = ("w", "r", "pend")
    def __init__(self):
        self.w = None
        self.r = []
        self.pend = []


class Bld:
    def __init__(self, nc, stack):
        self.nc = nc
        self.q = {e: [] for e in ENGS}
        self.scnt = {}
        self.waited = {e: {} for e in ENGS}
        self.esem = {}
        for e in ENGS:
            h = stack.enter_context(nc.semaphore("se_" + e))
            self.esem[e] = h
            self.scnt[id(h)] = 0
            self._hmap = getattr(self, "_hmap", {})
            self._hmap[id(h)] = h
        self._stack = stack
        self._chan = {}

    def _wait(self, eng, tok):
        if tok is None:
            return
        sem, val, peng = tok
        if peng == eng:
            pass
        if self.waited[eng].get(id(sem), 0) >= val:
            return
        self.waited[eng][id(sem)] = val
        self.q[eng].append(lambda h, s=sem, v=val: h.wait_ge(s, v))

    def _hazard(self, eng, reads, writes):
        toks = []
        for r in reads:
            toks.append(r.w)
        for w in writes:
            toks.append(w.w)
            toks.extend(w.r)
        for t in toks:
            self._wait(eng, t)

    def op(self, eng, fn, reads=(), writes=()):
        self._hazard(eng, reads, writes)
        sem = self.esem[eng]
        self.scnt[id(sem)] += 1
        cnt = self.scnt[id(sem)]
        def run(h, fn=fn, sem=sem):
            ins = fn(h)
            ins.then_inc(sem, 1)
        self.q[eng].append(run)
        tok = (sem, cnt, eng)
        for r in reads:
            r.r.append(tok)
        for w in writes:
            w.w = tok
            w.r = []
        return tok

    def dma(self, eng, ch, out_ap, in_ap, reads=(), writes=(), **kw):
        if ch not in self._chan:
            h = self._stack.enter_context(self.nc.semaphore("q_" + ch))
            self._chan[ch] = h
            self.scnt[id(h)] = 0
            self._hmap[id(h)] = h
        qsem = self._chan[ch]
        prev = self.scnt[id(qsem)]
        if prev:
            self._wait(eng, (qsem, prev, None))
        self._hazard(eng, reads, writes)
        self.scnt[id(qsem)] = prev + 16
        cnt = prev + 16
        def run(h, o=out_ap, i=in_ap, qs=qsem, kw=kw):
            h.dma_start(o, i, **kw).then_inc(qs, 16)
        self.q[eng].append(run)
        tok = (qsem, cnt, None)
        for r in reads:
            r.r.append(tok)
        for w in writes:
            w.w = tok
            w.r = []
        return tok

    def barrier(self):
        sems = [(self._hmap[k], v) for k, v in self.scnt.items()]
        for e in ENGS:
            for sem, val in sems:
                if val > 0:
                    self._wait(e, (sem, val, None))

    def cp(self, eng, out, in_, reads, writes):
        if eng == "scalar":
            return self.op(eng, lambda h: h.copy(out, in_), reads, writes)
        return self.op(eng, lambda h: h.tensor_copy(out, in_), reads, writes)

    def tt(self, eng, out, a, bb, alu, reads, writes):
        return self.op(eng, lambda h: h.tensor_tensor(out, a, bb, alu), reads, writes)

    def ts(self, eng, out, in0, s1, s2, op0, op1, reads, writes, accum=None):
        if eng == "scalar":
            assert op0 == ALU.mult and s2 is None and accum is None
            return self.op(eng, lambda h: h.activation(out, in0, AF.Identity,
                                                       scale=s1), reads, writes)
        if s2 is None and accum is None:
            return self.op(eng, lambda h: h.tensor_single_scalar(out, in0, s1, op0),
                           reads, writes)
        return self.op(eng, lambda h: h.tensor_scalar(out, in0, s1, s2, op0, op1,
                                                      accum_out=accum), reads, writes)

    def stt(self, eng, out, in0, sc, in1, op0, op1, reads, writes):
        return self.op(eng, lambda h: h.scalar_tensor_tensor(out, in0, sc, in1, op0, op1),
                       reads, writes)

    def act(self, out, in_, func, reads, writes, bias=0.0, scale=1.0, accum=None):
        return self.op("scalar",
                       lambda h: h.activation(out, in_, func, bias=bias, scale=scale,
                                              accum_out=accum), reads, writes)

    def red(self, out, in_, alu, reads, writes, absval=False):
        return self.op("vector",
                       lambda h: h.tensor_reduce(out, in_, AXX, alu,
                                                 apply_absolute_value=absval),
                       reads, writes)

    def mm(self, bank, out, lhsT, rhs, start, stop, reads, first, last):
        if first:
            self._hazard("tensor", (), (bank,))
            bank.pend = []
        self._hazard("tensor", reads, ())
        bank.pend.extend(reads)
        if not last:
            def run0(h, o=out, lt=lhsT, r=rhs, st=start, sp=stop):
                h.matmul(o, lt, r, start=st, stop=sp)
            self.q["tensor"].append(run0)
            return None
        sem = self.esem["tensor"]
        self.scnt[id(sem)] += 1
        cnt = self.scnt[id(sem)]
        def run(h, o=out, lt=lhsT, r=rhs, st=start, sp=stop, sem=sem):
            h.matmul(o, lt, r, start=st, stop=sp).then_inc(sem, 1)
        self.q["tensor"].append(run)
        tok = (sem, cnt, "tensor")
        for rr in bank.pend:
            rr.r.append(tok)
        bank.pend = []
        bank.w = tok
        bank.r = []
        return tok


def build_nc(debug=False, n_g0=G0, do_w=True, do_l1=True):
    global L0_STAGE, L0_SUB, SC_SUB, L1_SUB
    nc = bass.Bass("TRN2", target_bir_lowering=False)
    import contextlib
    stack = contextlib.ExitStack()

    d_in = {}
    def din(name, shape, dt):
        d_in[name] = nc.dram_tensor(name, shape, dt, kind="ExternalInput")
        return d_in[name]
    x_full = din("x_full", [L, D], F32)
    x_own = din("x_own", [HALF, D], F32)
    wproj = din("wproj", [D, WCOL], F32)
    wo_in = din("wo", [D, D], F32)
    convw_in = din("convw", [128, 32], F32)
    normw_in = din("normw", [1, 512], F32)
    packed_in = din("packed", [256, D], I32)
    wscale_in = din("wscale", [1, 1], F32)
    bias_in = din("bias", [1, D], F32)
    ar_in = din("Ar", [128, 1], F32)
    ai_in = din("Ai", [128, 1], F32)
    bmat_in = din("Bmat", [128, D], F32)
    cmat_in = din("Cmat", [D, 128], F32)
    wg_in = din("wg", [3072, D], I32)
    wu_in = din("wu", [3072, D], I32)
    wd_in = din("wd", [D, 3072], I32)
    idn_in = din("idn", [128, 128], BF16)
    masks_in = din("masks", [128, 3 * 128], BF16)
    ones512_in = din("ones512", [128, 512], BF16)
    onescol_in = din("onescol", [1, 128], BF16)
    pcsc_in = din("pcsc", [128, 4], F32)
    kconst_in = din("kconst", [128, 6], F32)

    out_d = nc.dram_tensor("out", [HALF, D], F32, kind="ExternalOutput")
    dbg = {}
    if debug:
        for nm, shp, dt in (("d_hT", [128, DC * TG], BF16),
                            ("d_qkg", [128, S8 * 512], BF16),
                            ("d_gated", [L, 512], BF16),
                            ("d_x1", [128, 32 * D], BF16),
                            ("d_g0", [128, D], BF16),
                            ("d_xint", [128, D], BF16),
                            ("d_ffnsb", [128, D], F32),
                            ("d_ffnraw", [128, D], F32),
                            ("d_biasb", [128, D], F32),
                            ("d_qcol", [128, 8], F32),
                            ("d_s0", [128, D], F32),
                            ("d_hre", [128, 4096], BF16),
                            ("d_y", [128, 4 * D], BF16),
                            ("d_sb", [128, 512], BF16)):
            dbg[nm] = nc.dram_tensor(nm, shp, dt, kind="ExternalOutput")

    gat_in = nc.dram_tensor("gat_in", [L, 512], BF16)
    gat_out = nc.dram_tensor("gat_out", [2 * L, 512], BF16)
    e_in = nc.dram_tensor("e_in", [128, 2], F32)
    e_out = nc.dram_tensor("e_out", [256, 2], F32)
    wgt_d = nc.dram_tensor("wgt_d", [HB * DC * 128, 128], BF16)
    wut_d = nc.dram_tensor("wut_d", [HB * DC * 128, 128], BF16)
    wdt_d = nc.dram_tensor("wdt_d", [HB * 128, D], BF16)
    r_gin, r_gout = Res(), Res()
    r_ein, r_eout = Res(), Res()
    r_wgt, r_wut, r_wdt = Res(), Res(), Res()
    r_out = Res()

    bld = Bld(nc, stack)
    st_pro = contextlib.ExitStack()
    st_lw = contextlib.ExitStack()
    st_l0 = contextlib.ExitStack()
    st_w = contextlib.ExitStack()
    st_l1 = contextlib.ExitStack()

    def sb(name, shape, dt, stk=None):
        h = (stk or stack).enter_context(nc.sbuf_tensor("s_" + name, shape, dt))
        return h, Res()

    def pb(name, cols=512):
        h = stack.enter_context(nc.psum_tensor(name, [128, cols], F32))
        return h, Res()

    idn, r_idn = sb("idn", [128, 128], BF16)
    msk, r_msk = sb("msk", [128, 3 * 128], BF16)
    ones512, r_ones = sb("ones512", [128, 512], BF16)
    onescol, r_onescol = sb("onescol", [1, 128], BF16)
    pcsc, r_pcsc = sb("pcsc", [128, 4], F32)
    kconst, r_kc = sb("kconst", [128, 6], F32)
    x1_d = nc.dram_tensor("x1_d", [HALF, D], BF16)
    r_x1 = Res()
    x1_dr = x1_d.ap().rearrange("(t p) d -> t p d", p=128)
    stat_r, r_statr = sb("stat_r", [128, 64], F32)
    stat_m, r_statm = sb("stat_m", [128, 64], F32)
    statw_r, r_statwr = sb("statw_r", [128, 32], F32)
    statw_m, r_statwm = sb("statw_m", [128, 32], F32)
    stat1_r, r_stat1r = sb("stat1_r", [128, 32], F32)
    stat1_m, r_stat1m = sb("stat1_m", [128, 32], F32)
    pm_re, r_pmre = sb("pm_re", [128, 512], F32)
    pm_im, r_pmim = sb("pm_im", [128, 512], F32)
    pp_re, r_ppre = sb("pp_re", [128, 512], F32)
    pp_im, r_ppim = sb("pp_im", [128, 512], F32)
    are_c, r_are = sb("are_c", [128, 1], F32)
    aim_c, r_aim = sb("aim_c", [128, 1], F32)
    ct_t, r_ct = sb("ct_t", [128, D], BF16)
    bdt_re, r_bdtre = sb("bdt_re", [128, DC * 128], BF16)
    bdt_im, r_bdtim = sb("bdt_im", [128, DC * 128], BF16)
    trash, r_trash = sb("trash", [128, 1024], BF16)
    sums_g, r_sumsg = sb("sums_g", [128, S8], F32)
    sqs_g, r_sqsg = sb("sqs_g", [128, S8], F32)
    sum1_g, r_sum1g = sb("sum1_g", [128, S8], F32)
    sq1_g, r_sq1g = sb("sq1_g", [128, S8], F32)
    wst = [sb(f"wst{i}", [128, 3072], BF16) for i in range(2)]
    wtt = [sb(f"wtt{i}", [128, 3072], BF16) for i in range(2)]
    scr, r_scr = sb("scr", [128, 16], F32)
    scr2, r_scr2 = sb("scr2", [128, 16], F32)

    pbank = []
    for i in range(8):
        pbank.append(pb(f"pb{i}"))
    PB = [p[0] for p in pbank]
    RB = [p[1] for p in pbank]

    MSL = msk[:, 0:128]
    MSU = msk[:, 128:256]
    MTI = msk[:, 256:384]

    rot_state = [0]
    def rotbank():
        i = rot_state[0]
        rot_state[0] = (i + 1) % 3
        return PB[i], RB[i]

    def evac(eng, out_ap, bank_res, psum_ap, writes, scale=None):
        if scale is None:
            return bld.cp(eng, out_ap, psum_ap, [bank_res], writes)
        return bld.act(out_ap, psum_ap, AF.Copy, [bank_res], writes, scale=scale)

    def transpose128(src_ap, src_res, dst_ap, dst_res, rhs_ap=None, rhs_res=None,
                     eng="vector"):
        bk, bkr = rotbank()
        r = [src_res] + ([rhs_res] if rhs_res is not None else [r_idn])
        bld.mm(bkr, bk[:, 0:128], src_ap, rhs_ap if rhs_ap is not None else idn[:, :],
               True, True, r, first=True, last=True)
        evac(eng, dst_ap, bkr, bk[:, 0:128], [dst_res])

    bld.dma("sync", "cst", idn[:, :], idn_in.ap(), (), (r_idn,))
    bld.dma("sync", "cst2", msk[:, :], masks_in.ap(), (), (r_msk,))
    bld.dma("sync", "cst3", ones512[:, :], ones512_in.ap(), (), (r_ones,))
    bld.dma("sync", "cst4", onescol[:, :], onescol_in.ap(), (), (r_onescol,))
    bld.dma("sync", "cst5", pcsc[:, :], pcsc_in.ap(), (), (r_pcsc,))
    bld.dma("sync", "cst6", kconst[:, :], kconst_in.ap(), (), (r_kc,))

    arc, r_arc = sb("arc", [128, 1], F32, st_pro)
    aic, r_aic = sb("aic", [128, 1], F32, st_pro)
    bld.dma("sync", "ar", arc[:, :], ar_in.ap(), (), (r_arc,))
    bld.dma("sync", "ai", aic[:, :], ai_in.ap(), (), (r_aic,))
    bmat, r_bmat = sb("bmat", [128, D], F32, st_pro)
    bld.dma("sync", "bm", bmat[:, :], bmat_in.ap(), (), (r_bmat,))
    cmat_b, r_cmatb = sb("cmat_b", [128, DC * 128], BF16, st_pro)
    bld.dma("gpsimd", "cm", cmat_b[:, :].rearrange("p (c w) -> p c w", w=128),
            cmat_in.ap().rearrange("(c p) w -> p c w", p=128), (), (r_cmatb,))

    pstage, r_pst = sb("pstage", [128, 2 * D], I32, st_pro)
    bld.dma("sync", "pk", pstage[:, :].rearrange("p (c w) -> p c w", w=D),
            packed_in.ap().rearrange("(c p) w -> p c w", p=128), (), (r_pst,))
    tern, r_tern = sb("tern", [128, DC * D], BF16, st_pro)
    ei, r_ei = sb("ei", [128, 2 * D], I32, st_pro)
    for sh in range(4):
        if sh == 0:
            bld.ts("vector", ei[:, :], pstage[:, :], 3, None,
                   ALU.bitwise_and, ALU.bypass, [r_pst], [r_ei])
        else:
            bld.ts("vector", ei[:, :], pstage[:, :], 2 * sh, 3,
                   ALU.logical_shift_right, ALU.bitwise_and, [r_pst], [r_ei])
        bld.ts("vector", tern[:, 2 * sh * D:(2 * sh + 2) * D], ei[:, :], -1.0, None,
               ALU.add, ALU.bypass, [r_ei], [r_tern])
    ternT_d = nc.dram_tensor("ternT_d", [D, D], BF16)
    r_ternTd = Res()
    tstg, r_tstg = sb("tstg", [128, D], BF16, st_pro)
    ternTd_r = ternT_d.ap().rearrange("(ic p) o -> ic p o", p=128)
    for ic in range(DC):
        for oc in range(DC):
            transpose128(tern[:, oc * D + ic * 128: oc * D + (ic + 1) * 128], r_tern,
                         tstg[:, oc * 128:(oc + 1) * 128], r_tstg,
                         eng=("vector" if (oc + ic) % 2 == 0 else "scalar"))
        bld.dma("sync", "ttd", ternTd_r[ic], tstg[:, :], [r_tstg], [r_ternTd])

    dre, r_dre = sb("dre", [128, 1], F32, st_pro)
    dim, r_dim = sb("dim", [128, 1], F32, st_pro)
    nre, r_nre = sb("nre", [128, 1], F32, st_pro)
    nim, r_nim = sb("nim", [128, 1], F32, st_pro)
    ire, r_ire = sb("ire", [128, 1], F32, st_pro)
    iim, r_iim = sb("iim", [128, 1], F32, st_pro)
    tmp1, r_tmp1 = sb("tmp1", [128, 4], F32, st_pro)
    bld.ts("vector", dre[:, :], arc[:, :], -0.005, 1.0, ALU.mult, ALU.add,
           [r_arc], [r_dre])
    bld.ts("vector", dim[:, :], aic[:, :], -0.005, None, ALU.mult, ALU.bypass,
           [r_aic], [r_dim])
    bld.ts("vector", nre[:, :], arc[:, :], 0.005, 1.0, ALU.mult, ALU.add,
           [r_arc], [r_nre])
    bld.ts("vector", nim[:, :], aic[:, :], 0.005, None, ALU.mult, ALU.bypass,
           [r_aic], [r_nim])
    bld.tt("vector", tmp1[:, 0:1], dre[:, :], dre[:, :], ALU.mult, [r_dre], [r_tmp1])
    bld.stt("vector", tmp1[:, 1:2], dim[:, :], 1.0, dim[:, :], ALU.mult, ALU.mult,
            [r_dim], [r_tmp1])
    bld.tt("vector", tmp1[:, 0:1], tmp1[:, 0:1], tmp1[:, 1:2], ALU.add,
           [r_tmp1], [r_tmp1])
    bld.op("vector", lambda h: h.reciprocal(tmp1[:, 2:3], tmp1[:, 0:1]),
           [r_tmp1], [r_tmp1])
    bld.tt("vector", ire[:, :], dre[:, :], tmp1[:, 2:3], ALU.mult,
           [r_dre, r_tmp1], [r_ire])
    bld.stt("vector", iim[:, :], dim[:, :], -1.0, tmp1[:, 2:3], ALU.mult, ALU.mult,
            [r_dim, r_tmp1], [r_iim])
    bld.tt("vector", tmp1[:, 0:1], nre[:, :], ire[:, :], ALU.mult, [r_nre, r_ire], [r_tmp1])
    bld.tt("vector", tmp1[:, 1:2], nim[:, :], iim[:, :], ALU.mult, [r_nim, r_iim], [r_tmp1])
    bld.tt("vector", are_c[:, :], tmp1[:, 0:1], tmp1[:, 1:2], ALU.subtract,
           [r_tmp1], [r_are])
    bld.tt("vector", tmp1[:, 0:1], nre[:, :], iim[:, :], ALU.mult, [r_nre, r_iim], [r_tmp1])
    bld.tt("vector", tmp1[:, 1:2], nim[:, :], ire[:, :], ALU.mult, [r_nim, r_ire], [r_tmp1])
    bld.tt("vector", aim_c[:, :], tmp1[:, 0:1], tmp1[:, 1:2], ALU.add,
           [r_tmp1], [r_aim])
    iar, r_iar = sb("iar", [128, 1], F32, st_pro)
    iai, r_iai = sb("iai", [128, 1], F32, st_pro)
    bld.tt("vector", tmp1[:, 0:1], are_c[:, :], are_c[:, :], ALU.mult, [r_are], [r_tmp1])
    bld.stt("vector", tmp1[:, 1:2], aim_c[:, :], 1.0, aim_c[:, :], ALU.mult, ALU.mult,
            [r_aim], [r_tmp1])
    bld.tt("vector", tmp1[:, 0:1], tmp1[:, 0:1], tmp1[:, 1:2], ALU.add, [r_tmp1], [r_tmp1])
    bld.op("vector", lambda h: h.reciprocal(tmp1[:, 2:3], tmp1[:, 0:1]), [r_tmp1], [r_tmp1])
    bld.tt("vector", iar[:, :], are_c[:, :], tmp1[:, 2:3], ALU.mult, [r_are, r_tmp1], [r_iar])
    bld.stt("vector", iai[:, :], aim_c[:, :], -1.0, tmp1[:, 2:3], ALU.mult, ALU.mult,
            [r_aim, r_tmp1], [r_iai])

    def build_powers(pre, pim, r_pre, r_pim, base_re_ap, base_im_ap, r_bre, r_bim):
        cwre, r_cwre = sb(f"cw_{pre.name}r", [128, 1], F32, st_pro)
        cwim, r_cwim = sb(f"cw_{pre.name}i", [128, 1], F32, st_pro)
        bld.cp("vector", cwre[:, :], base_re_ap, [r_bre], [r_cwre])
        bld.cp("vector", cwim[:, :], base_im_ap, [r_bim], [r_cwim])
        bld.op("vector", lambda h: h.memset(pre[:, 0:1], 1.0), (), (r_pre,))
        bld.op("vector", lambda h: h.memset(pim[:, 0:1], 0.0), (), (r_pim,))
        w = 1
        tre, r_tre = sb(f"tp_{pre.name}r", [128, 256], F32, st_pro)
        tim, r_tim = sb(f"tp_{pre.name}i", [128, 256], F32, st_pro)
        while w < 512:
            wl = min(w, 512 - w)
            bld.ts("vector", tre[:, 0:wl], pim[:, 0:wl], cwim[:, :], None,
                   ALU.mult, ALU.bypass, [r_pim, r_cwim], [r_tre])
            bld.stt("vector", pre[:, w:w + wl], pre[:, 0:wl], cwre[:, :], tre[:, 0:wl],
                    ALU.mult, ALU.subtract, [r_pre, r_cwre, r_tre], [r_pre])
            bld.ts("vector", tim[:, 0:wl], pim[:, 0:wl], cwre[:, :], None,
                   ALU.mult, ALU.bypass, [r_pim, r_cwre], [r_tim])
            bld.stt("vector", pim[:, w:w + wl], pre[:, 0:wl], cwim[:, :], tim[:, 0:wl],
                    ALU.mult, ALU.add, [r_pre, r_cwim, r_tim], [r_pim])
            bld.tt("vector", tre[:, 0:1], cwre[:, :], cwre[:, :], ALU.mult,
                   [r_cwre], [r_tre])
            bld.tt("vector", tim[:, 0:1], cwim[:, :], cwim[:, :], ALU.mult,
                   [r_cwim], [r_tim])
            bld.tt("vector", tre[:, 1:2], cwre[:, :], cwim[:, :], ALU.mult,
                   [r_cwre, r_cwim], [r_tre])
            bld.tt("vector", cwre[:, :], tre[:, 0:1], tim[:, 0:1], ALU.subtract,
                   [r_tre, r_tim], [r_cwre])
            bld.ts("vector", cwim[:, :], tre[:, 1:2], 2.0, None, ALU.mult, ALU.bypass,
                   [r_tre], [r_cwim])
            w *= 2

    build_powers(pm_re, pm_im, r_pmre, r_pmim, iar[:, :], iai[:, :], r_iar, r_iai)
    build_powers(pp_re, pp_im, r_ppre, r_ppim, are_c[:, :], aim_c[:, :], r_are, r_aim)

    sdre, r_sdre = sb("sdre", [128, 1], F32, st_pro)
    sdim, r_sdim = sb("sdim", [128, 1], F32, st_pro)
    bld.ts("vector", sdre[:, :], ire[:, :], 0.1, None, ALU.mult, ALU.bypass,
           [r_ire], [r_sdre])
    bld.ts("vector", sdim[:, :], iim[:, :], 0.1, None, ALU.mult, ALU.bypass,
           [r_iim], [r_sdim])
    bre_t, r_bret = sb("bre_t", [128, D], BF16, st_pro)
    bim_t, r_bimt = sb("bim_t", [128, D], BF16, st_pro)
    bld.ts("vector", bre_t[:, :], bmat[:, :], sdre[:, :], None, ALU.mult, ALU.bypass,
           [r_bmat, r_sdre], [r_bret])
    bld.ts("vector", bim_t[:, :], bmat[:, :], sdim[:, :], None, ALU.mult, ALU.bypass,
           [r_bmat, r_sdim], [r_bimt])
    for c in range(DC):
        transpose128(bre_t[:, c * 128:(c + 1) * 128], r_bret,
                     bdt_re[:, c * 128:(c + 1) * 128], r_bdtre, eng="vector")
        transpose128(bim_t[:, c * 128:(c + 1) * 128], r_bimt,
                     bdt_im[:, c * 128:(c + 1) * 128], r_bdtim, eng="scalar")
    for c in range(DC):
        transpose128(cmat_b[:, c * 128:(c + 1) * 128], r_cmatb,
                     ct_t[:, c * 128:(c + 1) * 128], r_ct,
                     eng=("vector" if c % 2 == 0 else "scalar"))

    wprep_jobs = []
    for (src, dst, rdst) in ((wg_in, wgt_d, r_wgt), (wu_in, wut_d, r_wut)):
        src_r = src.ap().rearrange("(m p) d -> m p d", p=128)
        dst_r = dst.ap().rearrange("(m k p) h -> m p k h", k=DC, p=128)
        for m in range(HB):
            def job(m=m, src_r=src_r, dst_r=dst_r, rdst=rdst):
                w0, rw0 = wst[m % 2]
                w1, rw1 = wtt[m % 2]
                bld.dma("gpsimd", "wl1%d" % (m % 2), w0[:, 0:D], src_r[m],
                        [rw0], [rw0])
                for kc in range(DC):
                    transpose128(w0[:, kc * 128:(kc + 1) * 128], rw0,
                                 w1[:, kc * 128:(kc + 1) * 128], rw1,
                                 eng=("vector" if kc % 2 == 0 else "scalar"))
                bld.dma("sync", "ws1%d" % (m % 2), dst_r[m],
                        w1[:, 0:DC * 128].rearrange("p (k h) -> p k h", h=128),
                        [rw1], [rdst])
            wprep_jobs.append(job)
    wdt_r = wdt_d.ap().rearrange("(m p) d -> p m d", p=128)
    for rt in range(DC):
        def job(rt=rt):
            w0, rw0 = wst[rt % 2]
            w1, rw1 = wtt[rt % 2]
            bld.dma("gpsimd", "wl1%d" % (rt % 2), w0[:, 0:3072],
                    wd_in.ap().rearrange("(r p) h -> r p h", p=128)[rt],
                    [rw0], [rw0])
            for m in range(HB):
                transpose128(w0[:, m * 128:(m + 1) * 128], rw0,
                             w1[:, m * 128:(m + 1) * 128], rw1,
                             eng=("vector" if m % 2 == 0 else "scalar"))
            bld.dma("sync", "ws1%d" % (rt % 2), wdt_r[:, :, rt * 128:(rt + 1) * 128],
                    w1[:, 0:HB * 128].rearrange("p (m c) -> p m c", c=128),
                    [rw1], [r_wdt])
        wprep_jobs.append(job)

    st_pro.close()
    bld.barrier()

    x_g, r_xg = sb("x_g", [128, S8 * D], BF16, st_lw)
    wpj_b, r_wpjb = sb("wpj_b", [128, DC * WCOL], BF16, st_l0)
    bld.dma("gpsimd", "wpj", wpj_b[:, :].rearrange("p (c w) -> p c w", w=WCOL),
            wproj.ap().rearrange("(c p) w -> p c w", p=128), (), (r_wpjb,))
    convw, r_convw = sb("convw", [128, 32], F32, st_l0)
    bld.dma("sync", "cvw", convw[:, :], convw_in.ap(), (), (r_convw,))
    dconv, r_dconv = sb("dconv", [128, 32 * 128], BF16, st_l0)
    for cc in range(DC):
        for tap in range(4):
            bld.ts("vector", dconv[:, (cc * 4 + tap) * 128:(cc * 4 + tap + 1) * 128],
                   idn[:, :], convw[:, cc * 4 + tap:cc * 4 + tap + 1], None,
                   ALU.mult, ALU.bypass, [r_idn, r_convw], [r_dconv])
    normw1, r_normw1 = sb("normw1", [1, 512], BF16, st_l0)
    bld.dma("gpsimd", "nw", normw1[:, :], normw_in.ap(), (), (r_normw1,))
    normwb, r_normwb = sb("normwb", [128, 512], BF16, st_l0)
    bk, bkr = rotbank()
    bld.mm(bkr, bk[:, 0:512], onescol[:, :], normw1[:, :], True, True,
           [r_onescol, r_normw1], first=True, last=True)
    evac("vector", normwb[:, :], bkr, bk[:, 0:512], [r_normwb])
    xnT, r_xnT = sb("xnT", [128, DC * (TG + 3)], BF16, st_l0)
    hT, r_hT = sb("hT", [128, DC * TG], BF16, st_l0)
    qkg, r_qkg = sb("qkg", [128, S8 * 512], BF16, st_l0)
    vg, r_vg = sb("vg", [128, S8 * 512], BF16, st_l0)
    sgnw, r_sgnw = sb("sgnw", [128, S8 * 512], BF16, st_l0)
    beta_g, r_betag = sb("beta_g", [128, S8 * 2], F32, st_l0)
    ssq_g, r_ssqg = sb("ssq_g", [128, S8 * 4], F32, st_l0)
    scl_g, r_sclg = sb("scl_g", [128, S8 * 4], F32, st_l0)
    gated_g, r_gatedg = sb("gated_g", [128, S8 * 512], BF16, st_l0)
    xn_bf, r_xnbf = sb("xn_bf", [128, D], BF16, st_l0)
    NI = 8
    scant = []
    for i in range(NI):
        d = {}
        for nm, w in (("dgq", 128), ("dgk", 128), ("dgkb", 128), ("qTn", 128),
                      ("kTn", 128), ("kbTn", 128), ("kn", 128), ("Y", 384),
                      ("TT", 128), ("Pa", 128), ("PTa", 128), ("Pb", 128),
                      ("PTb", 128), ("nW2T", 128), ("W1", 256), ("U", 256),
                      ("Mp", 128)):
            d[nm] = sb(f"sc{nm}_{i}", [128, w], BF16, st_l0)
        scant.append(d)
    S_f32, r_Sf = sb("S_f32", [128, 512], F32, st_l0)
    S_bf, r_Sbf = sb("S_bf", [128, 512], BF16, st_l0)
    betasc, r_betasc = sb("betasc", [128, 8], F32, st_l0)
    orst, r_orst = sb("orst", [128, 8], F32, st_l0)

    bld.op("vector", lambda h: h.memset(S_f32[:, :], 0.0), (), (r_Sf,))
    bld.op("vector", lambda h: h.memset(S_bf[:, :], 0.0), (), (r_Sbf,))
    bld.op("vector", lambda h: h.memset(
        xnT[:, :].rearrange("p (c w) -> p c w", w=TG + 3)[:, :, 0:3], 0.0),
        (), (r_xnT,))

    x_fr = x_full.ap().rearrange("(g s p) d -> g p s d", p=128, s=S8)

    def ln_stats(x_res, x_ap_sub, scol, sqcol, rs=None, rq=None):
        bld.act(trash[:, 0:1024], x_ap_sub, AF.Identity, [x_res],
                [r_trash, rs or r_sumsg], accum=scol)
        bld.act(trash[:, 0:1024], x_ap_sub, AF.Square, [x_res],
                [r_trash, rq or r_sqsg], accum=sqcol)

    def ln_rstd(scol_ap, sqcol_ap, n_inv, rst_ap, mrs_ap, r_rst, r_mrs, ncols,
                rs=None, rq=None):
        m = scr[:, 0:ncols]
        v = scr[:, 8:8 + ncols] if ncols <= 8 else None
        bld.ts("vector", m, scol_ap, n_inv, None, ALU.mult, ALU.bypass,
               [rs or r_sumsg], [r_scr])
        bld.ts("vector", scr2[:, 0:ncols], sqcol_ap, n_inv, None, ALU.mult, ALU.bypass,
               [rq or r_sqsg], [r_scr2])
        bld.stt("vector", v, m, -1.0, m, ALU.mult, ALU.mult, [r_scr], [r_scr])
        bld.tt("vector", v, scr2[:, 0:ncols], v, ALU.add, [r_scr2, r_scr], [r_scr])
        bld.act(scr2[:, 0:ncols], v, AF.Sqrt, [r_scr, r_kc], [r_scr2],
                bias=kconst[:, 0:1])
        bld.op("vector", lambda h, o=rst_ap, i=scr2[:, 0:ncols]: h.reciprocal(o, i),
               [r_scr2], [r_rst])
        bld.ts("vector", scr2[:, 8:8 + ncols], v, 1e-5, None, ALU.add, ALU.bypass,
               [r_scr], [r_scr2])
        bld.tt("vector", v, rst_ap, rst_ap, ALU.mult, [r_rst], [r_scr])
        bld.tt("vector", v, v, scr2[:, 8:8 + ncols], ALU.mult, [r_scr, r_scr2], [r_scr])
        bld.ts("vector", v, v, -0.5, 1.5, ALU.mult, ALU.add, [r_scr], [r_scr])
        bld.tt("vector", rst_ap, rst_ap, v, ALU.mult, [r_rst, r_scr], [r_rst])
        bld.stt("vector", mrs_ap, m, -1.0, rst_ap, ALU.mult, ALU.mult,
                [r_scr, r_rst], [r_mrs])

    def emitA1(g):
        xv = x_g[:, :].rearrange("p (s d) -> p s d", d=D)
        bld.dma("gpsimd", "xg", xv[:, 0:4], x_fr[g][:, 0:4], [r_xg], [r_xg])
        bld.dma("gpsimd", "xgb", xv[:, 4:8], x_fr[g][:, 4:8], [r_xg], [r_xg])
        for s in range(S8):
            ln_stats(r_xg, x_g[:, s * D:(s + 1) * D], sums_g[:, s:s + 1],
                     sqs_g[:, s:s + 1])
        ln_rstd(sums_g[:, 0:S8], sqs_g[:, 0:S8], 1.0 / D,
                stat_r[:, g * 8:(g + 1) * 8], stat_m[:, g * 8:(g + 1) * 8],
                r_statr, r_statm, 8)

    def emitA2(g, jobs=0):

        if g > 0:
            xr3 = xnT[:, :].rearrange("p (c w) -> p c w", w=TG + 3)
            bld.cp("vector", xr3[:, :, 0:3], xr3[:, :, TG:TG + 3], [r_xnT], [r_xnT])
        for s in range(S8):
            bld.act(xn_bf[:, 0:D], x_g[:, s * D:(s + 1) * D], AF.Identity,
                    [r_xg, r_statr, r_statm], [r_xnbf],
                    scale=stat_r[:, g * 8 + s:g * 8 + s + 1],
                    bias=stat_m[:, g * 8 + s:g * 8 + s + 1])
            for c in range(DC):
                transpose128(xn_bf[:, c * 128:(c + 1) * 128], r_xnbf,
                             xnT[:, c * (TG + 3) + 3 + s * 128:
                                 c * (TG + 3) + 3 + (s + 1) * 128], r_xnT,
                             eng=("vector" if (s + c) % 2 else "scalar"))
            if jobs:
                emit_wjobs(1)
        for c in range(DC):
            src = xnT[:, c * (TG + 3):(c + 1) * (TG + 3)]
            for nh in range(2):
                bk, bkr = rotbank()
                for tap in range(4):
                    bld.mm(bkr, bk[:, 0:512],
                           dconv[:, (c * 4 + tap) * 128:(c * 4 + tap + 1) * 128],
                           src[:, nh * 512 + tap: nh * 512 + tap + 512],
                           tap == 0, tap == 3, [r_dconv, r_xnT],
                           first=(tap == 0), last=(tap == 3))
                bld.act(hT[:, c * TG + nh * 512: c * TG + (nh + 1) * 512],
                        bk[:, 0:512], AF.Silu, [bkr], [r_hT])


    def emitB(g):
        for s in range(S8):
            if s % 2 == 0:
                bq, bk_, bv, bg = 5, 6, 7, 3
            else:
                bq, bk_, bv, bg = 0, 1, 2, 4
            for kc in range(DC):
                lt = hT[:, kc * TG + s * 128: kc * TG + (s + 1) * 128]
                w0 = kc * WCOL
                first, last = kc == 0, kc == DC - 1
                bld.mm(RB[bq], PB[bq][:, 0:256], lt, wpj_b[:, w0:w0 + 256],
                       first, False, [r_hT, r_wpjb], first=first, last=False)
                bld.mm(RB[bq], PB[bq][:, 256:258], lt, wpj_b[:, w0 + 1536:w0 + 1538],
                       False, last, [r_hT, r_wpjb], first=False, last=last)
                bld.mm(RB[bk_], PB[bk_][:, 0:256], lt, wpj_b[:, w0 + 256:w0 + 512],
                       first, last, [r_hT, r_wpjb], first=first, last=last)
                bld.mm(RB[bv], PB[bv][:, 0:512], lt, wpj_b[:, w0 + 512:w0 + 1024],
                       first, last, [r_hT, r_wpjb], first=first, last=last)
                bld.mm(RB[bg], PB[bg][:, 0:512], lt, wpj_b[:, w0 + 1024:w0 + 1536],
                       first, last, [r_hT, r_wpjb], first=first, last=last)
            bld.cp("vector", qkg[:, s * 512:s * 512 + 256], PB[bq][:, 0:256],
                   [RB[bq]], [r_qkg])
            bld.cp("vector", scr[:, 10:12], PB[bq][:, 256:258], [RB[bq]], [r_scr])
            bld.act(scr[:, 8:10], scr[:, 10:12], AF.Tanh, [r_scr, r_kc],
                    [r_scr], scale=kconst[:, 1:2])
            bld.ts("vector", beta_g[:, s * 2:s * 2 + 2], scr[:, 8:10], 0.5, 0.5,
                   ALU.mult, ALU.add, [r_scr], [r_betag])
            bld.cp("vector", qkg[:, s * 512 + 256:s * 512 + 512], PB[bk_][:, 0:256],
                   [RB[bk_]], [r_qkg])
            bld.cp("scalar", vg[:, s * 512:(s + 1) * 512], PB[bv][:, 0:512],
                   [RB[bv]], [r_vg])
            bld.act(sgnw[:, s * 512:(s + 1) * 512], PB[bg][:, 0:512], AF.Silu,
                    [RB[bg]], [r_sgnw])
            bld.tt("vector", sgnw[:, s * 512:(s + 1) * 512],
                   sgnw[:, s * 512:(s + 1) * 512], normwb[:, :], ALU.mult,
                   [r_sgnw, r_normwb], [r_sgnw])
            bld.tt("vector", trash[:, 0:512], qkg[:, s * 512:(s + 1) * 512],
                   qkg[:, s * 512:(s + 1) * 512], ALU.mult, [r_qkg], [r_trash])
            bld.red(ssq_g[:, s * 4:(s + 1) * 4],
                    trash[:, 0:512].rearrange("p (h t) -> p h t", h=4),
                    ALU.add, [r_trash], [r_ssqg])
        bld.act(scl_g[:, :], ssq_g[:, :], AF.Sqrt, [r_ssqg], [r_sclg])
        bld.ts("vector", scl_g[:, :], scl_g[:, :], 1e-12, None, ALU.max, ALU.bypass,
               [r_sclg], [r_sclg])
        bld.op("vector", lambda h: h.reciprocal(scl_g[:, :], scl_g[:, :]),
               [r_sclg], [r_sclg])


    def emitC(g):
        for half in range(2):
            insts = []
            for cb in range(4):
                s4 = half * 4 + cb
                for hh in range(2):
                    insts.append((cb * 2 + hh, s4, hh))
            for i, s4, hh in insts:
                T = scant[i]
                kt = qkg[:, s4 * 512 + 256 + hh * 128: s4 * 512 + 256 + (hh + 1) * 128]
                vt = vg[:, s4 * 512 + hh * 256: s4 * 512 + (hh + 1) * 256]
                bcol = beta_g[:, s4 * 2 + hh: s4 * 2 + hh + 1]
                sqc = scl_g[:, s4 * 4 + hh: s4 * 4 + hh + 1]
                skc = scl_g[:, s4 * 4 + 2 + hh: s4 * 4 + 3 + hh]
                bld.ts("vector", T["dgq"][0][:, :], idn[:, :], sqc, None,
                       ALU.mult, ALU.bypass, [r_idn, r_sclg], [T["dgq"][1]])
                bld.ts("vector", T["dgk"][0][:, :], idn[:, :], skc, None,
                       ALU.mult, ALU.bypass, [r_idn, r_sclg], [T["dgk"][1]])
                bld.tt("vector", betasc[:, i:i + 1], bcol, skc, ALU.mult,
                       [r_betag, r_sclg], [r_betasc])
                bld.ts("vector", T["dgkb"][0][:, :], idn[:, :], betasc[:, i:i + 1],
                       None, ALU.mult, ALU.bypass, [r_idn, r_betasc], [T["dgkb"][1]])
                bld.ts("scalar", T["kn"][0][:, :], kt, skc, None, ALU.mult,
                       ALU.bypass, [r_qkg, r_sclg], [T["kn"][1]])
                bld.ts("scalar", T["Y"][0][:, 256:384], kt, betasc[:, i:i + 1],
                       None, ALU.mult, ALU.bypass, [r_qkg, r_betasc], [T["Y"][1]])
                bld.ts("scalar", T["Y"][0][:, 0:256], vt, bcol, None,
                       ALU.mult, ALU.bypass, [r_vg, r_betag], [T["Y"][1]])
            for i, s4, hh in insts:
                T = scant[i]
                qt = qkg[:, s4 * 512 + hh * 128: s4 * 512 + (hh + 1) * 128]
                kt = qkg[:, s4 * 512 + 256 + hh * 128: s4 * 512 + 256 + (hh + 1) * 128]
                transpose128(qt, r_qkg, T["qTn"][0][:, :], T["qTn"][1],
                             T["dgq"][0][:, :], T["dgq"][1],
                             "vector" if i % 2 == 0 else "scalar")
                transpose128(kt, r_qkg, T["kTn"][0][:, :], T["kTn"][1],
                             T["dgk"][0][:, :], T["dgk"][1],
                             "scalar" if i % 2 == 0 else "vector")
                transpose128(kt, r_qkg, T["kbTn"][0][:, :], T["kbTn"][1],
                             T["dgkb"][0][:, :], T["dgkb"][1],
                             "vector" if i % 2 == 0 else "scalar")
            for i, s4, hh in insts:
                T = scant[i]
                bk, bkr = rotbank()
                bld.mm(bkr, bk[:, 0:128], T["kbTn"][0][:, :], T["kTn"][0][:, :],
                       True, True, [T["kbTn"][1], T["kTn"][1]], first=True, last=True)
                bld.tt("vector", T["Pa"][0][:, :], bk[:, 0:128], MSL, ALU.mult,
                       [bkr, r_msk], [T["Pa"][1]])
                bk, bkr = rotbank()
                bld.mm(bkr, bk[:, 0:128], T["kTn"][0][:, :], T["kbTn"][0][:, :],
                       True, True, [T["kTn"][1], T["kbTn"][1]], first=True, last=True)
                bld.tt("vector", T["PTa"][0][:, :], bk[:, 0:128], MSU, ALU.mult,
                       [bkr, r_msk], [T["PTa"][1]])
                bld.stt("vector", T["TT"][0][:, :], T["PTa"][0][:, :], -1.0,
                        idn[:, :], ALU.mult, ALU.add, [T["PTa"][1], r_idn],
                        [T["TT"][1]])
            curs = {i: ("Pa", "PTa") for i, _, _ in insts}
            nxts = {i: ("Pb", "PTb") for i, _, _ in insts}
            for kk in range(1, 7):
                for i, s4, hh in insts:
                    T = scant[i]
                    pc, ptc = (T[curs[i][0]], T[curs[i][1]])
                    pn, ptn = (T[nxts[i][0]], T[nxts[i][1]])
                    bk, bkr = rotbank()
                    bld.mm(bkr, bk[:, 0:128], ptc[0][:, :], pc[0][:, :], True, True,
                           [ptc[1], pc[1]], first=True, last=True)
                    evac("vector" if i % 2 else "scalar", pn[0][:, :], bkr,
                         bk[:, 0:128], [pn[1]])
                    if kk < 6:
                        bk, bkr = rotbank()
                        bld.mm(bkr, bk[:, 0:128], pc[0][:, :], ptc[0][:, :], True,
                               True, [pc[1], ptc[1]], first=True, last=True)
                        evac("scalar" if i % 2 else "vector", ptn[0][:, :], bkr,
                             bk[:, 0:128], [ptn[1]])
                for i, s4, hh in insts:
                    T = scant[i]
                    pn = T[nxts[i][0]]
                    bk, bkr = rotbank()
                    bld.mm(bkr, bk[:, 0:128], pn[0][:, :], T["TT"][0][:, :], True,
                           True, [pn[1], T["TT"][1]], first=True, last=True)
                    bld.tt("vector", T["TT"][0][:, :], T["TT"][0][:, :],
                           bk[:, 0:128], ALU.add, [T["TT"][1], bkr], [T["TT"][1]])
                    curs[i], nxts[i] = nxts[i], curs[i]
            for i, s4, hh in insts:
                T = scant[i]
                bk, bkr = rotbank()
                bld.mm(bkr, bk[:, 0:128], T["Y"][0][:, 256:384], T["TT"][0][:, :],
                       True, True, [T["Y"][1], T["TT"][1]], first=True, last=True)
                evac("scalar", T["nW2T"][0][:, :], bkr, bk[:, 0:128],
                     [T["nW2T"][1]], scale=-1.0)
                bk, bkr = rotbank()
                bld.mm(bkr, bk[:, 0:256], T["TT"][0][:, :], T["Y"][0][:, 0:256],
                       True, True, [T["TT"][1], T["Y"][1]], first=True, last=True)
                evac("vector", T["W1"][0][:, :], bkr, bk[:, 0:256], [T["W1"][1]])
            for cb in range(4):
                s4 = half * 4 + cb
                UB = [(PB[5], RB[5]), (PB[6], RB[6])]
                OB = [(PB[3], RB[3]), (PB[7], RB[7])]
                for hh in range(2):
                    i = cb * 2 + hh
                    T = scant[i]
                    Ssl = S_bf[:, hh * 256:(hh + 1) * 256]
                    ub, ubr = UB[hh]
                    bld.mm(ubr, ub[:, 0:256], T["nW2T"][0][:, :], Ssl, True, True,
                           [T["nW2T"][1], r_Sbf], first=True, last=True)
                    bld.tt("vector", T["U"][0][:, :], ub[:, 0:256],
                           T["W1"][0][:, :], ALU.add, [ubr, T["W1"][1]],
                           [T["U"][1]])
                    bk, bkr = rotbank()
                    bld.mm(bkr, bk[:, 0:128], T["kTn"][0][:, :], T["qTn"][0][:, :],
                           True, True, [T["kTn"][1], T["qTn"][1]], first=True,
                           last=True)
                    bld.tt("vector", T["Mp"][0][:, :], bk[:, 0:128], MTI, ALU.mult,
                           [bkr, r_msk], [T["Mp"][1]])
                for hh in range(2):
                    i = cb * 2 + hh
                    T = scant[i]
                    Ssl = S_bf[:, hh * 256:(hh + 1) * 256]
                    ob, obr = OB[hh]
                    bld.mm(obr, ob[:, 0:256], T["qTn"][0][:, :], Ssl, True, False,
                           [T["qTn"][1], r_Sbf], first=True, last=False)
                    bld.mm(obr, ob[:, 0:256], T["Mp"][0][:, :], T["U"][0][:, :],
                           False, True, [T["Mp"][1], T["U"][1]], first=False,
                           last=True)
                for hh in range(2):
                    i = cb * 2 + hh
                    T = scant[i]
                    bld.mm(RB[4], PB[4][:, hh * 256:(hh + 1) * 256],
                           T["kn"][0][:, :], T["U"][0][:, :], hh == 0, hh == 1,
                           [T["kn"][1], T["U"][1]], first=(hh == 0),
                           last=(hh == 1))
                for hh in range(2):
                    i = cb * 2 + hh
                    ob, obr = OB[hh]
                    oc0 = 4 * hh
                    bld.act(trash[:, 256 * hh:256 * hh + 256], ob[:, 0:256],
                            AF.Square, [obr], [r_trash, r_orst],
                            accum=orst[:, oc0:oc0 + 1])
                    bld.act(orst[:, oc0 + 1:oc0 + 2], orst[:, oc0:oc0 + 1],
                            AF.Sqrt, [r_orst, r_kc], [r_orst],
                            scale=kconst[:, 2:3], bias=kconst[:, 0:1])
                    bld.op("vector", lambda h, a=orst[:, oc0 + 2:oc0 + 3],
                           b=orst[:, oc0 + 1:oc0 + 2]: h.reciprocal(a, b),
                           [r_orst], [r_orst])
                    bld.stt("vector",
                            gated_g[:, s4 * 512 + hh * 256: s4 * 512 + (hh + 1) * 256],
                            ob[:, 0:256], orst[:, oc0 + 2:oc0 + 3],
                            sgnw[:, s4 * 512 + hh * 256: s4 * 512 + (hh + 1) * 256],
                            ALU.mult, ALU.mult, [obr, r_orst, r_sgnw], [r_gatedg])
                bld.tt("vector", S_f32[:, :], S_f32[:, :], PB[4][:, 0:512], ALU.add,
                       [r_Sf, RB[4]], [r_Sf])
                bld.cp("vector", S_bf[:, :], S_f32[:, :], [r_Sf], [r_Sbf])
        bld.dma("sync", "gat",
                gat_in.ap().rearrange("(g s p) c -> g p s c", p=128, s=S8)[g],
                gated_g[:, :].rearrange("p (s c) -> p s c", c=512),
                [r_gatedg], [r_gin])
        if debug:
            bld.dma("gpsimd", "dbg",
                    dbg["d_gated"].ap().rearrange("(g s p) c -> g p s c", p=128,
                                                  s=S8)[g],
                    gated_g[:, :].rearrange("p (s c) -> p s c", c=512),
                    [r_gatedg], [Res()])
        if debug and g == 0:
            bld.dma("gpsimd", "dbg", dbg["d_hT"].ap(), hT[:, :], [r_hT], [Res()])
            bld.dma("gpsimd", "dbg", dbg["d_qkg"].ap(), qkg[:, :], [r_qkg], [Res()])


    wjob_i = [0]
    def emit_wjobs(k):
        while wjob_i[0] < len(wprep_jobs) and k > 0:
            wprep_jobs[wjob_i[0]]()
            wjob_i[0] += 1
            k -= 1
    if n_g0 > 0:
        emitA1(0)
        emitA2(0)
    for g in range(n_g0):
        emitB(g)
        if g + 1 < n_g0:
            emitA1(g + 1)
        emitC(g)
        if g + 1 < n_g0:
            emitA2(g + 1, jobs=1)
        else:
            emit_wjobs(8)
    emit_wjobs(99)
    bld.barrier()
    st_l0.close()

    if do_w:
        bld.op("gpsimd", lambda h: h.collective_compute(
            "AllGather", ALU.bypass,
            replica_groups=[[0, 1], [2, 3], [4, 5], [6, 7]],
            ins=[gat_in.ap()], outs=[gat_out.ap()]), [r_gin], [r_gout])

    x_g2, r_xg2 = sb("x_g2", [128, S8 * D], BF16, st_w)
    xo_b = [(x_g, r_xg), (x_g2, r_xg2)]
    ternT, r_ternT = sb("ternT", [128, DC * D], BF16, st_w)
    bld.dma("sync", "ttl", ternT[:, :].rearrange("p (c w) -> p c w", w=D),
            ternT_d.ap().rearrange("(c p) w -> p c w", p=128),
            [r_ternTd], (r_ternT,))
    wo_b, r_wob = sb("wo_b", [128, DC * D], BF16, st_w)
    bld.dma("gpsimd", "wol", wo_b[:, :].rearrange("p (c w) -> p c w", w=D),
            wo_in.ap().rearrange("(c p) w -> p c w", p=128), (), (r_wob,))
    bias1, r_bias1 = sb("bias1", [1, D], BF16, st_w)
    bld.dma("gpsimd", "bi", bias1[:, :], bias_in.ap(), (), (r_bias1,))
    wsc1, r_wsc1 = sb("wsc1", [1, 1], BF16, st_w)
    bld.dma("gpsimd", "wsc", wsc1[:, :], wscale_in.ap(), (), (r_wsc1,))
    biasb, r_biasb = sb("biasb", [128, D], F32, st_w)
    for half in range(2):
        bk, bkr = rotbank()
        bld.mm(bkr, bk[:, 0:512], onescol[:, :], bias1[:, half * 512:(half + 1) * 512],
               True, True, [r_onescol, r_bias1], first=True, last=True)
        evac("vector", biasb[:, half * 512:(half + 1) * 512], bkr, bk[:, 0:512],
             [r_biasb])
    ws_col, r_ws = sb("ws_col", [128, 1], F32, st_w)
    bk, bkr = rotbank()
    bld.mm(bkr, bk[:, 0:1], onescol[:, :], wsc1[:, :], True, True,
           [r_onescol, r_wsc1], first=True, last=True)
    evac("vector", ws_col[:, :], bkr, bk[:, 0:1], [r_ws])
    xn_f = [sb(f"xn_f{i}", [128, D], F32, st_w) for i in range(2)]
    tq1 = [sb(f"tq1_{i}", [128, D], F32, st_w) for i in range(2)]
    tq2_s = sb("tq2_s", [128, D], F32, st_w)
    tq2 = [tq2_s, tq2_s]
    xint = [sb(f"xint{i}", [128, D], BF16, st_w) for i in range(2)]
    xq32_s = sb("xq32_s", [128, D], I32, st_w)
    xq32 = [xq32_s, xq32_s]
    xiT_all, r_xiTa = sb("xiT_all", [128, S8 * 1024], BF16, st_w)
    g_all, r_gall = sb("g_all", [128, S8 * D], BF16, st_w)
    gsel0, r_gsel0 = sb("gsel0", [128, S8 * 512], BF16, st_w)
    gsel1, r_gsel1 = sb("gsel1", [128, S8 * 512], BF16, st_w)
    glo, r_glo = sb("glo", [128, S8 * 512], BF16, st_w)
    gT = [sb(f"gT{i}", [128, DC * 128], BF16, st_w) for i in range(2)]
    qcol, r_qcol = sb("qcol", [128, 64], F32, st_w)
    x1t = [sb(f"x1t{i}", [128, D], BF16, st_w) for i in range(2)]

    xo_r = x_own.ap().rearrange("(g s p) d -> g p s d", p=128, s=S8)
    go_r = gat_out.ap().rearrange("(r h g s p) c -> r h g p s c",
                                  p=128, s=S8, g=GW, h=2)
    out_r = out_d.ap().rearrange("(g s p) d -> g s p d", p=128, s=S8)

    MLO = pcsc[:, 0:1]
    MHI = pcsc[:, 1:2]
    ZMK = pcsc[:, 2:3]

    def emitWA1(wg, s):
        p = s % 2
        xo_g, r_xog = xo_b[wg % 2]
        bld.act(xn_f[p][0][:, :], xo_g[:, s * D:(s + 1) * D], AF.Identity,
                [r_xog, r_statwr, r_statwm], [xn_f[p][1]],
                scale=statw_r[:, wg * 8 + s:wg * 8 + s + 1],
                bias=statw_m[:, wg * 8 + s:wg * 8 + s + 1])
        bld.red(qcol[:, s:s + 1], xn_f[p][0][:, :], ALU.max,
                [xn_f[p][1]], [r_qcol], absval=True)

    def emitWAsc():
        bld.ts("vector", qcol[:, 8:16], qcol[:, 0:8], 1e-5, None,
               ALU.max, ALU.bypass, [r_qcol], [r_qcol])
        bld.op("vector", lambda h, a=qcol[:, 16:24], b=qcol[:, 8:16]:
               h.reciprocal(a, b), [r_qcol], [r_qcol])
        bld.ts("vector", qcol[:, 16:24], qcol[:, 16:24], 127.0, None,
               ALU.mult, ALU.bypass, [r_qcol], [r_qcol])
        bld.ts("vector", qcol[:, 24:32], qcol[:, 8:16], ws_col[:, :], None,
               ALU.mult, ALU.bypass, [r_qcol, r_ws], [r_qcol])
        bld.ts("vector", qcol[:, 24:32], qcol[:, 24:32], 1.0 / 127, None,
               ALU.mult, ALU.bypass, [r_qcol], [r_qcol])

    def emitWA2(wg, s):
        p = s % 2
        xo_g, r_xog = xo_b[wg % 2]
        i = wg * 8 + s
        bld.tt("vector", qcol[:, 32 + s:33 + s], statw_r[:, i:i + 1],
               qcol[:, 16 + s:17 + s], ALU.mult, [r_statwr, r_qcol], [r_qcol])
        bld.tt("vector", qcol[:, 40 + s:41 + s], statw_m[:, i:i + 1],
               qcol[:, 16 + s:17 + s], ALU.mult, [r_statwm, r_qcol], [r_qcol])
        bld.act(tq1[p][0][:, :], xo_g[:, s * D:(s + 1) * D], AF.Identity,
                [r_xog, r_qcol], [tq1[p][1]],
                scale=qcol[:, 32 + s:33 + s], bias=qcol[:, 40 + s:41 + s])
        bld.cp("vector", xq32[p][0][:, :], tq1[p][0][:, :], [tq1[p][1]], [xq32[p][1]])
        bld.ts("vector", xint[p][0][:, :], xq32[p][0][:, :], 127.0, -128.0, ALU.min,
               ALU.max, [xq32[p][1]], [xint[p][1]])
        for c in range(DC):
            transpose128(xint[p][0][:, c * 128:(c + 1) * 128], xint[p][1],
                         xiT_all[:, s * 1024 + c * 128: s * 1024 + (c + 1) * 128],
                         r_xiTa, eng=("vector" if c % 2 else "scalar"))

    def emitWB(wg, s):
        p = s % 2
        f0, f1 = (5, 6) if s % 2 == 0 else (7, 3)
        for c in range(DC):
            first, last = c == 0, c == DC - 1
            xs = xiT_all[:, s * 1024 + c * 128: s * 1024 + (c + 1) * 128]
            bld.mm(RB[f0], PB[f0][:, 0:512], xs, ternT[:, c * D:c * D + 512],
                   first, last, [r_xiTa, r_ternT], first=first, last=last)
            bld.mm(RB[f1], PB[f1][:, 0:512], xs, ternT[:, c * D + 512:(c + 1) * D],
                   first, last, [r_xiTa, r_ternT], first=first, last=last)
        bld.stt("vector", tq1[p][0][:, 0:512], PB[f0][:, 0:512],
                qcol[:, 24 + s:25 + s], biasb[:, 0:512], ALU.mult, ALU.add,
                [RB[f0], r_qcol, r_biasb], [tq1[p][1]])
        bld.stt("vector", tq1[p][0][:, 512:1024], PB[f1][:, 0:512],
                qcol[:, 24 + s:25 + s], biasb[:, 512:1024], ALU.mult, ALU.add,
                [RB[f1], r_qcol, r_biasb], [tq1[p][1]])
        bld.act(g_all[:, s * D:(s + 1) * D], tq1[p][0][:, :], AF.Silu,
                [tq1[p][1]], [r_gall])

    def emitWC(wg, s):
        p = s % 2
        for c in range(DC):
            src = gsel0 if c < 4 else gsel1
            sres = r_gsel0 if c < 4 else r_gsel1
            cc = c % 4
            transpose128(src[:, s * 512 + cc * 128: s * 512 + (cc + 1) * 128],
                         sres, gT[p][0][:, c * 128:(c + 1) * 128], gT[p][1],
                         eng=("vector" if c % 2 else "scalar"))
        w0b, w1b = (7, 3) if s % 2 == 0 else (5, 6)
        for c in range(DC):
            first, last = c == 0, c == DC - 1
            bld.mm(RB[w0b], PB[w0b][:, 0:512], gT[p][0][:, c * 128:(c + 1) * 128],
                   wo_b[:, c * D:c * D + 512], first, last,
                   [gT[p][1], r_wob], first=first, last=last)
            bld.mm(RB[w1b], PB[w1b][:, 0:512], gT[p][0][:, c * 128:(c + 1) * 128],
                   wo_b[:, c * D + 512:(c + 1) * D], first, last,
                   [gT[p][1], r_wob], first=first, last=last)
        ti = wg * 8 + s
        xo_g, r_xog = xo_b[wg % 2]
        bld.tt("vector", tq2[p][0][:, 0:512], PB[w0b][:, 0:512],
               g_all[:, s * D:s * D + 512], ALU.mult, [RB[w0b], r_gall],
               [tq2[p][1]])
        bld.tt("vector", x1t[p][0][:, 0:512], tq2[p][0][:, 0:512],
               xo_g[:, s * D:s * D + 512], ALU.add, [tq2[p][1], r_xog], [x1t[p][1]])
        bld.tt("vector", tq2[p][0][:, 512:1024], PB[w1b][:, 0:512],
               g_all[:, s * D + 512:(s + 1) * D], ALU.mult, [RB[w1b], r_gall],
               [tq2[p][1]])
        bld.tt("vector", x1t[p][0][:, 512:1024], tq2[p][0][:, 512:1024],
               xo_g[:, s * D + 512:(s + 1) * D], ALU.add, [tq2[p][1], r_xog],
               [x1t[p][1]])
        ln_stats(x1t[p][1], x1t[p][0][:, :], sum1_g[:, s:s + 1],
                 sq1_g[:, s:s + 1], r_sum1g, r_sq1g)
        bld.dma("sync", "x1s%d" % p, x1_dr[ti], x1t[p][0][:, :], [x1t[p][1]], [r_x1])

    def w_load_stats(wg):
        xo_g, r_xog = xo_b[wg % 2]
        xv = xo_g[:, :].rearrange("p (s d) -> p s d", d=D)
        bld.dma("gpsimd", "xg0", xv[:, 0:4], xo_r[wg][:, 0:4], [r_xog], [r_xog])
        bld.dma("gpsimd", "xg1", xv[:, 4:8], xo_r[wg][:, 4:8], [r_xog], [r_xog])
        for s in range(S8):
            ln_stats(r_xog, xo_g[:, s * D:(s + 1) * D], sums_g[:, s:s + 1],
                     sqs_g[:, s:s + 1])
        ln_rstd(sums_g[:, 0:S8], sqs_g[:, 0:S8], 1.0 / D,
                statw_r[:, wg * 8:(wg + 1) * 8], statw_m[:, wg * 8:(wg + 1) * 8],
                r_statwr, r_statwm, 8)

    if do_w:
        w_load_stats(0)
    for wg in range(GW if do_w else 0):
        xo_g, r_xog = xo_b[wg % 2]
        for s in range(S8):
            emitWA1(wg, s)
        emitWAsc()
        for s in range(S8):
            emitWA2(wg, s)
        for s in range(S8):
            emitWB(wg, s)
        if wg + 1 < GW:
            w_load_stats(wg + 1)
        for sl, gres, gsl in ((0, r_gsel0, gsel0), (1, r_gsel1, gsel1)):
            bld.dma("sync", "glo", glo[:, :].rearrange("p (s c) -> p s c", c=512),
                    go_r[sl, 0, wg], [r_glo, r_gout], [r_glo])
            bld.ts("vector", gsl[:, :], glo[:, :], MLO, None, ALU.mult, ALU.bypass,
                   [r_glo, r_pcsc], [gres])
            bld.dma("sync", "glo", glo[:, :].rearrange("p (s c) -> p s c", c=512),
                    go_r[sl, 1, wg], [r_glo, r_gout], [r_glo])
            bld.stt("vector", gsl[:, :], glo[:, :], MHI, gsl[:, :], ALU.mult, ALU.add,
                    [r_glo, r_pcsc, gres], [gres])
        for s in range(S8):
            emitWC(wg, s)
        ln_rstd(sum1_g[:, 0:S8], sq1_g[:, 0:S8], 1.0 / D,
                stat1_r[:, wg * 8:(wg + 1) * 8], stat1_m[:, wg * 8:(wg + 1) * 8],
                r_stat1r, r_stat1m, 8, rs=r_sum1g, rq=r_sq1g)
    if debug and do_w:
        bld.dma("gpsimd", "dbg", dbg["d_x1"].ap().rearrange("p (t d) -> p t d", d=D),
                x1_d.ap().rearrange("(t p) d -> p t d", p=128), [r_x1], [Res()])

    bld.barrier()

    st_w.close()
    st_lw.close()

    xn1, r_xn1 = sb("xn1", [128, D], BF16, st_l1)
    x1_st2 = [sb(f"x1_st{i}", [128, 4 * D], BF16, st_l1) for i in range(2)]
    xn1T, r_xn1T = sb("xn1T", [128, DC * TS], BF16, st_l1)
    hre_all, r_hre = sb("hre_all", [128, HALF], BF16, st_l1)
    cre, r_cre = sb("cre", [128, TS], F32, st_l1)
    cim, r_cim = sb("cim", [128, TS], F32, st_l1)
    zre, r_zre = sb("zre", [128, TS], F32, st_l1)
    zim, r_zim = sb("zim", [128, TS], F32, st_l1)
    tl1, r_tl1 = sb("tl1", [128, TS], F32, st_l1)
    tl2, r_tl2 = sb("tl2", [128, TS], F32, st_l1)
    eloc, r_eloc = sb("eloc", [128, 8], F32, st_l1)
    hidT, r_hidT = sb("hidT", [128, HB * TS], BF16, st_l1)
    wdt_s, r_wdts = sb("wdt_s", [128, HB * D], BF16, st_l1)
    wgsl, r_wgsl = sb("wgsl", [128, 2 * DC * 128], BF16, st_l1)
    wusl, r_wusl = sb("wusl", [128, 2 * DC * 128], BF16, st_l1)
    y_bf, r_ybf = sb("y_bf", [128, 4 * D], BF16, st_l1)
    g2_bf, r_g2bf = sb("g2_bf", [128, D], BF16, st_l1)
    out_f, r_outf = sb("out_f", [128, D], F32, st_l1)

    def xn1_load(st):
        x1_st, r_x1st = x1_st2[st % 2]
        bld.dma("sync", "x1l%d" % (st % 2),
                x1_st[:, :].rearrange("p (t d) -> p t d", d=D),
                x1_dr[st * 4:(st + 1) * 4].rearrange("t p d -> p t d"),
                [r_x1, r_x1st], [r_x1st])

    def xn1_transpose(st):
        x1_st, r_x1st = x1_st2[st % 2]
        for sub in range(4):
            ti = st * 4 + sub
            bld.act(xn1[:, :], x1_st[:, sub * D:(sub + 1) * D], AF.Identity,
                    [r_x1st, r_stat1r, r_stat1m], [r_xn1],
                    scale=stat1_r[:, ti:ti + 1], bias=stat1_m[:, ti:ti + 1])
            for c in range(DC):
                transpose128(xn1[:, c * 128:(c + 1) * 128], r_xn1,
                             xn1T[:, c * TS + sub * 128: c * TS + (sub + 1) * 128],
                             r_xn1T, eng=("vector" if (sub + c) % 2 else "scalar"))

    bld.op("vector", lambda h: h.memset(eloc[:, 2:4], 0.0), (), (r_eloc,))
    n_l1a = (NST if do_l1 else 0) if L1_SUB >= 2 else 0
    if n_l1a:
        xn1_load(0)
    for st in range(n_l1a):
        xn1_transpose(st)
        if st + 1 < n_l1a:
            xn1_load(st + 1)
        b0_, b1_ = (5, 6) if st % 2 == 0 else (7, 3)
        for c in range(DC):
            first, last = c == 0, c == DC - 1
            bld.mm(RB[b0_], PB[b0_][:, 0:512], bdt_re[:, c * 128:(c + 1) * 128],
                   xn1T[:, c * TS:(c + 1) * TS], first, last,
                   [r_bdtre, r_xn1T], first=first, last=last)
            bld.mm(RB[b1_], PB[b1_][:, 0:512], bdt_im[:, c * 128:(c + 1) * 128],
                   xn1T[:, c * TS:(c + 1) * TS], first, last,
                   [r_bdtim, r_xn1T], first=first, last=last)
        bld.tt("vector", tl1[:, :], PB[b0_][:, 0:512], pm_re[:, :], ALU.mult,
               [RB[b0_], r_pmre], [r_tl1])
        bld.tt("vector", tl2[:, :], PB[b1_][:, 0:512], pm_im[:, :], ALU.mult,
               [RB[b1_], r_pmim], [r_tl2])
        bld.tt("vector", cre[:, :], tl1[:, :], tl2[:, :], ALU.subtract,
               [r_tl1, r_tl2], [r_cre])
        bld.tt("vector", tl1[:, :], PB[b1_][:, 0:512], pm_re[:, :], ALU.mult,
               [RB[b1_], r_pmre], [r_tl1])
        bld.tt("vector", tl2[:, :], PB[b0_][:, 0:512], pm_im[:, :], ALU.mult,
               [RB[b0_], r_pmim], [r_tl2])
        bld.tt("vector", cim[:, :], tl1[:, :], tl2[:, :], ALU.add,
               [r_tl1, r_tl2], [r_cim])
        bld.op("vector", lambda h: h.tensor_tensor_scan(
            zre[:, :], ones512[:, :], cre[:, :], eloc[:, 2:3], ALU.mult, ALU.add),
            [r_ones, r_cre, r_eloc], [r_zre])
        bld.op("vector", lambda h: h.tensor_tensor_scan(
            zim[:, :], ones512[:, :], cim[:, :], eloc[:, 3:4], ALU.mult, ALU.add),
            [r_ones, r_cim, r_eloc], [r_zim])
        bld.tt("vector", tl1[:, :], zre[:, :], pp_re[:, :], ALU.mult,
               [r_zre, r_ppre], [r_tl1])
        bld.tt("vector", tl2[:, :], zim[:, :], pp_im[:, :], ALU.mult,
               [r_zim, r_ppim], [r_tl2])
        bld.tt("vector", hre_all[:, st * TS:(st + 1) * TS], tl1[:, :], tl2[:, :],
               ALU.subtract, [r_tl1, r_tl2], [r_hre])
        bld.tt("vector", scr[:, 0:1], pp_re[:, 511:512], zre[:, 511:512], ALU.mult,
               [r_ppre, r_zre], [r_scr])
        bld.tt("vector", scr[:, 1:2], pp_im[:, 511:512], zim[:, 511:512], ALU.mult,
               [r_ppim, r_zim], [r_scr])
        bld.tt("vector", eloc[:, 0:1], scr[:, 0:1], scr[:, 1:2], ALU.subtract,
               [r_scr], [r_eloc])
        bld.tt("vector", scr[:, 0:1], pp_re[:, 511:512], zim[:, 511:512], ALU.mult,
               [r_ppre, r_zim], [r_scr])
        bld.tt("vector", scr[:, 1:2], pp_im[:, 511:512], zre[:, 511:512], ALU.mult,
               [r_ppim, r_zre], [r_scr])
        bld.tt("vector", eloc[:, 1:2], scr[:, 0:1], scr[:, 1:2], ALU.add,
               [r_scr], [r_eloc])
        bld.tt("vector", scr[:, 0:1], are_c[:, :], eloc[:, 0:1], ALU.mult,
               [r_are, r_eloc], [r_scr])
        bld.tt("vector", scr[:, 1:2], aim_c[:, :], eloc[:, 1:2], ALU.mult,
               [r_aim, r_eloc], [r_scr])
        bld.tt("vector", scr[:, 2:3], are_c[:, :], eloc[:, 1:2], ALU.mult,
               [r_are, r_eloc], [r_scr])
        bld.tt("vector", scr[:, 3:4], aim_c[:, :], eloc[:, 0:1], ALU.mult,
               [r_aim, r_eloc], [r_scr])
        bld.tt("vector", eloc[:, 2:3], scr[:, 0:1], scr[:, 1:2], ALU.subtract,
               [r_scr], [r_eloc])
        bld.tt("vector", eloc[:, 3:4], scr[:, 2:3], scr[:, 3:4], ALU.add,
               [r_scr], [r_eloc])

    if not do_l1 or L1_SUB < 3:
        bld.barrier()
        _finalize(nc, bld, stack, st_l1)
        return nc
    if debug:
        bld.dma("gpsimd", "dbg", dbg["d_hre"].ap(), hre_all[:, :], [r_hre], [Res()])

    wgt_r = wgt_d.ap().rearrange("(m k p) h -> m p k h", k=DC, p=128)
    wut_r = wut_d.ap().rearrange("(m k p) h -> m p k h", k=DC, p=128)
    wdt_rr = wdt_d.ap().rearrange("(m p) d -> p m d", p=128)
    n_l1b = NST if L1_SUB >= 4 else 0
    if n_l1b:
        bld.dma("sync", "wdt", wdt_s[:, :].rearrange("p (m d) -> p m d", d=D),
                wdt_rr, [r_wdt], [r_wdts])
        xn1_load(0)
    for st in range(n_l1b):
        xn1_transpose(st)
        for m in range(HB):
            mb = m % 2
            bld.dma("sync", "wg%d" % mb,
                    wgsl[:, mb * DC * 128:(mb + 1) * DC * 128].rearrange(
                        "p (k h) -> p k h", h=128), wgt_r[m], [r_wgt], [r_wgsl])
            bld.dma("sync", "wu%d" % mb,
                    wusl[:, mb * DC * 128:(mb + 1) * DC * 128].rearrange(
                        "p (k h) -> p k h", h=128), wut_r[m], [r_wut], [r_wusl])
            g0_, g1_ = (5, 6) if m % 2 == 0 else (7, 3)
            for c in range(DC):
                first, last = c == 0, c == DC - 1
                bld.mm(RB[g0_], PB[g0_][:, 0:512],
                       wgsl[:, mb * DC * 128 + c * 128: mb * DC * 128 + (c + 1) * 128],
                       xn1T[:, c * TS:(c + 1) * TS], first, last,
                       [r_wgsl, r_xn1T], first=first, last=last)
                bld.mm(RB[g1_], PB[g1_][:, 0:512],
                       wusl[:, mb * DC * 128 + c * 128: mb * DC * 128 + (c + 1) * 128],
                       xn1T[:, c * TS:(c + 1) * TS], first, last,
                       [r_wusl, r_xn1T], first=first, last=last)
            bld.act(trash[:, 512 * (m % 2):512 * (m % 2) + 512], PB[g0_][:, 0:512],
                    AF.Silu, [RB[g0_], r_kc], [r_trash], scale=kconst[:, 3:4])
            bld.tt("vector", hidT[:, m * TS:(m + 1) * TS], PB[g1_][:, 0:512],
                   trash[:, 512 * (m % 2):512 * (m % 2) + 512], ALU.mult,
                   [RB[g1_], r_trash], [r_hidT])
        if st == 0:
            bld.dma("sync", "ein", e_in.ap(), eloc[:, 0:2], [r_eloc], [r_ein])
            bld.op("gpsimd", lambda h: h.collective_compute(
                "AllGather", ALU.bypass,
                replica_groups=[[0, 1], [2, 3], [4, 5], [6, 7]],
                ins=[e_in.ap()], outs=[e_out.ap()]), [r_ein], [r_eout])
            bld.dma("sync", "eout", scr2[:, 0:2], e_out.ap()[0:128, :], [r_eout], [r_scr2])
            bld.ts("vector", eloc[:, 4:5], scr2[:, 0:1], ZMK, None, ALU.mult, ALU.bypass,
                   [r_scr2, r_pcsc], [r_eloc])
            bld.ts("vector", eloc[:, 5:6], scr2[:, 1:2], ZMK, None, ALU.mult, ALU.bypass,
                   [r_scr2, r_pcsc], [r_eloc])
            bld.tt("vector", scr[:, 0:1], pp_re[:, 511:512], are_c[:, :], ALU.mult,
                   [r_ppre, r_are], [r_scr])
            bld.tt("vector", scr[:, 1:2], pp_im[:, 511:512], aim_c[:, :], ALU.mult,
                   [r_ppim, r_aim], [r_scr])
            bld.tt("vector", scr[:, 4:5], scr[:, 0:1], scr[:, 1:2], ALU.subtract,
                   [r_scr], [r_scr])
            bld.tt("vector", scr[:, 0:1], pp_re[:, 511:512], aim_c[:, :], ALU.mult,
                   [r_ppre, r_aim], [r_scr])
            bld.tt("vector", scr[:, 1:2], pp_im[:, 511:512], are_c[:, :], ALU.mult,
                   [r_ppim, r_are], [r_scr])
            bld.tt("vector", scr[:, 5:6], scr[:, 0:1], scr[:, 1:2], ALU.add,
                   [r_scr], [r_scr])
            bld.cp("vector", scr[:, 6:7], are_c[:, :], [r_are], [r_scr])
            bld.cp("vector", scr[:, 7:8], aim_c[:, :], [r_aim], [r_scr])
            for stc in range(NST):
                bld.tt("vector", scr2[:, 0:1], scr[:, 6:7], eloc[:, 4:5], ALU.mult,
                       [r_scr, r_eloc], [r_scr2])
                bld.tt("vector", scr2[:, 1:2], scr[:, 7:8], eloc[:, 5:6], ALU.mult,
                       [r_scr, r_eloc], [r_scr2])
                bld.tt("vector", scr2[:, 4:5], scr2[:, 0:1], scr2[:, 1:2], ALU.subtract,
                       [r_scr2], [r_scr2])
                bld.tt("vector", scr2[:, 0:1], scr[:, 6:7], eloc[:, 5:6], ALU.mult,
                       [r_scr, r_eloc], [r_scr2])
                bld.tt("vector", scr2[:, 1:2], scr[:, 7:8], eloc[:, 4:5], ALU.mult,
                       [r_scr, r_eloc], [r_scr2])
                bld.tt("vector", scr2[:, 5:6], scr2[:, 0:1], scr2[:, 1:2], ALU.add,
                       [r_scr2], [r_scr2])
                bld.ts("vector", scr2[:, 6:7], scr2[:, 5:6], -1.0, None, ALU.mult, ALU.bypass,
                       [r_scr2], [r_scr2])
                hsl = hre_all[:, stc * TS:(stc + 1) * TS]
                bld.stt("vector", hsl, pp_re[:, :], scr2[:, 4:5], hsl, ALU.mult, ALU.add,
                        [r_ppre, r_scr2, r_hre], [r_hre])
                bld.stt("vector", hsl, pp_im[:, :], scr2[:, 6:7], hsl, ALU.mult, ALU.add,
                        [r_ppim, r_scr2, r_hre], [r_hre])
                bld.tt("vector", scr2[:, 0:1], scr[:, 6:7], scr[:, 4:5], ALU.mult,
                       [r_scr], [r_scr2])
                bld.tt("vector", scr2[:, 1:2], scr[:, 7:8], scr[:, 5:6], ALU.mult,
                       [r_scr], [r_scr2])
                bld.tt("vector", scr2[:, 2:3], scr[:, 6:7], scr[:, 5:6], ALU.mult,
                       [r_scr], [r_scr2])
                bld.tt("vector", scr2[:, 3:4], scr[:, 7:8], scr[:, 4:5], ALU.mult,
                       [r_scr], [r_scr2])
                bld.tt("vector", scr[:, 6:7], scr2[:, 0:1], scr2[:, 1:2], ALU.subtract,
                       [r_scr2], [r_scr])
                bld.tt("vector", scr[:, 7:8], scr2[:, 2:3], scr2[:, 3:4], ALU.add,
                       [r_scr2], [r_scr])

        if st + 1 < n_l1b:
            xn1_load(st + 1)
        for sub in range(4):
            ti = st * 4 + sub
            d0_, d1_ = (5, 6) if sub % 2 == 0 else (7, 3)
            y0_, y1_ = (2, 4) if sub % 2 == 0 else (0, 1)
            bld.mm(RB[y0_], PB[y0_][:, 0:512],
                   hre_all[:, ti * 128:(ti + 1) * 128], ct_t[:, 0:512],
                   True, True, [r_hre, r_ct], first=True, last=True)
            bld.mm(RB[y1_], PB[y1_][:, 0:512],
                   hre_all[:, ti * 128:(ti + 1) * 128], ct_t[:, 512:1024],
                   True, True, [r_hre, r_ct], first=True, last=True)
            bld.cp("scalar", y_bf[:, sub * D:sub * D + 512], PB[y0_][:, 0:512],
                   [RB[y0_]], [r_ybf])
            bld.cp("scalar", y_bf[:, sub * D + 512:(sub + 1) * D], PB[y1_][:, 0:512],
                   [RB[y1_]], [r_ybf])
            for m in range(HB):
                first, last = m == 0, m == HB - 1
                lt = hidT[:, m * TS + sub * 128: m * TS + (sub + 1) * 128]
                bld.mm(RB[d0_], PB[d0_][:, 0:512], lt, wdt_s[:, m * D:m * D + 512],
                       first, last, [r_hidT, r_wdts], first=first, last=last)
                bld.mm(RB[d1_], PB[d1_][:, 0:512], lt, wdt_s[:, m * D + 512:(m + 1) * D],
                       first, last, [r_hidT, r_wdts], first=first, last=last)
            bld.act(g2_bf[:, 0:512], PB[d0_][:, 0:512], AF.Silu, [RB[d0_], r_kc],
                    [r_g2bf], scale=kconst[:, 4:5])
            bld.act(g2_bf[:, 512:1024], PB[d1_][:, 0:512], AF.Silu, [RB[d1_], r_kc],
                    [r_g2bf], scale=kconst[:, 4:5])
            bld.tt("vector", out_f[:, 0:512], y_bf[:, sub * D:sub * D + 512],
                   g2_bf[:, 0:512], ALU.mult, [r_ybf, r_g2bf], [r_outf])
            bld.tt("vector", out_f[:, 512:1024],
                   y_bf[:, sub * D + 512:(sub + 1) * D], g2_bf[:, 512:1024],
                   ALU.mult, [r_ybf, r_g2bf], [r_outf])
            x1_stc, r_x1stc = x1_st2[st % 2]
            bld.tt("vector", out_f[:, 0:512], out_f[:, 0:512],
                   x1_stc[:, sub * D:sub * D + 512],
                   ALU.add, [r_outf, r_x1stc], [r_outf])
            bld.tt("vector", out_f[:, 512:1024], out_f[:, 512:1024],
                   x1_stc[:, sub * D + 512:(sub + 1) * D],
                   ALU.add, [r_outf, r_x1stc], [r_outf])
            bld.dma("gpsimd", "out", out_r[st // 2, (st % 2) * 4 + sub], out_f[:, :],
                    [r_outf], [r_out])
        if debug and st == 0:
            bld.dma("gpsimd", "dbg", dbg["d_y"].ap(), y_bf[:, :], [r_ybf], [Res()])

    bld.barrier()
    _finalize(nc, bld, stack, st_l1)
    return nc


def _finalize(nc, bld, stack, st_l1):
    with nc.Block() as blk:
        @blk.tensor
        def _(h):
            for f in bld.q["tensor"]:
                f(h)

        @blk.vector
        def _(h):
            for f in bld.q["vector"]:
                f(h)

        @blk.scalar
        def _(h):
            for f in bld.q["scalar"]:
                f(h)

        @blk.gpsimd
        def _(h):
            for f in bld.q["gpsimd"]:
                f(h)

        @blk.sync
        def _(h):
            for f in bld.q["sync"]:
                f(h)

    st_l1.close()
    stack.close()


def host_inputs(inputs):
    x = np.ascontiguousarray(inputs["L0_attn_hidden_states"], dtype=np.float32)
    Wq = np.asarray(inputs["L0_attn_Wq"], np.float32)
    Wk = np.asarray(inputs["L0_attn_Wk"], np.float32)
    Wv = np.asarray(inputs["L0_attn_Wv"], np.float32)
    Wb = np.asarray(inputs["L0_attn_Wb"], np.float32)
    Wg = np.asarray(inputs["L0_attn_Wg"], np.float32)
    conv_w = np.asarray(inputs["L0_attn_conv_w"], np.float32)
    convw_r = np.ascontiguousarray(
        conv_w.reshape(DC, 128, 4).transpose(1, 0, 2).reshape(128, 32))
    masks = np.zeros((128, 3 * 128), np.float32)
    ii = np.arange(128)
    masks[:, 0:128] = (ii[None, :] < ii[:, None])
    masks[:, 128:256] = (ii[None, :] > ii[:, None])
    masks[:, 256:384] = (ii[None, :] >= ii[:, None])
    bf = ml_dtypes.bfloat16
    common = dict(
        convw=convw_r,
        wo=np.ascontiguousarray(np.asarray(inputs["L0_attn_Wo"], np.float32)),
        normw=np.tile(np.asarray(inputs["L0_attn_norm_w"], np.float32), 2)[None, :],
        packed=np.ascontiguousarray(
            np.asarray(inputs["L0_ffn_packed_weight"], np.int32).reshape(256, D)),
        wscale=np.asarray(inputs["L0_ffn_weight_scale"], np.float32).reshape(1, 1),
        bias=np.asarray(inputs["L0_ffn_bias"], np.float32).reshape(1, D),
        Ar=np.asarray(inputs["L1_attn_A_real"], np.float32).reshape(128, 1),
        Ai=np.asarray(inputs["L1_attn_A_imag"], np.float32).reshape(128, 1),
        Bmat=np.ascontiguousarray(np.asarray(inputs["L1_attn_B"], np.float32)),
        Cmat=np.ascontiguousarray(np.asarray(inputs["L1_attn_C"], np.float32)),
        wg=np.ascontiguousarray(np.asarray(inputs["L1_ffn_w_gate"], np.int32)),
        wu=np.ascontiguousarray(np.asarray(inputs["L1_ffn_w_up"], np.int32)),
        wd=np.ascontiguousarray(np.asarray(inputs["L1_ffn_w_down"], np.int32)),
        idn=np.eye(128).astype(bf),
        masks=masks.astype(bf),
        ones512=np.ones((128, 512)).astype(bf),
        onescol=np.ones((1, 128)).astype(bf),
        kconst=np.tile(np.array([1e-5, 0.5, 1.0 / 256, 1.0 / 127,
                                 1.0 / 127 / 127, 0.0], np.float32), (128, 1)),
    )
    in_maps = []
    for c in range(8):
        b, z = c >> 1, c & 1
        h0 = 2 * z
        wproj = np.concatenate([
            Wq[:, h0 * 128:(h0 + 2) * 128], Wk[:, h0 * 128:(h0 + 2) * 128],
            Wv[:, h0 * 256:(h0 + 2) * 256], Wg[:, h0 * 256:(h0 + 2) * 256],
            Wb[:, h0:h0 + 2]], axis=1)
        pc = np.zeros((128, 4), np.float32)
        pc[:, 0] = 1.0 if z == 0 else 0.0
        pc[:, 1] = 1.0 if z == 1 else 0.0
        pc[:, 2] = 1.0 if z == 1 else 0.0
        m = dict(common)
        m.update(
            x_full=x[b],
            x_own=np.ascontiguousarray(x[b, z * HALF:(z + 1) * HALF]),
            wproj=np.ascontiguousarray(wproj),
            pcsc=pc,
        )
        in_maps.append(m)
    return in_maps


def kernel(**inputs):
    in_maps = host_inputs(inputs)
    nc = build_nc(debug=DEBUG)
    res = run_bass_kernel_spmd(nc, in_maps, core_ids=list(range(8)))
    out = np.empty((B, L, D), np.float32)
    for c in range(8):
        b, z = c >> 1, c & 1
        out[b, z * HALF:(z + 1) * HALF] = res.results[c]["out"]
    return out
